# version 05
# speedup vs baseline: 2.0669x; 1.2812x over previous
import os
os.environ["TRN_ALLOW_ICI_ALL_TO_ALL"] = "1"
import sys
sys.path.insert(0, '/opt/trn_rl_repo')
import math
import numpy as np
import ml_dtypes
import concourse.bass as bass
import concourse.mybir as mybir
from concourse.bass_utils import run_bass_kernel_spmd

F32 = mybir.dt.float32
BF16 = mybir.dt.bfloat16
AF = mybir.ActivationFunctionType
ALU = mybir.AluOpType
AX = mybir.AxisListType

BIGNEG = -3.0e38
N_CORES = 8
B, S, D = 2, 8192, 1024
TOK = 2048
NT = TOK // 128
BLK = 1024
NDC = D // 128

bf16 = ml_dtypes.bfloat16



def blocks_of_core(c):
    p = c % 4
    return (p, 7 - p)


def shard_rows(x2d, c):
    r = c // 4
    b0, b1 = blocks_of_core(c)
    base = r * S
    return np.ascontiguousarray(np.concatenate(
        [x2d[base + b0 * BLK: base + (b0 + 1) * BLK],
         x2d[base + b1 * BLK: base + (b1 + 1) * BLK]], axis=0))


def unshard_rows(shards):
    out = np.empty((B * S,) + shards[0].shape[1:], shards[0].dtype)
    for c in range(N_CORES):
        r = c // 4
        b0, b1 = blocks_of_core(c)
        base = r * S
        out[base + b0 * BLK: base + (b0 + 1) * BLK] = shards[c][:BLK]
        out[base + b1 * BLK: base + (b1 + 1) * BLK] = shards[c][BLK:]
    return out



ENGINES = ("sync", "tensor", "vector", "scalar", "gpsimd")
QUEUE_SEM = {"sync": "dma_s", "gpsimd": "dma_g", "scalar": "dma_c"}


class Tok:
    __slots__ = ("sem", "val")

    def __init__(self, sem, val):
        self.sem = sem
        self.val = val


class Prog:
    def __init__(self, nc):
        self.nc = nc
        self.streams = {e: [] for e in ENGINES}
        self.sems = {}
        self.cnt = {}
        for e in ENGINES:
            self.sems[e] = nc.alloc_semaphore("sem_" + e)
            self.cnt[e] = 0
        self.last_tok = {e: None for e in ENGINES}

    def _dma_sem(self, name):
        key = "dma_" + name
        if key not in self.sems:
            self.sems[key] = self.nc.alloc_semaphore("sem_" + key)
            self.cnt[key] = 0
        return key

    def _wait(self, stream, tok):
        if tok is None:
            return
        h = self.sems[tok.sem]
        v = tok.val
        self.streams[stream].append(lambda eng: eng.wait_ge(h, v))

    def op(self, engine, fn, after=(), chain=True):
        if after is None:
            after = ()
        elif isinstance(after, Tok):
            after = (after,)
        after = tuple(t for t in after if t is not None)
        if chain and engine != "tensor":
            self._wait(engine, self.last_tok[engine])
        for t in after:
            self._wait(engine, t)
        self.cnt[engine] += 1
        v = self.cnt[engine]
        sem_h = self.sems[engine]

        def run(eng):
            fn(eng).then_inc(sem_h, 1)
        self.streams[engine].append(run)
        t = Tok(engine, v)
        self.last_tok[engine] = t
        return t

    def dma(self, queue, out, in_, sem="d", after=(), **kw):
        if after is None:
            after = ()
        elif isinstance(after, Tok):
            after = (after,)
        after = tuple(t for t in after if t is not None)
        q = self._dma_sem(sem + "_" + queue)
        for t in after:
            self._wait(queue, t)
        self.cnt[q] += 16
        v = self.cnt[q]
        sem_h = self.sems[q]

        def run(eng):
            eng.dma_start(out=out, in_=in_, **kw).then_inc(sem_h, 16)
        self.streams[queue].append(run)
        return Tok(q, v)

    def idma(self, queue, out, in_, idx_ap, bound, sem="ig", after=()):
        if after is None:
            after = ()
        elif isinstance(after, Tok):
            after = (after,)
        after = tuple(t for t in after if t is not None)
        q = self._dma_sem(sem + "_" + queue)
        for t in after:
            self._wait(queue, t)
        self.cnt[q] += 16
        v = self.cnt[q]
        sem_h = self.sems[q]

        def run(eng):
            eng.indirect_dma_start(
                out, None, in_,
                IndirectOffsetOnAxis(ap=idx_ap, axis=0),
                bounds_check=bound, oob_is_err=False).then_inc(sem_h, 16)
        self.streams[queue].append(run)
        return Tok(q, v)

    def engine_tok(self, engine):
        return Tok(engine, self.cnt[engine])

    def finalize(self, final_toks):
        for t in final_toks:
            self._wait("sync", t)
        with self.nc.Block() as block:
            for e in ENGINES:
                stream = self.streams[e]
                if not stream:
                    continue

                def body(eng, stream=stream):
                    for f in stream:
                        f(eng)
                getattr(block, e)(body)


class Builder:

    def __init__(self, name):
        nc = bass.Bass("TRN2", target_bir_lowering=False, name=name)
        self.nc = nc
        self.p = Prog(nc)
        self.psum = [nc.alloc_psum_tensor(f"ps{i}", [128, 512], F32)
                     for i in range(8)]
        self.psum_free = [None] * 8
        self.inputs = {}
        self.outputs = {}
        self.consts = {}
        self._ident = None
        self._identf = None
        self.pt_free = [None, None]
        self.sc_col = nc.alloc_sbuf_tensor("sc_col", [128, 4], F32)
        self.bf_col = nc.alloc_sbuf_tensor("bf_col", [128, 64], BF16)

    def inp(self, name, shape, dtype=F32):
        t = self.nc.dram_tensor(name, list(shape), dtype, kind="ExternalInput").ap()
        self.inputs[name] = t
        return t

    def outp(self, name, shape, dtype=F32):
        t = self.nc.dram_tensor(name, list(shape), dtype, kind="ExternalOutput").ap()
        self.outputs[name] = t
        return t

    def const(self, name, arr):
        dt = {np.dtype(np.float32): F32, np.dtype(bf16): BF16,
              np.dtype(np.int32): mybir.dt.int32}[arr.dtype]
        t = self.nc.dram_tensor(name, list(arr.shape), dt, kind="ExternalInput").ap()
        self.consts[name] = np.ascontiguousarray(arr)
        return t

    def sb(self, name, shape, dtype=F32):
        return self.nc.alloc_sbuf_tensor(name, list(shape), dtype)

    def ident(self):
        if self._ident is None:
            dr = self.const("identc", np.eye(128, dtype=bf16))
            sb = self.sb("ident_sb", [128, 128], BF16)
            tk = self.p.dma("sync", sb[:], dr, sem="ident")
            self._ident = (sb, tk)
        return self._ident

    def identf(self):
        if self._identf is None:
            dr = self.const("identfc", np.eye(128, dtype=np.float32))
            sb = self.sb("identf_sbuf", [128, 128], F32)
            tk = self.p.dma("sync", sb[:], dr, sem="ident")
            self._identf = (sb, tk)
        return self._identf




def emit_ln_stats(b, sums, sumsq, mu, rstd, tmp, dcols, eps=1e-5, after=()):
    p = b.p
    inv = 1.0 / dcols
    p.op("vector", lambda v: v.tensor_scalar(mu, sums, inv, None, ALU.mult),
         after=after)
    p.op("vector", lambda v: v.tensor_mul(tmp, mu, mu))
    p.op("vector", lambda v: v.tensor_scalar(sums, sumsq, inv, eps, ALU.mult, ALU.add))
    p.op("vector", lambda v: v.tensor_sub(tmp, sums, tmp))
    tv = p.op("scalar", lambda s: s.activation(sums, tmp, AF.Sqrt),
              after=b.p.engine_tok("vector"))
    tk = p.op("vector", lambda v: v.reciprocal(rstd, sums), after=tv)
    return tk


def emit_ln(b, x_sb, xh_sb, scratch, stats, x_tok=None, ntiles=NT, dcols=D):
    p = b.p
    n = ntiles
    sums = stats[:, 0:n]
    sumsq = stats[:, n:2 * n]
    mu = stats[:, 2 * n:3 * n]
    rstd = stats[:, 3 * n:4 * n]
    tmp = stats[:, 4 * n:5 * n]
    last_sq = None
    for t in range(n):
        xt = x_sb[:, t * dcols:(t + 1) * dcols]
        last_sq = p.op("scalar", lambda s, xt=xt, t=t: s.activation(
            scratch[:], xt, AF.Square, accum_out=sumsq[:, t:t + 1]),
            after=x_tok)
        p.op("vector", lambda v, xt=xt, t=t: v.reduce_sum(
            sums[:, t:t + 1], xt, axis=AX.X), after=x_tok, chain=False)
    tk = emit_ln_stats(b, sums, sumsq, mu, rstd, tmp, dcols,
                       after=(last_sq, p.engine_tok("vector")))
    toks = []
    for t in range(n):
        xt = x_sb[:, t * dcols:(t + 1) * dcols]
        ot = xh_sb[:, t * dcols:(t + 1) * dcols]
        toks.append(p.op("vector", lambda v, xt=xt, ot=ot, t=t: v.tensor_scalar(
            ot, xt, mu[:, t:t + 1], rstd[:, t:t + 1], ALU.subtract, ALU.mult),
            after=tk, chain=False))
    return toks, mu, rstd


def emit_transpose(b, pairs, banks=(6, 7)):
    p = b.p
    ident, id_tok = b.ident()
    out_toks = []
    for i, pr in enumerate(pairs):
        src, dst, ready = pr[0], pr[1], pr[2]
        evac_after = pr[3] if len(pr) > 3 else None
        bk = banks[i % len(banks)]
        ps = b.psum[bk].bitcast(BF16)
        w = src.shape[-1]
        pc = src.shape[0]
        after = [id_tok]
        if ready is not None:
            after.append(ready)
        if b.psum_free[bk] is not None:
            after.append(b.psum_free[bk])
        tr = p.op("tensor", lambda t, src=src, ps=ps, w=w, pc=pc: t.transpose(
            ps[0:w, 0:pc], src, ident[0:pc, 0:pc]), after=after)
        ev = p.op("vector", lambda v, dst=dst, ps=ps, w=w, pc=pc: v.tensor_copy(
            dst, ps[0:w, 0:pc]), after=[tr, evac_after], chain=False)
        b.psum_free[bk] = ev
        out_toks.append(ev)
    return out_toks


def emit_tok_to_fm(b, src_tok, dst_fm, ready, ntiles=NT, dcols=D, banks=(6, 7)):
    if isinstance(ready, Tok) or ready is None:
        ready = [ready] * ntiles
    pairs = []
    tokw = ntiles * 128
    for t in range(ntiles):
        for dc in range(dcols // 128):
            src = src_tok[:, t * dcols + dc * 128: t * dcols + (dc + 1) * 128]
            dst = dst_fm[:, dc * tokw + t * 128: dc * tokw + (t + 1) * 128]
            pairs.append((src, dst, ready[t]))
    return emit_transpose(b, pairs, banks=banks)


def ln_to_fm(b, x_sb, xh2_sb, xhT_sb, scratch, stats, x_tok=None, ntiles=NT,
             dcols=D, banks=(6, 7)):
    p = b.p
    n = ntiles
    sums = stats[:, 0:n]
    sumsq = stats[:, n:2 * n]
    mu = stats[:, 2 * n:3 * n]
    rstd = stats[:, 3 * n:4 * n]
    tmp = stats[:, 4 * n:5 * n]
    last_sq = None
    for t in range(n):
        xt = x_sb[:, t * dcols:(t + 1) * dcols]
        last_sq = p.op("scalar", lambda s, xt=xt, t=t: s.activation(
            scratch[:, 0:dcols], xt, AF.Square, accum_out=sumsq[:, t:t + 1]),
            after=x_tok)
        p.op("vector", lambda v, xt=xt, t=t: v.reduce_sum(
            sums[:, t:t + 1], xt, axis=AX.X), after=x_tok, chain=False)
    tk = emit_ln_stats(b, sums, sumsq, mu, rstd, tmp, dcols,
                       after=(last_sq, p.engine_tok("vector")))
    tokw = n * 128
    slot_free = [None, None]
    for t in range(n):
        xt = x_sb[:, t * dcols:(t + 1) * dcols]
        slot = xh2_sb[:, (t % 2) * dcols:(t % 2 + 1) * dcols]
        ap = p.op("vector", lambda v, xt=xt, slot=slot, t=t: v.tensor_scalar(
            slot, xt, mu[:, t:t + 1], rstd[:, t:t + 1], ALU.subtract, ALU.mult),
            after=[tk, slot_free[t % 2]], chain=False)
        pairs = []
        for dc in range(dcols // 128):
            src = slot[:, dc * 128:(dc + 1) * 128]
            dst = xhT_sb[:, dc * tokw + t * 128: dc * tokw + (t + 1) * 128]
            pairs.append((src, dst, ap))
        evs = emit_transpose(b, pairs, banks=banks)
        slot_free[t % 2] = p.engine_tok("tensor")
    return p.engine_tok("vector")



DFF = 2816
NFC = DFF // 128


def build_ffn0():
    b = Builder("ffn0")
    p = b.p
    xin = b.inp("xin", [TOK, D])
    w1 = b.inp("w1", [2, D, DFF])
    w3 = b.inp("w3", [2, D, DFF])
    w2 = b.inp("w2", [2, DFF, D])
    xout = b.outp("xout", [TOK, D])
    if False:
        dbg_s = b.outp("dbg_s", [TOK, NE])
        dbg_we = b.outp("dbg_we", [TOK, NE])
        dbg_v = b.outp("dbg_v", [TOK, D])
        dbg_m = b.outp("dbg_m", [TOK, 13])
        dbg_g = b.outp("dbg_g", [128, 33])
        dbg_r = b.outp("dbg_r", [TOK, 72])
        dbg_c = b.outp("dbg_c", [TOK, 16])
        dbgbuf = b.sb("dbgbuf", [128, 128])

    x_sb = b.sb("x", [128, NT * D])
    xh_sb = b.sb("xh", [128, 2 * D], BF16)
    xhT = b.sb("xhT", [128, NDC * TOK], BF16)
    scratch = b.sb("scr", [128, D])
    stats = b.sb("stats", [128, 5 * NT])
    h_sb = b.sb("h", [128, NFC * 512], BF16)
    w13_sb = [b.sb(f"w13_{i}", [128, 2 * NDC * 128], BF16) for i in range(2)]
    w2_sb = b.sb("w2s", [128, NFC * D], BF16)
    out_sb = b.sb("out", [128, 512])
    silu_sb = b.sb("silu", [128, 512], BF16)

    xld = p.dma("sync", x_sb[:].rearrange("p (t d) -> p t d", d=D),
                xin.rearrange("(t p) d -> p t d", p=128), sem="x")
    ln_to_fm(b, x_sb, xh_sb, xhT, scratch, stats, x_tok=xld)
    fm_done = p.engine_tok("vector")

    xout_r = xout.rearrange("(t p) d -> p t d", p=128)
    w13_free = [None, None]
    h_free = None
    w2_free = None
    hm_prev = None
    final_toks = []
    for e in range(2):
        w2_tok = p.dma("gpsimd", w2_sb[:].rearrange("p (f d) -> p f d", d=D),
                       w2[e].rearrange("(f p) d -> p f d", p=128),
                       sem="w2", after=w2_free)
        for half in range(2):
            tok0 = e * BLK + half * 512
            for fc in range(NFC):
                buf = fc % 2
                wsb = w13_sb[buf]
                wtk1 = p.dma("gpsimd", wsb[:, 0:NDC * 128]
                             .rearrange("p (c f) -> p c f", f=128),
                             w1[e, :, fc * 128:(fc + 1) * 128]
                             .rearrange("(c p) f -> p c f", p=128),
                             sem=f"w13_{buf}", after=w13_free[buf])
                wtk3 = p.dma("gpsimd", wsb[:, NDC * 128:2 * NDC * 128]
                             .rearrange("p (c f) -> p c f", f=128),
                             w3[e, :, fc * 128:(fc + 1) * 128]
                             .rearrange("(c p) f -> p c f", p=128),
                             sem=f"w13_{buf}")
                for which, bank, wtk in ((0, 0, wtk3), (1, 1, wtk3)):
                    wv = wsb[:, which * NDC * 128:(which + 1) * NDC * 128]
                    after0 = [wtk, fm_done, b.psum_free[bank]]
                    for dc in range(NDC):
                        lhsT = wv[:, dc * 128:(dc + 1) * 128]
                        rhs = xhT[:, dc * TOK + tok0: dc * TOK + tok0 + 512]
                        p.op("tensor", lambda t, o=b.psum[bank][:, 0:512],
                             l=lhsT, r=rhs, st=(dc == 0), sp=(dc == NDC - 1):
                             t.matmul(o, l, r, start=st, stop=sp),
                             after=after0 if dc == 0 else ())
                mm = p.engine_tok("tensor")
                w13_free[buf] = mm
                sg = p.op("scalar", lambda s: s.activation(
                    silu_sb[:], b.psum[0][:, 0:512], AF.Silu),
                    after=[mm, hm_prev])
                hv = h_sb[:, fc * 512:(fc + 1) * 512]
                hm = p.op("vector", lambda v, hv=hv: v.tensor_mul(
                    hv, silu_sb[:], b.psum[1][:, 0:512]),
                    after=[sg, mm] + ([h_free] if fc == 0 else []))
                b.psum_free[0] = hm
                b.psum_free[1] = hm
                hm_prev = hm
            h_done = hm_prev
            for tt in range(4):
                for dc2 in range(2):
                    bank = 2 + tt % 2 * 2 + dc2
                    after0 = [w2_tok, h_done, b.psum_free[bank]]
                    for fc in range(NFC):
                        lhsT = h_sb[:, fc * 512 + tt * 128: fc * 512 + (tt + 1) * 128]
                        rhs = w2_sb[:, fc * D + dc2 * 512: fc * D + (dc2 + 1) * 512]
                        p.op("tensor", lambda t, o=b.psum[bank][:, 0:512],
                             l=lhsT, r=rhs, st=(fc == 0), sp=(fc == NFC - 1):
                             t.matmul(o, l, r, start=st, stop=sp),
                             after=after0 if fc == 0 else ())
                    mm = p.engine_tok("tensor")
                    gt = (tok0 + tt * 128) // 128
                    xs = x_sb[:, gt * D + dc2 * 512: gt * D + (dc2 + 1) * 512]
                    ev = p.op("vector", lambda v, xs=xs, bank=bank: v.tensor_add(
                        out_sb[:], b.psum[bank][:, 0:512], xs),
                        after=mm, chain=True)
                    b.psum_free[bank] = ev
                    st = p.dma("sync", xout_r[:, gt, dc2 * 512:(dc2 + 1) * 512],
                               out_sb[:], sem="st", after=ev)
                    final_toks.append(st)
                    p.last_tok["vector"] = st
            h_free = p.engine_tok("tensor")
        w2_free = p.engine_tok("tensor")
    p.finalize(final_toks[-1:])
    return b



NE = 8
NI = 32


def build_ffn1(debug=False):
    b = Builder("ffn1")
    p = b.p
    xin = b.inp("xin", [TOK, D])
    Wv = b.inp("Wv", [D, D])
    bv = b.inp("bv", [D])
    Wg = b.inp("Wg", [NE, D])
    bg = b.inp("bg", [NE])
    Wp = b.inp("Wp", [NE, NE, D])
    alpha = b.inp("alpha", [1])
    g_r = b.inp("g_r", [NE])
    b_r = b.inp("b_r", [NE])
    g_c = b.inp("g_c", [NE])
    b_c = b.inp("b_c", [NE])
    W1 = b.inp("W1", [NE, NI, D])
    b1 = b.inp("b1", [NE, NI])
    W2 = b.inp("W2", [NE, D, NI])
    b2 = b.inp("b2", [NE, D])
    xout = b.outp("xout", [TOK, D])
    if debug:
        dbg_s = b.outp("dbg_s", [TOK, NE])
        dbg_we = b.outp("dbg_we", [TOK, NE])
        dbg_v = b.outp("dbg_v", [TOK, D])
        dbg_m = b.outp("dbg_m", [TOK, 13])
        dbg_g = b.outp("dbg_g", [128, 33])
        dbg_r = b.outp("dbg_r", [TOK, 72])
        dbg_c = b.outp("dbg_c", [TOK, 16])
        dbgbuf = b.sb("dbgbuf", [128, 128])

    x_sb = b.sb("x", [128, NT * D])
    xh_sb = b.sb("xh", [128, 2 * D], BF16)
    xhT = b.sb("xhT", [128, NDC * TOK], BF16)
    VT = b.sb("VT", [128, NDC * TOK], BF16)
    v2 = b.sb("v2", [128, 2 * D], BF16)
    scratch = b.sb("scr", [128, D])
    stats = b.sb("stats", [128, 96])
    WvT = b.sb("WvT", [128, NDC * D], BF16)
    WgpT = b.sb("WgpT", [128, NDC * 80], BF16)
    W1T = b.sb("W1T", [128, NDC * NE * NI], BF16)
    W2nat = b.sb("W2nat", [128, NE * NDC * NI], BF16)
    W2b = b.sb("W2b", [64, NE * D], BF16)
    rows = b.sb("rows", [1, 2 * D + 72 + 256])
    rows_bf = b.sb("rows_bf", [1, D + 80], BF16)
    b1rows = b.sb("b1rows", [1, NE * NI], BF16)
    gains = b.sb("gains", [128, 32])
    sig = b.sb("sig", [128, 1])
    galph = b.sb("galph", [1, 2])
    gate = b.sb("gate", [128, 16])
    we_sb = b.sb("we", [128, NE])
    lncr = b.sb("lncr", [128, 2 * NE])
    hslot = b.sb("hslot", [128, 40], BF16)
    hT = b.sb("hT", [64, 128], BF16)
    ones_bf = b.sb("ones_bf", [1, 128], BF16)
    ones_f = b.sb("ones_f", [1, 128])
    out_sb = b.sb("outs", [128, 512])

    wv_bf = b.nc.dram_tensor("wv_bf", [D, D], BF16, kind="Internal").ap()
    wp_bf = b.nc.dram_tensor("wp_bf", [NE * NE, D], BF16, kind="Internal").ap()
    w1_bf = b.nc.dram_tensor("w1_bf", [NE * NI, D], BF16, kind="Internal").ap()

    xld = p.dma("sync", x_sb[:].rearrange("p (t d) -> p t d", d=D),
                xin.rearrange("(t p) d -> p t d", p=128), sem="x")
    c1 = p.dma("gpsimd", wv_bf, Wv, sem="wcast")
    wg_nat = b.sb("wg_nat", [8, D], BF16)
    c2 = p.dma("gpsimd", wg_nat[:], Wg, sem="wcast")
    c3 = p.dma("gpsimd", wp_bf, Wp.rearrange("e q d -> (e q) d"), sem="wcast")
    c4 = p.dma("gpsimd", w1_bf, W1.rearrange("e i d -> (e i) d"), sem="wcast")
    tts = []
    for dc in range(NDC):
        tts.append(p.dma("sync", WvT[:, dc * D:(dc + 1) * D],
                         wv_bf[:, dc * 128:(dc + 1) * 128], transpose=True,
                         sem="wT", after=c4))
        tts.append(p.dma("sync", WgpT[:, dc * 80 + 16:dc * 80 + 80],
                         wp_bf[:, dc * 128:(dc + 1) * 128], transpose=True,
                         sem="wT"))
        tts.append(p.dma("sync", W1T[:, dc * 256:(dc + 1) * 256],
                         w1_bf[:, dc * 128:(dc + 1) * 128], transpose=True,
                         sem="wT"))
    wT_tok = tts[-1]
    emit_transpose(b, [(wg_nat[0:8, dc * 128:(dc + 1) * 128],
                        WgpT[:, dc * 80:dc * 80 + 8], c4)
                       for dc in range(NDC)], banks=(4, 5))
    wgT_tok = b.p.engine_tok("vector")
    gap0 = p.op("vector", lambda v: v.memset(
        WgpT[:].rearrange("p (c e) -> p c e", e=80)[:, :, 8:16], 0.0))
    w2n_toks = []
    for e in range(NE):
        for dc in range(NDC):
            w2n_toks.append(p.dma(
                "gpsimd", W2nat[:, (e * NDC + dc) * NI:(e * NDC + dc + 1) * NI],
                W2[e, dc * 128:(dc + 1) * 128, :], sem="w2n"))
        p.dma("gpsimd", W2b[32:33, e * D:(e + 1) * D], b2[e:e + 1, :], sem="rows")
    p.dma("gpsimd", b1rows[:], b1.rearrange("e i -> (e i)")
          .rearrange("(o n) -> o n", o=1), sem="rows")
    p.dma("gpsimd", rows_bf[:, 0:D], bv.rearrange("(o n) -> o n", o=1), sem="rows")
    p.dma("gpsimd", rows_bf[:, D:D + 8], bg.rearrange("(o n) -> o n", o=1),
          sem="rows")
    p.dma("sync", gains[:, 0:8], g_r.rearrange("(o n) -> o n", o=1)
          .broadcast_to([128, 8]), sem="rows")
    p.dma("sync", gains[:, 8:16], b_r.rearrange("(o n) -> o n", o=1)
          .broadcast_to([128, 8]), sem="rows")
    p.dma("sync", gains[:, 16:24], g_c.rearrange("(o n) -> o n", o=1)
          .broadcast_to([128, 8]), sem="rows")
    p.dma("sync", gains[:, 24:32], b_c.rearrange("(o n) -> o n", o=1)
          .broadcast_to([128, 8]), sem="rows")
    al_tok = p.dma("sync", rows[:, 32:33], alpha.rearrange("(o n) -> o n", o=1),
                   sem="rows")
    rows_tok = Tok("dma_rows_sync", p.cnt["dma_rows_sync"])
    zpad = p.op("vector", lambda v: v.memset(rows_bf[:, D + 8:D + 80], 0.0))
    onet = p.op("vector", lambda v: v.memset(ones_bf[:], 1.0))
    p.op("vector", lambda v: v.memset(hslot[:, 33:40], 0.0))

    sig_dr = b.nc.dram_tensor("sig_dr", [1, 1], F32, kind="Internal").ap()
    sg_a = p.op("scalar", lambda s: s.activation(galph[:, 0:1], rows[:, 32:33],
                                                 AF.Sigmoid), after=al_tok)
    sd1 = p.dma("sync", sig_dr, galph[:, 0:1], sem="sig", after=sg_a)
    ev_s = p.dma("sync", sig[:], sig_dr.broadcast_to([128, 1]), sem="sig",
                 after=sd1)

    if debug:
        gcop = p.op("vector", lambda v: v.tensor_copy(dbgbuf[:, 0:32],
                                                      gains[:, 0:32]),
                    after=[rows_tok, ev_s])
        gcop2 = p.op("vector", lambda v: v.tensor_copy(dbgbuf[:, 32:33], sig[:]))
        gdump = p.dma("sync", dbg_g, dbgbuf[:, 0:33], sem="dbg", after=gcop2)
        p.last_tok["vector"] = gdump
    ln_to_fm(b, x_sb, xh_sb, xhT, scratch, stats, x_tok=xld)
    fm_done = p.engine_tok("vector")

    for e in range(NE):
        pairs = []
        for dc in range(NDC):
            src = W2nat[:, (e * NDC + dc) * NI:(e * NDC + dc + 1) * NI]
            dst = W2b[0:32, e * D + dc * 128: e * D + (dc + 1) * 128]
            pairs.append((src, dst, w2n_toks[-1]))
        emit_transpose(b, pairs, banks=(4, 5))

    rowsbf_tok = Tok("dma_rows_gpsimd", p.cnt["dma_rows_gpsimd"])
    vfree = {}
    for t in range(NT):
        for half in range(2):
            bank = 2 + half
            after0 = [fm_done, zpad, onet, rowsbf_tok, wT_tok,
                      b.psum_free[bank]]
            for hc in range(NDC):
                lhsT = xhT[:, hc * TOK + t * 128: hc * TOK + (t + 1) * 128]
                rhs = WvT[:, hc * D + half * 512: hc * D + (half + 1) * 512]
                p.op("tensor", lambda tt, o=b.psum[bank][:, 0:512], l=lhsT,
                     r=rhs, st=(hc == 0): tt.matmul(o, l, r, start=st,
                                                    stop=False),
                     after=after0 if hc == 0 else ())
            p.op("tensor", lambda tt, o=b.psum[bank][:, 0:512],
                 r=rows_bf[:, half * 512:(half + 1) * 512]:
                 tt.matmul(o, ones_bf[:], r, start=False, stop=True))
            mm = p.engine_tok("tensor")
            xs = x_sb[:, t * D + half * 512: t * D + (half + 1) * 512]
            ad = p.op("vector", lambda v, xs=xs, bank=bank: v.tensor_add(
                xs, xs, b.psum[bank][:, 0:512]), after=mm)
            vslot = v2[:, (t % 2) * D + half * 512:(t % 2) * D + (half + 1) * 512]
            cv = p.op("scalar", lambda s, vslot=vslot, bank=bank: s.copy(
                vslot, b.psum[bank][:, 0:512]),
                after=[mm, ad, vfree.get((t % 2, half))])
            b.psum_free[bank] = cv
            prs = []
            for q in range(4):
                dc = half * 4 + q
                src = vslot[:, q * 128:(q + 1) * 128]
                dst = VT[:, dc * TOK + t * 128: dc * TOK + (t + 1) * 128]
                prs.append((src, dst, cv))
            emit_transpose(b, prs, banks=(6, 7))
            vfree[(t % 2, half)] = p.engine_tok("tensor")
    vt_done = p.engine_tok("vector")

    xout_r = xout.rearrange("(t p) d -> p t d", p=128)
    final_toks = []
    wet = None
    for t in range(NT):
        after0 = [vt_done, b.psum_free[0]]
        for dc in range(NDC):
            lhsT = VT[:, dc * TOK + t * 128: dc * TOK + (t + 1) * 128]
            rhs = WgpT[:, dc * 80:(dc + 1) * 80]
            p.op("tensor", lambda tt, l=lhsT, r=rhs, st=(dc == 0):
                 tt.matmul(b.psum[0][:, 0:80], l, r, start=st, stop=False),
                 after=after0 if dc == 0 else ())
        p.op("tensor", lambda tt: tt.matmul(
            b.psum[0][:, 0:80], ones_bf[:], rows_bf[:, D:D + 80],
            start=False, stop=True))
        mm = p.engine_tok("tensor")
        sq = p.op("scalar", lambda s: s.activation(scratch[:, 0:64],
                                                   b.psum[0][:, 16:80], AF.Square),
                  after=[mm, wet])
        rc = p.op("vector", lambda v: v.tensor_reduce(
            gate[:, 8:16], scratch[:, 0:64].rearrange("p (e q) -> p e q", q=8),
            axis=AX.X, op=ALU.add),
            after=[sq] + ([rows_tok, ev_s] if t == 0 else []))
        if debug:
            rpc = p.op("vector", lambda v: v.tensor_copy(dbgbuf[:, 33:105],
                                                         b.psum[0][:, 8:80]),
                       after=mm)
            rpd = p.dma("sync", dbg_r.rearrange("(t p) e -> p t e", p=128)[:, t, :],
                        dbgbuf[:, 33:105], sem="dbg", after=rpc)
            p.last_tok["vector"] = rpd
        rcp = p.op("vector", lambda v: v.tensor_copy(gate[:, 0:8],
                                                     b.psum[0][:, 0:8]), after=mm)
        b.psum_free[0] = rcp
        p.op("vector", lambda v: v.tensor_scalar(gate[:, 8:16], gate[:, 8:16],
                                                 0.125, None, ALU.mult))
        cs = p.op("scalar", lambda s: s.activation(gate[:, 8:16], gate[:, 8:16],
                                                   AF.Sqrt),
                  after=p.engine_tok("vector"))
        if debug:
            ccp = p.op("vector", lambda v: v.tensor_copy(dbgbuf[:, 105:121],
                                                         gate[:, 0:16]),
                       after=cs)
            ccd = p.dma("sync", dbg_c.rearrange("(t p) e -> p t e", p=128)[:, t, :],
                        dbgbuf[:, 105:121], sem="dbg", after=ccp)
            p.last_tok["vector"] = ccd
        for k, (src, dst) in enumerate(((gate[:, 0:8], lncr[:, 0:8]),
                                        (gate[:, 8:16], lncr[:, 8:16]))):
            p.op("vector", lambda v, src=src: v.reduce_sum(
                stats[:, 80:81], src, axis=AX.X), after=cs)
            p.op("vector", lambda v, src=src: v.tensor_scalar(
                stats[:, 80:81], stats[:, 80:81], 0.125, None, ALU.mult))
            p.op("vector", lambda v, src=src, dst=dst: v.tensor_scalar(
                dst, src, stats[:, 80:81], None, ALU.subtract))
            p.op("vector", lambda v, dst=dst: v.tensor_mul(
                scratch[:, 64:72], dst, dst))
            p.op("vector", lambda v: v.reduce_sum(
                stats[:, 81:82], scratch[:, 64:72], axis=AX.X))
            p.op("vector", lambda v: v.tensor_scalar(
                stats[:, 81:82], stats[:, 81:82], 0.125, 1e-5, ALU.mult, ALU.add))
            sqv = p.op("scalar", lambda s: s.activation(
                stats[:, 82:83], stats[:, 81:82], AF.Sqrt),
                after=p.engine_tok("vector"))
            p.op("vector", lambda v: v.reciprocal(stats[:, 83:84],
                                                  stats[:, 82:83]), after=sqv)
            p.op("vector", lambda v, dst=dst: v.tensor_scalar(
                dst, dst, stats[:, 83:84], None, ALU.mult))
        p.op("vector", lambda v: v.tensor_mul(lncr[:, 8:16], lncr[:, 8:16],
                                              gains[:, 16:24]))
        p.op("vector", lambda v: v.tensor_add(lncr[:, 8:16], lncr[:, 8:16],
                                              gains[:, 24:32]))
        p.op("vector", lambda v: v.tensor_scalar(lncr[:, 8:16], lncr[:, 8:16],
                                                 sig[:, 0:1], None, ALU.mult))
        p.op("vector", lambda v: v.tensor_mul(lncr[:, 0:8], lncr[:, 0:8],
                                              gains[:, 0:8]))
        p.op("vector", lambda v: v.tensor_add(lncr[:, 0:8], lncr[:, 0:8],
                                              gains[:, 8:16]))
        sdone = p.op("vector", lambda v: v.tensor_add(gate[:, 0:8], lncr[:, 0:8],
                                              lncr[:, 8:16]))
        if debug:
            scop = p.op("vector", lambda v: v.tensor_copy(stats[:, 84:92],
                                                          gate[:, 0:8]))
            p.dma("sync", dbg_s.rearrange("(t p) e -> p t e", p=128)[:, t, :],
                  stats[:, 84:92], sem="dbg", after=scop)
        p.op("vector", lambda v: v.reduce_max(gate[:, 8:9], gate[:, 0:8],
                                              axis=AX.X))
        p.op("vector", lambda v: v.tensor_scalar(we_sb[:], gate[:, 0:8],
                                                 gate[:, 8:9], None, ALU.is_equal))
        p.op("vector", lambda v: v.scalar_tensor_tensor(
            gate[:, 0:8], we_sb[:], BIGNEG, gate[:, 0:8], ALU.mult, ALU.add))
        p.op("vector", lambda v: v.reduce_max(gate[:, 9:10], gate[:, 0:8],
                                              axis=AX.X))
        p.op("vector", lambda v: v.tensor_scalar(gate[:, 10:11], gate[:, 8:9],
                                                 gate[:, 9:10], None,
                                                 ALU.subtract))
        sgw = p.op("scalar", lambda s: s.activation(gate[:, 11:12], gate[:, 10:11],
                                                    AF.Sigmoid),
                   after=p.engine_tok("vector"))
        p.op("vector", lambda v: v.tensor_scalar(gate[:, 12:13], gate[:, 11:12],
                                                 -1.0, 1.0, ALU.mult, ALU.add),
             after=sgw)
        p.op("vector", lambda v: v.tensor_scalar(scratch[:, 0:8], gate[:, 0:8],
                                                 gate[:, 9:10], None, ALU.is_equal))
        p.op("vector", lambda v: v.tensor_scalar(we_sb[:], we_sb[:],
                                                 gate[:, 11:12], None, ALU.mult))
        wet = p.op("vector", lambda v: v.scalar_tensor_tensor(
            we_sb[:], scratch[:, 0:8], gate[:, 12:13], we_sb[:],
            ALU.mult, ALU.add))
        if debug:
            wcop = p.op("vector", lambda v: v.tensor_copy(stats[:, 64:77],
                                                          gate[:, 0:13]))
            p.dma("sync", dbg_m.rearrange("(t p) e -> p t e", p=128)[:, t, :],
                  stats[:, 64:77], sem="dbg", after=wcop)
            p.dma("sync", dbg_we.rearrange("(t p) e -> p t e", p=128)[:, t, :],
                  we_sb[:], sem="dbg", after=wet)
            p.dma("sync", dbg_v.rearrange("(t p) d -> p t d", p=128)[:, t, :],
                  x_sb[:, t * D:(t + 1) * D], sem="dbg", after=wet)
        tr_prev = None
        dmm_prev = None
        for e in range(NE):
            hafter = [wet, b.psum_free[1]]
            for dc in range(NDC):
                lhsT = VT[:, dc * TOK + t * 128: dc * TOK + (t + 1) * 128]
                rhs = W1T[:, dc * 256 + e * NI: dc * 256 + (e + 1) * NI]
                p.op("tensor", lambda tt, l=lhsT, r=rhs, st=(dc == 0):
                     tt.matmul(b.psum[1][:, 0:NI], l, r, start=st, stop=False),
                     after=hafter if dc == 0 else ())
            p.op("tensor", lambda tt, e=e: tt.matmul(
                b.psum[1][:, 0:NI], ones_bf[:],
                b1rows[:, e * NI:(e + 1) * NI], start=False, stop=True))
            hmm = p.engine_tok("tensor")
            hsl = p.op("scalar", lambda s: s.activation(hslot[:, 0:NI],
                                                        b.psum[1][:, 0:NI],
                                                        AF.Silu),
                       after=[hmm, tr_prev])
            b.psum_free[1] = hsl
            hsc = p.op("vector", lambda v, e=e: v.tensor_scalar(
                hslot[:, 0:NI], hslot[:, 0:NI], we_sb[:, e:e + 1], None,
                ALU.mult), after=hsl)
            wcp = p.op("vector", lambda v, e=e: v.tensor_copy(
                hslot[:, 32:33], we_sb[:, e:e + 1]))
            tr = emit_transpose(b, [(hslot[:, 0:40], hT[0:40, :], wcp,
                                     dmm_prev)], banks=(6,))[0]
            tr_prev = p.engine_tok("tensor")
            for dc2 in range(2):
                bank = 2 + dc2
                dafter = [tr, b.psum_free[bank]] if e == 0 else [tr]
                p.op("tensor", lambda tt, e=e, dc2=dc2, bank=bank,
                     st=(e == 0), sp=(e == NE - 1): tt.matmul(
                         b.psum[bank][:, 0:512], hT[0:33, :],
                         W2b[0:33, e * D + dc2 * 512: e * D + (dc2 + 1) * 512],
                         start=st, stop=sp),
                     after=dafter)
            dmm_prev = p.engine_tok("tensor")
        dmm = p.engine_tok("tensor")
        for dc2 in range(2):
            bank = 2 + dc2
            xs = x_sb[:, t * D + dc2 * 512: t * D + (dc2 + 1) * 512]
            ev = p.op("vector", lambda v, xs=xs, bank=bank: v.tensor_add(
                out_sb[:], b.psum[bank][:, 0:512], xs), after=dmm)
            b.psum_free[bank] = ev
            st = p.dma("sync", xout_r[:, t, dc2 * 512:(dc2 + 1) * 512],
                       out_sb[:], sem="st", after=ev)
            final_toks.append(st)
            p.last_tok["vector"] = st
    p.finalize(final_toks[-1:])
    return b



DC_KV = 256
NH = 16
DK = 64


def shard_att(x2d, c):
    r, r4 = c // 4, c % 4
    v = x2d[r * S:(r + 1) * S].reshape(S // 128, 128, -1)[r4::4]
    return np.ascontiguousarray(v.reshape(TOK, -1))


def unshard_att(shards):
    out = np.empty((B * S,) + shards[0].shape[1:], shards[0].dtype)
    ov = out.reshape(B, S // 128, 128, -1)
    for c in range(N_CORES):
        r, r4 = c // 4, c % 4
        ov[r, r4::4] = shards[c].reshape(NT, 128, -1)
    return out


def att_mask(c):
    r4 = c % 4
    col = np.arange(512)[None, :]
    row = np.arange(128)[:, None]
    return np.where(col - 128 * r4 > row, np.float32(BIGNEG),
                    np.float32(0.0)).astype(np.float32)


def build_l0a_pre():
    b = Builder("l0a_pre")
    p = b.p
    xin = b.inp("xin", [TOK, D])
    W_dq = b.inp("W_dq", [D, DC_KV])
    W_uq = b.inp("W_uq", [DC_KV, D])
    W_dkv = b.inp("W_dkv", [D, DC_KV])
    qt_o = b.outp("qt", [D, TOK], BF16)
    ckv_o = b.outp("ckv", [TOK, DC_KV], BF16)

    x_sb = b.sb("x", [128, NT * D])
    xh_sb = b.sb("xh", [128, 2 * D], BF16)
    xhT = b.sb("xhT", [128, NDC * TOK], BF16)
    scratch = b.sb("scr", [128, D])
    stats = b.sb("stats", [128, 5 * NT])
    wdq = b.sb("wdq", [128, NDC * DC_KV], BF16)
    wuq = b.sb("wuq", [128, 2 * D], BF16)
    wdkv = b.sb("wdkv", [128, NDC * DC_KV], BF16)
    cqT = b.sb("cqT", [128, 2 * TOK], BF16)
    ev_sb = b.sb("ev", [128, 512], BF16)

    xld = p.dma("sync", x_sb[:].rearrange("p (t d) -> p t d", d=D),
                xin.rearrange("(t p) d -> p t d", p=128), sem="x")
    w1t = p.dma("gpsimd", wdq[:].rearrange("p (c n) -> p c n", n=DC_KV),
                W_dq.rearrange("(c p) n -> p c n", p=128), sem="w")
    w2t = p.dma("gpsimd", wuq[:].rearrange("p (c n) -> p c n", n=D),
                W_uq.rearrange("(c p) n -> p c n", p=128), sem="w")
    w3t = p.dma("gpsimd", wdkv[:].rearrange("p (c n) -> p c n", n=DC_KV),
                W_dkv.rearrange("(c p) n -> p c n", p=128), sem="w")
    wtok = w3t
    ln_to_fm(b, x_sb, xh_sb, xhT, scratch, stats, x_tok=xld)
    fm_done = p.engine_tok("vector")

    qt_r = qt_o.rearrange("(c p) n -> p c n", p=128)
    ckv_r = ckv_o.rearrange("(t p) n -> p t n", p=128)
    final = []
    for cc in range(2):
        for s4 in range(4):
            bank = s4 % 2
            after0 = [wtok, fm_done, b.psum_free[bank]]
            for dc in range(NDC):
                p.op("tensor", lambda t, cc=cc, s4=s4, dc=dc, bank=bank,
                     st=(dc == 0), sp=(dc == NDC - 1): t.matmul(
                         b.psum[bank][:, 0:512],
                         wdq[:, dc * DC_KV + cc * 128: dc * DC_KV + (cc + 1) * 128],
                         xhT[:, dc * TOK + s4 * 512: dc * TOK + (s4 + 1) * 512],
                         start=st, stop=sp), after=after0 if dc == 0 else ())
            mm = p.engine_tok("tensor")
            ev = p.op("vector", lambda v, cc=cc, s4=s4, bank=bank: v.tensor_copy(
                cqT[:, cc * TOK + s4 * 512: cc * TOK + (s4 + 1) * 512],
                b.psum[bank][:, 0:512]), after=mm)
            b.psum_free[bank] = ev
    cq_done = p.engine_tok("vector")
    for hc in range(NDC):
        for s4 in range(4):
            bank = 2 + s4 % 2
            after0 = [cq_done, b.psum_free[bank]]
            for cc in range(2):
                p.op("tensor", lambda t, hc=hc, s4=s4, cc=cc, bank=bank,
                     st=(cc == 0), sp=(cc == 1): t.matmul(
                         b.psum[bank][:, 0:512],
                         wuq[:, cc * D + hc * 128: cc * D + (hc + 1) * 128],
                         cqT[:, cc * TOK + s4 * 512: cc * TOK + (s4 + 1) * 512],
                         start=st, stop=sp), after=after0 if cc == 0 else ())
            mm = p.engine_tok("tensor")
            ev = p.op("vector", lambda v, bank=bank: v.tensor_copy(
                ev_sb[:], b.psum[bank][:, 0:512]), after=mm)
            b.psum_free[bank] = ev
            st = p.dma("sync", qt_r[:, hc, s4 * 512:(s4 + 1) * 512], ev_sb[:],
                       sem="st", after=ev)
            final.append(st)
            p.last_tok["vector"] = st
    ckv_sb = b.sb("ckvs", [128, 2 * DC_KV], BF16)
    for t in range(NT):
        bank = 4 + t % 2
        after0 = [wtok, fm_done, b.psum_free[bank]]
        for dc in range(NDC):
            p.op("tensor", lambda tt, t=t, dc=dc, bank=bank,
                 st=(dc == 0), sp=(dc == NDC - 1): tt.matmul(
                     b.psum[bank][:, 0:DC_KV],
                     xhT[:, dc * TOK + t * 128: dc * TOK + (t + 1) * 128],
                     wdkv[:, dc * DC_KV:(dc + 1) * DC_KV],
                     start=st, stop=sp), after=after0 if dc == 0 else ())
        mm = p.engine_tok("tensor")
        slot = ckv_sb[:, (t % 2) * DC_KV:(t % 2 + 1) * DC_KV]
        ev = p.op("vector", lambda v, slot=slot, bank=bank: v.tensor_copy(
            slot, b.psum[bank][:, 0:DC_KV]), after=mm)
        b.psum_free[bank] = ev
        st = p.dma("sync", ckv_r[:, t, :], slot, sem="st", after=ev)
        final.append(st)
        p.last_tok["vector"] = st
    p.finalize(final[-1:])
    return b


def build_l0a_attn():
    b = Builder("l0a_attn")
    p = b.p
    xin = b.inp("xin", [TOK, D])
    qt_i = b.inp("qt", [D, TOK], BF16)
    ckvT_i = b.inp("ckvT", [DC_KV, S], BF16)
    cmask = b.inp("cmask", [128, 512])
    W_uk = b.inp("W_uk", [DC_KV, D])
    W_uv = b.inp("W_uv", [DC_KV, D])
    W_o = b.inp("W_o", [D, D])
    xout = b.outp("xout", [TOK, D])

    QT = b.sb("QT", [128, NDC * TOK], BF16)
    ckvT = b.sb("ckvT_sb", [128, 2 * S], BF16)
    wuk = b.sb("wuk", [128, 2 * D], BF16)
    wuv = b.sb("wuv", [128, 2 * D], BF16)
    wo = b.sb("wo", [128, NDC * D], BF16)
    KT = b.sb("KT", [128, S], BF16)
    Vg = b.sb("Vg", [128, S], BF16)
    P = b.sb("P", [128, S], BF16)
    PT = b.sb("PT", [128, 4 * 128], BF16)
    outfm = b.sb("outfm", [128, NDC * TOK], BF16)
    lsums = b.sb("lsums", [128, 20])
    mask_sb = b.sb("mask", [128, 512])
    ind2 = b.const("ind2", np.concatenate(
        [np.repeat(np.eye(2, dtype=bf16)[0:1], 64, 0),
         np.repeat(np.eye(2, dtype=bf16)[1:2], 64, 0)], 0))
    ind2_sb = b.sb("ind2_sb", [128, 2], BF16)
    norm_sb = b.sb("norm", [128, 4])
    qkbf = b.sb("qkbf", [128, 1], BF16)
    msc = b.sb("msc", [1, 2])
    negm = b.sb("negm", [128, 2])
    negm_dr = b.nc.dram_tensor("negm_dr", [1, 2], F32, kind="Internal").ap()
    xs_sb = b.sb("xs", [128, 2 * D])
    out_sb = b.sb("outs", [128, 512])

    qld = p.dma("sync", QT[:].rearrange("p (c n) -> p c n", n=TOK),
                qt_i.rearrange("(c p) n -> p c n", p=128), sem="x")
    ckld = p.dma("sync", ckvT[:].rearrange("p (c n) -> p c n", n=S),
                 ckvT_i.rearrange("(c p) n -> p c n", p=128), sem="x")
    w1t = p.dma("gpsimd", wuk[:].rearrange("p (c n) -> p c n", n=D),
                W_uk.rearrange("(c p) n -> p c n", p=128), sem="w")
    w2t = p.dma("gpsimd", wuv[:].rearrange("p (c n) -> p c n", n=D),
                W_uv.rearrange("(c p) n -> p c n", p=128), sem="w")
    w3t = p.dma("gpsimd", wo[:].rearrange("p (c n) -> p c n", n=D),
                W_o.rearrange("(c p) n -> p c n", p=128), sem="w")
    mld = p.dma("sync", mask_sb[:], cmask, sem="m")
    ild = p.dma("sync", ind2_sb[:], ind2, sem="m")
    mld = ild
    ident_sb, id_tok = b.ident()
    wtok = w3t
    inp_tok = ckld

    final = []
    for hg in range(NDC):
        kv_free = p.engine_tok("tensor")
        for sc in range(S // 512):
            bank = sc % 2
            after0 = [wtok, inp_tok, b.psum_free[bank], kv_free]
            for cc in range(2):
                p.op("tensor", lambda t, sc=sc, cc=cc, bank=bank, hg=hg,
                     st=(cc == 0), sp=(cc == 1): t.matmul(
                         b.psum[bank][:, 0:512],
                         wuk[:, cc * D + hg * 128: cc * D + (hg + 1) * 128],
                         ckvT[:, cc * S + sc * 512: cc * S + (sc + 1) * 512],
                         start=st, stop=sp), after=after0 if cc == 0 else ())
            mm = p.engine_tok("tensor")
            ev = p.op("vector", lambda v, sc=sc, bank=bank: v.tensor_copy(
                KT[:, sc * 512:(sc + 1) * 512], b.psum[bank][:, 0:512]),
                after=mm)
            b.psum_free[bank] = ev
        for kt in range(S // 128):
            bank = 2 + kt % 2
            after0 = [wtok, inp_tok, b.psum_free[bank], kv_free]
            for cc in range(2):
                p.op("tensor", lambda t, kt=kt, cc=cc, bank=bank, hg=hg,
                     st=(cc == 0), sp=(cc == 1): t.matmul(
                         b.psum[bank][:, 0:128],
                         ckvT[:, cc * S + kt * 128: cc * S + (kt + 1) * 128],
                         wuv[:, cc * D + hg * 128: cc * D + (hg + 1) * 128],
                         start=st, stop=sp), after=after0 if cc == 0 else ())
            mm = p.engine_tok("tensor")
            ev = p.op("vector", lambda v, kt=kt, bank=bank: v.tensor_copy(
                Vg[:, kt * 128:(kt + 1) * 128], b.psum[bank][:, 0:128]),
                after=mm)
            b.psum_free[bank] = ev
        kv_done = p.engine_tok("vector")
        t1 = p.op("vector", lambda v: v.reduce_max(
            norm_sb[:, 1:2], KT[:], axis=AX.X, apply_absolute_value=True),
            after=kv_done)
        t2 = p.op("vector", lambda v, hg=hg: v.reduce_max(
            norm_sb[:, 0:1], QT[:, hg * TOK:(hg + 1) * TOK], axis=AX.X,
            apply_absolute_value=True), after=inp_tok)
        t3 = p.op("vector", lambda v: v.tensor_mul(norm_sb[:, 2:3],
                                                   norm_sb[:, 0:1],
                                                   norm_sb[:, 1:2]))
        t4 = p.op("vector", lambda v: v.tensor_copy(qkbf[:], norm_sb[:, 2:3]))
        mmn = p.op("tensor", lambda t: t.matmul(
            b.psum[4][0:1, 0:2], qkbf[:], ind2_sb[:],
            start=True, stop=True), after=[t4, ild, b.psum_free[4]])
        t5 = p.op("vector", lambda v: v.tensor_copy(msc[:], b.psum[4][0:1, 0:2]),
                  after=mmn)
        b.psum_free[4] = t5
        t6 = p.op("scalar", lambda s: s.mul(msc[:], msc[:], -0.125),
                  after=t5)
        nd1 = p.dma("sync", negm_dr, msc[:], sem="negm", after=t6)
        nd2 = p.dma("sync", negm[:], negm_dr.broadcast_to([128, 2]), sem="negm",
                    after=nd1)
        p_free = None
        for h in range(2):
            hb = hg * 128 + h * 64
            for i in range(NT):
                nkb = i + 1
                for kb in range(nkb):
                    bank = kb % 2
                    after0 = [kv_done, nd2, mld, b.psum_free[bank],
                              (p_free if kb == 0 else None)]
                    p.op("tensor", lambda t, i=i, kb=kb, bank=bank, h=h, hg=hg:
                         t.matmul(b.psum[bank][:, 0:512],
                                  QT[h * 64:(h + 1) * 64,
                                     hg * TOK + i * 128: hg * TOK + (i + 1) * 128],
                                  KT[h * 64:(h + 1) * 64, kb * 512:(kb + 1) * 512],
                                  start=True, stop=True), after=after0)
                    mm = p.engine_tok("tensor")
                    if kb == nkb - 1:
                        mk = p.op("vector", lambda v, bank=bank: v.tensor_add(
                            b.psum[bank][:, 0:512], b.psum[bank][:, 0:512],
                            mask_sb[:]), after=mm, chain=False)
                        mm = mk
                    ex = p.op("scalar", lambda s, kb=kb, h=h, bank=bank:
                              s.activation(P[:, kb * 512:(kb + 1) * 512],
                                           b.psum[bank][:, 0:512], AF.Exp,
                                           bias=negm[:, h:h + 1], scale=0.125,
                                           accum_out=lsums[:, kb:kb + 1]),
                              after=[mm] + ([p_free] if kb == 0 and
                                            p_free is not None else []))
                    b.psum_free[bank] = ex
                lt = p.op("vector", lambda v, nkb=nkb: v.reduce_sum(
                    lsums[:, 16:17], lsums[:, 0:nkb], axis=AX.X),
                    after=p.engine_tok("scalar"))
                lr = p.op("vector", lambda v: v.reciprocal(lsums[:, 17:18],
                                                           lsums[:, 16:17]))
                pssc = []
                for kb in range(nkb):
                    pssc.append(p.op("vector", lambda v, kb=kb: v.tensor_scalar(
                        P[:, kb * 512:(kb + 1) * 512],
                        P[:, kb * 512:(kb + 1) * 512],
                        lsums[:, 17:18], None, ALU.mult),
                        after=lr, chain=False))
                psc_done = pssc[-1]
                ob = 6 + h
                pv_after = [b.psum_free[ob]]
                pt_free = [None, None]
                for kt in range(4 * nkb):
                    tb = 4 + kt % 2
                    trafter = [psc_done, b.psum_free[tb], id_tok]
                    tr = p.op("tensor", lambda t, kt=kt, tb=tb: t.transpose(
                        b.psum[tb].bitcast(BF16)[0:128, 0:128],
                        P[:, kt * 128:(kt + 1) * 128], ident_sb[:]),
                        after=trafter)
                    tev = p.op("vector", lambda v, kt=kt, tb=tb: v.tensor_copy(
                        PT[:, (kt % 2) * 128:(kt % 2 + 1) * 128],
                        b.psum[tb].bitcast(BF16)[0:128, 0:128]),
                        after=[tr, pt_free[kt % 2]], chain=False)
                    b.psum_free[tb] = tev
                    p.op("tensor", lambda t, kt=kt, h=h, ob=ob, st=(kt == 0),
                         sp=(kt == 4 * nkb - 1): t.matmul(
                             b.psum[ob][0:64, 0:128],
                             Vg[:, kt * 128 + h * 64: kt * 128 + h * 64 + 64],
                             PT[:, (kt % 2) * 128:(kt % 2 + 1) * 128],
                             start=st, stop=sp),
                         after=[tev] + (pv_after if kt == 0 else []))
                    pt_free[kt % 2] = p.engine_tok("tensor")
                mmo = p.engine_tok("tensor")
                oev = p.op("vector", lambda v, hg=hg, h=h, i=i: v.tensor_copy(
                    outfm[h * 64:(h + 1) * 64,
                          hg * TOK + i * 128: hg * TOK + (i + 1) * 128],
                    b.psum[6 + h][0:64, 0:128]), after=mmo)
                b.psum_free[ob] = oev
                p_free = p.engine_tok("tensor")
    att_done = p.engine_tok("vector")
    xout_r = xout.rearrange("(t p) d -> p t d", p=128)
    xld_toks = [None, None]
    for t in range(NT):
        sl = t % 2
        xt = p.dma("sync", xs_sb[:, sl * D:(sl + 1) * D],
                   xin.rearrange("(t p) d -> p t d", p=128)[:, t, :],
                   sem=f"xs{sl}", after=xld_toks[sl])
        for dc2 in range(2):
            bank = dc2
            after0 = [att_done, wtok, b.psum_free[bank]]
            for hc in range(NDC):
                p.op("tensor", lambda tt, t=t, dc2=dc2, hc=hc, bank=bank,
                     st=(hc == 0), sp=(hc == NDC - 1): tt.matmul(
                         b.psum[bank][:, 0:512],
                         outfm[:, hc * TOK + t * 128: hc * TOK + (t + 1) * 128],
                         wo[:, hc * D + dc2 * 512: hc * D + (dc2 + 1) * 512],
                         start=st, stop=sp), after=after0 if hc == 0 else ())
            mm = p.engine_tok("tensor")
            ev = p.op("vector", lambda v, t=t, dc2=dc2, sl=sl, bank=bank:
                      v.tensor_add(out_sb[:], b.psum[bank][:, 0:512],
                                   xs_sb[:, sl * D + dc2 * 512:
                                         sl * D + (dc2 + 1) * 512]),
                      after=[mm, xt])
            b.psum_free[bank] = ev
            st = p.dma("sync", xout_r[:, t, dc2 * 512:(dc2 + 1) * 512],
                       out_sb[:], sem="st", after=ev)
            final.append(st)
            p.last_tok["vector"] = st
        xld_toks[sl] = p.engine_tok("vector")
    p.finalize(final[-1:])
    return b




def rope_tables(c):
    r, r4 = c // 4, c % 4
    tiles = np.arange(NT) * 4 + r4
    t = (tiles[:, None] * 128 + np.arange(128)[None, :]).reshape(-1)
    hd = 64
    inv = 1.0 / (10000.0 ** (np.arange(0, hd, 2, dtype=np.float64) / hd))
    fr = t[:, None].astype(np.float64) * inv[None, :]
    rope = np.concatenate([np.sin(fr), np.cos(fr)], -1)
    return (np.ascontiguousarray(rope[:, 0::2]).astype(np.float32),
            np.ascontiguousarray(rope[:, 1::2]).astype(np.float32))


def build_l1b_qkv():
    b = Builder("l1b_qkv")
    p = b.p
    xin = b.inp("xin", [TOK, D])
    wqkv = b.inp("wqkv", [D, 3 * D])
    bqkv = b.inp("bqkv", [3 * D])
    sin_t = b.inp("sin_t", [TOK, 32])
    cos_t = b.inp("cos_t", [TOK, 32])
    q_o = b.outp("q_o", [TOK, D], BF16)
    k_o = b.outp("k_o", [TOK, D], BF16)
    v_o = b.outp("v_o", [TOK, D], BF16)

    x_sb = b.sb("x", [128, NT * D])
    xh_sb = b.sb("xh", [128, 2 * D], BF16)
    xhT = b.sb("xhT", [128, NDC * TOK], BF16)
    scratch = b.sb("scr", [128, D])
    stats = b.sb("stats", [128, 5 * NT])
    wq_sb = b.sb("wq", [128, NDC * 3 * D], BF16)
    brow = b.sb("brow", [1, 3 * D], BF16)
    ones_bf = b.sb("ones_bf", [1, 128], BF16)
    sin_sb = b.sb("sin", [128, NT * 32])
    cos_sb = b.sb("cos", [128, NT * 32])
    qk_sb = b.sb("qks", [128, 2 * D])
    rot_sb = b.sb("rot", [128, 2 * D], BF16)
    v_sb = b.sb("vs", [128, 512], BF16)

    xld = p.dma("sync", x_sb[:].rearrange("p (t d) -> p t d", d=D),
                xin.rearrange("(t p) d -> p t d", p=128), sem="x")
    wld = p.dma("gpsimd", wq_sb[:].rearrange("p (c n) -> p c n", n=3 * D),
                wqkv.rearrange("(c p) n -> p c n", p=128), sem="w")
    bld = p.dma("gpsimd", brow[:], bqkv.rearrange("(o n) -> o n", o=1), sem="w")
    sld = p.dma("sync", sin_sb[:].rearrange("p (t n) -> p t n", n=32),
                sin_t.rearrange("(t p) n -> p t n", p=128), sem="m")
    cld = p.dma("sync", cos_sb[:].rearrange("p (t n) -> p t n", n=32),
                cos_t.rearrange("(t p) n -> p t n", p=128), sem="m")
    ot = p.op("vector", lambda v: v.memset(ones_bf[:], 1.0))
    ln_to_fm(b, x_sb, xh_sb, xhT, scratch, stats, x_tok=xld)
    fm_done = p.engine_tok("vector")
    wtok = bld

    q_r = q_o.rearrange("(t p) d -> p t d", p=128)
    k_r = k_o.rearrange("(t p) d -> p t d", p=128)
    v_r = v_o.rearrange("(t p) d -> p t d", p=128)
    final = []
    st_qk = None
    st_v = None
    for t in range(NT):
        for col in range(6):
            bank = col
            after0 = [fm_done, cld, ot, wtok, b.psum_free[bank]]
            for dc in range(NDC):
                p.op("tensor", lambda tt, t=t, col=col, dc=dc, bank=bank,
                     st=(dc == 0): tt.matmul(
                         b.psum[bank][:, 0:512],
                         xhT[:, dc * TOK + t * 128: dc * TOK + (t + 1) * 128],
                         wq_sb[:, dc * 3 * D + col * 512:
                               dc * 3 * D + (col + 1) * 512],
                         start=st, stop=False), after=after0 if dc == 0 else ())
            p.op("tensor", lambda tt, col=col, bank=bank: tt.matmul(
                b.psum[bank][:, 0:512], ones_bf[:],
                brow[:, col * 512:(col + 1) * 512], start=False, stop=True))
        mm = p.engine_tok("tensor")
        eq = []
        for col in range(4):
            eq.append(p.op("scalar", lambda s, col=col: s.copy(
                qk_sb[:, col * 512:(col + 1) * 512], b.psum[col][:, 0:512]),
                after=[mm, st_qk]))
        for col in range(4, 6):
            ev = p.op("vector", lambda v, col=col: v.tensor_copy(
                v_sb[:], b.psum[col][:, 0:512]),
                after=[mm, st_v])
            st_v = p.dma("sync", v_r[:, t, (col - 4) * 512:(col - 3) * 512],
                         v_sb[:], sem="st", after=ev)
            final.append(st_v)
            p.last_tok["vector"] = st_v
        for col in range(6):
            b.psum_free[col] = (eq[col] if col < 4 else st_v)
        qk_done = p.engine_tok("scalar")
        sc = sin_sb[:, t * 32:(t + 1) * 32]
        cc = cos_sb[:, t * 32:(t + 1) * 32]
        for half in range(2):
            base = half * D
            qkv_v = qk_sb[:, base:base + D].rearrange("p (h j) -> p h j", j=64)
            x1 = qkv_v[:, :, 0:64:2]
            x2 = qkv_v[:, :, 1:64:2]
            scb = sc.rearrange("p (o j) -> p o j", o=1).broadcast_to([128, 16, 32])
            ccb = cc.rearrange("p (o j) -> p o j", o=1).broadcast_to([128, 16, 32])
            od = rot_sb[:, base:base + D].rearrange("p (h j) -> p h j", j=64)
            s1 = scratch[:, 0:512].rearrange("p (h j) -> p h j", j=32)
            s2 = scratch[:, 512:1024].rearrange("p (h j) -> p h j", j=32)
            p.op("vector", lambda v, x1=x1, ccb=ccb, s1=s1: v.tensor_tensor(
                s1, x1, ccb, ALU.mult),
                after=[qk_done] + ([st_qk] if st_qk else []))
            p.op("vector", lambda v, x2=x2, scb=scb, s2=s2: v.tensor_tensor(
                s2, x2, scb, ALU.mult))
            p.op("vector", lambda v, od=od, s1=s1, s2=s2: v.tensor_tensor(
                od[:, :, 0:32], s1, s2, ALU.subtract))
            p.op("vector", lambda v, x1=x1, scb=scb, s1=s1: v.tensor_tensor(
                s1, x1, scb, ALU.mult))
            p.op("vector", lambda v, x2=x2, ccb=ccb, s2=s2: v.tensor_tensor(
                s2, x2, ccb, ALU.mult))
            p.op("vector", lambda v, od=od, s1=s1, s2=s2: v.tensor_tensor(
                od[:, :, 32:64], s1, s2, ALU.add))
        rot_done = p.engine_tok("vector")
        stq = p.dma("sync", q_r[:, t, :], rot_sb[:, 0:D], sem="st",
                    after=rot_done)
        st_qk = p.dma("sync", k_r[:, t, :], rot_sb[:, D:2 * D], sem="st")
        final.append(st_qk)
        p.last_tok["vector"] = st_qk
    p.finalize(final[-1:])
    return b


def build_l1b_attn():
    b = Builder("l1b_attn")
    p = b.p
    xin = b.inp("xin", [TOK, D])
    qt_i = b.inp("qt", [D, TOK], BF16)
    ktT_i = b.inp("ktT", [D, S], BF16)
    v_i = b.inp("v", [S, D], BF16)
    cmask = b.inp("cmask", [128, 512])
    out_w = b.inp("out_w", [D, D])
    out_b = b.inp("out_b", [D])
    xout = b.outp("xout", [TOK, D])

    QT = b.sb("QT", [128, NDC * TOK], BF16)
    KT = b.sb("KT", [128, S], BF16)
    Vg = b.sb("Vg", [128, S], BF16)
    P = b.sb("P", [128, S], BF16)
    PT = b.sb("PT", [128, 4 * 128], BF16)
    outfm = b.sb("outfm", [128, NDC * TOK], BF16)
    wo = b.sb("wo", [128, NDC * D], BF16)
    ob_row = b.sb("ob_row", [1, D], BF16)
    ones_bf = b.sb("ones_bf", [1, 128], BF16)
    lsums = b.sb("lsums", [128, 20])
    mask_sb = b.sb("mask", [128, 512])
    ind2 = b.const("ind2c", np.concatenate(
        [np.repeat(np.eye(2, dtype=bf16)[0:1], 64, 0),
         np.repeat(np.eye(2, dtype=bf16)[1:2], 64, 0)], 0))
    ind2_sb = b.sb("ind2_sbuf", [128, 2], BF16)
    norm_sb = b.sb("norm", [128, 4])
    qkbf = b.sb("qkbf", [128, 1], BF16)
    msc = b.sb("msc", [1, 2])
    negm = b.sb("negm", [128, 2])
    negm_dr = b.nc.dram_tensor("negm_dr", [1, 2], F32, kind="Internal").ap()
    xs_sb = b.sb("xs", [128, 2 * D])
    out_sb = b.sb("outs", [128, 512])
    ident_sb, id_tok = b.ident()

    qld = p.dma("sync", QT[:].rearrange("p (c n) -> p c n", n=TOK),
                qt_i.rearrange("(c p) n -> p c n", p=128), sem="x")
    wld = p.dma("gpsimd", wo[:].rearrange("p (c n) -> p c n", n=D),
                out_w.rearrange("(c p) n -> p c n", p=128), sem="w")
    bld = p.dma("gpsimd", ob_row[:], out_b.rearrange("(o n) -> o n", o=1),
                sem="w")
    mld0 = p.dma("sync", mask_sb[:], cmask, sem="m")
    ild = p.dma("sync", ind2_sb[:], ind2, sem="m")
    mld = ild
    ot = p.op("vector", lambda v: v.memset(ones_bf[:], 1.0))
    wtok = bld
    inp_tok = qld

    final = []
    p_free_hg = None
    for hg in range(NDC):
        kv_free = p.engine_tok("tensor")
        ktld = p.dma("sync", KT[:],
                     ktT_i[hg * 128:(hg + 1) * 128, :]
                     .rearrange("(o p) n -> p o n", p=128)[:, 0, :],
                     sem="kv", after=kv_free)
        vld_parts = []
        for part in range(8):
            vld_parts.append(p.dma(
                "sync", Vg[:, part * 1024:(part + 1) * 1024]
                .rearrange("p (kt n) -> p kt n", n=128),
                v_i.rearrange("(kt p) d -> p kt d", p=128)
                [:, part * 8:(part + 1) * 8, hg * 128:(hg + 1) * 128],
                sem="kv"))
        kv_done = vld_parts[-1]
        nsq = []
        for which, src, nch in ((0, QT[:, hg * TOK:(hg + 1) * TOK], TOK // 512),
                                (1, KT[:], S // 512)):
            sq = p.op("scalar", lambda s, src=src: s.activation(
                P[:, 0:src.shape[-1]], src, AF.Square),
                after=[kv_done, inp_tok, (p_free_hg if hg > 0 else None),
                       p.engine_tok("tensor")])
            mx = None
            for chv in range(nch):
                mmq = p.op("tensor", lambda t, chv=chv, src=src: t.matmul(
                    b.psum[4][0:2, 0:512], ind2_sb[:],
                    P[:, chv * 512:(chv + 1) * 512], start=True, stop=True),
                    after=[sq, ild, b.psum_free[4]])
                rm = p.op("vector", lambda v, which=which, chv=chv: (
                    v.reduce_max(norm_sb[0:2, which:which + 1],
                                 b.psum[4][0:2, 0:512], axis=AX.X)
                    if chv == 0 else
                    v.tensor_reduce(norm_sb[0:2, 2:3], b.psum[4][0:2, 0:512],
                                    axis=AX.X, op=ALU.max)), after=mmq)
                b.psum_free[4] = rm
                if chv > 0:
                    rm = p.op("vector", lambda v, which=which: v.tensor_max(
                        norm_sb[0:2, which:which + 1],
                        norm_sb[0:2, which:which + 1], norm_sb[0:2, 2:3]))
                mx = rm
            nsq.append(mx)
        t3 = p.op("vector", lambda v: v.tensor_mul(
            norm_sb[0:2, 3:4], norm_sb[0:2, 0:1], norm_sb[0:2, 1:2]),
            after=nsq)
        t4 = p.op("scalar", lambda s: s.activation(norm_sb[0:2, 3:4],
                                                   norm_sb[0:2, 3:4], AF.Sqrt),
                  after=t3)
        t6 = p.op("scalar", lambda s: s.mul(norm_sb[0:2, 3:4],
                                            norm_sb[0:2, 3:4], -0.125))
        nd1 = p.dma("sync", negm_dr.rearrange("o n -> (o n)")
                    .rearrange("(n o) -> n o", o=1), norm_sb[0:2, 3:4],
                    sem="negm", after=t6)
        nd2 = p.dma("sync", negm[:], negm_dr.broadcast_to([128, 2]), sem="negm",
                    after=nd1)
        p_free_sq = p.engine_tok("tensor")
        p_free = p_free_sq
        for h in range(2):
            for i in range(NT):
                nkb = i + 1
                for kb in range(nkb):
                    bank = kb % 2
                    after0 = [kv_done, nd2, mld0, mld, b.psum_free[bank],
                              (p_free if kb == 0 else None)]
                    p.op("tensor", lambda t, i=i, kb=kb, bank=bank, h=h, hg=hg:
                         t.matmul(b.psum[bank][:, 0:512],
                                  QT[h * 64:(h + 1) * 64,
                                     hg * TOK + i * 128: hg * TOK + (i + 1) * 128],
                                  KT[h * 64:(h + 1) * 64, kb * 512:(kb + 1) * 512],
                                  start=True, stop=True), after=after0)
                    mm = p.engine_tok("tensor")
                    if kb == nkb - 1:
                        mk = p.op("vector", lambda v, bank=bank: v.tensor_add(
                            b.psum[bank][:, 0:512], b.psum[bank][:, 0:512],
                            mask_sb[:]), after=mm, chain=False)
                        mm = mk
                    ex = p.op("scalar", lambda s, kb=kb, h=h, bank=bank:
                              s.activation(P[:, kb * 512:(kb + 1) * 512],
                                           b.psum[bank][:, 0:512], AF.Exp,
                                           bias=negm[:, h:h + 1], scale=0.125,
                                           accum_out=lsums[:, kb:kb + 1]),
                              after=[mm] + ([p_free] if kb == 0 and
                                            p_free is not None else []))
                    b.psum_free[bank] = ex
                lt = p.op("vector", lambda v, nkb=nkb: v.reduce_sum(
                    lsums[:, 16:17], lsums[:, 0:nkb], axis=AX.X),
                    after=p.engine_tok("scalar"))
                lr = p.op("vector", lambda v: v.reciprocal(lsums[:, 17:18],
                                                           lsums[:, 16:17]))
                pssc = []
                for kb in range(nkb):
                    pssc.append(p.op("vector", lambda v, kb=kb: v.tensor_scalar(
                        P[:, kb * 512:(kb + 1) * 512],
                        P[:, kb * 512:(kb + 1) * 512],
                        lsums[:, 17:18], None, ALU.mult),
                        after=lr, chain=False))
                psc_done = pssc[-1]
                ob = 6 + h
                pv_after = [b.psum_free[ob]]
                pt_free = [None, None]
                for kt in range(4 * nkb):
                    tb = 4 + kt % 2
                    trafter = [psc_done, b.psum_free[tb], id_tok]
                    tr = p.op("tensor", lambda t, kt=kt, tb=tb: t.transpose(
                        b.psum[tb].bitcast(BF16)[0:128, 0:128],
                        P[:, kt * 128:(kt + 1) * 128], ident_sb[:]),
                        after=trafter)
                    tev = p.op("vector", lambda v, kt=kt, tb=tb: v.tensor_copy(
                        PT[:, (kt % 2) * 128:(kt % 2 + 1) * 128],
                        b.psum[tb].bitcast(BF16)[0:128, 0:128]),
                        after=[tr, pt_free[kt % 2]], chain=False)
                    b.psum_free[tb] = tev
                    p.op("tensor", lambda t, kt=kt, h=h, ob=ob, st=(kt == 0),
                         sp=(kt == 4 * nkb - 1): t.matmul(
                             b.psum[ob][0:64, 0:128],
                             Vg[:, kt * 128 + h * 64: kt * 128 + h * 64 + 64],
                             PT[:, (kt % 2) * 128:(kt % 2 + 1) * 128],
                             start=st, stop=sp),
                         after=[tev] + (pv_after if kt == 0 else []))
                    pt_free[kt % 2] = p.engine_tok("tensor")
                mmo = p.engine_tok("tensor")
                oev = p.op("vector", lambda v, hg=hg, h=h, i=i: v.tensor_copy(
                    outfm[h * 64:(h + 1) * 64,
                          hg * TOK + i * 128: hg * TOK + (i + 1) * 128],
                    b.psum[6 + h][0:64, 0:128]), after=mmo)
                b.psum_free[ob] = oev
                p_free = p.engine_tok("tensor")
        p_free_hg = p.engine_tok("tensor")
    att_done = p.engine_tok("vector")
    xout_r = xout.rearrange("(t p) d -> p t d", p=128)
    xld_toks = [None, None]
    for t in range(NT):
        sl = t % 2
        xt = p.dma("sync", xs_sb[:, sl * D:(sl + 1) * D],
                   xin.rearrange("(t p) d -> p t d", p=128)[:, t, :],
                   sem=f"xs{sl}", after=xld_toks[sl])
        for dc2 in range(2):
            bank = dc2
            after0 = [att_done, wtok, b.psum_free[bank]]
            for hc in range(NDC):
                p.op("tensor", lambda tt, t=t, dc2=dc2, hc=hc, bank=bank,
                     st=(hc == 0), sp=(hc == NDC - 1): tt.matmul(
                         b.psum[bank][:, 0:512],
                         outfm[:, hc * TOK + t * 128: hc * TOK + (t + 1) * 128],
                         wo[:, hc * D + dc2 * 512: hc * D + (dc2 + 1) * 512],
                         start=st, stop=sp), after=after0 if hc == 0 else ())
            mm = p.engine_tok("tensor")
            ev = p.op("vector", lambda v, t=t, dc2=dc2, sl=sl, bank=bank:
                      v.tensor_add(out_sb[:], b.psum[bank][:, 0:512],
                                   xs_sb[:, sl * D + dc2 * 512:
                                         sl * D + (dc2 + 1) * 512]),
                      after=[mm, xt])
            b.psum_free[bank] = ev
            st = p.dma("sync", xout_r[:, t, dc2 * 512:(dc2 + 1) * 512],
                       out_sb[:], sem="st", after=ev)
            final.append(st)
            p.last_tok["vector"] = st
        xld_toks[sl] = p.engine_tok("vector")
    p.finalize(final[-1:])
    return b




def rope_tables(c):
    r, r4 = c // 4, c % 4
    tiles = np.arange(NT) * 4 + r4
    t = (tiles[:, None] * 128 + np.arange(128)[None, :]).reshape(-1)
    hd = 64
    inv = 1.0 / (10000.0 ** (np.arange(0, hd, 2, dtype=np.float64) / hd))
    fr = t[:, None].astype(np.float64) * inv[None, :]
    rope = np.concatenate([np.sin(fr), np.cos(fr)], -1)
    return (np.ascontiguousarray(rope[:, 0::2]).astype(np.float32),
            np.ascontiguousarray(rope[:, 1::2]).astype(np.float32))


def build_l1b_qkv():
    b = Builder("l1b_qkv")
    p = b.p
    xin = b.inp("xin", [TOK, D])
    wqkv = b.inp("wqkv", [D, 3 * D])
    bqkv = b.inp("bqkv", [3 * D])
    sin_t = b.inp("sin_t", [TOK, 32])
    cos_t = b.inp("cos_t", [TOK, 32])
    q_o = b.outp("q_o", [TOK, D], BF16)
    k_o = b.outp("k_o", [TOK, D], BF16)
    v_o = b.outp("v_o", [TOK, D], BF16)

    x_sb = b.sb("x", [128, NT * D])
    xh_sb = b.sb("xh", [128, 2 * D], BF16)
    xhT = b.sb("xhT", [128, NDC * TOK], BF16)
    scratch = b.sb("scr", [128, D])
    stats = b.sb("stats", [128, 5 * NT])
    wq_sb = b.sb("wq", [128, NDC * 3 * D], BF16)
    brow = b.sb("brow", [1, 3 * D], BF16)
    ones_bf = b.sb("ones_bf", [1, 128], BF16)
    sin_sb = b.sb("sin", [128, NT * 32])
    cos_sb = b.sb("cos", [128, NT * 32])
    qk_sb = b.sb("qks", [128, 2 * D])
    rot_sb = b.sb("rot", [128, 2 * D], BF16)
    v_sb = b.sb("vs", [128, 512], BF16)

    xld = p.dma("sync", x_sb[:].rearrange("p (t d) -> p t d", d=D),
                xin.rearrange("(t p) d -> p t d", p=128), sem="x")
    wld = p.dma("gpsimd", wq_sb[:].rearrange("p (c n) -> p c n", n=3 * D),
                wqkv.rearrange("(c p) n -> p c n", p=128), sem="w")
    bld = p.dma("gpsimd", brow[:], bqkv.rearrange("(o n) -> o n", o=1), sem="w")
    sld = p.dma("sync", sin_sb[:].rearrange("p (t n) -> p t n", n=32),
                sin_t.rearrange("(t p) n -> p t n", p=128), sem="m")
    cld = p.dma("sync", cos_sb[:].rearrange("p (t n) -> p t n", n=32),
                cos_t.rearrange("(t p) n -> p t n", p=128), sem="m")
    ot = p.op("vector", lambda v: v.memset(ones_bf[:], 1.0))
    ln_to_fm(b, x_sb, xh_sb, xhT, scratch, stats, x_tok=xld)
    fm_done = p.engine_tok("vector")
    wtok = bld

    q_r = q_o.rearrange("(t p) d -> p t d", p=128)
    k_r = k_o.rearrange("(t p) d -> p t d", p=128)
    v_r = v_o.rearrange("(t p) d -> p t d", p=128)
    final = []
    st_qk = None
    st_v = None
    for t in range(NT):
        for col in range(6):
            bank = col
            after0 = [fm_done, cld, ot, wtok, b.psum_free[bank]]
            for dc in range(NDC):
                p.op("tensor", lambda tt, t=t, col=col, dc=dc, bank=bank,
                     st=(dc == 0): tt.matmul(
                         b.psum[bank][:, 0:512],
                         xhT[:, dc * TOK + t * 128: dc * TOK + (t + 1) * 128],
                         wq_sb[:, dc * 3 * D + col * 512:
                               dc * 3 * D + (col + 1) * 512],
                         start=st, stop=False), after=after0 if dc == 0 else ())
            p.op("tensor", lambda tt, col=col, bank=bank: tt.matmul(
                b.psum[bank][:, 0:512], ones_bf[:],
                brow[:, col * 512:(col + 1) * 512], start=False, stop=True))
        mm = p.engine_tok("tensor")
        eq = []
        for col in range(4):
            eq.append(p.op("scalar", lambda s, col=col: s.copy(
                qk_sb[:, col * 512:(col + 1) * 512], b.psum[col][:, 0:512]),
                after=[mm, st_qk]))
        for col in range(4, 6):
            ev = p.op("vector", lambda v, col=col: v.tensor_copy(
                v_sb[:], b.psum[col][:, 0:512]),
                after=[mm, st_v])
            st_v = p.dma("sync", v_r[:, t, (col - 4) * 512:(col - 3) * 512],
                         v_sb[:], sem="st", after=ev)
            final.append(st_v)
            p.last_tok["vector"] = st_v
        for col in range(6):
            b.psum_free[col] = (eq[col] if col < 4 else st_v)
        qk_done = p.engine_tok("scalar")
        sc = sin_sb[:, t * 32:(t + 1) * 32]
        cc = cos_sb[:, t * 32:(t + 1) * 32]
        for half in range(2):
            base = half * D
            qkv_v = qk_sb[:, base:base + D].rearrange("p (h j) -> p h j", j=64)
            x1 = qkv_v[:, :, 0:64:2]
            x2 = qkv_v[:, :, 1:64:2]
            scb = sc.rearrange("p (o j) -> p o j", o=1).broadcast_to([128, 16, 32])
            ccb = cc.rearrange("p (o j) -> p o j", o=1).broadcast_to([128, 16, 32])
            od = rot_sb[:, base:base + D].rearrange("p (h j) -> p h j", j=64)
            s1 = scratch[:, 0:512].rearrange("p (h j) -> p h j", j=32)
            s2 = scratch[:, 512:1024].rearrange("p (h j) -> p h j", j=32)
            p.op("vector", lambda v, x1=x1, ccb=ccb, s1=s1: v.tensor_tensor(
                s1, x1, ccb, ALU.mult),
                after=[qk_done] + ([st_qk] if st_qk else []))
            p.op("vector", lambda v, x2=x2, scb=scb, s2=s2: v.tensor_tensor(
                s2, x2, scb, ALU.mult))
            p.op("vector", lambda v, od=od, s1=s1, s2=s2: v.tensor_tensor(
                od[:, :, 0:32], s1, s2, ALU.subtract))
            p.op("vector", lambda v, x1=x1, scb=scb, s1=s1: v.tensor_tensor(
                s1, x1, scb, ALU.mult))
            p.op("vector", lambda v, x2=x2, ccb=ccb, s2=s2: v.tensor_tensor(
                s2, x2, ccb, ALU.mult))
            p.op("vector", lambda v, od=od, s1=s1, s2=s2: v.tensor_tensor(
                od[:, :, 32:64], s1, s2, ALU.add))
        rot_done = p.engine_tok("vector")
        stq = p.dma("sync", q_r[:, t, :], rot_sb[:, 0:D], sem="st",
                    after=rot_done)
        st_qk = p.dma("sync", k_r[:, t, :], rot_sb[:, D:2 * D], sem="st")
        final.append(st_qk)
        p.last_tok["vector"] = st_qk
    p.finalize(final[-1:])
    return b


def build_l1b_attn():
    b = Builder("l1b_attn")
    p = b.p
    xin = b.inp("xin", [TOK, D])
    qt_i = b.inp("qt", [D, TOK], BF16)
    ktT_i = b.inp("ktT", [D, S], BF16)
    v_i = b.inp("v", [S, D], BF16)
    cmask = b.inp("cmask", [128, 512])
    out_w = b.inp("out_w", [D, D])
    out_b = b.inp("out_b", [D])
    xout = b.outp("xout", [TOK, D])

    QT = b.sb("QT", [128, NDC * TOK], BF16)
    KT = b.sb("KT", [128, S], BF16)
    Vg = b.sb("Vg", [128, S], BF16)
    P = b.sb("P", [128, S], BF16)
    PT = b.sb("PT", [128, 4 * 128], BF16)
    outfm = b.sb("outfm", [128, NDC * TOK], BF16)
    wo = b.sb("wo", [128, NDC * D], BF16)
    ob_row = b.sb("ob_row", [1, D], BF16)
    ones_bf = b.sb("ones_bf", [1, 128], BF16)
    lsums = b.sb("lsums", [128, 20])
    mask_sb = b.sb("mask", [128, 512])
    ind2 = b.const("ind2c", np.concatenate(
        [np.repeat(np.eye(2, dtype=bf16)[0:1], 64, 0),
         np.repeat(np.eye(2, dtype=bf16)[1:2], 64, 0)], 0))
    ind2_sb = b.sb("ind2_sbuf", [128, 2], BF16)
    norm_sb = b.sb("norm", [128, 4])
    qkbf = b.sb("qkbf", [128, 1], BF16)
    msc = b.sb("msc", [1, 2])
    negm = b.sb("negm", [128, 2])
    negm_dr = b.nc.dram_tensor("negm_dr", [1, 2], F32, kind="Internal").ap()
    xs_sb = b.sb("xs", [128, 2 * D])
    out_sb = b.sb("outs", [128, 512])
    ident_sb, id_tok = b.ident()

    qld = p.dma("sync", QT[:].rearrange("p (c n) -> p c n", n=TOK),
                qt_i.rearrange("(c p) n -> p c n", p=128), sem="x")
    wld = p.dma("gpsimd", wo[:].rearrange("p (c n) -> p c n", n=D),
                out_w.rearrange("(c p) n -> p c n", p=128), sem="w")
    bld = p.dma("gpsimd", ob_row[:], out_b.rearrange("(o n) -> o n", o=1),
                sem="w")
    mld0 = p.dma("sync", mask_sb[:], cmask, sem="m")
    ild = p.dma("sync", ind2_sb[:], ind2, sem="m")
    mld = ild
    ot = p.op("vector", lambda v: v.memset(ones_bf[:], 1.0))
    wtok = bld
    inp_tok = qld

    final = []
    p_free_hg = None
    for hg in range(NDC):
        kv_free = p.engine_tok("tensor")
        ktld = p.dma("sync", KT[:],
                     ktT_i[hg * 128:(hg + 1) * 128, :]
                     .rearrange("(o p) n -> p o n", p=128)[:, 0, :],
                     sem="kv", after=kv_free)
        vld_parts = []
        for part in range(8):
            vld_parts.append(p.dma(
                "sync", Vg[:, part * 1024:(part + 1) * 1024]
                .rearrange("p (kt n) -> p kt n", n=128),
                v_i.rearrange("(kt p) d -> p kt d", p=128)
                [:, part * 8:(part + 1) * 8, hg * 128:(hg + 1) * 128],
                sem="kv"))
        kv_done = vld_parts[-1]
        nsq = []
        for which, src, nch in ((0, QT[:, hg * TOK:(hg + 1) * TOK], TOK // 512),
                                (1, KT[:], S // 512)):
            sq = p.op("scalar", lambda s, src=src: s.activation(
                P[:, 0:src.shape[-1]], src, AF.Square),
                after=[kv_done, inp_tok, (p_free_hg if hg > 0 else None),
                       p.engine_tok("tensor")])
            mx = None
            for chv in range(nch):
                mmq = p.op("tensor", lambda t, chv=chv, src=src: t.matmul(
                    b.psum[4][0:2, 0:512], ind2_sb[:],
                    P[:, chv * 512:(chv + 1) * 512], start=True, stop=True),
                    after=[sq, ild, b.psum_free[4]])
                rm = p.op("vector", lambda v, which=which, chv=chv: (
                    v.reduce_max(norm_sb[0:2, which:which + 1],
                                 b.psum[4][0:2, 0:512], axis=AX.X)
                    if chv == 0 else
                    v.tensor_reduce(norm_sb[0:2, 2:3], b.psum[4][0:2, 0:512],
                                    axis=AX.X, op=ALU.max)), after=mmq)
                b.psum_free[4] = rm
                if chv > 0:
                    rm = p.op("vector", lambda v, which=which: v.tensor_max(
                        norm_sb[0:2, which:which + 1],
                        norm_sb[0:2, which:which + 1], norm_sb[0:2, 2:3]))
                mx = rm
            nsq.append(mx)
        t3 = p.op("vector", lambda v: v.tensor_mul(
            norm_sb[0:2, 3:4], norm_sb[0:2, 0:1], norm_sb[0:2, 1:2]),
            after=nsq)
        t4 = p.op("scalar", lambda s: s.activation(norm_sb[0:2, 3:4],
                                                   norm_sb[0:2, 3:4], AF.Sqrt),
                  after=t3)
        t6 = p.op("scalar", lambda s: s.mul(norm_sb[0:2, 3:4],
                                            norm_sb[0:2, 3:4], -0.125))
        nd1 = p.dma("sync", negm_dr.rearrange("o n -> (o n)")
                    .rearrange("(n o) -> n o", o=1), norm_sb[0:2, 3:4],
                    sem="negm", after=t6)
        nd2 = p.dma("sync", negm[:], negm_dr.broadcast_to([128, 2]), sem="negm",
                    after=nd1)
        p_free_sq = p.engine_tok("tensor")
        p_free = p_free_sq
        for h in range(2):
            for i in range(NT):
                nkb = i + 1
                for kb in range(nkb):
                    bank = kb % 2
                    after0 = [kv_done, nd2, mld0, mld, b.psum_free[bank],
                              (p_free if kb == 0 else None)]
                    p.op("tensor", lambda t, i=i, kb=kb, bank=bank, h=h, hg=hg:
                         t.matmul(b.psum[bank][:, 0:512],
                                  QT[h * 64:(h + 1) * 64,
                                     hg * TOK + i * 128: hg * TOK + (i + 1) * 128],
                                  KT[h * 64:(h + 1) * 64, kb * 512:(kb + 1) * 512],
                                  start=True, stop=True), after=after0)
                    mm = p.engine_tok("tensor")
                    if kb == nkb - 1:
                        mk = p.op("vector", lambda v, bank=bank: v.tensor_add(
                            b.psum[bank][:, 0:512], b.psum[bank][:, 0:512],
                            mask_sb[:]), after=mm, chain=False)
                        mm = mk
                    ex = p.op("scalar", lambda s, kb=kb, h=h, bank=bank:
                              s.activation(P[:, kb * 512:(kb + 1) * 512],
                                           b.psum[bank][:, 0:512], AF.Exp,
                                           bias=negm[:, h:h + 1], scale=0.125,
                                           accum_out=lsums[:, kb:kb + 1]),
                              after=[mm] + ([p_free] if kb == 0 and
                                            p_free is not None else []))
                    b.psum_free[bank] = ex
                lt = p.op("vector", lambda v, nkb=nkb: v.reduce_sum(
                    lsums[:, 16:17], lsums[:, 0:nkb], axis=AX.X),
                    after=p.engine_tok("scalar"))
                lr = p.op("vector", lambda v: v.reciprocal(lsums[:, 17:18],
                                                           lsums[:, 16:17]))
                pssc = []
                for kb in range(nkb):
                    pssc.append(p.op("vector", lambda v, kb=kb: v.tensor_scalar(
                        P[:, kb * 512:(kb + 1) * 512],
                        P[:, kb * 512:(kb + 1) * 512],
                        lsums[:, 17:18], None, ALU.mult),
                        after=lr, chain=False))
                psc_done = pssc[-1]
                ob = 6 + h
                pv_after = [b.psum_free[ob]]
                pt_free = [None, None]
                for kt in range(4 * nkb):
                    tb = 4 + kt % 2
                    trafter = [psc_done, b.psum_free[tb], id_tok]
                    tr = p.op("tensor", lambda t, kt=kt, tb=tb: t.transpose(
                        b.psum[tb].bitcast(BF16)[0:128, 0:128],
                        P[:, kt * 128:(kt + 1) * 128], ident_sb[:]),
                        after=trafter)
                    tev = p.op("vector", lambda v, kt=kt, tb=tb: v.tensor_copy(
                        PT[:, (kt % 2) * 128:(kt % 2 + 1) * 128],
                        b.psum[tb].bitcast(BF16)[0:128, 0:128]),
                        after=[tr, pt_free[kt % 2]], chain=False)
                    b.psum_free[tb] = tev
                    p.op("tensor", lambda t, kt=kt, h=h, ob=ob, st=(kt == 0),
                         sp=(kt == 4 * nkb - 1): t.matmul(
                             b.psum[ob][0:64, 0:128],
                             Vg[:, kt * 128 + h * 64: kt * 128 + h * 64 + 64],
                             PT[:, (kt % 2) * 128:(kt % 2 + 1) * 128],
                             start=st, stop=sp),
                         after=[tev] + (pv_after if kt == 0 else []))
                    pt_free[kt % 2] = p.engine_tok("tensor")
                mmo = p.engine_tok("tensor")
                oev = p.op("vector", lambda v, hg=hg, h=h, i=i: v.tensor_copy(
                    outfm[h * 64:(h + 1) * 64,
                          hg * TOK + i * 128: hg * TOK + (i + 1) * 128],
                    b.psum[6 + h][0:64, 0:128]), after=mmo)
                b.psum_free[ob] = oev
                p_free = p.engine_tok("tensor")
        p_free_hg = p.engine_tok("tensor")
    att_done = p.engine_tok("vector")
    xout_r = xout.rearrange("(t p) d -> p t d", p=128)
    xld_toks = [None, None]
    for t in range(NT):
        sl = t % 2
        xt = p.dma("sync", xs_sb[:, sl * D:(sl + 1) * D],
                   xin.rearrange("(t p) d -> p t d", p=128)[:, t, :],
                   sem="xs" + str(sl), after=xld_toks[sl])
        for dc2 in range(2):
            bank = dc2
            after0 = [att_done, wtok, ot, b.psum_free[bank]]
            for hc in range(NDC):
                p.op("tensor", lambda tt, t=t, dc2=dc2, hc=hc, bank=bank,
                     st=(hc == 0): tt.matmul(
                         b.psum[bank][:, 0:512],
                         outfm[:, hc * TOK + t * 128: hc * TOK + (t + 1) * 128],
                         wo[:, hc * D + dc2 * 512: hc * D + (dc2 + 1) * 512],
                         start=st, stop=False), after=after0 if hc == 0 else ())
            p.op("tensor", lambda tt, dc2=dc2, bank=bank: tt.matmul(
                b.psum[bank][:, 0:512], ones_bf[:],
                ob_row[:, dc2 * 512:(dc2 + 1) * 512], start=False, stop=True))
            mm = p.engine_tok("tensor")
            ev = p.op("vector", lambda v, t=t, dc2=dc2, sl=sl, bank=bank:
                      v.tensor_add(out_sb[:], b.psum[bank][:, 0:512],
                                   xs_sb[:, sl * D + dc2 * 512:
                                         sl * D + (dc2 + 1) * 512]),
                      after=[mm, xt])
            b.psum_free[bank] = ev
            st = p.dma("sync", xout_r[:, t, dc2 * 512:(dc2 + 1) * 512],
                       out_sb[:], sem="st", after=ev)
            final.append(st)
            p.last_tok["vector"] = st
        xld_toks[sl] = p.engine_tok("vector")
    p.finalize(final[-1:])
    return b




def build_l1a_qk():
    b = Builder("l1a_qk")
    p = b.p
    xin = b.inp("xin", [TOK, D])
    wqkv = b.inp("wqkv", [D, 3 * D])
    qt_o = b.outp("qt", [D, TOK], BF16)
    kt_o = b.outp("kt", [D, TOK], BF16)
    ktok_o = b.outp("ktok", [TOK, D], BF16)

    x_sb = b.sb("x", [128, NT * D])
    xh_sb = b.sb("xh", [128, 2 * D], BF16)
    xhT = b.sb("xhT", [128, NDC * TOK], BF16)
    scratch = b.sb("scr", [128, D])
    stats = b.sb("stats", [128, 5 * NT])
    wqk = b.sb("wqk", [128, NDC * 2 * D], BF16)
    ev_sb = b.sb("ev", [128, 2 * 512], BF16)

    xld = p.dma("sync", x_sb[:].rearrange("p (t d) -> p t d", d=D),
                xin.rearrange("(t p) d -> p t d", p=128), sem="x")
    wld = p.dma("gpsimd", wqk[:].rearrange("p (c n) -> p c n", n=2 * D),
                wqkv.rearrange("(c p) n -> p c n", p=128)[:, :, 0:2 * D],
                sem="w")
    ln_to_fm(b, x_sb, xh_sb, xhT, scratch, stats, x_tok=xld)
    fm_done = p.engine_tok("vector")

    qt_r = qt_o.rearrange("(c p) n -> p c n", p=128)
    kt_r = kt_o.rearrange("(c p) n -> p c n", p=128)
    ktok_r = ktok_o.rearrange("(t p) d -> p t d", p=128)
    final = []
    for which, outr in ((0, qt_r), (1, kt_r)):
        for hc in range(NDC):
            for s4 in range(4):
                bank = s4 % 2 + which * 2
                after0 = [fm_done, wld, b.psum_free[bank]]
                for dc in range(NDC):
                    p.op("tensor", lambda t, which=which, hc=hc, s4=s4, dc=dc,
                         bank=bank, st=(dc == 0), sp=(dc == NDC - 1): t.matmul(
                             b.psum[bank][:, 0:512],
                             wqk[:, dc * 2 * D + which * D + hc * 128:
                                 dc * 2 * D + which * D + (hc + 1) * 128],
                             xhT[:, dc * TOK + s4 * 512: dc * TOK + (s4 + 1) * 512],
                             start=st, stop=sp), after=after0 if dc == 0 else ())
                mm = p.engine_tok("tensor")
                sl = ev_sb[:, (s4 % 2) * 512:(s4 % 2 + 1) * 512]
                ev = p.op("vector", lambda v, sl=sl, bank=bank: v.tensor_copy(
                    sl, b.psum[bank][:, 0:512]), after=mm)
                b.psum_free[bank] = ev
                st = p.dma("sync", outr[:, hc, s4 * 512:(s4 + 1) * 512], sl,
                           sem="st", after=ev)
                final.append(st)
                p.last_tok["vector"] = st
    ev2 = b.sb("ev2", [128, 2 * 512], BF16)
    for t in range(NT):
        for dc2 in range(2):
            bank = 4 + (t * 2 + dc2) % 2
            after0 = [fm_done, wld, b.psum_free[bank]]
            for dc in range(NDC):
                p.op("tensor", lambda tt, t=t, dc2=dc2, dc=dc, bank=bank,
                     st=(dc == 0), sp=(dc == NDC - 1): tt.matmul(
                         b.psum[bank][:, 0:512],
                         xhT[:, dc * TOK + t * 128: dc * TOK + (t + 1) * 128],
                         wqk[:, dc * 2 * D + D + dc2 * 512:
                             dc * 2 * D + D + (dc2 + 1) * 512],
                         start=st, stop=sp), after=after0 if dc == 0 else ())
            mm = p.engine_tok("tensor")
            sl = ev2[:, dc2 * 512:(dc2 + 1) * 512]
            ev = p.op("vector", lambda v, sl=sl, bank=bank: v.tensor_copy(
                sl, b.psum[bank][:, 0:512]), after=mm)
            b.psum_free[bank] = ev
            st = p.dma("sync", ktok_r[:, t, dc2 * 512:(dc2 + 1) * 512], sl,
                       sem="st", after=ev)
            final.append(st)
            p.last_tok["vector"] = st
    p.finalize(final[-1:])
    return b


def build_l1a_attn():
    b = Builder("l1a_attn")
    p = b.p
    xin = b.inp("xin", [TOK, D])
    qt_i = b.inp("qt", [D, TOK], BF16)
    kt_i = b.inp("kt", [D, TOK], BF16)
    ktok_i = b.inp("ktok", [TOK, D], BF16)
    khT_i = b.inp("khT", [D, 256], BF16)
    khtok_i = b.inp("khtok", [256, D], BF16)
    w_out = b.inp("w_out", [D, D])
    b_out = b.inp("b_out", [D])
    xout = b.outp("xout", [TOK, D])

    QT = b.sb("QT", [128, NDC * TOK], BF16)
    KTf = b.sb("KTf", [128, NDC * TOK], BF16)
    Ktok = b.sb("Ktok", [128, NT * D], BF16)
    KhT = b.sb("KhT", [128, NDC * 256], BF16)
    Khtok = b.sb("Khtok", [128, 2 * D], BF16)
    wo = b.sb("wo", [128, NDC * D], BF16)
    ob_row = b.sb("ob_row", [1, D], BF16)
    ones_bf = b.sb("ones_bf", [1, 128], BF16)
    Pw = b.sb("Pw", [128, 256], BF16)
    PTw = b.sb("PTw", [128, 256], BF16)
    ao = b.sb("ao", [128, D], BF16)
    aoT = b.sb("aoT", [128, NDC * 128], BF16)
    lcol = b.sb("lcol", [128, 6])
    maskw = b.const("maskw", np.where(
        np.arange(128)[None, :] > np.arange(128)[:, None],
        np.float32(BIGNEG), np.float32(0.0)).astype(np.float32))
    mask_sb = b.sb("maskw_sb", [128, 128])
    xs_sb = b.sb("xs", [128, 2 * D])
    out_sb = b.sb("outs", [128, 512])
    ident_sb, id_tok = b.ident()

    qld = p.dma("sync", QT[:].rearrange("p (c n) -> p c n", n=TOK),
                qt_i.rearrange("(c p) n -> p c n", p=128), sem="x")
    kld = p.dma("sync", KTf[:].rearrange("p (c n) -> p c n", n=TOK),
                kt_i.rearrange("(c p) n -> p c n", p=128), sem="x")
    k2ld = p.dma("sync", Ktok[:].rearrange("p (t d) -> p t d", d=D),
                 ktok_i.rearrange("(t p) d -> p t d", p=128), sem="x")
    h1 = p.dma("sync", KhT[:].rearrange("p (c n) -> p c n", n=256),
               khT_i.rearrange("(c p) n -> p c n", p=128), sem="x")
    h2 = p.dma("sync", Khtok[:].rearrange("p (t d) -> p t d", d=D),
               khtok_i.rearrange("(t p) d -> p t d", p=128), sem="x")
    wld = p.dma("gpsimd", wo[:].rearrange("p (c n) -> p c n", n=D),
                w_out.rearrange("(c p) n -> p c n", p=128), sem="w")
    bld = p.dma("gpsimd", ob_row[:], b_out.rearrange("(o n) -> o n", o=1),
                sem="w")
    mld = p.dma("sync", mask_sb[:], maskw, sem="m")
    ot = p.op("vector", lambda v: v.memset(ones_bf[:], 1.0))
    inp_tok = h2
    wtok = bld

    xout_r = xout.rearrange("(t p) d -> p t d", p=128)
    final = []
    pw_free = None
    xld_toks = [None, None]
    for t in range(NT):
        for wb in range(2):
            bank = wb
            after0 = [inp_tok, qld, kld, mld, b.psum_free[bank]]
            for dc in range(NDC):
                if wb == 1:
                    rhs = KTf[:, dc * TOK + t * 128: dc * TOK + (t + 1) * 128]
                elif t % 8 == 0:
                    hb = t // 8
                    rhs = KhT[:, dc * 256 + hb * 128: dc * 256 + (hb + 1) * 128]
                else:
                    rhs = KTf[:, dc * TOK + (t - 1) * 128: dc * TOK + t * 128]
                lhsT = QT[:, dc * TOK + t * 128: dc * TOK + (t + 1) * 128]
                p.op("tensor", lambda tt, lhsT=lhsT, rhs=rhs, bank=bank,
                     st=(dc == 0), sp=(dc == NDC - 1): tt.matmul(
                         b.psum[bank][:, 0:128], lhsT, rhs, start=st, stop=sp),
                     after=after0 if dc == 0 else ())
        mm = p.engine_tok("tensor")
        mk = p.op("vector", lambda v: v.tensor_add(
            b.psum[1][:, 0:128], b.psum[1][:, 0:128], mask_sb[:]),
            after=mm, chain=False)
        r1 = p.op("vector", lambda v: v.reduce_max(
            lcol[:, 0:1], b.psum[0][:, 0:128], axis=AX.X), after=mm)
        r2 = p.op("vector", lambda v: v.reduce_max(
            lcol[:, 1:2], b.psum[1][:, 0:128], axis=AX.X), after=mk)
        r3 = p.op("vector", lambda v: v.tensor_max(lcol[:, 0:1], lcol[:, 0:1],
                                                   lcol[:, 1:2]))
        r4 = p.op("vector", lambda v: v.tensor_scalar(
            lcol[:, 1:2], lcol[:, 0:1], -0.03125, None, ALU.mult))
        e1 = p.op("scalar", lambda s: s.activation(
            Pw[:, 0:128], b.psum[0][:, 0:128], AF.Exp, bias=lcol[:, 1:2],
            scale=0.03125, accum_out=lcol[:, 2:3]), after=[r4, pw_free])
        e2 = p.op("scalar", lambda s: s.activation(
            Pw[:, 128:256], b.psum[1][:, 0:128], AF.Exp, bias=lcol[:, 1:2],
            scale=0.03125, accum_out=lcol[:, 3:4]), after=r4)
        b.psum_free[0] = e1
        b.psum_free[1] = e2
        l1 = p.op("vector", lambda v: v.tensor_add(lcol[:, 4:5], lcol[:, 2:3],
                                                   lcol[:, 3:4]), after=e2)
        l2 = p.op("vector", lambda v: v.reciprocal(lcol[:, 5:6], lcol[:, 4:5]))
        ps = p.op("vector", lambda v: v.tensor_scalar(
            Pw[:], Pw[:], lcol[:, 5:6], None, ALU.mult))
        evs = emit_transpose(b, [(Pw[:, 0:128], PTw[:, 0:128], ps),
                                 (Pw[:, 128:256], PTw[:, 128:256], ps)],
                             banks=(4, 5))
        pw_free = p.engine_tok("tensor")
        for dc2 in range(2):
            bank = 2 + dc2
            after0 = [evs[1], b.psum_free[bank]]
            for ks in range(2):
                if ks == 0:
                    if t % 8 == 0:
                        vsrc = Khtok[:, (t // 8) * D + dc2 * 512:
                                     (t // 8) * D + (dc2 + 1) * 512]
                    else:
                        vsrc = Ktok[:, (t - 1) * D + dc2 * 512:
                                    (t - 1) * D + (dc2 + 1) * 512]
                else:
                    vsrc = Ktok[:, t * D + dc2 * 512: t * D + (dc2 + 1) * 512]
                p.op("tensor", lambda tt, ks=ks, vsrc=vsrc, bank=bank,
                     st=(ks == 0), sp=(ks == 1): tt.matmul(
                         b.psum[bank][:, 0:512],
                         PTw[:, ks * 128:(ks + 1) * 128], vsrc,
                         start=st, stop=sp), after=after0 if ks == 0 else ())
            mm2 = p.engine_tok("tensor")
            aev = p.op("scalar", lambda sc, dc2=dc2, bank=bank: sc.copy(
                ao[:, dc2 * 512:(dc2 + 1) * 512], b.psum[bank][:, 0:512]),
                after=mm2)
            b.psum_free[bank] = aev
        ao_done = p.engine_tok("scalar")
        prs = []
        for dc in range(NDC):
            prs.append((ao[:, dc * 128:(dc + 1) * 128],
                        aoT[:, dc * 128:(dc + 1) * 128], ao_done))
        evs2 = emit_transpose(b, prs, banks=(4, 5))
        aoT_done = evs2[-1]
        sl = t % 2
        xt = p.dma("sync", xs_sb[:, sl * D:(sl + 1) * D],
                   xin.rearrange("(tt p) d -> p tt d", p=128)[:, t, :],
                   sem="xs" + str(sl), after=xld_toks[sl])
        for dc2 in range(2):
            bank = 6 + dc2
            after0 = [aoT_done, wtok, ot, b.psum_free[bank]]
            for hc in range(NDC):
                p.op("tensor", lambda tt, hc=hc, dc2=dc2, bank=bank,
                     st=(hc == 0): tt.matmul(
                         b.psum[bank][:, 0:512],
                         aoT[:, hc * 128:(hc + 1) * 128],
                         wo[:, hc * D + dc2 * 512: hc * D + (dc2 + 1) * 512],
                         start=st, stop=False), after=after0 if hc == 0 else ())
            p.op("tensor", lambda tt, dc2=dc2, bank=bank: tt.matmul(
                b.psum[bank][:, 0:512], ones_bf[:],
                ob_row[:, dc2 * 512:(dc2 + 1) * 512], start=False, stop=True))
            mm3 = p.engine_tok("tensor")
            ev = p.op("vector", lambda v, dc2=dc2, sl=sl, bank=bank:
                      v.tensor_add(out_sb[:], b.psum[bank][:, 0:512],
                                   xs_sb[:, sl * D + dc2 * 512:
                                         sl * D + (dc2 + 1) * 512]),
                      after=[mm3, xt])
            b.psum_free[bank] = ev
            st = p.dma("sync", xout_r[:, t, dc2 * 512:(dc2 + 1) * 512],
                       out_sb[:], sem="st", after=ev)
            final.append(st)
            p.last_tok["vector"] = st
        xld_toks[sl] = p.engine_tok("vector")
        pw_free = p.engine_tok("tensor")
    p.finalize(final[-1:])
    return b



LCH = 64
NCH = TOK // LCH
NST = 16


def build_ssm_tables():
    b = Builder("ssm_tables")
    p = b.p
    A_i = b.inp("A_g", [128, NST, NST])
    B_i = b.inp("B_g", [128, NST])
    C_i = b.inp("C_g", [128, NST])
    D_i = b.inp("D_g", [128])
    k_o = b.outp("k_o", [128, LCH])
    ur_o = b.outp("ur_o", [128, LCH * NST])
    vt_o = b.outp("vt_o", [128, LCH * NST])

    AL = b.sb("AL", [128, 6 * 256])
    U = b.sb("U", [128, LCH * NST])
    W = b.sb("W", [128, LCH * NST])
    Ur = b.sb("Ur", [128, LCH * NST])
    Bm = b.sb("Bm", [128, NST])
    Cm = b.sb("Cm", [128, NST])
    Dm = b.sb("Dm", [128, 1])
    kt = b.sb("kt", [128, LCH])
    tmp = b.sb("tmp", [128, 1024])

    ld1 = p.dma("sync", AL[:, 0:256], A_i.rearrange("p n m -> p (n m)"), sem="x")
    ld2 = p.dma("sync", Bm[:], B_i, sem="x")
    ld3 = p.dma("sync", Cm[:], C_i, sem="x")
    ld4 = p.dma("sync", Dm[:], D_i.rearrange("(p o) -> p o", o=1), sem="x")
    lds = ld4

    def matgrid(buf, off):
        return buf[:, off:off + 256].rearrange("p (n m) -> p n m", m=16)

    for j in range(5):
        src = AL[:, j * 256:(j + 1) * 256]
        dst = AL[:, (j + 1) * 256:(j + 2) * 256]
        for m in range(NST):
            in0 = src.rearrange("p (n m) -> p n m", m=16)[:, :, m:m + 1] \
                .broadcast_to([128, 16, 16])
            in1 = src[:, m * 16:(m + 1) * 16] \
                .rearrange("p (o q) -> p o q", o=1).broadcast_to([128, 16, 16])
            dv = dst.rearrange("p (n q) -> p n q", q=16)
            if m == 0:
                p.op("vector", lambda v, in0=in0, in1=in1, dv=dv:
                     v.tensor_tensor(dv, in0, in1, ALU.mult), after=lds)
            else:
                p.op("vector", lambda v, in0=in0, in1=in1:
                     v.tensor_tensor(tmp[:, 0:256]
                                     .rearrange("p (n q) -> p n q", q=16),
                                     in0, in1, ALU.mult))
                p.op("vector", lambda v, dv=dv: v.tensor_add(
                    dv, dv, tmp[:, 0:256].rearrange("p (n q) -> p n q", q=16)))

    p.op("vector", lambda v: v.tensor_copy(U[:, 0:16], Bm[:]), after=lds)
    for m in range(NST):
        in0 = AL[:, m * 16:(m + 1) * 16]
        cmc = Cm[:, m:m + 1].broadcast_to([128, 16])
        if m == 0:
            p.op("vector", lambda v, in0=in0, cmc=cmc: v.tensor_tensor(
                W[:, 0:16], in0, cmc, ALU.mult))
        else:
            p.op("vector", lambda v, in0=in0, cmc=cmc: v.tensor_tensor(
                tmp[:, 0:16], in0, cmc, ALU.mult))
            p.op("vector", lambda v: v.tensor_add(W[:, 0:16], W[:, 0:16],
                                                  tmp[:, 0:16]))

    for buf, trans in ((U, False), (W, True)):
        for j, L in enumerate((1, 2, 4, 8, 16, 32)):
            alo = j * 256
            for m in range(NST):
                if not trans:
                    in0 = AL[:, alo:alo + 256] \
                        .rearrange("p (n m) -> p n m", m=16)[:, :, m:m + 1] \
                        .rearrange("p n o -> p o n").broadcast_to([128, L, 16])
                    in1 = buf[:, 0:L * 16] \
                        .rearrange("p (t n) -> p t n", n=16)[:, :, m:m + 1] \
                        .broadcast_to([128, L, 16])
                else:
                    in0 = AL[:, alo + m * 16:alo + (m + 1) * 16] \
                        .rearrange("p (o n) -> p o n", o=1) \
                        .broadcast_to([128, L, 16])
                    in1 = buf[:, 0:L * 16] \
                        .rearrange("p (t n) -> p t n", n=16)[:, :, m:m + 1] \
                        .broadcast_to([128, L, 16])
                dv = buf[:, L * 16:2 * L * 16].rearrange("p (t n) -> p t n",
                                                         n=16)
                if m == 0:
                    p.op("vector", lambda v, in0=in0, in1=in1, dv=dv:
                         v.tensor_tensor(dv, in0, in1, ALU.mult))
                else:
                    p.op("vector", lambda v, in0=in0, in1=in1, L=L:
                         v.tensor_tensor(tmp[:, 0:L * 16]
                                         .rearrange("p (t n) -> p t n", n=16),
                                         in0, in1, ALU.mult))
                    p.op("vector", lambda v, dv=dv, L=L: v.tensor_add(
                        dv, dv, tmp[:, 0:L * 16]
                        .rearrange("p (t n) -> p t n", n=16)))

    cb = Cm[:].rearrange("p (o n) -> p o n", o=1).broadcast_to([128, LCH, 16])
    p.op("vector", lambda v: v.tensor_tensor(
        tmp[:, 0:1024].rearrange("p (t n) -> p t n", n=16),
        U[:].rearrange("p (t n) -> p t n", n=16), cb, ALU.mult))
    p.op("vector", lambda v: v.tensor_reduce(
        kt[:], tmp[:, 0:1024].rearrange("p (t n) -> p t n", n=16),
        axis=AX.X, op=ALU.add))
    p.op("vector", lambda v: v.tensor_add(kt[:, 0:1], kt[:, 0:1], Dm[:]))
    for sidx in range(LCH):
        p.op("vector", lambda v, sidx=sidx: v.tensor_copy(
            Ur[:, sidx * 16:(sidx + 1) * 16],
            U[:, (LCH - 1 - sidx) * 16:(LCH - sidx) * 16]), chain=(sidx == 0))
    done = p.engine_tok("vector")
    s1 = p.dma("sync", k_o, kt[:], sem="st", after=done)
    s2 = p.dma("sync", ur_o, Ur[:], sem="st")
    s3 = p.dma("sync", vt_o, W[:], sem="st")
    p.finalize([s3])
    return b


def build_l0b_a():
    b = Builder("l0b_a")
    p = b.p
    xin = b.inp("xin", [TOK, D])
    kt_i = b.inp("ktab", [D, LCH])
    ur_i = b.inp("urtab", [D, LCH * NST])
    yc_o = b.outp("yconv", [D, TOK])
    inj_o = b.outp("injs", [D, NCH * NST])

    x_sb = b.sb("x", [128, NT * D])
    xh_sb = b.sb("xh", [128, 2 * D], BF16)
    xhT = b.sb("xhT", [128, NDC * TOK], BF16)
    scratch = b.sb("scr", [128, D])
    stats = b.sb("stats", [128, 5 * NT])
    ktab = b.sb("ktab_sb", [128, NDC * LCH])
    urt = b.sb("urt", [128, NDC * LCH * NST])
    inj = b.sb("inj", [128, NDC * NCH * NST])
    tmp = b.sb("tmpb", [128, NCH * LCH])

    xld = p.dma("sync", x_sb[:].rearrange("p (t d) -> p t d", d=D),
                xin.rearrange("(t p) d -> p t d", p=128), sem="x")
    kld = p.dma("sync", ktab[:].rearrange("p (g n) -> p g n", n=LCH),
                kt_i.rearrange("(g p) n -> p g n", p=128), sem="w")
    uld = p.dma("sync", urt[:].rearrange("p (g n) -> p g n", n=LCH * NST),
                ur_i.rearrange("(g p) n -> p g n", p=128), sem="w")
    ln_to_fm(b, x_sb, xh_sb, xhT, scratch, stats, x_tok=xld)
    fm_done = p.engine_tok("vector")

    y = x_sb
    final = []
    for g in range(NDC):
        for tau in range(LCH):
            kcol = ktab[:, g * LCH + tau: g * LCH + tau + 1]
            xv = xhT[:, g * TOK: g * TOK + TOK] \
                .rearrange("p (c s) -> p c s", s=LCH)[:, :, 0:LCH - tau]
            yv = y[:, g * TOK: g * TOK + TOK] \
                .rearrange("p (c s) -> p c s", s=LCH)[:, :, tau:LCH]
            if tau == 0:
                p.op("vector", lambda v, xv=xv, yv=yv, kcol=kcol:
                     v.tensor_scalar(yv, xv, kcol, None, ALU.mult),
                     after=[fm_done, uld])
            else:
                p.op("vector", lambda v, xv=xv, yv=yv, kcol=kcol:
                     v.scalar_tensor_tensor(yv, xv, kcol, yv, ALU.mult,
                                            ALU.add))
    for g in range(NDC):
        for n in range(NST):
            xv = xhT[:, g * TOK: g * TOK + TOK] \
                .rearrange("p (c s) -> p c s", s=LCH)
            uv = urt[:, g * LCH * NST + n: g * LCH * NST + n + 1] \
                .rearrange("p (o q) -> p o q", o=1)
            uvb = urt[:, g * LCH * NST:(g + 1) * LCH * NST] \
                .rearrange("p (s n) -> p s n", n=NST)[:, :, n:n + 1] \
                .rearrange("p s o -> p o s").broadcast_to([128, NCH, LCH])
            tv = tmp[:].rearrange("p (c s) -> p c s", s=LCH)
            p.op("vector", lambda v, xv=xv, uvb=uvb, tv=tv:
                 v.tensor_tensor(tv, xv, uvb, ALU.mult))
            iv = inj[:, g * NCH * NST:(g + 1) * NCH * NST] \
                .rearrange("p (c q) -> p c q", q=NST)[:, :, n:n + 1]
            p.op("vector", lambda v, iv=iv, tv=tv: v.tensor_reduce(
                iv, tv, axis=AX.X, op=ALU.add))
    done = p.engine_tok("vector")
    s1 = p.dma("sync", yc_o.rearrange("(g p) n -> p g n", p=128),
               y[:].rearrange("p (g n) -> p g n", n=TOK), sem="st", after=done)
    s2 = p.dma("sync", inj_o.rearrange("(g p) n -> p g n", p=128),
               inj[:].rearrange("p (g n) -> p g n", n=NCH * NST), sem="st")
    p.finalize([s2])
    return b


def build_l0b_b():
    b = Builder("l0b_b")
    p = b.p
    xin = b.inp("xin", [TOK, D])
    yc_i = b.inp("yconv", [D, TOK])
    ip_i = b.inp("injprev", [D, NCH * NST])
    vt_i = b.inp("vtab", [D, LCH * NST])
    lnw_i = b.inp("ln_w", [D])
    lnb_i = b.inp("ln_b", [D])
    wo_i = b.inp("W_out", [D, D])
    bo_i = b.inp("b_out", [D])
    xout = b.outp("xout", [TOK, D])

    x_sb = b.sb("x", [128, NT * D])
    y_sb = b.sb("y", [128, NDC * TOK])
    injp = b.sb("injp", [128, NDC * NCH * NST])
    vg = b.sb("vg", [128, 2 * LCH * NST])
    tmp = b.sb("tmpb", [128, NCH * LCH])
    stats = b.sb("stats", [128, 96])
    scratch = b.sb("scr", [128, D])
    woT = b.sb("woT", [128, NDC * D], BF16)
    borow = b.sb("borow", [1, D], BF16)
    lnwb = b.sb("lnwb", [128, 2 * D])
    ytok = b.sb("ytok", [128, 2 * D])
    gel = b.sb("gel", [128, 2 * D], BF16)
    gfm = b.sb("gfm", [128, NDC * 128], BF16)
    ones_bf = b.sb("ones_bf", [1, 128], BF16)
    idf = b.const("identf", np.eye(128, dtype=np.float32))
    idf_sb = b.sb("identf_sb", [128, 128])
    out_sb = b.sb("outs", [128, 512])
    wo_bf = b.nc.dram_tensor("wo_bf", [D, D], BF16, kind="Internal").ap()

    xld = p.dma("sync", x_sb[:].rearrange("p (t d) -> p t d", d=D),
                xin.rearrange("(t p) d -> p t d", p=128), sem="x")
    yld = p.dma("sync", y_sb[:].rearrange("p (g n) -> p g n", n=TOK),
                yc_i.rearrange("(g p) n -> p g n", p=128), sem="x")
    ild = p.dma("sync", injp[:].rearrange("p (g n) -> p g n", n=NCH * NST),
                ip_i.rearrange("(g p) n -> p g n", p=128), sem="x")
    lwld = p.dma("sync", lnwb[:, 0:D], lnw_i.rearrange("(o n) -> o n", o=1)
                 .broadcast_to([128, D]), sem="w")
    lbld = p.dma("sync", lnwb[:, D:2 * D], lnb_i.rearrange("(o n) -> o n", o=1)
                 .broadcast_to([128, D]), sem="w")
    bold = p.dma("gpsimd", borow[:], bo_i.rearrange("(o n) -> o n", o=1),
                 sem="w2")
    wcast = p.dma("gpsimd", wo_bf, wo_i, sem="w2")
    wTs = []
    for dc in range(NDC):
        wTs.append(p.dma("sync", woT[:, dc * D:(dc + 1) * D],
                         wo_bf[:, dc * 128:(dc + 1) * 128], transpose=True,
                         sem="wT", after=wcast))
    wT_tok = wTs[-1]
    idld = p.dma("sync", idf_sb[:], idf, sem="w")
    ident_sb, id_tok = b.ident()
    ot = p.op("vector", lambda v: v.memset(ones_bf[:], 1.0))

    n = NT
    sums = stats[:, 0:n]; sumsq = stats[:, n:2 * n]; mu = stats[:, 2 * n:3 * n]
    rstd = stats[:, 3 * n:4 * n]; tmps = stats[:, 4 * n:5 * n]
    last_sq = None
    for t in range(NT):
        xt = x_sb[:, t * D:(t + 1) * D]
        last_sq = p.op("scalar", lambda s, xt=xt, t=t: s.activation(
            scratch[:], xt, AF.Square, accum_out=sumsq[:, t:t + 1]), after=xld)
        p.op("vector", lambda v, xt=xt, t=t: v.reduce_sum(
            sums[:, t:t + 1], xt, axis=AX.X), after=xld, chain=False)
    ln_tok = emit_ln_stats(b, sums, sumsq, mu, rstd, tmps, D,
                           after=(last_sq, p.engine_tok("vector")))

    vld = [None, None]
    for g in range(NDC):
        sl = g % 2
        vld[sl] = p.dma("sync", vg[:, sl * LCH * NST:(sl + 1) * LCH * NST],
                        vt_i.rearrange("(gg p) n -> p gg n", p=128)[:, g, :],
                        sem="vt" + str(sl), after=vld[sl])
        for nn in range(NST):
            vv = vg[:, sl * LCH * NST:(sl + 1) * LCH * NST] \
                .rearrange("p (t q) -> p t q", q=NST)[:, :, nn:nn + 1] \
                .rearrange("p t o -> p o t").broadcast_to([128, NCH, LCH])
            iv = injp[:, g * NCH * NST:(g + 1) * NCH * NST] \
                .rearrange("p (c q) -> p c q", q=NST)[:, :, nn:nn + 1] \
                .broadcast_to([128, NCH, LCH])
            tv = tmp[:].rearrange("p (c s) -> p c s", s=LCH)
            p.op("vector", lambda v, vv=vv, iv=iv, tv=tv: v.tensor_tensor(
                tv, vv, iv, ALU.mult), after=[vld[sl], ild, yld])
            yv = y_sb[:, g * TOK:(g + 1) * TOK] \
                .rearrange("p (c s) -> p c s", s=LCH)
            p.op("vector", lambda v, yv=yv, tv=tv: v.tensor_add(yv, yv, tv))
        vld[sl] = Tok("vector", p.cnt["vector"])
    corr_done = p.engine_tok("vector")

    xout_r = xout.rearrange("(t p) d -> p t d", p=128)
    final = []
    slot_free = [None, None]
    gfm_free = None
    ist = stats[:, 80:86]
    for t in range(NT):
        sl = t % 2
        yt = ytok[:, sl * D:(sl + 1) * D]
        for dc in range(NDC):
            src = y_sb[:, dc * TOK + t * 128: dc * TOK + (t + 1) * 128]
            dst = yt[:, dc * 128:(dc + 1) * 128]
            bk = (4, 5)[dc % 2]
            ps = b.psum[bk]
            tr = p.op("tensor", lambda tt, src=src, ps=ps: tt.transpose(
                ps[0:128, 0:128], src, idf_sb[:]),
                after=[corr_done, idld, b.psum_free[bk], slot_free[sl]])
            ev = p.op("vector", lambda v, dst=dst, ps=ps: v.tensor_copy(
                dst, ps[0:128, 0:128]), after=tr, chain=False)
            b.psum_free[bk] = ev
        ytr = p.engine_tok("vector")
        i0 = p.op("scalar", lambda s, yt=yt: s.activation(
            scratch[:], yt, AF.Square, accum_out=ist[:, 1:2]),
            after=[ytr, ln_tok])
        i1 = p.op("vector", lambda v, yt=yt: v.reduce_sum(
            ist[:, 0:1], yt, axis=AX.X), after=ytr)
        p.op("vector", lambda v: v.tensor_scalar(ist[:, 2:3], ist[:, 0:1],
                                                 1.0 / D, None, ALU.mult))
        p.op("vector", lambda v: v.tensor_mul(ist[:, 4:5], ist[:, 2:3],
                                              ist[:, 2:3]))
        p.op("vector", lambda v: v.tensor_scalar(ist[:, 5:6], ist[:, 1:2],
                                                 1.0 / D, 1e-5, ALU.mult,
                                                 ALU.add), after=i0)
        p.op("vector", lambda v: v.tensor_sub(ist[:, 4:5], ist[:, 5:6],
                                              ist[:, 4:5]))
        iq = p.op("scalar", lambda s: s.activation(ist[:, 5:6], ist[:, 4:5],
                                                   AF.Sqrt),
                  after=p.engine_tok("vector"))
        p.op("vector", lambda v: v.reciprocal(ist[:, 3:4], ist[:, 5:6]),
             after=iq)
        gl = gel[:, sl * D:(sl + 1) * D]
        p.op("vector", lambda v, yt=yt: v.tensor_scalar(
            scratch[:], yt, ist[:, 2:3], ist[:, 3:4], ALU.subtract, ALU.mult))
        p.op("vector", lambda v: v.tensor_mul(scratch[:], scratch[:],
                                              lnwb[:, 0:D]),
             after=[lwld])
        gv = p.op("vector", lambda v: v.tensor_add(scratch[:], scratch[:],
                                                   lnwb[:, D:2 * D]))
        ge = p.op("scalar", lambda s, gl=gl: s.activation(gl, scratch[:],
                                                          AF.Gelu),
                  after=[gv, gfm_free])
        prs2 = []
        for dc in range(NDC):
            prs2.append((gl[:, dc * 128:(dc + 1) * 128],
                         gfm[:, dc * 128:(dc + 1) * 128], ge))
        evs2 = emit_transpose(b, prs2, banks=(6, 7))
        g_done = evs2[-1]
        for dc2 in range(2):
            bank = dc2
            after0 = [g_done, wT_tok, bold, ot, b.psum_free[bank]]
            for hc in range(NDC):
                p.op("tensor", lambda tt, hc=hc, dc2=dc2, bank=bank,
                     st=(hc == 0): tt.matmul(
                         b.psum[bank][:, 0:512],
                         gfm[:, hc * 128:(hc + 1) * 128],
                         woT[:, hc * D + dc2 * 512: hc * D + (dc2 + 1) * 512],
                         start=st, stop=False), after=after0 if hc == 0 else ())
            p.op("tensor", lambda tt, dc2=dc2, bank=bank: tt.matmul(
                b.psum[bank][:, 0:512], ones_bf[:],
                borow[:, dc2 * 512:(dc2 + 1) * 512], start=False, stop=True))
            mm = p.engine_tok("tensor")
            xs = x_sb[:, t * D + dc2 * 512: t * D + (dc2 + 1) * 512]
            xh1 = p.op("vector", lambda v, xs=xs, t=t: v.tensor_scalar(
                scratch[:, dc2 * 512:(dc2 + 1) * 512] if False else
                scratch[:, 0:512], xs, mu[:, t:t + 1], rstd[:, t:t + 1],
                ALU.subtract, ALU.mult), after=[ln_tok, mm])
            ev1 = p.op("vector", lambda v, xs=xs, bank=bank: v.tensor_add(
                out_sb[:], b.psum[bank][:, 0:512], xs))
            ev2 = p.op("vector", lambda v: v.tensor_add(
                out_sb[:], out_sb[:], scratch[:, 0:512]))
            b.psum_free[bank] = ev2
            st = p.dma("sync", xout_r[:, t, dc2 * 512:(dc2 + 1) * 512],
                       out_sb[:], sem="st", after=ev2)
            final.append(st)
            p.last_tok["vector"] = st
        slot_free[sl] = p.engine_tok("tensor")
        gfm_free = p.engine_tok("tensor")
    p.finalize(final[-1:])
    return b


def _run(b, in_maps):
    for m in in_maps:
        m.update(b.consts)
    res = run_bass_kernel_spmd(b.nc, in_maps, core_ids=list(range(N_CORES)))
    return res.results



from concourse.bass import IndirectOffsetOnAxis

I32 = mybir.dt.int32


def fused_gather_idx_ffn0(c):
    r, r4 = c // 4, c % 4
    j = r4
    idx = np.empty(2048, np.int32)
    kpos = 0
    for e in (2 * j, 2 * j + 1):
        for t in range(e * 1024, (e + 1) * 1024):
            g = t // 128
            i = g % 4
            idx[kpos] = i * 2048 + (g // 4) * 128 + t % 128
            kpos += 1
    return idx.reshape(16, 128)


def fused_gather_idx_delta(c):
    r, r4 = c // 4, c % 4
    idx = np.empty(2048, np.int32)
    for tl in range(16):
        g = 4 * tl + r4
        for p in range(128):
            t = g * 128 + p
            e = t // 1024
            jsrc = e // 2
            pos = (e % 2) * 1024 + (t % 1024)
            idx[tl * 128 + p] = jsrc * 2048 + pos
    return idx.reshape(16, 128)


def fused_gather_idx_prevtile(c, oob=100000):
    r, r4 = c // 4, c % 4
    idx = np.empty((16, 128), np.int32)
    for tl in range(16):
        g = 4 * tl + r4 - 1
        if g < 0:
            idx[tl] = oob
        else:
            i = g % 4
            idx[tl] = i * 2048 + (g // 4) * 128 + np.arange(128)
    return idx


def fused_mask4(c):
    r4 = c % 4
    kp = np.arange(128)[:, None]
    qp = np.arange(128)[None, :]
    out = np.zeros((128, 4 * 128), np.float32)
    for cc in range(4):
        if cc > r4:
            out[:, cc * 128:(cc + 1) * 128] = BIGNEG
        elif cc == r4:
            out[:, cc * 128:(cc + 1) * 128] = np.where(kp > qp, BIGNEG, 0.0)
    return out.astype(bf16)


def emit_mha_st(b, KT, Vge, QT2, negm, outfm, mask4_sb, scale, kv_tok,
                q_tok, negm_tok, mask_tok, f32ev, pt_slot, hg, sbanks=(0, 1),
                obank=2, t2bank=3):
    p = b.p
    ident_sb, id_tok = b.ident()
    idf_sb, idf_tok = b.identf()
    for h in range(2):
        for gI in range(4):
            ob = b.psum[obank]
            ob_free = b.psum_free[obank]
            nkt = 16 * (gI + 1)
            pv_q = []
            for kt in range(nkt):
                jmin = max(0, (kt - 16 * gI - 4) // 4 + 1) if kt >= 16 * gI + 4 else 0
                live = 4 - jmin
                sb_ = sbanks[kt % 2]
                ps = b.psum[sb_]
                after0 = [kv_tok, q_tok, negm_tok, b.psum_free[sb_]]
                p.op("tensor", lambda t, h=h, gI=gI, kt=kt, jmin=jmin,
                     live=live, ps=ps: t.matmul(
                         ps[:, 0:live * 128],
                         KT[h * 64:(h + 1) * 64, kt * 128:(kt + 1) * 128],
                         QT2[h * 64:(h + 1) * 64,
                             (4 * gI + jmin) * 128:(4 * gI + 4) * 128],
                         start=True, stop=True), after=after0)
                mm = p.engine_tok("tensor")
                if kt >= 16 * gI:
                    jstar = (kt - 16 * gI) // 4
                    cc = kt - 16 * gI - 4 * jstar
                    mk = p.op("vector", lambda v, ps=ps, jstar=jstar, cc=cc,
                              jmin=jmin: v.tensor_add(
                                  ps[:, (jstar - jmin) * 128:
                                     (jstar - jmin + 1) * 128],
                                  ps[:, (jstar - jmin) * 128:
                                     (jstar - jmin + 1) * 128],
                                  mask4_sb[:, cc * 128:(cc + 1) * 128]),
                              after=[mm, mask_tok], chain=False)
                    mm = mk
                pslot = pt_slot[:, (kt % 2) * 512:(kt % 2) * 512 + live * 128]
                ex = p.op("scalar", lambda s, ps=ps, pslot=pslot, h=h:
                          s.activation(pslot, ps[:, 0:pslot.shape[-1]], AF.Exp,
                                       bias=negm[:, h:h + 1], scale=scale),
                          after=[mm, b.pt_free[kt % 2]])
                b.psum_free[sb_] = ex
                pv_q.append((kt, ex, jmin, pslot))
                if len(pv_q) >= 2:
                    kq, exq, jm, pl = pv_q.pop(0)
                    p.op("tensor", lambda t, kq=kq, h=h, jm=jm, pl=pl, ob=ob,
                         st=(kq == 0), sp=(kq == nkt - 1):
                         t.matmul(ob[0:65, jm * 128:512],
                                  Vge[:, kq * 130 + h * 65:
                                      kq * 130 + h * 65 + 65],
                                  pl, start=st, stop=sp),
                         after=[exq] + ([ob_free] if kq == 0 else []))
                    b.pt_free[kq % 2] = p.engine_tok("tensor")
            while pv_q:
                kq, exq, jm, pl = pv_q.pop(0)
                p.op("tensor", lambda t, kq=kq, h=h, jm=jm, pl=pl, ob=ob,
                     st=(kq == 0), sp=(kq == nkt - 1):
                     t.matmul(ob[0:65, jm * 128:512],
                              Vge[:, kq * 130 + h * 65: kq * 130 + h * 65 + 65],
                              pl, start=st, stop=sp),
                     after=[exq] + ([ob_free] if kq == 0 else []))
                b.pt_free[kq % 2] = p.engine_tok("tensor")
            mmo = p.engine_tok("tensor")
            evf = p.op("vector", lambda v, ob=ob: v.tensor_copy(
                f32ev[0:65, 0:512], ob[0:65, 0:512]), after=mmo)
            b.psum_free[obank] = evf
            for j in range(4):
                tb = b.psum[t2bank]
                tr1 = p.op("tensor", lambda t, j=j, tb=tb: t.transpose(
                    tb[0:128, 0:65], f32ev[0:65, j * 128:(j + 1) * 128],
                    idf_sb[0:65, 0:65]),
                    after=[evf, idf_tok, b.psum_free[t2bank]])
                rc = p.op("vector", lambda v, tb=tb: v.reciprocal(
                    b.sc_col[:, 0:1], tb[0:128, 64:65]), after=tr1)
                ts = p.op("vector", lambda v, tb=tb: v.tensor_scalar(
                    b.bf_col[:, 0:64], tb[0:128, 0:64], b.sc_col[:, 0:1],
                    None, ALU.mult))
                b.psum_free[t2bank] = ts
                tr2 = p.op("tensor", lambda t, j=j, tb=tb: t.transpose(
                    b.psum[t2bank].bitcast(BF16)[0:64, 256:384],
                    b.bf_col[:, 0:64], ident_sb[:]),
                    after=[ts, id_tok])
                ev2 = p.op("vector", lambda v, h=h, gI=gI, j=j, hg=hg:
                           v.tensor_copy(
                               outfm[h * 64:(h + 1) * 64,
                                     hg * 2048 + (4 * gI + j) * 128:
                                     hg * 2048 + (4 * gI + j + 1) * 128],
                               b.psum[t2bank].bitcast(BF16)[0:64, 256:384]),
                           after=tr2, chain=False)
                b.psum_free[t2bank] = ev2


def fused_barrier(b):
    p = b.p
    toks = [Tok(e, p.cnt[e]) for e in ENGINES]
    toks += [Tok(k, p.cnt[k]) for k in p.sems if k.startswith("dma_")]
    for e in ENGINES:
        for t in toks:
            if t.val > 0:
                p._wait(e, t)


class NS:
    pass


def build_fused(upto=99, dbg=()):
    b = Builder("fused")
    p = b.p
    c = NS()
    c.dbg = set(dbg)

    def dump(name, ap, dt=None):
        if name in c.dbg:
            fused_barrier(b)
            o = b.outp("dbg_" + name, [128, ap.shape[-1]],
                       dt if dt is not None else BF16)
            c.final.append(p.dma("sync", o, ap, sem="dbg"))
            fused_barrier(b)
    c.dump = dump
    c.xin = b.inp("xin", [TOK, D])
    c.cmask4 = b.inp("cmask4", [128, 512], BF16)
    c.sin_t = b.inp("sin_t", [TOK, 32])
    c.cos_t = b.inp("cos_t", [TOK, 32])
    c.A_g = b.inp("A_g", [128, NST, NST])
    c.B_g = b.inp("B_g", [128, NST])
    c.C_g = b.inp("C_g", [128, NST])
    c.D_g = b.inp("D_g", [128])
    c.idx_prev = b.inp("idx_prev", [16, 128], I32)
    c.W_dq = b.inp("W_dq", [D, DC_KV])
    c.W_uq = b.inp("W_uq", [DC_KV, D])
    c.W_dkv = b.inp("W_dkv", [D, DC_KV])
    c.W_uk = b.inp("W_uk", [DC_KV, D])
    c.W_uv = b.inp("W_uv", [DC_KV, D])
    c.W_o = b.inp("W_o", [D, D])
    c.f0w1 = b.inp("f0w1", [D, DFF])
    c.f0w3 = b.inp("f0w3", [D, DFF])
    c.f0w2 = b.inp("f0w2", [DFF, D])
    c.ln_w = b.inp("ln_w", [D])
    c.ln_b = b.inp("ln_b", [D])
    c.Wb_out = b.inp("Wb_out", [D, D])
    c.bb_out = b.inp("bb_out", [D])
    c.wqkv_a = b.inp("wqkv_a", [D, 3 * D])
    c.wout_a = b.inp("wout_a", [D, D])
    c.bout_a = b.inp("bout_a", [D])
    c.qkv_w = b.inp("qkv_w", [D, 3 * D])
    c.qkv_b = b.inp("qkv_b", [3 * D])
    c.cmaskb = b.inp("cmaskb", [128, 512])
    c.out_w = b.inp("out_w", [D, D])
    c.out_b = b.inp("out_b", [D])
    c.Wv = b.inp("Wv", [D, D])
    c.bv = b.inp("bv", [D])
    c.Wg = b.inp("Wg", [NE, D])
    c.WgT = b.inp("WgT", [D, NE])
    c.bg = b.inp("bg", [NE])
    c.Wp = b.inp("Wp", [NE, NE, D])
    c.alpha = b.inp("alpha", [1])
    c.g_r = b.inp("g_r", [NE])
    c.b_r = b.inp("b_r", [NE])
    c.g_c = b.inp("g_c", [NE])
    c.b_c = b.inp("b_c", [NE])
    c.W1 = b.inp("W1", [NE, NI, D])
    c.b1 = b.inp("b1", [NE, NI])
    c.W2 = b.inp("W2", [NE, D, NI])
    c.b2 = b.inp("b2", [NE, D])
    c.xout = b.outp("xout", [TOK, D])

    c.x = b.sb("x_res", [128, NT * D])
    c.xhT = b.sb("fxhT", [128, NDC * TOK], BF16)
    c.A = b.sb("arA", [128, 16384], BF16)
    c.Bb = b.sb("arB", [128, 16384], BF16)
    c.Cb = b.sb("arC", [128, 16512], BF16)
    c.S4 = b.sb("arS", [128, 3072])
    c.scratch = c.S4[:, 1024:2048]
    c.stats = b.sb("fstats", [128, 96])
    c.ones_bf = b.sb("fones", [1, 128], BF16)
    c.negm = b.sb("fnegm", [128, 2])
    c.norm = b.sb("fnorm", [128, 4])
    c.msc = b.sb("fmsc", [1, 2])
    c.gains = b.sb("fgains", [128, 32])
    c.sig = b.sb("fsig", [128, 1])
    c.galph = b.sb("fgalph", [1, 2])
    c.mask4 = b.sb("fmask4", [128, 512], BF16)
    c.maskw = b.sb("fmaskw", [128, 128], BF16)
    c.idxs = b.sb("fidxs", [128, 48], I32)
    c.qkbf = b.sb("fqkbf", [128, 1], BF16)

    dr = b.nc.dram_tensor
    c.tab_in = dr("tab_in", [128, 2112], F32, kind="Internal").ap()
    c.tab_out = dr("tab_out", [1024, 2112], F32, kind="Internal").ap()
    c.ckv_in = dr("ckv_in", [256, TOK], BF16, kind="Internal").ap()
    c.ckv_out = dr("ckv_out", [1024, TOK], BF16, kind="Internal").ap()
    c.tok_in = dr("tok_in", [TOK, D], BF16, kind="Internal").ap()
    c.tok_out = dr("tok_out", [TOK, D], BF16, kind="Internal").ap()
    c.tok2_in = dr("tok2_in", [TOK, D], BF16, kind="Internal").ap()
    c.tok2_out = dr("tok2_out", [TOK, D], BF16, kind="Internal").ap()
    c.fm_in = dr("fm_in", [D, TOK], BF16, kind="Internal").ap()
    c.fm_out = dr("fm_out", [4 * D, TOK], BF16, kind="Internal").ap()
    c.halo_out = dr("halo_out", [4 * TOK, D], BF16, kind="Internal").ap()
    c.inj_in = dr("inj_in", [TOK, 128], F32, kind="Internal").ap()
    c.inj_out = dr("inj_out", [4 * TOK, 128], F32, kind="Internal").ap()
    c.negm_dr = dr("fnegm_dr", [1, 2], F32, kind="Internal").ap()
    c.sig_dr = dr("fsig_dr", [1, 1], F32, kind="Internal").ap()
    c.w_bf = dr("fw_bf", [D, D], BF16, kind="Internal").ap()
    c.w_bf2 = dr("fw_bf2", [NE * NE, D], BF16, kind="Internal").ap()
    c.w_bf3 = dr("fw_bf3", [NE * NI, D], BF16, kind="Internal").ap()

    c.final = []
    c.sect = 0

    xld = p.dma("sync", c.x[:].rearrange("p (t d) -> p t d", d=D),
                c.xin.rearrange("(t p) d -> p t d", p=128), sem="x")
    p.dma("sync", c.mask4[:], c.cmask4, sem="cst")
    mwc = b.const("maskwc", np.where(
        np.arange(128)[None, :] > np.arange(128)[:, None],
        np.float32(BIGNEG), np.float32(0.0)).astype(bf16))
    p.dma("sync", c.maskw[:], mwc, sem="cst")
    cst_tok = p.dma("sync", c.idxs[:, 32:48],
                    c.idx_prev.rearrange("t p -> p t"), sem="cst",
                    allow_slow_non_contiguous=True)
    ot = p.op("vector", lambda v: v.memset(c.ones_bf[:], 1.0))
    b.ident()
    b.identf()
    c.xld = xld
    c.cst_tok = cst_tok
    c.ot = ot
    emit_fused_tables(b, c)
    if upto >= 1:
        emit_fused_l0a(b, c)
    if upto >= 2:
        emit_fused_ffn0(b, c)
    if upto >= 3:
        emit_fused_ssm(b, c)
    if upto >= 4:
        emit_fused_ffn0(b, c)
    if upto >= 5:
        emit_fused_l1a(b, c)
    if upto >= 6:
        emit_fused_ffn1(b, c)
    if upto >= 7:
        emit_fused_l1b(b, c)
    if upto >= 8:
        emit_fused_ffn1(b, c)

    fused_barrier(b)
    xout_r = c.xout.rearrange("(t p) d -> p t d", p=128)
    for t in range(NT):
        st = p.dma("sync", xout_r[:, t, :], c.x[:, t * D:(t + 1) * D], sem="st")
        c.final.append(st)
    p.finalize(c.final[-1:])
    return b


def emit_fused_tables(b, c):
    p = b.p
    Af = c.A[:].bitcast(F32)
    AL = Af[:, 0:1536]
    U = Af[:, 1536:2560]
    W = Af[:, 2560:3584]
    Ur = Af[:, 3584:4608]
    tmp = Af[:, 4608:5632]
    Bm = Af[:, 5632:5648]
    Cm = Af[:, 5648:5664]
    Dm = Af[:, 5664:5665]
    kt = Af[:, 5680:5744]
    ld1 = p.dma("sync", AL[:, 0:256], c.A_g.rearrange("p n m -> p (n m)"),
                sem="tb")
    p.dma("sync", Bm, c.B_g, sem="tb")
    p.dma("sync", Cm, c.C_g, sem="tb")
    lds = p.dma("sync", Dm, c.D_g.rearrange("(p o) -> p o", o=1), sem="tb")
    for j in range(5):
        src = AL[:, j * 256:(j + 1) * 256]
        dst = AL[:, (j + 1) * 256:(j + 2) * 256]
        for m in range(NST):
            in0 = src.rearrange("p (n mm) -> p n mm", mm=16)[:, :, m:m + 1] \
                .broadcast_to([128, 16, 16])
            in1 = src[:, m * 16:(m + 1) * 16] \
                .rearrange("p (o q) -> p o q", o=1).broadcast_to([128, 16, 16])
            dv = dst.rearrange("p (n q) -> p n q", q=16)
            if m == 0:
                p.op("vector", lambda v, in0=in0, in1=in1, dv=dv:
                     v.tensor_tensor(dv, in0, in1, ALU.mult), after=lds)
            else:
                p.op("vector", lambda v, in0=in0, in1=in1:
                     v.tensor_tensor(tmp[:, 0:256]
                                     .rearrange("p (n q) -> p n q", q=16),
                                     in0, in1, ALU.mult))
                p.op("vector", lambda v, dv=dv: v.tensor_add(
                    dv, dv, tmp[:, 0:256].rearrange("p (n q) -> p n q", q=16)))
    p.op("vector", lambda v: v.tensor_copy(U[:, 0:16], Bm), after=lds)
    for m in range(NST):
        in0 = AL[:, m * 16:(m + 1) * 16]
        cmc = Cm[:, m:m + 1].broadcast_to([128, 16])
        if m == 0:
            p.op("vector", lambda v, in0=in0, cmc=cmc: v.tensor_tensor(
                W[:, 0:16], in0, cmc, ALU.mult))
        else:
            p.op("vector", lambda v, in0=in0, cmc=cmc: v.tensor_tensor(
                tmp[:, 0:16], in0, cmc, ALU.mult))
            p.op("vector", lambda v: v.tensor_add(W[:, 0:16], W[:, 0:16],
                                                  tmp[:, 0:16]))
    for buf, trans in ((U, False), (W, True)):
        for j, L in enumerate((1, 2, 4, 8, 16, 32)):
            alo = j * 256
            for m in range(NST):
                if not trans:
                    in0 = AL[:, alo:alo + 256] \
                        .rearrange("p (n mm) -> p n mm", mm=16)[:, :, m:m + 1] \
                        .rearrange("p n o -> p o n").broadcast_to([128, L, 16])
                else:
                    in0 = AL[:, alo + m * 16:alo + (m + 1) * 16] \
                        .rearrange("p (o n) -> p o n", o=1) \
                        .broadcast_to([128, L, 16])
                in1 = buf[:, 0:L * 16] \
                    .rearrange("p (t n) -> p t n", n=16)[:, :, m:m + 1] \
                    .broadcast_to([128, L, 16])
                dv = buf[:, L * 16:2 * L * 16].rearrange("p (t n) -> p t n",
                                                         n=16)
                if m == 0:
                    p.op("vector", lambda v, in0=in0, in1=in1, dv=dv:
                         v.tensor_tensor(dv, in0, in1, ALU.mult))
                else:
                    p.op("vector", lambda v, in0=in0, in1=in1, L=L:
                         v.tensor_tensor(tmp[:, 0:L * 16]
                                         .rearrange("p (t n) -> p t n", n=16),
                                         in0, in1, ALU.mult))
                    p.op("vector", lambda v, dv=dv, L=L: v.tensor_add(
                        dv, dv, tmp[:, 0:L * 16]
                        .rearrange("p (t n) -> p t n", n=16)))
    cb = Cm.rearrange("p (o n) -> p o n", o=1).broadcast_to([128, LCH, 16])
    p.op("vector", lambda v: v.tensor_tensor(
        tmp[:, 0:1024].rearrange("p (t n) -> p t n", n=16),
        U[:].rearrange("p (t n) -> p t n", n=16), cb, ALU.mult))
    p.op("vector", lambda v: v.tensor_reduce(
        kt[:], tmp[:, 0:1024].rearrange("p (t n) -> p t n", n=16),
        axis=AX.X, op=ALU.add))
    p.op("vector", lambda v: v.tensor_add(kt[:, 0:1], kt[:, 0:1], Dm))
    for sidx in range(LCH):
        p.op("vector", lambda v, sidx=sidx: v.tensor_copy(
            Ur[:, sidx * 16:(sidx + 1) * 16],
            U[:, (LCH - 1 - sidx) * 16:(LCH - sidx) * 16]), chain=(sidx == 0))
    done = p.engine_tok("vector")
    d1 = p.dma("sync", c.tab_in[:, 0:64], kt, sem="tb2", after=done)
    d2 = p.dma("sync", c.tab_in[:, 64:1088], Ur, sem="tb2")
    d3 = p.dma("sync", c.tab_in[:, 1088:2112], W, sem="tb2")
    agt = p.op("gpsimd", lambda g: g.collective_compute(
        "AllGather", ALU.bypass, replica_groups=[list(range(8))],
        ins=[c.tab_in.opt()], outs=[c.tab_out.opt()]), after=d3)
    c.tab_tok = agt


def emit_fused_ln(b, c, x_tok=None):
    xh2 = c.S4[:].bitcast(BF16)[:, 0:2048]
    return ln_to_fm(b, c.x, xh2, c.xhT, c.scratch, c.stats, x_tok=x_tok,
                    banks=(6, 7))


def emit_bound_negm(b, c, KT, QT2, kv_tok, q_tok, scale):
    p = b.p
    if not hasattr(c, "ind2_sb"):
        ind2 = b.const("find2", np.concatenate(
            [np.repeat(np.eye(2, dtype=bf16)[0:1], 64, 0),
             np.repeat(np.eye(2, dtype=bf16)[1:2], 64, 0)], 0))
        c.ind2_sb = b.sb("find2_sb", [128, 2], BF16)
        c.ind2_tok = p.dma("sync", c.ind2_sb[:], ind2, sem="cst")
    t1 = p.op("vector", lambda v: v.reduce_max(
        c.norm[:, 1:2], KT, axis=AX.X, apply_absolute_value=True),
        after=kv_tok)
    t2 = p.op("vector", lambda v: v.reduce_max(
        c.norm[:, 0:1], QT2, axis=AX.X, apply_absolute_value=True),
        after=q_tok)
    p.op("vector", lambda v: v.tensor_mul(c.norm[:, 2:3], c.norm[:, 0:1],
                                          c.norm[:, 1:2]))
    t4 = p.op("vector", lambda v: v.tensor_copy(c.qkbf[:], c.norm[:, 2:3]))
    mmn = p.op("tensor", lambda t: t.matmul(
        b.psum[5][0:1, 0:2], c.qkbf[:], c.ind2_sb[:], start=True, stop=True),
        after=[t4, c.ind2_tok, b.psum_free[5]])
    t5 = p.op("vector", lambda v: v.tensor_copy(c.msc[:], b.psum[5][0:1, 0:2]),
              after=mmn)
    b.psum_free[5] = t5
    t6 = p.op("scalar", lambda s2, scale=scale: s2.mul(c.msc[:], c.msc[:],
                                                       -scale), after=t5)
    nd1 = p.dma("sync", c.negm_dr, c.msc[:], sem="negm", after=t6)
    nd2 = p.dma("sync", c.negm[:], c.negm_dr.broadcast_to([128, 2]),
                sem="negm", after=nd1)
    return nd2


def emit_fused_l0a(b, c):
    p = b.p
    fused_barrier(b)
    fm_tok = emit_fused_ln(b, c, x_tok=c.xld)
    S4b = c.S4[:].bitcast(BF16)
    cqT = S4b[:, 2048:6144]
    QT2 = c.xhT[:, 6144:8192]
    PTs = c.xhT[:, 8192:9216]
    f32ev = c.xhT[:].bitcast(F32)[:, 4608:5120]
    wdq = c.Cb[:, 0:2048]
    wdkv = c.Cb[:, 2048:4096]
    ckev = c.Cb[:, 4096:5120]
    w1t = p.dma("gpsimd", wdq.rearrange("p (cc n) -> p cc n", n=DC_KV),
                c.W_dq.rearrange("(cc p) n -> p cc n", p=128), sem="w0")
    wtok = p.dma("gpsimd", wdkv.rearrange("p (cc n) -> p cc n", n=DC_KV),
                 c.W_dkv.rearrange("(cc p) n -> p cc n", p=128), sem="w0")
    ckv_st = []
    wdqv = wdq.rearrange("p (dcc n) -> p dcc n", n=128)
    wdkvv = wdkv.rearrange("p (dcc n) -> p dcc n", n=128)
    for cc in range(2):
        for s4 in range(4):
            bank = s4 % 2
            after0 = [wtok, fm_tok, b.psum_free[bank]]
            for dc in range(NDC):
                p.op("tensor", lambda t, cc=cc, s4=s4, dc=dc, bank=bank,
                     st=(dc == 0), sp=(dc == NDC - 1): t.matmul(
                         b.psum[bank][:, 0:512],
                         wdqv[:, dc * 2 + cc, :],
                         c.xhT[:, dc * TOK + s4 * 512: dc * TOK + (s4 + 1) * 512],
                         start=st, stop=sp), after=after0 if dc == 0 else ())
            mm = p.engine_tok("tensor")
            ev = p.op("vector", lambda v, cc=cc, s4=s4, bank=bank: v.tensor_copy(
                cqT[:, cc * TOK + s4 * 512: cc * TOK + (s4 + 1) * 512],
                b.psum[bank][:, 0:512]), after=mm)
            b.psum_free[bank] = ev
            bank2 = 2 + s4 % 2
            after1 = [wtok, fm_tok, b.psum_free[bank2]]
            for dc in range(NDC):
                p.op("tensor", lambda t, cc=cc, s4=s4, dc=dc, bank2=bank2,
                     st=(dc == 0), sp=(dc == NDC - 1): t.matmul(
                         b.psum[bank2][:, 0:512],
                         wdkvv[:, dc * 2 + cc, :],
                         c.xhT[:, dc * TOK + s4 * 512: dc * TOK + (s4 + 1) * 512],
                         start=st, stop=sp), after=after1 if dc == 0 else ())
            mm2 = p.engine_tok("tensor")
            slot = ckev[:, (s4 % 2) * 512:(s4 % 2 + 1) * 512]
            ev2 = p.op("vector", lambda v, slot=slot, bank2=bank2: v.tensor_copy(
                slot, b.psum[bank2][:, 0:512]), after=mm2)
            b.psum_free[bank2] = ev2
            st = p.dma("sync", c.ckv_in[cc * 128:(cc + 1) * 128,
                                        s4 * 512:(s4 + 1) * 512],
                       slot, sem="ckv", after=ev2)
            p.last_tok["vector"] = st
            ckv_st.append(st)
    c.dump("xhT", c.xhT[:])
    c.dump("cqT", cqT)
    ag = p.op("gpsimd", lambda g: g.collective_compute(
        "AllGather", ALU.bypass, replica_groups=[[0, 1, 2, 3], [4, 5, 6, 7]],
        ins=[c.ckv_in.opt()], outs=[c.ckv_out.opt()]), after=ckv_st[-1])
    stage = c.Cb[:, 0:16384]
    ckvT = c.Bb[:]
    sld = p.dma("sync", stage.rearrange("p (x n) -> p x n", n=TOK),
                c.ckv_out.rearrange("(x p) n -> p x n", p=128), sem="ckvl",
                after=[ag, Tok("tensor", p.cnt["tensor"])])
    cp_toks = []
    for i in range(4):
        for cc in range(2):
            src = stage[:, (i * 2 + cc) * TOK:(i * 2 + cc + 1) * TOK] \
                .rearrange("p (tl q) -> p tl q", q=128)
            dst = ckvT[:, cc * S:(cc + 1) * S] \
                .rearrange("p (tt q) -> p tt q", q=128)[:, i::4, :]
            cp_toks.append(p.op("vector", lambda v, src=src, dst=dst:
                                v.tensor_copy(dst, src),
                                after=[sld], chain=False))
    ckv_tok = cp_toks[-1]
    p.last_tok["vector"] = ckv_tok
    c.dump("ckvT", ckvT)
    wuk = c.xhT[:, 0:2048].rearrange("p (cc n) -> p cc n", n=D)
    wuv = c.xhT[:, 2048:4096].rearrange("p (cc n) -> p cc n", n=D)
    wuq = c.xhT[:, 4096:6144].rearrange("p (cc n) -> p cc n", n=D)
    mm_all = Tok("tensor", p.cnt["tensor"])
    p.dma("gpsimd", wuk, c.W_uk.rearrange("(cc p) n -> p cc n", p=128),
          sem="w0b", after=[mm_all, fm_tok])
    p.dma("gpsimd", wuv, c.W_uv.rearrange("(cc p) n -> p cc n", p=128),
          sem="w0b")
    wkv_tok = p.dma("gpsimd", wuq,
                    c.W_uq.rearrange("(cc p) n -> p cc n", p=128), sem="w0b")
    KT = c.Cb[:, 0:8192]
    Vge = c.Cb[:, 8192:16512]
    outfm = c.A[:]
    qt2_tok = None
    for hg in range(NDC):
        kv_free = p.engine_tok("tensor")
        q_evs = []
        for s4 in range(4):
            bank = 4
            after0 = [wkv_tok, b.psum_free[bank], kv_free, qt2_tok]
            for cc in range(2):
                p.op("tensor", lambda t, s4=s4, cc=cc, hg=hg, bank=bank,
                     st=(cc == 0), sp=(cc == 1): t.matmul(
                         b.psum[bank][:, 0:512],
                         wuq[:, cc, hg * 128:(hg + 1) * 128],
                         cqT[:, cc * TOK + s4 * 512: cc * TOK + (s4 + 1) * 512],
                         start=st, stop=sp), after=after0 if cc == 0 else ())
            mm = p.engine_tok("tensor")
            ev = p.op("vector", lambda v, s4=s4, bank=bank: v.tensor_copy(
                QT2[:, s4 * 512:(s4 + 1) * 512], b.psum[bank][:, 0:512]),
                after=mm)
            b.psum_free[bank] = ev
            q_evs.append(ev)
        q_tok = q_evs[-1]
        for sc in range(16):
            bank = 4 + sc % 2
            after0 = [wkv_tok, ckv_tok, b.psum_free[bank], kv_free]
            for cc in range(2):
                p.op("tensor", lambda t, sc=sc, cc=cc, hg=hg, bank=bank,
                     st=(cc == 0), sp=(cc == 1): t.matmul(
                         b.psum[bank][:, 0:512],
                         wuk[:, cc, hg * 128:(hg + 1) * 128],
                         ckvT[:, cc * S + sc * 512: cc * S + (sc + 1) * 512],
                         start=st, stop=sp), after=after0 if cc == 0 else ())
            mm = p.engine_tok("tensor")
            ev = p.op("vector", lambda v, sc=sc, bank=bank: v.tensor_copy(
                KT[:, sc * 512:(sc + 1) * 512], b.psum[bank][:, 0:512]),
                after=mm)
            b.psum_free[bank] = ev
    
        one1 = p.op("vector", lambda v, hg=hg: v.memset(
            Vge.rearrange("p (kt q) -> p kt q", q=130)[:, :, 64:65], 1.0),
            after=[Tok("tensor", p.cnt["tensor"])], chain=True)
        one2 = p.op("vector", lambda v, hg=hg: v.memset(
            Vge.rearrange("p (kt q) -> p kt q", q=130)[:, :, 129:130], 1.0))
        for kb4 in range(16):
            bank = 4 + kb4 % 2
            after0 = [wkv_tok, ckv_tok, b.psum_free[bank]]
            for sub in range(4):
                kt = kb4 * 4 + sub
                for cc in range(2):
                    p.op("tensor", lambda t, kt=kt, sub=sub, cc=cc, hg=hg,
                         bank=bank, st=(sub == 0 and cc == 0),
                         sp=(sub == 3 and cc == 1): t.matmul(
                             b.psum[bank][:, sub * 128:(sub + 1) * 128],
                             ckvT[:, cc * S + kt * 128: cc * S + (kt + 1) * 128],
                             wuv[:, cc, hg * 128:(hg + 1) * 128],
                             start=st, stop=sp),
                         after=after0 if (sub == 0 and cc == 0) else ())
            mm = p.engine_tok("tensor")
            dstv = Vge.rearrange("p (kt q) -> p kt q", q=130)[
                :, kb4 * 4:(kb4 + 1) * 4, :] \
                .rearrange("p a (h n) -> p a h n", n=65)[:, :, :, 0:64]
            ev = p.op("vector", lambda v, kb4=kb4, bank=bank, dstv=dstv:
                      v.tensor_copy(
                          dstv,
                          b.psum[bank][:, 0:512]
                          .rearrange("p (a h n) -> p a h n", h=2, n=64)),
                      after=[mm, one2], chain=False)
            b.psum_free[bank] = ev
        kvdone = p.engine_tok("vector")
        if hg == 0:
            c.dump("KT0", KT)
            c.dump("Vge0", Vge)
            c.dump("QT20", QT2)
        negm_tok = emit_bound_negm(b, c, KT, QT2, kvdone, q_tok, 0.125)
        if hg == 0:
            c.dump("negm0", c.negm[:], F32)
        emit_mha_st(b, c.Cb[:, 0:8192], Vge, QT2, c.negm, outfm, c.mask4,
                    0.125, kvdone, q_tok, negm_tok, c.cst_tok, f32ev,
                    PTs, hg, sbanks=(0, 1), obank=2, t2bank=3)
        qt2_tok = p.engine_tok("tensor")
    c.dump("outfm", outfm)
    att_done = p.engine_tok("vector")
    wo_v = c.xhT[:, 6144:14336].rearrange("p (hc n) -> p hc n", n=D)
    wkv_tok = p.dma("gpsimd", wo_v,
                    c.W_o.rearrange("(hc p) n -> p hc n", p=128), sem="w0d",
                    after=[att_done, Tok("tensor", p.cnt["tensor"])])
    for t in range(NT):
        for dc2 in range(2):
            bank = 4 + dc2
            after0 = [att_done, wkv_tok, b.psum_free[bank]]
            for hc in range(NDC):
                p.op("tensor", lambda tt, t=t, dc2=dc2, hc=hc, bank=bank,
                     st=(hc == 0), sp=(hc == NDC - 1): tt.matmul(
                         b.psum[bank][:, 0:512],
                         outfm[:, hc * TOK + t * 128: hc * TOK + (t + 1) * 128],
                         wo_v[:, hc, dc2 * 512:(dc2 + 1) * 512],
                         start=st, stop=sp), after=after0 if hc == 0 else ())
            mm = p.engine_tok("tensor")
            xs = c.x[:, t * D + dc2 * 512: t * D + (dc2 + 1) * 512]
            ev = p.op("vector", lambda v, xs=xs, bank=bank: v.tensor_add(
                xs, xs, b.psum[bank][:, 0:512]), after=mm)
            b.psum_free[bank] = ev


def emit_fused_ffn0(b, c):
    p = b.p
    fused_barrier(b)
    xh_tok = c.A[:]
    ap_toks, _, _ = emit_ln(b, c.x, xh_tok, c.scratch, c.stats)
    ln_done = p.engine_tok("vector")
    ctb = p.dma("sync", c.tok_in.rearrange("(t p) d -> p t d", p=128),
                xh_tok.rearrange("p (t d) -> p t d", d=D), sem="fx",
                after=ln_done)
    a2a = p.op("gpsimd", lambda g: g.collective_compute(
        "AllToAll", ALU.bypass, replica_groups=[list(range(N_CORES))],
        ins=[c.tok_in.opt()], outs=[c.tok_out.opt()]), after=ctb)
    lds = []
    for qt in range(16):
        rho, dlt = qt // 8, qt % 8
        ot = 8 * rho + 2 * (dlt % 4) + dlt // 4
        lds.append(p.dma("sync", xh_tok[:, qt * D:(qt + 1) * D],
                         c.tok_out[ot * 128:(ot + 1) * 128, :],
                         sem="fg", after=(a2a if qt == 0 else None)))
    c.dump("xhex", xh_tok)
    xeT = c.Bb[:]
    prs = []
    for t in range(NT):
        for dc in range(NDC):
            prs.append((xh_tok[:, t * D + dc * 128: t * D + (dc + 1) * 128],
                        xeT[:, dc * TOK + t * 128: dc * TOK + (t + 1) * 128],
                        lds[-1]))
    emit_transpose(b, prs, banks=(6, 7))
    fm_done = p.engine_tok("vector")
    c.dump("xeT", xeT)
    h_sb = c.Cb[:, 0:11264]
    wslot = [c.Cb[:, 11264:13312], c.Cb[:, 13312:15360]]
    silu_sb = c.S4[:].bitcast(BF16)[:, 0:512]
    dev_sb = c.S4[:].bitcast(BF16)[:, 512:1024]
    wfree = [None, None]
    hm_prev = None
    h_free = None
    dst_toks = []
    for blk4 in range(4):
        if True:
            tok0 = blk4 * 512
            for fc in range(NFC):
                buf = fc % 2
                wsb = wslot[buf]
                wtk1 = p.dma("gpsimd", wsb[:, 0:1024]
                             .rearrange("p (cc f) -> p cc f", f=128),
                             c.f0w1[:, fc * 128:(fc + 1) * 128]
                             .rearrange("(cc p) f -> p cc f", p=128),
                             sem="fw" + str(buf), after=wfree[buf])
                wtk3 = p.dma("gpsimd", wsb[:, 1024:2048]
                             .rearrange("p (cc f) -> p cc f", f=128),
                             c.f0w3[:, fc * 128:(fc + 1) * 128]
                             .rearrange("(cc p) f -> p cc f", p=128),
                             sem="fw" + str(buf))
                for which, bank in ((0, 0), (1, 1)):
                    wv = wsb[:, which * 1024:(which + 1) * 1024]
                    after0 = [wtk3, fm_done, b.psum_free[bank]]
                    for dc in range(NDC):
                        p.op("tensor", lambda t, wv=wv, dc=dc, bank=bank,
                             tok0=tok0, st=(dc == 0), sp=(dc == NDC - 1):
                             t.matmul(b.psum[bank][:, 0:512],
                                      wv[:, dc * 128:(dc + 1) * 128],
                                      xeT[:, dc * TOK + tok0:
                                          dc * TOK + tok0 + 512],
                                      start=st, stop=sp),
                             after=after0 if dc == 0 else ())
                mm = p.engine_tok("tensor")
                wfree[buf] = mm
                sg = p.op("scalar", lambda s2: s2.activation(
                    silu_sb, b.psum[0][:, 0:512], AF.Silu),
                    after=[mm, hm_prev])
                hv = h_sb[:, fc * 512:(fc + 1) * 512]
                hm = p.op("vector", lambda v, hv=hv: v.tensor_mul(
                    hv, silu_sb, b.psum[1][:, 0:512]),
                    after=[sg, mm] + ([h_free] if fc == 0 else []))
                b.psum_free[0] = hm
                b.psum_free[1] = hm
                hm_prev = hm
            h_done = hm_prev
            if blk4 == 0:
                c.dump("hsb0", h_sb)
            for fc in range(NFC):
                buf = fc % 2
                wsb = wslot[buf]
                w2t = p.dma("gpsimd",
                            wsb[:, 0:1024].rearrange("p (o d) -> p o d", o=1),
                            c.f0w2[fc * 128:(fc + 1) * 128, :]
                            .rearrange("(o p) d -> p o d", p=128),
                            sem="fw" + str(buf),
                            after=[wfree[buf], h_done])
                after1 = [w2t] + ([pk for pk in b.psum_free] if fc == 0 else [])
                first = True
                for tt in range(4):
                    for dc2 in range(2):
                        bank = tt * 2 + dc2
                        p.op("tensor", lambda t, fc=fc, tt=tt, dc2=dc2,
                             bank=bank, wsb=wsb, st=(fc == 0),
                             sp=(fc == NFC - 1): t.matmul(
                                 b.psum[bank][:, 0:512],
                                 h_sb[:, fc * 512 + tt * 128:
                                      fc * 512 + (tt + 1) * 128],
                                 wsb[:, dc2 * 512:(dc2 + 1) * 512],
                                 start=st, stop=sp),
                             after=after1 if first else ())
                        first = False
                wfree[buf] = p.engine_tok("tensor")
            dmm = p.engine_tok("tensor")
            for tt in range(4):
                for dc2 in range(2):
                    bank = tt * 2 + dc2
                    qt = (tok0 + tt * 128) // 128
                    dtile = (4 * (qt // 8) + qt % 4) * 2 + (qt % 8) // 4
                    ev = p.op("vector", lambda v, bank=bank: v.tensor_copy(
                        dev_sb[:, 0:512], b.psum[bank][:, 0:512]),
                        after=dmm)
                    b.psum_free[bank] = ev
                    st = p.dma("sync",
                               c.tok2_in.rearrange("(t p) d -> p t d", p=128)
                               [:, dtile, dc2 * 512:(dc2 + 1) * 512],
                               dev_sb[:, 0:512], sem="fd", after=ev)
                    dst_toks.append(st)
                    p.last_tok["vector"] = st
            h_free = p.engine_tok("tensor")
    c.dump("hsb", h_sb)
    a2a2 = p.op("gpsimd", lambda g: g.collective_compute(
        "AllToAll", ALU.bypass, replica_groups=[list(range(N_CORES))],
        ins=[c.tok2_in.opt()], outs=[c.tok2_out.opt()]),
        after=dst_toks[-1])
    tr_done = Tok("tensor", p.cnt["tensor"])
    gt = p.dma("sync", xh_tok[:].rearrange("p (t d) -> p t d", d=D),
               c.tok2_out.rearrange("(t p) d -> p t d", p=128),
               sem="fg2", after=[a2a2, tr_done])
    for t in range(NT):
        p.op("vector", lambda v, t=t: v.tensor_add(
            c.x[:, t * D:(t + 1) * D], c.x[:, t * D:(t + 1) * D],
            xh_tok[:, t * D:(t + 1) * D]), after=gt)
    c.dump("delta", xh_tok)


def emit_fused_ssm(b, c):
    p = b.p
    fused_barrier(b)
    fm_tok = emit_fused_ln(b, c)
    urev = c.Cb[:, 0:8192]
    ktabf = c.Cb[:, 8192:9216].bitcast(F32)
    tmp = c.Cb[:].bitcast(F32)[:, 6144:8192]
    kld = p.dma("sync", ktabf.rearrange("p (g n) -> p g n", n=LCH),
                c.tab_out.rearrange("(g p) m -> p g m", p=128)[:, :, 0:64],
                sem="sw", after=c.tab_tok)
    uld = p.dma("gpsimd", urev.rearrange("p (g n) -> p g n", n=1024),
                c.tab_out.rearrange("(g p) m -> p g m", p=128)[:, :, 64:1088],
                sem="sw", after=c.tab_tok)
    yA = c.A[:].bitcast(F32)
    yB = c.Bb[:].bitcast(F32)

    def yv(g):
        return (yA if g < 4 else yB)[:, (g % 4) * TOK:(g % 4 + 1) * TOK]

    vslot = c.S4[:, 0:1024]
    injs_ = [c.S4[:, 1024:1536], c.S4[:, 1536:2048]]
    vld = None
    ild_ = None
    inj_st = []
    for g in range(NDC):
        yg = yv(g)
        xg = c.xhT[:, g * TOK:(g + 1) * TOK]
        for tau in range(LCH):
            kcol = ktabf[:, g * LCH + tau: g * LCH + tau + 1]
            xv = xg.rearrange("p (cc s) -> p cc s", s=LCH)[:, :, 0:LCH - tau]
            yvv = yg.rearrange("p (cc s) -> p cc s", s=LCH)[:, :, tau:LCH]
            if tau == 0:
                p.op("vector", lambda v, xv=xv, yvv=yvv, kcol=kcol:
                     v.tensor_scalar(yvv, xv, kcol, None, ALU.mult),
                     after=[fm_tok, kld, uld])
            else:
                p.op("vector", lambda v, xv=xv, yvv=yvv, kcol=kcol:
                     v.scalar_tensor_tensor(yvv, xv, kcol, yvv, ALU.mult,
                                            ALU.add))
        sl = g % 2
        injg = injs_[sl]
        for n in range(NST):
            uvb = urev.rearrange("p (gg x) -> p gg x", x=1024)[
                :, g, :].rearrange("p (ss n) -> p ss n", n=NST)[:, :, n:n + 1] \
                .rearrange("p ss o -> p o ss").broadcast_to([128, NCH, LCH])
            xv = xg.rearrange("p (cc s) -> p cc s", s=LCH)
            tv = tmp.rearrange("p (cc s) -> p cc s", s=LCH)
            p.op("vector", lambda v, xv=xv, uvb=uvb, tv=tv:
                 v.tensor_tensor(tv, xv, uvb, ALU.mult),
                 after=(ild_ if n == 0 else None))
            iv = injg.rearrange("p (cc n) -> p cc n", n=NST)[:, :, n:n + 1]
            p.op("vector", lambda v, iv=iv, tv=tv: v.tensor_reduce(
                iv, tv, axis=AX.X, op=ALU.add))
        inj_done = p.engine_tok("vector")
        ist = p.dma("sync",
                    c.inj_in.rearrange("(tl p) gn -> p tl gn", p=128)
                    [:, :, g * NST:(g + 1) * NST],
                    injg.rearrange("p (cc n) -> p cc n", n=NST)[:, 1::2, :],
                    sem="si", after=inj_done)
        inj_st.append(ist)
        vld = p.dma("sync", vslot.rearrange("p (o n) -> p o n", o=1),
                    c.tab_out.rearrange("(gg p) m -> p gg m", p=128)
                    [:, g:g + 1, 1088:2112], sem="sv" + str(sl),
                    after=[vld, ild_])
        for n in range(NST):
            vvb = vslot.rearrange("p (tt n) -> p tt n", n=NST)[:, :, n:n + 1] \
                .rearrange("p tt o -> p o tt") \
                .rearrange("p o tt -> p (o tt)") \
                .rearrange("p (o tt) -> p o tt", o=1) \
                .broadcast_to([128, NT, LCH])
            ivb = injg.rearrange("p (cc n) -> p cc n", n=NST)[
                :, 0::2, n:n + 1].broadcast_to([128, NT, LCH])
            tv = tmp.rearrange("p (cc s) -> p cc s", s=LCH)[:, 0:NT, :]
            p.op("vector", lambda v, vvb=vvb, ivb=ivb, tv=tv:
                 v.tensor_tensor(tv, vvb, ivb, ALU.mult),
                 after=(vld if n == 0 else None))
            yo = yg.rearrange("p (tl u s) -> p tl u s", u=2, s=LCH)[:, :, 1, :]
            p.op("vector", lambda v, yo=yo, tv=tv: v.tensor_add(yo, yo, tv))
        ild_ = p.engine_tok("vector")
    agi = p.op("gpsimd", lambda g2: g2.collective_compute(
        "AllGather", ALU.bypass, replica_groups=[[0, 1, 2, 3], [4, 5, 6, 7]],
        ins=[c.inj_in.opt()], outs=[c.inj_out.opt()]), after=inj_st[-1])
    injga = c.S4[:, 1024:3072]
    mz = p.op("vector", lambda v: v.memset(injga, 0.0), after=ild_)
    g_toks = []
    for tl in range(NT):
        g_toks.append(p.idma(
            "gpsimd", injga[:, tl * 128:(tl + 1) * 128],
            c.inj_out, c.idxs[:, 32 + tl:33 + tl], 4 * TOK, sem="sg",
            after=[agi if tl == 0 else None, mz, c.cst_tok]))
    gat_done = g_toks[-1]
    vslot2 = [vslot, tmp[:, 1024:2048]]
    vld2_ = [None, None]
    ce_tok = [None, None]
    for g in range(NDC):
        sl2 = g % 2
        vs = vslot2[sl2]
        vld2 = p.dma("sync", vs.rearrange("p (o n) -> p o n", o=1),
                     c.tab_out.rearrange("(gg p) m -> p gg m", p=128)
                     [:, g:g + 1, 1088:2112], sem="sv2" + str(sl2),
                     after=[vld2_[sl2], ce_tok[sl2],
                            ild_ if g < 2 else None])
        vld2_[sl2] = vld2
        yg = yv(g)
        for n in range(NST):
            vvb = vs.rearrange("p (tt n) -> p tt n", n=NST)[:, :, n:n + 1] \
                .rearrange("p tt o -> p o tt").broadcast_to([128, NT, LCH])
            ivb = injga.rearrange("p (tl gn) -> p tl gn", gn=128)[
                :, :, g * NST + n: g * NST + n + 1].broadcast_to([128, NT, LCH])
            tv = tmp.rearrange("p (cc s) -> p cc s", s=LCH)[:, 0:NT, :]
            p.op("vector", lambda v, vvb=vvb, ivb=ivb, tv=tv:
                 v.tensor_tensor(tv, vvb, ivb, ALU.mult),
                 after=[vld2 if n == 0 else None,
                        gat_done if n == 0 else None])
            yo = yg.rearrange("p (tl u s) -> p tl u s", u=2, s=LCH)[:, :, 0, :]
            p.op("vector", lambda v, yo=yo, tv=tv: v.tensor_add(yo, yo, tv))
        ce_tok[sl2] = p.engine_tok("vector")
    corr_done = p.engine_tok("vector")
    c.dump("yg0", yv(0), F32)
    c.dump("injga", injga, F32)
    c.dump("injg7", injs_[1], F32)
    xhf = c.xhT[:].bitcast(F32)
    ytok = xhf[:, 0:2048]
    gel = c.xhT[:, 4096:6144]
    gfm = c.xhT[:, 6144:7168]
    woutT = c.xhT[:, 7168:15360]
    borow = c.xhT[0:1, 15360:16384]
    lnw_b = c.S4[:, 0:1024]
    lnb_b = c.S4[:, 2048:3072]
    xh_dead = Tok("vector", p.cnt["vector"])
    lw1 = p.dma("sync", lnw_b, c.ln_w.rearrange("(o n) -> o n", o=1)
                .broadcast_to([128, D]), sem="sl", after=corr_done)
    lw2 = p.dma("sync", lnb_b, c.ln_b.rearrange("(o n) -> o n", o=1)
                .broadcast_to([128, D]), sem="sl")
    bol = p.dma("gpsimd", borow.rearrange("o (x n) -> o x n", x=1),
                c.bb_out.rearrange("(o n) -> o n", o=1)
                .rearrange("o (x n) -> o x n", x=1), sem="sl2",
                after=corr_done)
    wc = p.dma("gpsimd", c.w_bf, c.Wb_out, sem="sl2")
    wT_toks = []
    for dc in range(NDC):
        wT_toks.append(p.dma("sync", woutT[:, dc * D:(dc + 1) * D],
                             c.w_bf[:, dc * 128:(dc + 1) * 128],
                             transpose=True, sem="swT", after=wc))
    wT_tok = wT_toks[-1]
    idf_sb, idf_tok = b.identf()
    ident_sb, id_tok = b.ident()
    n_ = NT
    mu = c.stats[:, 2 * n_:3 * n_]
    rstd = c.stats[:, 3 * n_:4 * n_]
    ist = c.stats[:, 80:86]
    slot_free = [None, None]
    gfm_free = None
    for t in range(NT):
        sl = t % 2
        yt = ytok[:, sl * D:(sl + 1) * D]
        for dc in range(NDC):
            src = yv(dc)[:, t * 128:(t + 1) * 128].bitcast(BF16) \
                if False else yv(dc)[:, t * 128:(t + 1) * 128]
            bk = (4, 5)[dc % 2]
            ps = b.psum[bk]
            tr = p.op("tensor", lambda tt, src=src, ps=ps: tt.transpose(
                ps[0:128, 0:128], src, idf_sb[:]),
                after=[corr_done, idf_tok, b.psum_free[bk], slot_free[sl],
                       wT_tok])
            ev = p.op("vector", lambda v, yt=yt, dc=dc, ps=ps: v.tensor_copy(
                yt[:, dc * 128:(dc + 1) * 128], ps[0:128, 0:128]),
                after=tr, chain=False)
            b.psum_free[bk] = ev
        ytr = p.engine_tok("vector")
        i0 = p.op("scalar", lambda s2, yt=yt: s2.activation(
            c.scratch, yt, AF.Square, accum_out=ist[:, 1:2]), after=ytr)
        p.op("vector", lambda v, yt=yt: v.reduce_sum(
            ist[:, 0:1], yt, axis=AX.X), after=ytr)
        p.op("vector", lambda v: v.tensor_scalar(ist[:, 2:3], ist[:, 0:1],
                                                 1.0 / D, None, ALU.mult))
        p.op("vector", lambda v: v.tensor_mul(ist[:, 4:5], ist[:, 2:3],
                                              ist[:, 2:3]))
        p.op("vector", lambda v: v.tensor_scalar(ist[:, 5:6], ist[:, 1:2],
                                                 1.0 / D, 1e-5, ALU.mult,
                                                 ALU.add), after=i0)
        p.op("vector", lambda v: v.tensor_sub(ist[:, 4:5], ist[:, 5:6],
                                              ist[:, 4:5]))
        iq = p.op("scalar", lambda s2: s2.activation(ist[:, 5:6], ist[:, 4:5],
                                                     AF.Sqrt),
                  after=p.engine_tok("vector"))
        p.op("vector", lambda v: v.reciprocal(ist[:, 3:4], ist[:, 5:6]),
             after=iq)
        gl = gel[:, sl * D:(sl + 1) * D] if False else gel[:, 0:1024]
        p.op("vector", lambda v, yt=yt: v.tensor_scalar(
            tmp[:, 0:1024], yt, ist[:, 2:3], ist[:, 3:4], ALU.subtract,
            ALU.mult))
        p.op("vector", lambda v: v.tensor_mul(tmp[:, 0:1024], tmp[:, 0:1024],
                                              lnw_b), after=[lw2])
        gv = p.op("vector", lambda v: v.tensor_add(tmp[:, 0:1024],
                                                   tmp[:, 0:1024], lnb_b))
        ge = p.op("scalar", lambda s2, gl=gl: s2.activation(gl, tmp[:, 0:1024],
                                                            AF.Gelu),
                  after=[gv, gfm_free])
        prs2 = []
        for dc in range(NDC):
            prs2.append((gl[:, dc * 128:(dc + 1) * 128],
                         gfm[:, dc * 128:(dc + 1) * 128], ge))
        evs2 = emit_transpose(b, prs2, banks=(6, 7))
        g_done = evs2[-1]
        for dc2 in range(2):
            bank = dc2
            after0 = [g_done, wT_tok, bol, c.ot, b.psum_free[bank]]
            for hc in range(NDC):
                p.op("tensor", lambda tt, hc=hc, dc2=dc2, bank=bank,
                     st=(hc == 0): tt.matmul(
                         b.psum[bank][:, 0:512],
                         gfm[:, hc * 128:(hc + 1) * 128],
                         woutT[:, hc * D + dc2 * 512: hc * D + (dc2 + 1) * 512],
                         start=st, stop=False), after=after0 if hc == 0 else ())
            p.op("tensor", lambda tt, dc2=dc2, bank=bank: tt.matmul(
                b.psum[bank][:, 0:512], c.ones_bf[:],
                borow[:, dc2 * 512:(dc2 + 1) * 512], start=False, stop=True))
            mm = p.engine_tok("tensor")
            xs = c.x[:, t * D + dc2 * 512: t * D + (dc2 + 1) * 512]
            xh1 = p.op("vector", lambda v, xs=xs, t=t: v.tensor_scalar(
                tmp[:, 1024:1536], xs, mu[:, t:t + 1], rstd[:, t:t + 1],
                ALU.subtract, ALU.mult), after=[mm])
            ev1 = p.op("vector", lambda v, xs=xs, bank=bank: v.tensor_add(
                xs, b.psum[bank][:, 0:512], xs))
            ev2 = p.op("vector", lambda v, xs=xs: v.tensor_add(
                xs, xs, tmp[:, 1024:1536]))
            b.psum_free[bank] = ev2
        slot_free[sl] = p.engine_tok("tensor")
        gfm_free = p.engine_tok("tensor")


def emit_fused_l1a(b, c):
    p = b.p
    fused_barrier(b)
    fm_tok = emit_fused_ln(b, c)
    S4b = c.S4[:].bitcast(BF16)
    wsl = [S4b[:, 0:1024], S4b[:, 1024:2048]]
    QT = c.A[:]
    KTf = c.Bb[:]
    Ktok = c.Cb[:, 0:16384]
    wo = c.xhT[:, 0:8192]
    halo = c.xhT[:, 8192:12288]
    khfm = c.xhT[:, 12288:16384]
    Pw = S4b[:, 2048:2304]
    PTw = S4b[:, 2304:2560]
    ao = S4b[:, 2560:3584]
    aoT = S4b[:, 3584:4608]
    ob_row = S4b[0:1, 4608:5632]
    lcol = c.stats[:, 80:86]
    wfree = [None, None]

    def produce(which, dstf, first_after=None):
        for hc in range(NDC):
            sl = hc % 2
            wld = p.dma("gpsimd",
                        wsl[sl].rearrange("p (dcc n) -> p dcc n", n=128),
                        c.wqkv_a.rearrange("(dcc p) n -> p dcc n", p=128)
                        [:, :, which * D + hc * 128: which * D + (hc + 1) * 128],
                        sem="lw" + str(sl),
                        after=[wfree[sl], first_after if hc < 2 else None])
            for s4 in range(4):
                bank = s4 % 2
                after0 = [fm_tok, wld, b.psum_free[bank]]
                for dc in range(NDC):
                    p.op("tensor", lambda t, sl=sl, dc=dc, s4=s4, bank=bank,
                         st=(dc == 0), sp=(dc == NDC - 1): t.matmul(
                             b.psum[bank][:, 0:512],
                             wsl[sl][:, dc * 128:(dc + 1) * 128],
                             c.xhT[:, dc * TOK + s4 * 512:
                                   dc * TOK + (s4 + 1) * 512],
                             start=st, stop=sp), after=after0 if dc == 0 else ())
                mm = p.engine_tok("tensor")
                ev = p.op("vector", lambda v, hc=hc, s4=s4, bank=bank,
                          dstf=dstf: v.tensor_copy(
                              dstf[:, hc * TOK + s4 * 512:
                                   hc * TOK + (s4 + 1) * 512],
                              b.psum[bank][:, 0:512]), after=mm)
                b.psum_free[bank] = ev
            wfree[sl] = p.engine_tok("tensor")

    produce(1, KTf, first_after=fm_tok)
    kdone_v = p.engine_tok("vector")
    prs = []
    for t in range(NT):
        for hc in range(NDC):
            prs.append((KTf[:, hc * TOK + t * 128: hc * TOK + (t + 1) * 128],
                        Ktok[:, t * D + hc * 128: t * D + (hc + 1) * 128],
                        kdone_v))
    evs = emit_transpose(b, prs, banks=(4, 5))
    stg = p.dma("sync", c.tok_in.rearrange("(t p) d -> p t d", p=128),
                Ktok.rearrange("p (t d) -> p t d", d=D), sem="lkst",
                after=evs[-1])
    ag = p.op("gpsimd", lambda g: g.collective_compute(
        "AllGather", ALU.bypass, replica_groups=[[0, 1, 2, 3], [4, 5, 6, 7]],
        ins=[c.tok_in.opt()], outs=[c.halo_out.opt()]), after=stg)
    produce(0, QT)
    q_done_t = p.engine_tok("tensor")
    q_done_v = p.engine_tok("vector")
    wol = p.dma("gpsimd", wo.rearrange("p (cc n) -> p cc n", n=D),
                c.wout_a.rearrange("(cc p) n -> p cc n", p=128), sem="lwo",
                after=q_done_t)
    bol = p.dma("gpsimd", ob_row.rearrange("o (x n) -> o x n", x=1),
                c.bout_a.rearrange("(o n) -> o n", o=1)
                .rearrange("o (x n) -> o x n", x=1), sem="lwo")
    mz = p.op("vector", lambda v: v.memset(halo, 0.0), after=q_done_t)
    c.dump("l1a_qt", QT)
    pw_free = None
    slot_free4 = [None, None, None, None]
    for t in range(NT):
        sl4 = t % 4
        gt = p.idma("gpsimd", halo[:, sl4 * 1024:(sl4 + 1) * 1024],
                    c.halo_out, c.idxs[:, 32 + t:33 + t], 4 * TOK,
                    sem="hg" + str(sl4),
                    after=[ag if t == 0 else None, mz if t < 4 else None,
                           c.cst_tok if t == 0 else None, slot_free4[sl4]])
        fm_prs = []
        for hc in range(NDC):
            fm_prs.append((halo[:, sl4 * 1024 + hc * 128:
                                sl4 * 1024 + (hc + 1) * 128],
                           khfm[:, sl4 * 1024 + hc * 128:
                                sl4 * 1024 + (hc + 1) * 128], gt))
        evh = emit_transpose(b, fm_prs, banks=(4, 5))
        for wb in range(2):
            bank = wb
            after0 = [q_done_v if t == 0 else None, b.psum_free[bank],
                      evh[-1] if wb == 0 else None]
            for dc in range(NDC):
                if wb == 0:
                    rhs = khfm[:, sl4 * 1024 + dc * 128:
                               sl4 * 1024 + (dc + 1) * 128]
                else:
                    rhs = KTf[:, dc * TOK + t * 128: dc * TOK + (t + 1) * 128]
                lhsT = QT[:, dc * TOK + t * 128: dc * TOK + (t + 1) * 128]
                p.op("tensor", lambda tt, lhsT=lhsT, rhs=rhs, bank=bank,
                     st=(dc == 0), sp=(dc == NDC - 1): tt.matmul(
                         b.psum[bank][:, 0:128], lhsT, rhs, start=st, stop=sp),
                     after=after0 if dc == 0 else ())
        mm = p.engine_tok("tensor")
        mk = p.op("vector", lambda v: v.tensor_add(
            b.psum[1][:, 0:128], b.psum[1][:, 0:128], c.maskw[:]),
            after=mm, chain=False)
        r1 = p.op("vector", lambda v: v.reduce_max(
            lcol[:, 0:1], b.psum[0][:, 0:128], axis=AX.X), after=mm)
        r2 = p.op("vector", lambda v: v.reduce_max(
            lcol[:, 1:2], b.psum[1][:, 0:128], axis=AX.X), after=mk)
        r3 = p.op("vector", lambda v: v.tensor_max(lcol[:, 0:1], lcol[:, 0:1],
                                                   lcol[:, 1:2]))
        r4 = p.op("vector", lambda v: v.tensor_scalar(
            lcol[:, 1:2], lcol[:, 0:1], -0.03125, None, ALU.mult))
        e1 = p.op("scalar", lambda s: s.activation(
            Pw[:, 0:128], b.psum[0][:, 0:128], AF.Exp, bias=lcol[:, 1:2],
            scale=0.03125, accum_out=lcol[:, 2:3]), after=[r4, pw_free])
        e2 = p.op("scalar", lambda s: s.activation(
            Pw[:, 128:256], b.psum[1][:, 0:128], AF.Exp, bias=lcol[:, 1:2],
            scale=0.03125, accum_out=lcol[:, 3:4]), after=r4)
        b.psum_free[0] = e1
        b.psum_free[1] = e2
        l1 = p.op("vector", lambda v: v.tensor_add(lcol[:, 4:5], lcol[:, 2:3],
                                                   lcol[:, 3:4]), after=e2)
        l2 = p.op("vector", lambda v: v.reciprocal(lcol[:, 5:6], lcol[:, 4:5]))
        ps = p.op("vector", lambda v: v.tensor_scalar(
            Pw[:], Pw[:], lcol[:, 5:6], None, ALU.mult))
        evp = emit_transpose(b, [(Pw[:, 0:128], PTw[:, 0:128], ps),
                                 (Pw[:, 128:256], PTw[:, 128:256], ps)],
                             banks=(4, 5))
        pw_free = p.engine_tok("tensor")
        for dc2 in range(2):
            bank = 2 + dc2
            after0 = [evp[1], b.psum_free[bank]]
            for ks in range(2):
                if ks == 0:
                    vsrc = halo[:, sl4 * 1024 + dc2 * 512:
                                sl4 * 1024 + (dc2 + 1) * 512]
                else:
                    vsrc = Ktok[:, t * D + dc2 * 512: t * D + (dc2 + 1) * 512]
                p.op("tensor", lambda tt, ks=ks, vsrc=vsrc, bank=bank,
                     st=(ks == 0), sp=(ks == 1): tt.matmul(
                         b.psum[bank][:, 0:512],
                         PTw[:, ks * 128:(ks + 1) * 128], vsrc,
                         start=st, stop=sp), after=after0 if ks == 0 else ())
            mm2 = p.engine_tok("tensor")
            aev = p.op("scalar", lambda sc, dc2=dc2, bank=bank: sc.copy(
                ao[:, dc2 * 512:(dc2 + 1) * 512], b.psum[bank][:, 0:512]),
                after=mm2)
            b.psum_free[bank] = aev
        ao_done = p.engine_tok("scalar")
        prs2 = []
        for dc in range(NDC):
            prs2.append((ao[:, dc * 128:(dc + 1) * 128],
                         aoT[:, dc * 128:(dc + 1) * 128], ao_done))
        evs2 = emit_transpose(b, prs2, banks=(4, 5))
        aoT_done = evs2[-1]
        for dc2 in range(2):
            bank = 6 + dc2
            after0 = [aoT_done, wol, bol, c.ot, b.psum_free[bank]]
            for hc in range(NDC):
                p.op("tensor", lambda tt, hc=hc, dc2=dc2, bank=bank,
                     st=(hc == 0): tt.matmul(
                         b.psum[bank][:, 0:512],
                         aoT[:, hc * 128:(hc + 1) * 128],
                         wo[:, hc * D + dc2 * 512: hc * D + (dc2 + 1) * 512],
                         start=st, stop=False), after=after0 if hc == 0 else ())
            p.op("tensor", lambda tt, dc2=dc2, bank=bank: tt.matmul(
                b.psum[bank][:, 0:512], c.ones_bf[:],
                ob_row[:, dc2 * 512:(dc2 + 1) * 512], start=False, stop=True))
            mm3 = p.engine_tok("tensor")
            xs = c.x[:, t * D + dc2 * 512: t * D + (dc2 + 1) * 512]
            ev = p.op("vector", lambda v, xs=xs, bank=bank: v.tensor_add(
                xs, b.psum[bank][:, 0:512], xs), after=mm3)
            b.psum_free[bank] = ev
        slot_free4[sl4] = p.engine_tok("tensor")
        pw_free = p.engine_tok("tensor")


def emit_fused_ffn1(b, c):
    p = b.p
    fused_barrier(b)
    S4b = c.S4[:].bitcast(BF16)
    VT = c.A[:]
    WvT = c.Bb[:, 0:8192]
    WgpT = c.Bb[:, 8192:8832]
    W1T = c.Bb[:, 8832:10880]
    W2nat = c.Bb[:, 10880:12928]
    W2b = c.Cb[0:64, 0:8192]
    rows_bf = c.Cb[0:1, 8192:9296]
    b1rows = c.Cb[0:1, 9296:9552]
    v2 = S4b[:, 4096:6144]
    hslot = S4b[:, 0:64]
    hT = S4b[0:64, 64:192]
    gate = c.stats[:, 0:16]
    we_sb = c.stats[:, 16:24]
    lncr = c.stats[:, 24:40]
    scratch = c.scratch

    w2z = p.op("vector", lambda v: v.memset(W2b[32:64, :], 0.0))
    c1 = p.dma("gpsimd", c.w_bf, c.Wv, sem="wcast")
    for dc in range(NDC):
        p.dma("gpsimd", WgpT[:, dc * 80:dc * 80 + 8],
              c.WgT[dc * 128:(dc + 1) * 128, :], sem="wgt")
    c3 = p.dma("gpsimd", c.w_bf2, c.Wp.rearrange("e q d -> (e q) d"),
               sem="wcast")
    c4 = p.dma("gpsimd", c.w_bf3, c.W1.rearrange("e i d -> (e i) d"),
               sem="wcast")
    tts = []
    for dc in range(NDC):
        tts.append(p.dma("sync", WvT[:, dc * D:(dc + 1) * D],
                         c.w_bf[:, dc * 128:(dc + 1) * 128], transpose=True,
                         sem="wT", after=c4))
        tts.append(p.dma("sync", WgpT[:, dc * 80 + 16:dc * 80 + 80],
                         c.w_bf2[:, dc * 128:(dc + 1) * 128], transpose=True,
                         sem="wT"))
        tts.append(p.dma("sync", W1T[:, dc * 256:(dc + 1) * 256],
                         c.w_bf3[:, dc * 128:(dc + 1) * 128], transpose=True,
                         sem="wT"))
    wT_tok = tts[-1]
    wgt_tok = Tok("dma_wgt_gpsimd", p.cnt["dma_wgt_gpsimd"])
    gap0 = p.op("vector", lambda v: v.memset(
        WgpT[:].rearrange("p (cc e) -> p cc e", e=80)[:, :, 8:16], 0.0))
    w2n_toks = []
    for e in range(NE):
        for dc in range(NDC):
            w2n_toks.append(p.dma(
                "gpsimd", W2nat[:, (e * NDC + dc) * NI:(e * NDC + dc + 1) * NI],
                c.W2[e, dc * 128:(dc + 1) * 128, :], sem="w2n"))
        p.dma("gpsimd", W2b[32:33, e * D:(e + 1) * D], c.b2[e:e + 1, :],
              sem="rows", after=(w2z if e == 0 else None))
    p.dma("gpsimd", b1rows[:], c.b1.rearrange("e i -> (e i)")
          .rearrange("(o n) -> o n", o=1), sem="rows")
    p.dma("gpsimd", rows_bf[:, 0:D], c.bv.rearrange("(o n) -> o n", o=1),
          sem="rows")
    p.dma("gpsimd", rows_bf[:, D:D + 8], c.bg.rearrange("(o n) -> o n", o=1),
          sem="rows")
    p.dma("sync", c.gains[:, 0:8], c.g_r.rearrange("(o n) -> o n", o=1)
          .broadcast_to([128, 8]), sem="rows_s")
    p.dma("sync", c.gains[:, 8:16], c.b_r.rearrange("(o n) -> o n", o=1)
          .broadcast_to([128, 8]), sem="rows_s")
    p.dma("sync", c.gains[:, 16:24], c.g_c.rearrange("(o n) -> o n", o=1)
          .broadcast_to([128, 8]), sem="rows_s")
    p.dma("sync", c.gains[:, 24:32], c.b_c.rearrange("(o n) -> o n", o=1)
          .broadcast_to([128, 8]), sem="rows_s")
    al_tok = p.dma("sync", c.msc[:, 0:1],
                   c.alpha.rearrange("(o n) -> o n", o=1), sem="rows_s")
    rows_tok = Tok("dma_rows_s_sync", p.cnt["dma_rows_s_sync"])
    rowsbf_tok = Tok("dma_rows_gpsimd", p.cnt["dma_rows_gpsimd"])
    zpad = p.op("vector", lambda v: v.memset(rows_bf[:, D + 8:D + 80], 0.0))
    sg_a = p.op("scalar", lambda s: s.activation(c.galph[:, 0:1],
                                                 c.msc[:, 0:1], AF.Sigmoid),
                after=al_tok)
    sd1 = p.dma("sync", c.sig_dr, c.galph[:, 0:1], sem="sig", after=sg_a)
    ev_s = p.dma("sync", c.sig[:], c.sig_dr.broadcast_to([128, 1]), sem="sig",
                 after=sd1)
    fm_done = emit_fused_ln(b, c)
    p.op("vector", lambda v: v.memset(hslot[:, 33:64], 0.0), after=fm_done)
    for e in range(NE):
        pairs = []
        for dc in range(NDC):
            src = W2nat[:, (e * NDC + dc) * NI:(e * NDC + dc + 1) * NI]
            dst = W2b[0:32, e * D + dc * 128: e * D + (dc + 1) * 128]
            pairs.append((src, dst, w2n_toks[-1]))
        emit_transpose(b, pairs, banks=(4, 5))

    vfree = {}
    for t in range(NT):
        for half in range(2):
            bank = 2 + half
            after0 = [fm_done, zpad, c.ot, rowsbf_tok, wT_tok,
                      b.psum_free[bank]]
            for hc in range(NDC):
                lhsT = c.xhT[:, hc * TOK + t * 128: hc * TOK + (t + 1) * 128]
                rhs = WvT[:, hc * D + half * 512: hc * D + (half + 1) * 512]
                p.op("tensor", lambda tt, o=b.psum[bank][:, 0:512], l=lhsT,
                     r=rhs, st=(hc == 0): tt.matmul(o, l, r, start=st,
                                                    stop=False),
                     after=after0 if hc == 0 else ())
            p.op("tensor", lambda tt, o=b.psum[bank][:, 0:512],
                 r=rows_bf[:, half * 512:(half + 1) * 512]:
                 tt.matmul(o, c.ones_bf[:], r, start=False, stop=True))
            mm = p.engine_tok("tensor")
            xs = c.x[:, t * D + half * 512: t * D + (half + 1) * 512]
            ad = p.op("vector", lambda v, xs=xs, bank=bank: v.tensor_add(
                xs, xs, b.psum[bank][:, 0:512]), after=mm)
            vslot = v2[:, (t % 2) * D + half * 512:
                       (t % 2) * D + (half + 1) * 512]
            cv = p.op("scalar", lambda s, vslot=vslot, bank=bank: s.copy(
                vslot, b.psum[bank][:, 0:512]),
                after=[mm, ad, vfree.get((t % 2, half))])
            b.psum_free[bank] = cv
            prs = []
            for q in range(4):
                dc = half * 4 + q
                src = vslot[:, q * 128:(q + 1) * 128]
                dst = VT[:, dc * TOK + t * 128: dc * TOK + (t + 1) * 128]
                prs.append((src, dst, cv))
            emit_transpose(b, prs, banks=(6, 7))
            vfree[(t % 2, half)] = p.engine_tok("tensor")
    vt_done = p.engine_tok("vector")
    if "f1_bar" in c.dbg:
        fused_barrier(b)
    c.dump("f1_vt", VT)

    xw = [wT_tok, rows_tok, ev_s, rowsbf_tok] if "f1_xw" in c.dbg else []
    wet = None
    for t in range(NT):
        after0 = [vt_done, b.psum_free[0], wgt_tok if t == 0 else None] + xw
        for dc in range(NDC):
            lhsT = VT[:, dc * TOK + t * 128: dc * TOK + (t + 1) * 128]
            rhs = WgpT[:, dc * 80:(dc + 1) * 80]
            p.op("tensor", lambda tt, l=lhsT, r=rhs, st=(dc == 0):
                 tt.matmul(b.psum[0][:, 0:80], l, r, start=st, stop=False),
                 after=after0 if dc == 0 else ())
        p.op("tensor", lambda tt: tt.matmul(
            b.psum[0][:, 0:80], c.ones_bf[:], rows_bf[:, D:D + 80],
            start=False, stop=True))
        mm = p.engine_tok("tensor")
        ser = "f1_ser" in c.dbg
        sq = p.op("scalar", lambda s: s.activation(
            scratch[:, 0:64], b.psum[0][:, 16:80], AF.Square),
            after=[mm, wet])
        rc = p.op("vector", lambda v: v.tensor_reduce(
            gate[:, 8:16], scratch[:, 0:64].rearrange("p (e q) -> p e q", q=8),
            axis=AX.X, op=ALU.add),
            after=[sq] + ([rows_tok, ev_s] if t == 0 else []))
        rcp = p.op("vector", lambda v: v.tensor_copy(gate[:, 0:8],
                                                     b.psum[0][:, 0:8]),
                   after=[mm, sq if ser else None])
        if t == 5 and "f1_psv" in c.dbg:
            p.op("vector", lambda v: v.tensor_copy(c.S4[:, 2048:2112],
                                                   scratch[:, 0:64]))
            p.op("vector", lambda v: v.tensor_copy(c.S4[:, 2112:2120],
                                                   gate[:, 8:16]))
        psvall = "f1_psvall" in c.dbg
        b.psum_free[0] = rcp
        p.op("vector", lambda v: v.tensor_scalar(gate[:, 8:16], gate[:, 8:16],
                                                 0.125, None, ALU.mult))
        cs = p.op("scalar", lambda s: s.activation(gate[:, 8:16],
                                                   gate[:, 8:16], AF.Sqrt),
                  after=p.engine_tok("vector"))
        if psvall:
            p.op("vector", lambda v, t=t: v.tensor_copy(
                c.S4[:, 2048 + 40 * t:2048 + 40 * t + 16], gate[:, 0:16]),
                after=cs)
        for k, (src, dst) in enumerate(((gate[:, 0:8], lncr[:, 0:8]),
                                        (gate[:, 8:16], lncr[:, 8:16]))):
            p.op("vector", lambda v, src=src: v.reduce_sum(
                c.stats[:, 80:81], src, axis=AX.X), after=cs)
            p.op("vector", lambda v, src=src: v.tensor_scalar(
                c.stats[:, 80:81], c.stats[:, 80:81], 0.125, None, ALU.mult))
            p.op("vector", lambda v, src=src, dst=dst: v.tensor_scalar(
                dst, src, c.stats[:, 80:81], None, ALU.subtract))
            p.op("vector", lambda v, dst=dst: v.tensor_mul(
                scratch[:, 64:72], dst, dst))
            p.op("vector", lambda v: v.reduce_sum(
                c.stats[:, 81:82], scratch[:, 64:72], axis=AX.X))
            p.op("vector", lambda v: v.tensor_scalar(
                c.stats[:, 81:82], c.stats[:, 81:82], 0.125, 1e-5, ALU.mult,
                ALU.add))
            sqv = p.op("scalar", lambda s: s.activation(
                c.stats[:, 82:83], c.stats[:, 81:82], AF.Sqrt),
                after=p.engine_tok("vector"))
            p.op("vector", lambda v: v.reciprocal(c.stats[:, 83:84],
                                                  c.stats[:, 82:83]),
                 after=sqv)
            p.op("vector", lambda v, dst=dst: v.tensor_scalar(
                dst, dst, c.stats[:, 83:84], None, ALU.mult))
        p.op("vector", lambda v: v.tensor_mul(lncr[:, 8:16], lncr[:, 8:16],
                                              c.gains[:, 16:24]))
        p.op("vector", lambda v: v.tensor_add(lncr[:, 8:16], lncr[:, 8:16],
                                              c.gains[:, 24:32]))
        p.op("vector", lambda v: v.tensor_scalar(lncr[:, 8:16], lncr[:, 8:16],
                                                 c.sig[:, 0:1], None,
                                                 ALU.mult))
        p.op("vector", lambda v: v.tensor_mul(lncr[:, 0:8], lncr[:, 0:8],
                                              c.gains[:, 0:8]))
        p.op("vector", lambda v: v.tensor_add(lncr[:, 0:8], lncr[:, 0:8],
                                              c.gains[:, 8:16]))
        p.op("vector", lambda v: v.tensor_add(gate[:, 0:8], lncr[:, 0:8],
                                              lncr[:, 8:16]))
        if psvall:
            p.op("vector", lambda v, t=t: v.tensor_copy(
                c.S4[:, 2048 + 40 * t + 16:2048 + 40 * t + 32], lncr[:, 0:16]))
        p.op("vector", lambda v: v.reduce_max(gate[:, 8:9], gate[:, 0:8],
                                              axis=AX.X))
        p.op("vector", lambda v: v.tensor_scalar(we_sb[:], gate[:, 0:8],
                                                 gate[:, 8:9], None,
                                                 ALU.is_equal))
        p.op("vector", lambda v: v.scalar_tensor_tensor(
            gate[:, 0:8], we_sb[:], BIGNEG, gate[:, 0:8], ALU.mult, ALU.add))
        p.op("vector", lambda v: v.reduce_max(gate[:, 9:10], gate[:, 0:8],
                                              axis=AX.X))
        p.op("vector", lambda v: v.tensor_scalar(gate[:, 10:11], gate[:, 8:9],
                                                 gate[:, 9:10], None,
                                                 ALU.subtract))
        sgw = p.op("scalar", lambda s: s.activation(
            gate[:, 11:12], gate[:, 10:11], AF.Sigmoid),
            after=p.engine_tok("vector"))
        p.op("vector", lambda v: v.tensor_scalar(gate[:, 12:13],
                                                 gate[:, 11:12],
                                                 -1.0, 1.0, ALU.mult, ALU.add),
             after=sgw)
        p.op("vector", lambda v: v.tensor_scalar(scratch[:, 0:8], gate[:, 0:8],
                                                 gate[:, 9:10], None,
                                                 ALU.is_equal))
        p.op("vector", lambda v: v.tensor_scalar(we_sb[:], we_sb[:],
                                                 gate[:, 11:12], None,
                                                 ALU.mult))
        wet = p.op("vector", lambda v: v.scalar_tensor_tensor(
            we_sb[:], scratch[:, 0:8], gate[:, 12:13], we_sb[:],
            ALU.mult, ALU.add))
        if psvall:
            wet = p.op("vector", lambda v, t=t: v.tensor_copy(
                c.S4[:, 2048 + 40 * t + 32:2048 + 40 * t + 40], we_sb[:]))
        if t == 5:
            c.dump("f1_we5", we_sb, F32)
            c.dump("f1_g5", gate, F32)
        tr_prev = None
        dmm_prev = None
        for e in range(NE):
            hafter = [wet, b.psum_free[1]] + xw
            for dc in range(NDC):
                lhsT = VT[:, dc * TOK + t * 128: dc * TOK + (t + 1) * 128]
                rhs = W1T[:, dc * 256 + e * NI: dc * 256 + (e + 1) * NI]
                p.op("tensor", lambda tt, l=lhsT, r=rhs, st=(dc == 0):
                     tt.matmul(b.psum[1][:, 0:NI], l, r, start=st,
                               stop=False), after=hafter if dc == 0 else ())
            p.op("tensor", lambda tt, e=e: tt.matmul(
                b.psum[1][:, 0:NI], c.ones_bf[:],
                b1rows[:, e * NI:(e + 1) * NI], start=False, stop=True))
            hmm = p.engine_tok("tensor")
            hsl = p.op("scalar", lambda s: s.activation(hslot[:, 0:NI],
                                                        b.psum[1][:, 0:NI],
                                                        AF.Silu),
                       after=[hmm, tr_prev])
            b.psum_free[1] = hsl
            hsc = p.op("vector", lambda v, e=e: v.tensor_scalar(
                hslot[:, 0:NI], hslot[:, 0:NI], we_sb[:, e:e + 1], None,
                ALU.mult), after=hsl)
            wcp = p.op("vector", lambda v, e=e: v.tensor_copy(
                hslot[:, 32:33], we_sb[:, e:e + 1]))
            tr = emit_transpose(b, [(hslot[:, 0:64], hT[0:64, :], wcp,
                                     dmm_prev)], banks=(6,))[0]
            tr_prev = p.engine_tok("tensor")
            for dc2 in range(2):
                bank = 2 + dc2
                dafter = [tr, b.psum_free[bank]] if e == 0 else [tr]
                p.op("tensor", lambda tt, e=e, dc2=dc2, bank=bank,
                     st=(e == 0), sp=(e == NE - 1): tt.matmul(
                         b.psum[bank][:, 0:512], hT[0:64, :],
                         W2b[0:64, e * D + dc2 * 512: e * D + (dc2 + 1) * 512],
                         start=st, stop=sp),
                     after=dafter)
            dmm_prev = p.engine_tok("tensor")
        dmm = p.engine_tok("tensor")
        for dc2 in range(2):
            bank = 2 + dc2
            xs = c.x[:, t * D + dc2 * 512: t * D + (dc2 + 1) * 512]
            ev = p.op("vector", lambda v, xs=xs, bank=bank: v.tensor_add(
                xs, xs, b.psum[bank][:, 0:512]), after=dmm)
            b.psum_free[bank] = ev
    c.dump("f1_xhT_end", c.xhT[:])
    c.dump("f1_vt_end", VT)
    c.dump("f1_psv", c.S4[:, 2048:2120], F32)
    c.dump("f1_psvall", c.S4[:, 2048:2688], F32)


def emit_fused_l1b(b, c):
    p = b.p
    fused_barrier(b)
    fm_tok = emit_fused_ln(b, c)
    S4b = c.S4[:].bitcast(BF16)
    QT = c.A[:]
    wqq = c.Bb[:, 0:8192]
    wqk = c.Bb[:, 8192:16384]
    wqv = c.Cb[:, 0:8192]
    brow = c.Cb[0:1, 8192:11264]
    kfm = [c.Cb[:, 12288:13312], c.Cb[:, 13312:14336]]
    vsl = [c.Cb[:, 14336:15360], c.Cb[:, 15360:16384]]
    rot = S4b[:, 0:2048]
    sin_sb = c.S4[:, 2048:2560]
    cos_sb = c.S4[:, 2560:3072]
    scratch = c.scratch
    if not hasattr(c, "ind2_sb"):
        ind2 = b.const("find2", np.concatenate(
            [np.repeat(np.eye(2, dtype=bf16)[0:1], 64, 0),
             np.repeat(np.eye(2, dtype=bf16)[1:2], 64, 0)], 0))
        c.ind2_sb = b.sb("find2_sb", [128, 2], BF16)
        c.ind2_tok = p.dma("sync", c.ind2_sb[:], ind2, sem="cst2")
    ident_sb, id_tok = b.ident()

    wl1 = p.dma("gpsimd", wqq.rearrange("p (dcc n) -> p dcc n", n=1024),
                c.qkv_w.rearrange("(dcc p) n -> p dcc n", p=128)[:, :, 0:D],
                sem="bw")
    wl2 = p.dma("gpsimd", wqk.rearrange("p (dcc n) -> p dcc n", n=1024),
                c.qkv_w.rearrange("(dcc p) n -> p dcc n", p=128)
                [:, :, D:2 * D], sem="bw")
    wl3 = p.dma("gpsimd", wqv.rearrange("p (dcc n) -> p dcc n", n=1024),
                c.qkv_w.rearrange("(dcc p) n -> p dcc n", p=128)
                [:, :, 2 * D:3 * D], sem="bw")
    bl = p.dma("gpsimd", brow.rearrange("o (x n) -> o x n", x=1),
               c.qkv_b.rearrange("(o n) -> o n", o=1)
               .rearrange("o (x n) -> o x n", x=1), sem="bw")
    sld = p.dma("sync", sin_sb.rearrange("p (t n) -> p t n", n=32),
                c.sin_t.rearrange("(t p) n -> p t n", p=128), sem="bm")
    cldd = p.dma("sync", cos_sb.rearrange("p (t n) -> p t n", n=32),
                 c.cos_t.rearrange("(t p) n -> p t n", p=128), sem="bm")
    wtok = Tok("dma_bw_gpsimd", p.cnt["dma_bw_gpsimd"])

    stg_k = []
    stg_v = []
    kslot_free = [None, None]
    vslot_free = [None, None]
    rope_done = None
    for t in range(NT):
        for col in range(6):
            bank = col
            wbuf = (wqq, wqq, wqk, wqk, wqv, wqv)[col]
            after0 = [fm_tok, cldd, c.ot, wtok, b.psum_free[bank]]
            for dc in range(NDC):
                p.op("tensor", lambda tt, t=t, col=col, dc=dc, bank=bank,
                     wbuf=wbuf, st=(dc == 0): tt.matmul(
                         b.psum[bank][:, 0:512],
                         c.xhT[:, dc * TOK + t * 128: dc * TOK + (t + 1) * 128],
                         wbuf[:, dc * 1024 + (col % 2) * 512:
                              dc * 1024 + (col % 2 + 1) * 512],
                         start=st, stop=False), after=after0 if dc == 0 else ())
            p.op("tensor", lambda tt, col=col, bank=bank: tt.matmul(
                b.psum[bank][:, 0:512], c.ones_bf[:],
                brow[:, col * 512:(col + 1) * 512], start=False, stop=True))
        mm = p.engine_tok("tensor")
        sv = t % 2
        for col in (4, 5):
            ev = p.op("vector", lambda v, col=col, sv=sv: v.tensor_copy(
                vsl[sv][:, (col - 4) * 512:(col - 3) * 512],
                b.psum[col][:, 0:512]),
                after=[mm, vslot_free[sv] if col == 4 else None])
            b.psum_free[col] = ev
        sv_t = p.dma("sync", c.tok_in.rearrange("(t p) d -> p t d", p=128)
                     [:, t, :], vsl[sv], sem="bv", after=p.engine_tok("vector"))
        stg_v.append(sv_t)
        vslot_free[sv] = sv_t
        sc = sin_sb[:, t * 32:(t + 1) * 32]
        cc2 = cos_sb[:, t * 32:(t + 1) * 32]
        scb = sc.rearrange("p (o j) -> p o j", o=1).broadcast_to([128, 8, 32])
        ccb = cc2.rearrange("p (o j) -> p o j", o=1).broadcast_to([128, 8, 32])
        for col in range(4):
            pvv = b.psum[col][:, 0:512].rearrange("p (h j) -> p h j", j=64)
            x1 = pvv[:, :, 0:64:2]
            x2 = pvv[:, :, 1:64:2]
            od = rot[:, col * 512:(col + 1) * 512] \
                .rearrange("p (h j) -> p h j", j=64)
            s1 = scratch[:, 0:256].rearrange("p (h j) -> p h j", j=32)
            s2 = scratch[:, 256:512].rearrange("p (h j) -> p h j", j=32)
            r1 = p.op("vector", lambda v, x1=x1, ccb=ccb, s1=s1:
                      v.tensor_tensor(s1, x1, ccb, ALU.mult),
                      after=[mm, rope_done if col == 0 else None])
            p.op("vector", lambda v, x2=x2, scb=scb, s2=s2:
                 v.tensor_tensor(s2, x2, scb, ALU.mult))
            p.op("vector", lambda v, od=od, s1=s1, s2=s2:
                 v.tensor_tensor(od[:, :, 0:32], s1, s2, ALU.subtract))
            p.op("vector", lambda v, x1=x1, scb=scb, s1=s1:
                 v.tensor_tensor(s1, x1, scb, ALU.mult))
            p.op("vector", lambda v, x2=x2, ccb=ccb, s2=s2:
                 v.tensor_tensor(s2, x2, ccb, ALU.mult))
            rlast = p.op("vector", lambda v, od=od, s1=s1, s2=s2:
                         v.tensor_tensor(od[:, :, 32:64], s1, s2, ALU.add))
            b.psum_free[col] = rlast
        rope_done = p.engine_tok("vector")
        sk = t % 2
        prsq = []
        prsk = []
        for dc in range(NDC):
            prsq.append((rot[:, dc * 128:(dc + 1) * 128],
                         QT[:, dc * TOK + t * 128: dc * TOK + (t + 1) * 128],
                         rope_done))
            prsk.append((rot[:, 1024 + dc * 128:1024 + (dc + 1) * 128],
                         kfm[sk][:, dc * 128:(dc + 1) * 128],
                         rope_done, kslot_free[sk] if dc == 0 else None))
        emit_transpose(b, prsq, banks=(6, 7))
        emit_transpose(b, prsk, banks=(6, 7))
        ktr = p.engine_tok("vector")
        sk_t = p.dma("sync", c.fm_in.rearrange("(dcc p) n -> p dcc n", p=128)
                     [:, :, t * 128:(t + 1) * 128],
                     kfm[sk].rearrange("p (dcc n) -> p dcc n", n=128),
                     sem="bk", after=ktr)
        stg_k.append(sk_t)
        kslot_free[sk] = sk_t
        rope_done = p.engine_tok("tensor")
    prod_done_t = p.engine_tok("tensor")
    agk = p.op("gpsimd", lambda g: g.collective_compute(
        "AllGather", ALU.bypass, replica_groups=[[0, 1, 2, 3], [4, 5, 6, 7]],
        ins=[c.fm_in.opt()], outs=[c.fm_out.opt()]), after=stg_k[-1])
    agv = p.op("gpsimd", lambda g: g.collective_compute(
        "AllGather", ALU.bypass, replica_groups=[[0, 1, 2, 3], [4, 5, 6, 7]],
        ins=[c.tok_in.opt()], outs=[c.halo_out.opt()]), after=stg_v[-1])
    wo = c.Bb[:, 8192:16384]
    P = c.Bb[:, 0:8192]
    KT = c.Cb[:, 0:8192]
    Vg = c.Cb[:, 8192:16384]
    outfm = c.xhT[:]
    lsums = c.stats[:, 0:20]
    wcst = p.dma("gpsimd", c.w_bf, c.out_w, sem="bw2", after=prod_done_t)
    wld = p.dma("gpsimd", wo.rearrange("p (cc n) -> p cc n", n=D),
                c.w_bf.rearrange("(cc p) n -> p cc n", p=128), sem="bw2",
                after=wcst)
    ob_row = c.S4[:].bitcast(BF16)[0:1, 5120:6144]
    bld = p.dma("gpsimd", ob_row.rearrange("o (x n) -> o x n", x=1),
                c.out_b.rearrange("(o n) -> o n", o=1)
                .rearrange("o (x n) -> o x n", x=1), sem="bw2",
                after=prod_done_t)
    wtok2 = Tok("dma_bw2_gpsimd", p.cnt["dma_bw2_gpsimd"])

    q_ready = p.engine_tok("vector")
    Vge = c.Cb[:, 8192:16512]
    KTb = c.Cb[:, 0:8192]
    vgv = Vge.rearrange("p (ktt n2) -> p ktt n2", n2=130)
    p.op("vector", lambda v: v.memset(vgv[:, :, 64:65], 1.0),
         after=prod_done_t)
    p.op("vector", lambda v: v.memset(vgv[:, :, 129:130], 1.0))
    ones_vge = p.engine_tok("vector")
    f32ev = c.S4[:, 0:512]
    pt_slot = c.S4[:].bitcast(BF16)[:, 2048:3072]
    nscr = c.Bb[:, 0:8192]
    for hg in range(NDC):
        kv_free = p.engine_tok("tensor")
        kt_l = []
        vg_l = []
        for l in range(4):
            kt_l.append(p.dma(
                "sync", KTb.rearrange("p (u l4 n) -> p u l4 n", l4=4, n=128)
                [:, :, l, :],
                c.fm_out.rearrange("(l dcp pp) n -> pp l dcp n", pp=128, dcp=8)
                [:, l, hg, :].rearrange("p (u n) -> p u n", n=128),
                sem="bkv", after=[agk if hg == 0 and l == 0 else None,
                                  kv_free if l == 0 else None]))
            for hh in range(2):
                vg_l.append(p.dma(
                    "sync",
                    Vge.rearrange("p (u l4 hh n) -> p u l4 hh n",
                                  l4=4, hh=2, n=65)[:, :, l, hh, 0:64],
                    c.halo_out.rearrange("(l u pp) d -> pp l u d",
                                         pp=128, u=16)
                    [:, l, :, hg * 128 + hh * 64: hg * 128 + hh * 64 + 64],
                    sem="bkv",
                    after=([agv, stg_k[-1], ones_vge]
                           if hg == 0 and l == 0 and hh == 0 else None)))
        kv_done = vg_l[-1]
        nsq = []
        for which, src, nch in ((0, c.A[:, hg * TOK:(hg + 1) * TOK],
                                 TOK // 512), (1, KTb, S // 512)):
            sq = p.op("scalar", lambda s, src=src: s.activation(
                nscr[:, 0:src.shape[-1]], src, AF.Square),
                after=[kv_done, p.engine_tok("tensor")])
            mx = None
            for chv in range(nch):
                mmq = p.op("tensor", lambda t, chv=chv: t.matmul(
                    b.psum[4][0:2, 0:512], c.ind2_sb[:],
                    nscr[:, chv * 512:(chv + 1) * 512], start=True, stop=True),
                    after=[sq, c.ind2_tok, b.psum_free[4]])
                rm = p.op("vector", lambda v, which=which, chv=chv: (
                    v.reduce_max(c.norm[0:2, which:which + 1],
                                 b.psum[4][0:2, 0:512], axis=AX.X)
                    if chv == 0 else
                    v.tensor_reduce(c.norm[0:2, 2:3], b.psum[4][0:2, 0:512],
                                    axis=AX.X, op=ALU.max)), after=mmq)
                b.psum_free[4] = rm
                if chv > 0:
                    rm = p.op("vector", lambda v, which=which: v.tensor_max(
                        c.norm[0:2, which:which + 1],
                        c.norm[0:2, which:which + 1], c.norm[0:2, 2:3]))
                mx = rm
            nsq.append(mx)
        t3 = p.op("vector", lambda v: v.tensor_mul(
            c.norm[0:2, 3:4], c.norm[0:2, 0:1], c.norm[0:2, 1:2]),
            after=nsq)
        t4 = p.op("scalar", lambda s: s.activation(c.norm[0:2, 3:4],
                                                   c.norm[0:2, 3:4], AF.Sqrt),
                  after=t3)
        t6 = p.op("scalar", lambda s: s.mul(c.norm[0:2, 3:4],
                                            c.norm[0:2, 3:4], -0.125))
        nd1 = p.dma("sync", c.negm_dr.rearrange("o n -> (o n)")
                    .rearrange("(n o) -> n o", o=1), c.norm[0:2, 3:4],
                    sem="bnegm", after=t6)
        nd2 = p.dma("sync", c.negm[:], c.negm_dr.broadcast_to([128, 2]),
                    sem="bnegm", after=nd1)
        emit_mha_st(b, KTb, Vge, c.A[:, hg * TOK:(hg + 1) * TOK], c.negm,
                    outfm, c.mask4[:], 0.125, kv_done, q_ready, nd2,
                    c.cst_tok, f32ev, pt_slot, hg,
                    sbanks=(0, 1), obank=2, t2bank=3)
    att_done = p.engine_tok("vector")
    for t in range(NT):
        for dc2 in range(2):
            bank = dc2
            after0 = [att_done, wld, wtok2, c.ot, b.psum_free[bank]]
            for hc in range(NDC):
                p.op("tensor", lambda tt, t=t, dc2=dc2, hc=hc, bank=bank,
                     st=(hc == 0): tt.matmul(
                         b.psum[bank][:, 0:512],
                         outfm[:, hc * TOK + t * 128: hc * TOK + (t + 1) * 128],
                         wo[:, hc * D + dc2 * 512: hc * D + (dc2 + 1) * 512],
                         start=st, stop=False), after=after0 if hc == 0 else ())
            p.op("tensor", lambda tt, dc2=dc2, bank=bank: tt.matmul(
                b.psum[bank][:, 0:512], c.ones_bf[:],
                ob_row[:, dc2 * 512:(dc2 + 1) * 512], start=False, stop=True))
            mm = p.engine_tok("tensor")
            xs = c.x[:, t * D + dc2 * 512: t * D + (dc2 + 1) * 512]
            ev = p.op("vector", lambda v, xs=xs, bank=bank: v.tensor_add(
                xs, b.psum[bank][:, 0:512], xs), after=mm)
            b.psum_free[bank] = ev


_BUILDERS = {}


def _get(name, fn):
    if name not in _BUILDERS:
        _BUILDERS[name] = fn()
    return _BUILDERS[name]


def kernel(**inputs):
    g = lambda n: np.asarray(inputs[n])
    x = np.ascontiguousarray(g("L0_attn_a_x").reshape(B * S, D))
    b = _get("fused", lambda: build_fused(upto=8))
    maps = []
    for c in range(N_CORES):
        s_t, c_t = rope_tables(c)
        maps.append({
            "xin": shard_att(x, c),
            "cmask4": fused_mask4(c),
            "cmaskb": att_mask(c),
            "sin_t": s_t, "cos_t": c_t,
            "A_g": np.ascontiguousarray(g("L0_attn_b_A")[c * 128:(c + 1) * 128]),
            "B_g": np.ascontiguousarray(g("L0_attn_b_Bm")[c * 128:(c + 1) * 128]),
            "C_g": np.ascontiguousarray(g("L0_attn_b_Cm")[c * 128:(c + 1) * 128]),
            "D_g": np.ascontiguousarray(g("L0_attn_b_Dm")[c * 128:(c + 1) * 128]),
            "idx_prev": fused_gather_idx_prevtile(c),
            "W_dq": g("L0_attn_a_W_dq"), "W_uq": g("L0_attn_a_W_uq"),
            "W_dkv": g("L0_attn_a_W_dkv"), "W_uk": g("L0_attn_a_W_uk"),
            "W_uv": g("L0_attn_a_W_uv"), "W_o": g("L0_attn_a_W_o"),
            "f0w1": np.ascontiguousarray(g("L0_ffn_w1")[c]),
            "f0w3": np.ascontiguousarray(g("L0_ffn_w3")[c]),
            "f0w2": np.ascontiguousarray(g("L0_ffn_w2")[c]),
            "ln_w": g("L0_attn_b_ln_w"), "ln_b": g("L0_attn_b_ln_b"),
            "Wb_out": g("L0_attn_b_W_out"), "bb_out": g("L0_attn_b_b_out"),
            "wqkv_a": g("L1_attn_a_w_qkv"), "wout_a": g("L1_attn_a_w_out"),
            "bout_a": g("L1_attn_a_b_out"),
            "qkv_w": g("L1_attn_b_qkv_w"), "qkv_b": g("L1_attn_b_qkv_b"),
            "out_w": g("L1_attn_b_out_w"), "out_b": g("L1_attn_b_out_b"),
            "Wv": g("L1_ffn_Wv"), "bv": g("L1_ffn_bv"),
            "Wg": g("L1_ffn_Wg"),
            "WgT": np.ascontiguousarray(g("L1_ffn_Wg").T),
            "bg": g("L1_ffn_bg"), "Wp": g("L1_ffn_Wp"),
            "alpha": np.asarray(g("L1_ffn_alpha"), np.float32).reshape(1),
            "g_r": g("L1_ffn_g_r"), "b_r": g("L1_ffn_b_r"),
            "g_c": g("L1_ffn_g_c"), "b_c": g("L1_ffn_b_c"),
            "W1": g("L1_ffn_W1"), "b1": g("L1_ffn_b1"),
            "W2": g("L1_ffn_W2"), "b2": g("L1_ffn_b2"),
        })
    res = _run(b, maps)
    x = unshard_att([res[c]["xout"] for c in range(N_CORES)])
    return np.ascontiguousarray(x.reshape(B, S, D)).astype(np.float32, copy=False)


# version 09 (final)
# speedup vs baseline: 2.1655x; 1.0477x over previous
import os
os.environ["TRN_ALLOW_ICI_ALL_TO_ALL"] = "1"
import sys
sys.path.insert(0, '/opt/trn_rl_repo')
import math
import numpy as np
import ml_dtypes
import concourse.bass as bass
import concourse.mybir as mybir
from concourse.bass_utils import run_bass_kernel_spmd

F32 = mybir.dt.float32
BF16 = mybir.dt.bfloat16
AF = mybir.ActivationFunctionType
ALU = mybir.AluOpType
AX = mybir.AxisListType

BIGNEG = -3.0e38
N_CORES = 8
B, S, D = 2, 8192, 1024
TOK = 2048
NT = TOK // 128
BLK = 1024
NDC = D // 128

bf16 = ml_dtypes.bfloat16



def blocks_of_core(c):
    p = c % 4
    return (p, 7 - p)


def shard_rows(x2d, c):
    r = c // 4
    b0, b1 = blocks_of_core(c)
    base = r * S
    return np.ascontiguousarray(np.concatenate(
        [x2d[base + b0 * BLK: base + (b0 + 1) * BLK],
         x2d[base + b1 * BLK: base + (b1 + 1) * BLK]], axis=0))


def unshard_rows(shards):
    out = np.empty((B * S,) + shards[0].shape[1:], shards[0].dtype)
    for c in range(N_CORES):
        r = c // 4
        b0, b1 = blocks_of_core(c)
        base = r * S
        out[base + b0 * BLK: base + (b0 + 1) * BLK] = shards[c][:BLK]
        out[base + b1 * BLK: base + (b1 + 1) * BLK] = shards[c][BLK:]
    return out



ENGINES = ("sync", "tensor", "vector", "scalar", "gpsimd")
QUEUE_SEM = {"sync": "dma_s", "gpsimd": "dma_g", "scalar": "dma_c"}


class Tok:
    __slots__ = ("sem", "val")

    def __init__(self, sem, val):
        self.sem = sem
        self.val = val


class Prog:
    def __init__(self, nc):
        self.nc = nc
        self.streams = {e: [] for e in ENGINES}
        self.sems = {}
        self.cnt = {}
        for e in ENGINES:
            self.sems[e] = nc.alloc_semaphore("sem_" + e)
            self.cnt[e] = 0
        self.last_tok = {e: None for e in ENGINES}

    def _dma_sem(self, name):
        key = "dma_" + name
        if key not in self.sems:
            self.sems[key] = self.nc.alloc_semaphore("sem_" + key)
            self.cnt[key] = 0
        return key

    def _wait(self, stream, tok):
        if tok is None:
            return
        h = self.sems[tok.sem]
        v = tok.val
        self.streams[stream].append(lambda eng: eng.wait_ge(h, v))

    def op(self, engine, fn, after=(), chain=True):
        if after is None:
            after = ()
        elif isinstance(after, Tok):
            after = (after,)
        after = tuple(t for t in after if t is not None)
        if chain and engine != "tensor":
            self._wait(engine, self.last_tok[engine])
        for t in after:
            self._wait(engine, t)
        self.cnt[engine] += 1
        v = self.cnt[engine]
        sem_h = self.sems[engine]

        def run(eng):
            fn(eng).then_inc(sem_h, 1)
        self.streams[engine].append(run)
        t = Tok(engine, v)
        self.last_tok[engine] = t
        return t

    def dma(self, queue, out, in_, sem="d", after=(), **kw):
        if after is None:
            after = ()
        elif isinstance(after, Tok):
            after = (after,)
        after = tuple(t for t in after if t is not None)
        q = self._dma_sem(sem + "_" + queue)
        for t in after:
            self._wait(queue, t)
        self.cnt[q] += 16
        v = self.cnt[q]
        sem_h = self.sems[q]

        def run(eng):
            eng.dma_start(out=out, in_=in_, **kw).then_inc(sem_h, 16)
        self.streams[queue].append(run)
        return Tok(q, v)

    def idma(self, queue, out, in_, idx_ap, bound, sem="ig", after=()):
        if after is None:
            after = ()
        elif isinstance(after, Tok):
            after = (after,)
        after = tuple(t for t in after if t is not None)
        q = self._dma_sem(sem + "_" + queue)
        for t in after:
            self._wait(queue, t)
        self.cnt[q] += 16
        v = self.cnt[q]
        sem_h = self.sems[q]

        def run(eng):
            eng.indirect_dma_start(
                out, None, in_,
                IndirectOffsetOnAxis(ap=idx_ap, axis=0),
                bounds_check=bound, oob_is_err=False).then_inc(sem_h, 16)
        self.streams[queue].append(run)
        return Tok(q, v)

    def engine_tok(self, engine):
        return Tok(engine, self.cnt[engine])

    def finalize(self, final_toks):
        for t in final_toks:
            self._wait("sync", t)
        with self.nc.Block() as block:
            for e in ENGINES:
                stream = self.streams[e]
                if not stream:
                    continue

                def body(eng, stream=stream):
                    for f in stream:
                        f(eng)
                getattr(block, e)(body)


class Builder:

    def __init__(self, name):
        nc = bass.Bass("TRN2", target_bir_lowering=False, name=name)
        self.nc = nc
        self.p = Prog(nc)
        self.psum = [nc.alloc_psum_tensor(f"ps{i}", [128, 512], F32)
                     for i in range(8)]
        self.psum_free = [None] * 8
        self.inputs = {}
        self.outputs = {}
        self.consts = {}
        self._ident = None
        self._identf = None
        self.pt_free = [None, None]
        self.sc_col = nc.alloc_sbuf_tensor("sc_col", [128, 4], F32)
        self.bf_col = nc.alloc_sbuf_tensor("bf_col", [128, 64], BF16)

    def inp(self, name, shape, dtype=F32):
        t = self.nc.dram_tensor(name, list(shape), dtype, kind="ExternalInput").ap()
        self.inputs[name] = t
        return t

    def outp(self, name, shape, dtype=F32):
        t = self.nc.dram_tensor(name, list(shape), dtype, kind="ExternalOutput").ap()
        self.outputs[name] = t
        return t

    def const(self, name, arr):
        dt = {np.dtype(np.float32): F32, np.dtype(bf16): BF16,
              np.dtype(np.int32): mybir.dt.int32}[arr.dtype]
        t = self.nc.dram_tensor(name, list(arr.shape), dt, kind="ExternalInput").ap()
        self.consts[name] = np.ascontiguousarray(arr)
        return t

    def sb(self, name, shape, dtype=F32):
        return self.nc.alloc_sbuf_tensor(name, list(shape), dtype)

    def ident(self):
        if self._ident is None:
            dr = self.const("identc", np.eye(128, dtype=bf16))
            sb = self.sb("ident_sb", [128, 128], BF16)
            tk = self.p.dma("sync", sb[:], dr, sem="ident")
            self._ident = (sb, tk)
        return self._ident

    def identf(self):
        if self._identf is None:
            dr = self.const("identfc", np.eye(128, dtype=np.float32))
            sb = self.sb("identf_sbuf", [128, 128], F32)
            tk = self.p.dma("sync", sb[:], dr, sem="ident")
            self._identf = (sb, tk)
        return self._identf




def emit_ln_stats(b, sums, sumsq, mu, rstd, tmp, dcols, eps=1e-5, after=()):
    p = b.p
    inv = 1.0 / dcols
    p.op("vector", lambda v: v.tensor_scalar(mu, sums, inv, None, ALU.mult),
         after=after)
    p.op("vector", lambda v: v.tensor_mul(tmp, mu, mu))
    p.op("vector", lambda v: v.tensor_scalar(sums, sumsq, inv, eps, ALU.mult, ALU.add))
    p.op("vector", lambda v: v.tensor_sub(tmp, sums, tmp))
    tv = p.op("scalar", lambda s: s.activation(sums, tmp, AF.Sqrt),
              after=b.p.engine_tok("vector"))
    tk = p.op("vector", lambda v: v.reciprocal(rstd, sums), after=tv)
    return tk


def emit_ln(b, x_sb, xh_sb, scratch, stats, x_tok=None, ntiles=NT, dcols=D):
    p = b.p
    n = ntiles
    sums = stats[:, 0:n]
    sumsq = stats[:, n:2 * n]
    mu = stats[:, 2 * n:3 * n]
    rstd = stats[:, 3 * n:4 * n]
    tmp = stats[:, 4 * n:5 * n]
    last_sq = None
    for t in range(n):
        xt = x_sb[:, t * dcols:(t + 1) * dcols]
        last_sq = p.op("scalar", lambda s, xt=xt, t=t: s.activation(
            scratch[:], xt, AF.Square, accum_out=sumsq[:, t:t + 1]),
            after=x_tok)
        p.op("vector", lambda v, xt=xt, t=t: v.reduce_sum(
            sums[:, t:t + 1], xt, axis=AX.X), after=x_tok, chain=False)
    tk = emit_ln_stats(b, sums, sumsq, mu, rstd, tmp, dcols,
                       after=(last_sq, p.engine_tok("vector")))
    toks = []
    for t in range(n):
        xt = x_sb[:, t * dcols:(t + 1) * dcols]
        ot = xh_sb[:, t * dcols:(t + 1) * dcols]
        toks.append(p.op("vector", lambda v, xt=xt, ot=ot, t=t: v.tensor_scalar(
            ot, xt, mu[:, t:t + 1], rstd[:, t:t + 1], ALU.subtract, ALU.mult),
            after=tk, chain=False))
    return toks, mu, rstd


def emit_transpose(b, pairs, banks=(6, 7)):
    p = b.p
    ident, id_tok = b.ident()
    out_toks = []
    for i, pr in enumerate(pairs):
        src, dst, ready = pr[0], pr[1], pr[2]
        evac_after = pr[3] if len(pr) > 3 else None
        bk = banks[i % len(banks)]
        ps = b.psum[bk].bitcast(BF16)
        w = src.shape[-1]
        pc = src.shape[0]
        after = [id_tok]
        if ready is not None:
            after.append(ready)
        if b.psum_free[bk] is not None:
            after.append(b.psum_free[bk])
        tr = p.op("tensor", lambda t, src=src, ps=ps, w=w, pc=pc: t.transpose(
            ps[0:w, 0:pc], src, ident[0:pc, 0:pc]), after=after)
        ev = p.op("vector", lambda v, dst=dst, ps=ps, w=w, pc=pc: v.tensor_copy(
            dst, ps[0:w, 0:pc]), after=[tr, evac_after], chain=False)
        b.psum_free[bk] = ev
        out_toks.append(ev)
    return out_toks


def emit_tok_to_fm(b, src_tok, dst_fm, ready, ntiles=NT, dcols=D, banks=(6, 7)):
    if isinstance(ready, Tok) or ready is None:
        ready = [ready] * ntiles
    pairs = []
    tokw = ntiles * 128
    for t in range(ntiles):
        for dc in range(dcols // 128):
            src = src_tok[:, t * dcols + dc * 128: t * dcols + (dc + 1) * 128]
            dst = dst_fm[:, dc * tokw + t * 128: dc * tokw + (t + 1) * 128]
            pairs.append((src, dst, ready[t]))
    return emit_transpose(b, pairs, banks=banks)


def ln_to_fm(b, x_sb, xh2_sb, xhT_sb, scratch, stats, x_tok=None, ntiles=NT,
             dcols=D, banks=(6, 7)):
    p = b.p
    n = ntiles
    sums = stats[:, 0:n]
    sumsq = stats[:, n:2 * n]
    mu = stats[:, 2 * n:3 * n]
    rstd = stats[:, 3 * n:4 * n]
    tmp = stats[:, 4 * n:5 * n]
    last_sq = None
    for t in range(n):
        xt = x_sb[:, t * dcols:(t + 1) * dcols]
        last_sq = p.op("scalar", lambda s, xt=xt, t=t: s.activation(
            scratch[:, 0:dcols], xt, AF.Square, accum_out=sumsq[:, t:t + 1]),
            after=x_tok)
        p.op("vector", lambda v, xt=xt, t=t: v.reduce_sum(
            sums[:, t:t + 1], xt, axis=AX.X), after=x_tok, chain=False)
    tk = emit_ln_stats(b, sums, sumsq, mu, rstd, tmp, dcols,
                       after=(last_sq, p.engine_tok("vector")))
    tokw = n * 128
    slot_free = [None, None]
    for t in range(n):
        xt = x_sb[:, t * dcols:(t + 1) * dcols]
        slot = xh2_sb[:, (t % 2) * dcols:(t % 2 + 1) * dcols]
        ap = p.op("vector", lambda v, xt=xt, slot=slot, t=t: v.tensor_scalar(
            slot, xt, mu[:, t:t + 1], rstd[:, t:t + 1], ALU.subtract, ALU.mult),
            after=[tk, slot_free[t % 2]], chain=False)
        pairs = []
        for dc in range(dcols // 128):
            src = slot[:, dc * 128:(dc + 1) * 128]
            dst = xhT_sb[:, dc * tokw + t * 128: dc * tokw + (t + 1) * 128]
            pairs.append((src, dst, ap))
        evs = emit_transpose(b, pairs, banks=banks)
        slot_free[t % 2] = p.engine_tok("tensor")
    return p.engine_tok("vector")



DFF = 2816
NFC = DFF // 128


def build_ffn0():
    b = Builder("ffn0")
    p = b.p
    xin = b.inp("xin", [TOK, D])
    w1 = b.inp("w1", [2, D, DFF])
    w3 = b.inp("w3", [2, D, DFF])
    w2 = b.inp("w2", [2, DFF, D])
    xout = b.outp("xout", [TOK, D])
    if False:
        dbg_s = b.outp("dbg_s", [TOK, NE])
        dbg_we = b.outp("dbg_we", [TOK, NE])
        dbg_v = b.outp("dbg_v", [TOK, D])
        dbg_m = b.outp("dbg_m", [TOK, 13])
        dbg_g = b.outp("dbg_g", [128, 33])
        dbg_r = b.outp("dbg_r", [TOK, 72])
        dbg_c = b.outp("dbg_c", [TOK, 16])
        dbgbuf = b.sb("dbgbuf", [128, 128])

    x_sb = b.sb("x", [128, NT * D])
    xh_sb = b.sb("xh", [128, 2 * D], BF16)
    xhT = b.sb("xhT", [128, NDC * TOK], BF16)
    scratch = b.sb("scr", [128, D])
    stats = b.sb("stats", [128, 5 * NT])
    h_sb = b.sb("h", [128, NFC * 512], BF16)
    w13_sb = [b.sb(f"w13_{i}", [128, 2 * NDC * 128], BF16) for i in range(2)]
    w2_sb = b.sb("w2s", [128, NFC * D], BF16)
    out_sb = b.sb("out", [128, 512])
    silu_sb = b.sb("silu", [128, 512], BF16)

    xld = p.dma("sync", x_sb[:].rearrange("p (t d) -> p t d", d=D),
                xin.rearrange("(t p) d -> p t d", p=128), sem="x")
    ln_to_fm(b, x_sb, xh_sb, xhT, scratch, stats, x_tok=xld)
    fm_done = p.engine_tok("vector")

    xout_r = xout.rearrange("(t p) d -> p t d", p=128)
    w13_free = [None, None]
    h_free = None
    w2_free = None
    hm_prev = None
    final_toks = []
    for e in range(2):
        w2_tok = p.dma("gpsimd", w2_sb[:].rearrange("p (f d) -> p f d", d=D),
                       w2[e].rearrange("(f p) d -> p f d", p=128),
                       sem="w2", after=w2_free)
        for half in range(2):
            tok0 = e * BLK + half * 512
            for fc in range(NFC):
                buf = fc % 2
                wsb = w13_sb[buf]
                wtk1 = p.dma("gpsimd", wsb[:, 0:NDC * 128]
                             .rearrange("p (c f) -> p c f", f=128),
                             w1[e, :, fc * 128:(fc + 1) * 128]
                             .rearrange("(c p) f -> p c f", p=128),
                             sem=f"w13_{buf}", after=w13_free[buf])
                wtk3 = p.dma("gpsimd", wsb[:, NDC * 128:2 * NDC * 128]
                             .rearrange("p (c f) -> p c f", f=128),
                             w3[e, :, fc * 128:(fc + 1) * 128]
                             .rearrange("(c p) f -> p c f", p=128),
                             sem=f"w13_{buf}")
                for which, bank, wtk in ((0, 0, wtk3), (1, 1, wtk3)):
                    wv = wsb[:, which * NDC * 128:(which + 1) * NDC * 128]
                    after0 = [wtk, fm_done, b.psum_free[bank]]
                    for dc in range(NDC):
                        lhsT = wv[:, dc * 128:(dc + 1) * 128]
                        rhs = xhT[:, dc * TOK + tok0: dc * TOK + tok0 + 512]
                        p.op("tensor", lambda t, o=b.psum[bank][:, 0:512],
                             l=lhsT, r=rhs, st=(dc == 0), sp=(dc == NDC - 1):
                             t.matmul(o, l, r, start=st, stop=sp),
                             after=after0 if dc == 0 else ())
                mm = p.engine_tok("tensor")
                w13_free[buf] = mm
                sg = p.op("scalar", lambda s: s.activation(
                    silu_sb[:], b.psum[0][:, 0:512], AF.Silu),
                    after=[mm, hm_prev])
                hv = h_sb[:, fc * 512:(fc + 1) * 512]
                hm = p.op("vector", lambda v, hv=hv: v.tensor_mul(
                    hv, silu_sb[:], b.psum[1][:, 0:512]),
                    after=[sg, mm] + ([h_free] if fc == 0 else []))
                b.psum_free[0] = hm
                b.psum_free[1] = hm
                hm_prev = hm
            h_done = hm_prev
            for tt in range(4):
                for dc2 in range(2):
                    bank = 2 + tt % 2 * 2 + dc2
                    after0 = [w2_tok, h_done, b.psum_free[bank]]
                    for fc in range(NFC):
                        lhsT = h_sb[:, fc * 512 + tt * 128: fc * 512 + (tt + 1) * 128]
                        rhs = w2_sb[:, fc * D + dc2 * 512: fc * D + (dc2 + 1) * 512]
                        p.op("tensor", lambda t, o=b.psum[bank][:, 0:512],
                             l=lhsT, r=rhs, st=(fc == 0), sp=(fc == NFC - 1):
                             t.matmul(o, l, r, start=st, stop=sp),
                             after=after0 if fc == 0 else ())
                    mm = p.engine_tok("tensor")
                    gt = (tok0 + tt * 128) // 128
                    xs = x_sb[:, gt * D + dc2 * 512: gt * D + (dc2 + 1) * 512]
                    ev = p.op("vector", lambda v, xs=xs, bank=bank: v.tensor_add(
                        out_sb[:], b.psum[bank][:, 0:512], xs),
                        after=mm, chain=True)
                    b.psum_free[bank] = ev
                    st = p.dma("sync", xout_r[:, gt, dc2 * 512:(dc2 + 1) * 512],
                               out_sb[:], sem="st", after=ev)
                    final_toks.append(st)
                    p.last_tok["vector"] = st
            h_free = p.engine_tok("tensor")
        w2_free = p.engine_tok("tensor")
    p.finalize(final_toks[-1:])
    return b



NE = 8
NI = 32


def build_ffn1(debug=False):
    b = Builder("ffn1")
    p = b.p
    xin = b.inp("xin", [TOK, D])
    Wv = b.inp("Wv", [D, D])
    bv = b.inp("bv", [D])
    Wg = b.inp("Wg", [NE, D])
    bg = b.inp("bg", [NE])
    Wp = b.inp("Wp", [NE, NE, D])
    alpha = b.inp("alpha", [1])
    g_r = b.inp("g_r", [NE])
    b_r = b.inp("b_r", [NE])
    g_c = b.inp("g_c", [NE])
    b_c = b.inp("b_c", [NE])
    W1 = b.inp("W1", [NE, NI, D])
    b1 = b.inp("b1", [NE, NI])
    W2 = b.inp("W2", [NE, D, NI])
    b2 = b.inp("b2", [NE, D])
    xout = b.outp("xout", [TOK, D])
    if debug:
        dbg_s = b.outp("dbg_s", [TOK, NE])
        dbg_we = b.outp("dbg_we", [TOK, NE])
        dbg_v = b.outp("dbg_v", [TOK, D])
        dbg_m = b.outp("dbg_m", [TOK, 13])
        dbg_g = b.outp("dbg_g", [128, 33])
        dbg_r = b.outp("dbg_r", [TOK, 72])
        dbg_c = b.outp("dbg_c", [TOK, 16])
        dbgbuf = b.sb("dbgbuf", [128, 128])

    x_sb = b.sb("x", [128, NT * D])
    xh_sb = b.sb("xh", [128, 2 * D], BF16)
    xhT = b.sb("xhT", [128, NDC * TOK], BF16)
    VT = b.sb("VT", [128, NDC * TOK], BF16)
    v2 = b.sb("v2", [128, 2 * D], BF16)
    scratch = b.sb("scr", [128, D])
    stats = b.sb("stats", [128, 96])
    WvT = b.sb("WvT", [128, NDC * D], BF16)
    WgpT = b.sb("WgpT", [128, NDC * 80], BF16)
    W1T = b.sb("W1T", [128, NDC * NE * NI], BF16)
    W2nat = b.sb("W2nat", [128, NE * NDC * NI], BF16)
    W2b = b.sb("W2b", [64, NE * D], BF16)
    rows = b.sb("rows", [1, 2 * D + 72 + 256])
    rows_bf = b.sb("rows_bf", [1, D + 80], BF16)
    b1rows = b.sb("b1rows", [1, NE * NI], BF16)
    gains = b.sb("gains", [128, 32])
    sig = b.sb("sig", [128, 1])
    galph = b.sb("galph", [1, 2])
    gate = b.sb("gate", [128, 16])
    we_sb = b.sb("we", [128, NE])
    lncr = b.sb("lncr", [128, 2 * NE])
    hslot = b.sb("hslot", [128, 40], BF16)
    hT = b.sb("hT", [64, 128], BF16)
    ones_bf = b.sb("ones_bf", [1, 128], BF16)
    ones_f = b.sb("ones_f", [1, 128])
    out_sb = b.sb("outs", [128, 512])

    wv_bf = b.nc.dram_tensor("wv_bf", [D, D], BF16, kind="Internal").ap()
    wp_bf = b.nc.dram_tensor("wp_bf", [NE * NE, D], BF16, kind="Internal").ap()
    w1_bf = b.nc.dram_tensor("w1_bf", [NE * NI, D], BF16, kind="Internal").ap()

    xld = p.dma("sync", x_sb[:].rearrange("p (t d) -> p t d", d=D),
                xin.rearrange("(t p) d -> p t d", p=128), sem="x")
    c1 = p.dma("gpsimd", wv_bf, Wv, sem="wcast")
    wg_nat = b.sb("wg_nat", [8, D], BF16)
    c2 = p.dma("gpsimd", wg_nat[:], Wg, sem="wcast")
    c3 = p.dma("gpsimd", wp_bf, Wp.rearrange("e q d -> (e q) d"), sem="wcast")
    c4 = p.dma("gpsimd", w1_bf, W1.rearrange("e i d -> (e i) d"), sem="wcast")
    tts = []
    for dc in range(NDC):
        tts.append(p.dma("sync", WvT[:, dc * D:(dc + 1) * D],
                         wv_bf[:, dc * 128:(dc + 1) * 128], transpose=True,
                         sem="wT", after=c4))
        tts.append(p.dma("sync", WgpT[:, dc * 80 + 16:dc * 80 + 80],
                         wp_bf[:, dc * 128:(dc + 1) * 128], transpose=True,
                         sem="wT"))
        tts.append(p.dma("sync", W1T[:, dc * 256:(dc + 1) * 256],
                         w1_bf[:, dc * 128:(dc + 1) * 128], transpose=True,
                         sem="wT"))
    wT_tok = tts[-1]
    emit_transpose(b, [(wg_nat[0:8, dc * 128:(dc + 1) * 128],
                        WgpT[:, dc * 80:dc * 80 + 8], c4)
                       for dc in range(NDC)], banks=(4, 5))
    wgT_tok = b.p.engine_tok("vector")
    gap0 = p.op("vector", lambda v: v.memset(
        WgpT[:].rearrange("p (c e) -> p c e", e=80)[:, :, 8:16], 0.0))
    w2n_toks = []
    for e in range(NE):
        for dc in range(NDC):
            w2n_toks.append(p.dma(
                "gpsimd", W2nat[:, (e * NDC + dc) * NI:(e * NDC + dc + 1) * NI],
                W2[e, dc * 128:(dc + 1) * 128, :], sem="w2n"))
        p.dma("gpsimd", W2b[32:33, e * D:(e + 1) * D], b2[e:e + 1, :], sem="rows")
    p.dma("gpsimd", b1rows[:], b1.rearrange("e i -> (e i)")
          .rearrange("(o n) -> o n", o=1), sem="rows")
    p.dma("gpsimd", rows_bf[:, 0:D], bv.rearrange("(o n) -> o n", o=1), sem="rows")
    p.dma("gpsimd", rows_bf[:, D:D + 8], bg.rearrange("(o n) -> o n", o=1),
          sem="rows")
    p.dma("sync", gains[:, 0:8], g_r.rearrange("(o n) -> o n", o=1)
          .broadcast_to([128, 8]), sem="rows")
    p.dma("sync", gains[:, 8:16], b_r.rearrange("(o n) -> o n", o=1)
          .broadcast_to([128, 8]), sem="rows")
    p.dma("sync", gains[:, 16:24], g_c.rearrange("(o n) -> o n", o=1)
          .broadcast_to([128, 8]), sem="rows")
    p.dma("sync", gains[:, 24:32], b_c.rearrange("(o n) -> o n", o=1)
          .broadcast_to([128, 8]), sem="rows")
    al_tok = p.dma("sync", rows[:, 32:33], alpha.rearrange("(o n) -> o n", o=1),
                   sem="rows")
    rows_tok = Tok("dma_rows_sync", p.cnt["dma_rows_sync"])
    zpad = p.op("vector", lambda v: v.memset(rows_bf[:, D + 8:D + 80], 0.0))
    onet = p.op("vector", lambda v: v.memset(ones_bf[:], 1.0))
    p.op("vector", lambda v: v.memset(hslot[:, 33:40], 0.0))

    sig_dr = b.nc.dram_tensor("sig_dr", [1, 1], F32, kind="Internal").ap()
    sg_a = p.op("scalar", lambda s: s.activation(galph[:, 0:1], rows[:, 32:33],
                                                 AF.Sigmoid), after=al_tok)
    sd1 = p.dma("sync", sig_dr, galph[:, 0:1], sem="sig", after=sg_a)
    ev_s = p.dma("sync", sig[:], sig_dr.broadcast_to([128, 1]), sem="sig",
                 after=sd1)

    if debug:
        gcop = p.op("vector", lambda v: v.tensor_copy(dbgbuf[:, 0:32],
                                                      gains[:, 0:32]),
                    after=[rows_tok, ev_s])
        gcop2 = p.op("vector", lambda v: v.tensor_copy(dbgbuf[:, 32:33], sig[:]))
        gdump = p.dma("sync", dbg_g, dbgbuf[:, 0:33], sem="dbg", after=gcop2)
        p.last_tok["vector"] = gdump
    ln_to_fm(b, x_sb, xh_sb, xhT, scratch, stats, x_tok=xld)
    fm_done = p.engine_tok("vector")

    for e in range(NE):
        pairs = []
        for dc in range(NDC):
            src = W2nat[:, (e * NDC + dc) * NI:(e * NDC + dc + 1) * NI]
            dst = W2b[0:32, e * D + dc * 128: e * D + (dc + 1) * 128]
            pairs.append((src, dst, w2n_toks[-1]))
        emit_transpose(b, pairs, banks=(4, 5))

    rowsbf_tok = Tok("dma_rows_gpsimd", p.cnt["dma_rows_gpsimd"])
    vfree = {}
    for t in range(NT):
        for half in range(2):
            bank = 2 + half
            after0 = [fm_done, zpad, onet, rowsbf_tok, wT_tok,
                      b.psum_free[bank]]
            for hc in range(NDC):
                lhsT = xhT[:, hc * TOK + t * 128: hc * TOK + (t + 1) * 128]
                rhs = WvT[:, hc * D + half * 512: hc * D + (half + 1) * 512]
                p.op("tensor", lambda tt, o=b.psum[bank][:, 0:512], l=lhsT,
                     r=rhs, st=(hc == 0): tt.matmul(o, l, r, start=st,
                                                    stop=False),
                     after=after0 if hc == 0 else ())
            p.op("tensor", lambda tt, o=b.psum[bank][:, 0:512],
                 r=rows_bf[:, half * 512:(half + 1) * 512]:
                 tt.matmul(o, ones_bf[:], r, start=False, stop=True))
            mm = p.engine_tok("tensor")
            xs = x_sb[:, t * D + half * 512: t * D + (half + 1) * 512]
            ad = p.op("vector", lambda v, xs=xs, bank=bank: v.tensor_add(
                xs, xs, b.psum[bank][:, 0:512]), after=mm)
            vslot = v2[:, (t % 2) * D + half * 512:(t % 2) * D + (half + 1) * 512]
            cv = p.op("scalar", lambda s, vslot=vslot, bank=bank: s.copy(
                vslot, b.psum[bank][:, 0:512]),
                after=[mm, ad, vfree.get((t % 2, half))])
            b.psum_free[bank] = cv
            prs = []
            for q in range(4):
                dc = half * 4 + q
                src = vslot[:, q * 128:(q + 1) * 128]
                dst = VT[:, dc * TOK + t * 128: dc * TOK + (t + 1) * 128]
                prs.append((src, dst, cv))
            emit_transpose(b, prs, banks=(6, 7))
            vfree[(t % 2, half)] = p.engine_tok("tensor")
    vt_done = p.engine_tok("vector")

    xout_r = xout.rearrange("(t p) d -> p t d", p=128)
    final_toks = []
    wet = None
    for t in range(NT):
        after0 = [vt_done, b.psum_free[0]]
        for dc in range(NDC):
            lhsT = VT[:, dc * TOK + t * 128: dc * TOK + (t + 1) * 128]
            rhs = WgpT[:, dc * 80:(dc + 1) * 80]
            p.op("tensor", lambda tt, l=lhsT, r=rhs, st=(dc == 0):
                 tt.matmul(b.psum[0][:, 0:80], l, r, start=st, stop=False),
                 after=after0 if dc == 0 else ())
        p.op("tensor", lambda tt: tt.matmul(
            b.psum[0][:, 0:80], ones_bf[:], rows_bf[:, D:D + 80],
            start=False, stop=True))
        mm = p.engine_tok("tensor")
        sq = p.op("scalar", lambda s: s.activation(scratch[:, 0:64],
                                                   b.psum[0][:, 16:80], AF.Square),
                  after=[mm, wet])
        rc = p.op("vector", lambda v: v.tensor_reduce(
            gate[:, 8:16], scratch[:, 0:64].rearrange("p (e q) -> p e q", q=8),
            axis=AX.X, op=ALU.add),
            after=[sq] + ([rows_tok, ev_s] if t == 0 else []))
        if debug:
            rpc = p.op("vector", lambda v: v.tensor_copy(dbgbuf[:, 33:105],
                                                         b.psum[0][:, 8:80]),
                       after=mm)
            rpd = p.dma("sync", dbg_r.rearrange("(t p) e -> p t e", p=128)[:, t, :],
                        dbgbuf[:, 33:105], sem="dbg", after=rpc)
            p.last_tok["vector"] = rpd
        rcp = p.op("vector", lambda v: v.tensor_copy(gate[:, 0:8],
                                                     b.psum[0][:, 0:8]), after=mm)
        b.psum_free[0] = rcp
        p.op("vector", lambda v: v.tensor_scalar(gate[:, 8:16], gate[:, 8:16],
                                                 0.125, None, ALU.mult))
        cs = p.op("scalar", lambda s: s.activation(gate[:, 8:16], gate[:, 8:16],
                                                   AF.Sqrt),
                  after=p.engine_tok("vector"))
        if debug:
            ccp = p.op("vector", lambda v: v.tensor_copy(dbgbuf[:, 105:121],
                                                         gate[:, 0:16]),
                       after=cs)
            ccd = p.dma("sync", dbg_c.rearrange("(t p) e -> p t e", p=128)[:, t, :],
                        dbgbuf[:, 105:121], sem="dbg", after=ccp)
            p.last_tok["vector"] = ccd
        for k, (src, dst) in enumerate(((gate[:, 0:8], lncr[:, 0:8]),
                                        (gate[:, 8:16], lncr[:, 8:16]))):
            p.op("vector", lambda v, src=src: v.reduce_sum(
                stats[:, 80:81], src, axis=AX.X), after=cs)
            p.op("vector", lambda v, src=src: v.tensor_scalar(
                stats[:, 80:81], stats[:, 80:81], 0.125, None, ALU.mult))
            p.op("vector", lambda v, src=src, dst=dst: v.tensor_scalar(
                dst, src, stats[:, 80:81], None, ALU.subtract))
            p.op("vector", lambda v, dst=dst: v.tensor_mul(
                scratch[:, 64:72], dst, dst))
            p.op("vector", lambda v: v.reduce_sum(
                stats[:, 81:82], scratch[:, 64:72], axis=AX.X))
            p.op("vector", lambda v: v.tensor_scalar(
                stats[:, 81:82], stats[:, 81:82], 0.125, 1e-5, ALU.mult, ALU.add))
            sqv = p.op("scalar", lambda s: s.activation(
                stats[:, 82:83], stats[:, 81:82], AF.Sqrt),
                after=p.engine_tok("vector"))
            p.op("vector", lambda v: v.reciprocal(stats[:, 83:84],
                                                  stats[:, 82:83]), after=sqv)
            p.op("vector", lambda v, dst=dst: v.tensor_scalar(
                dst, dst, stats[:, 83:84], None, ALU.mult))
        p.op("vector", lambda v: v.tensor_mul(lncr[:, 8:16], lncr[:, 8:16],
                                              gains[:, 16:24]))
        p.op("vector", lambda v: v.tensor_add(lncr[:, 8:16], lncr[:, 8:16],
                                              gains[:, 24:32]))
        p.op("vector", lambda v: v.tensor_scalar(lncr[:, 8:16], lncr[:, 8:16],
                                                 sig[:, 0:1], None, ALU.mult))
        p.op("vector", lambda v: v.tensor_mul(lncr[:, 0:8], lncr[:, 0:8],
                                              gains[:, 0:8]))
        p.op("vector", lambda v: v.tensor_add(lncr[:, 0:8], lncr[:, 0:8],
                                              gains[:, 8:16]))
        sdone = p.op("vector", lambda v: v.tensor_add(gate[:, 0:8], lncr[:, 0:8],
                                              lncr[:, 8:16]))
        if debug:
            scop = p.op("vector", lambda v: v.tensor_copy(stats[:, 84:92],
                                                          gate[:, 0:8]))
            p.dma("sync", dbg_s.rearrange("(t p) e -> p t e", p=128)[:, t, :],
                  stats[:, 84:92], sem="dbg", after=scop)
        p.op("vector", lambda v: v.reduce_max(gate[:, 8:9], gate[:, 0:8],
                                              axis=AX.X))
        p.op("vector", lambda v: v.tensor_scalar(we_sb[:], gate[:, 0:8],
                                                 gate[:, 8:9], None, ALU.is_equal))
        p.op("vector", lambda v: v.scalar_tensor_tensor(
            gate[:, 0:8], we_sb[:], BIGNEG, gate[:, 0:8], ALU.mult, ALU.add))
        p.op("vector", lambda v: v.reduce_max(gate[:, 9:10], gate[:, 0:8],
                                              axis=AX.X))
        p.op("vector", lambda v: v.tensor_scalar(gate[:, 10:11], gate[:, 8:9],
                                                 gate[:, 9:10], None,
                                                 ALU.subtract))
        sgw = p.op("scalar", lambda s: s.activation(gate[:, 11:12], gate[:, 10:11],
                                                    AF.Sigmoid),
                   after=p.engine_tok("vector"))
        p.op("vector", lambda v: v.tensor_scalar(gate[:, 12:13], gate[:, 11:12],
                                                 -1.0, 1.0, ALU.mult, ALU.add),
             after=sgw)
        p.op("vector", lambda v: v.tensor_scalar(scratch[:, 0:8], gate[:, 0:8],
                                                 gate[:, 9:10], None, ALU.is_equal))
        p.op("vector", lambda v: v.tensor_scalar(we_sb[:], we_sb[:],
                                                 gate[:, 11:12], None, ALU.mult))
        wet = p.op("vector", lambda v: v.scalar_tensor_tensor(
            we_sb[:], scratch[:, 0:8], gate[:, 12:13], we_sb[:],
            ALU.mult, ALU.add))
        if debug:
            wcop = p.op("vector", lambda v: v.tensor_copy(stats[:, 64:77],
                                                          gate[:, 0:13]))
            p.dma("sync", dbg_m.rearrange("(t p) e -> p t e", p=128)[:, t, :],
                  stats[:, 64:77], sem="dbg", after=wcop)
            p.dma("sync", dbg_we.rearrange("(t p) e -> p t e", p=128)[:, t, :],
                  we_sb[:], sem="dbg", after=wet)
            p.dma("sync", dbg_v.rearrange("(t p) d -> p t d", p=128)[:, t, :],
                  x_sb[:, t * D:(t + 1) * D], sem="dbg", after=wet)
        tr_prev = None
        dmm_prev = None
        for e in range(NE):
            hafter = [wet, b.psum_free[1]]
            for dc in range(NDC):
                lhsT = VT[:, dc * TOK + t * 128: dc * TOK + (t + 1) * 128]
                rhs = W1T[:, dc * 256 + e * NI: dc * 256 + (e + 1) * NI]
                p.op("tensor", lambda tt, l=lhsT, r=rhs, st=(dc == 0):
                     tt.matmul(b.psum[1][:, 0:NI], l, r, start=st, stop=False),
                     after=hafter if dc == 0 else ())
            p.op("tensor", lambda tt, e=e: tt.matmul(
                b.psum[1][:, 0:NI], ones_bf[:],
                b1rows[:, e * NI:(e + 1) * NI], start=False, stop=True))
            hmm = p.engine_tok("tensor")
            hsl = p.op("scalar", lambda s: s.activation(hslot[:, 0:NI],
                                                        b.psum[1][:, 0:NI],
                                                        AF.Silu),
                       after=[hmm, tr_prev])
            b.psum_free[1] = hsl
            hsc = p.op("vector", lambda v, e=e: v.tensor_scalar(
                hslot[:, 0:NI], hslot[:, 0:NI], we_sb[:, e:e + 1], None,
                ALU.mult), after=hsl)
            wcp = p.op("vector", lambda v, e=e: v.tensor_copy(
                hslot[:, 32:33], we_sb[:, e:e + 1]))
            tr = emit_transpose(b, [(hslot[:, 0:40], hT[0:40, :], wcp,
                                     dmm_prev)], banks=(6,))[0]
            tr_prev = p.engine_tok("tensor")
            for dc2 in range(2):
                bank = 2 + dc2
                dafter = [tr, b.psum_free[bank]] if e == 0 else [tr]
                p.op("tensor", lambda tt, e=e, dc2=dc2, bank=bank,
                     st=(e == 0), sp=(e == NE - 1): tt.matmul(
                         b.psum[bank][:, 0:512], hT[0:33, :],
                         W2b[0:33, e * D + dc2 * 512: e * D + (dc2 + 1) * 512],
                         start=st, stop=sp),
                     after=dafter)
            dmm_prev = p.engine_tok("tensor")
        dmm = p.engine_tok("tensor")
        for dc2 in range(2):
            bank = 2 + dc2
            xs = x_sb[:, t * D + dc2 * 512: t * D + (dc2 + 1) * 512]
            ev = p.op("vector", lambda v, xs=xs, bank=bank: v.tensor_add(
                out_sb[:], b.psum[bank][:, 0:512], xs), after=dmm)
            b.psum_free[bank] = ev
            st = p.dma("sync", xout_r[:, t, dc2 * 512:(dc2 + 1) * 512],
                       out_sb[:], sem="st", after=ev)
            final_toks.append(st)
            p.last_tok["vector"] = st
    p.finalize(final_toks[-1:])
    return b



DC_KV = 256
NH = 16
DK = 64


def shard_att(x2d, c):
    r, r4 = c // 4, c % 4
    v = x2d[r * S:(r + 1) * S].reshape(S // 128, 128, -1)[r4::4]
    return np.ascontiguousarray(v.reshape(TOK, -1))


def unshard_att(shards):
    out = np.empty((B * S,) + shards[0].shape[1:], shards[0].dtype)
    ov = out.reshape(B, S // 128, 128, -1)
    for c in range(N_CORES):
        r, r4 = c // 4, c % 4
        ov[r, r4::4] = shards[c].reshape(NT, 128, -1)
    return out


def att_mask(c):
    r4 = c % 4
    col = np.arange(512)[None, :]
    row = np.arange(128)[:, None]
    return np.where(col - 128 * r4 > row, np.float32(BIGNEG),
                    np.float32(0.0)).astype(np.float32)


def build_l0a_pre():
    b = Builder("l0a_pre")
    p = b.p
    xin = b.inp("xin", [TOK, D])
    W_dq = b.inp("W_dq", [D, DC_KV])
    W_uq = b.inp("W_uq", [DC_KV, D])
    W_dkv = b.inp("W_dkv", [D, DC_KV])
    qt_o = b.outp("qt", [D, TOK], BF16)
    ckv_o = b.outp("ckv", [TOK, DC_KV], BF16)

    x_sb = b.sb("x", [128, NT * D])
    xh_sb = b.sb("xh", [128, 2 * D], BF16)
    xhT = b.sb("xhT", [128, NDC * TOK], BF16)
    scratch = b.sb("scr", [128, D])
    stats = b.sb("stats", [128, 5 * NT])
    wdq = b.sb("wdq", [128, NDC * DC_KV], BF16)
    wuq = b.sb("wuq", [128, 2 * D], BF16)
    wdkv = b.sb("wdkv", [128, NDC * DC_KV], BF16)
    cqT = b.sb("cqT", [128, 2 * TOK], BF16)
    ev_sb = b.sb("ev", [128, 512], BF16)

    xld = p.dma("sync", x_sb[:].rearrange("p (t d) -> p t d", d=D),
                xin.rearrange("(t p) d -> p t d", p=128), sem="x")
    w1t = p.dma("gpsimd", wdq[:].rearrange("p (c n) -> p c n", n=DC_KV),
                W_dq.rearrange("(c p) n -> p c n", p=128), sem="w")
    w2t = p.dma("gpsimd", wuq[:].rearrange("p (c n) -> p c n", n=D),
                W_uq.rearrange("(c p) n -> p c n", p=128), sem="w")
    w3t = p.dma("gpsimd", wdkv[:].rearrange("p (c n) -> p c n", n=DC_KV),
                W_dkv.rearrange("(c p) n -> p c n", p=128), sem="w")
    wtok = w3t
    ln_to_fm(b, x_sb, xh_sb, xhT, scratch, stats, x_tok=xld)
    fm_done = p.engine_tok("vector")

    qt_r = qt_o.rearrange("(c p) n -> p c n", p=128)
    ckv_r = ckv_o.rearrange("(t p) n -> p t n", p=128)
    final = []
    for cc in range(2):
        for s4 in range(4):
            bank = s4 % 2
            after0 = [wtok, fm_done, b.psum_free[bank]]
            for dc in range(NDC):
                p.op("tensor", lambda t, cc=cc, s4=s4, dc=dc, bank=bank,
                     st=(dc == 0), sp=(dc == NDC - 1): t.matmul(
                         b.psum[bank][:, 0:512],
                         wdq[:, dc * DC_KV + cc * 128: dc * DC_KV + (cc + 1) * 128],
                         xhT[:, dc * TOK + s4 * 512: dc * TOK + (s4 + 1) * 512],
                         start=st, stop=sp), after=after0 if dc == 0 else ())
            mm = p.engine_tok("tensor")
            ev = p.op("vector", lambda v, cc=cc, s4=s4, bank=bank: v.tensor_copy(
                cqT[:, cc * TOK + s4 * 512: cc * TOK + (s4 + 1) * 512],
                b.psum[bank][:, 0:512]), after=mm)
            b.psum_free[bank] = ev
    cq_done = p.engine_tok("vector")
    for hc in range(NDC):
        for s4 in range(4):
            bank = 2 + s4 % 2
            after0 = [cq_done, b.psum_free[bank]]
            for cc in range(2):
                p.op("tensor", lambda t, hc=hc, s4=s4, cc=cc, bank=bank,
                     st=(cc == 0), sp=(cc == 1): t.matmul(
                         b.psum[bank][:, 0:512],
                         wuq[:, cc * D + hc * 128: cc * D + (hc + 1) * 128],
                         cqT[:, cc * TOK + s4 * 512: cc * TOK + (s4 + 1) * 512],
                         start=st, stop=sp), after=after0 if cc == 0 else ())
            mm = p.engine_tok("tensor")
            ev = p.op("vector", lambda v, bank=bank: v.tensor_copy(
                ev_sb[:], b.psum[bank][:, 0:512]), after=mm)
            b.psum_free[bank] = ev
            st = p.dma("sync", qt_r[:, hc, s4 * 512:(s4 + 1) * 512], ev_sb[:],
                       sem="st", after=ev)
            final.append(st)
            p.last_tok["vector"] = st
    ckv_sb = b.sb("ckvs", [128, 2 * DC_KV], BF16)
    for t in range(NT):
        bank = 4 + t % 2
        after0 = [wtok, fm_done, b.psum_free[bank]]
        for dc in range(NDC):
            p.op("tensor", lambda tt, t=t, dc=dc, bank=bank,
                 st=(dc == 0), sp=(dc == NDC - 1): tt.matmul(
                     b.psum[bank][:, 0:DC_KV],
                     xhT[:, dc * TOK + t * 128: dc * TOK + (t + 1) * 128],
                     wdkv[:, dc * DC_KV:(dc + 1) * DC_KV],
                     start=st, stop=sp), after=after0 if dc == 0 else ())
        mm = p.engine_tok("tensor")
        slot = ckv_sb[:, (t % 2) * DC_KV:(t % 2 + 1) * DC_KV]
        ev = p.op("vector", lambda v, slot=slot, bank=bank: v.tensor_copy(
            slot, b.psum[bank][:, 0:DC_KV]), after=mm)
        b.psum_free[bank] = ev
        st = p.dma("sync", ckv_r[:, t, :], slot, sem="st", after=ev)
        final.append(st)
        p.last_tok["vector"] = st
    p.finalize(final[-1:])
    return b


def build_l0a_attn():
    b = Builder("l0a_attn")
    p = b.p
    xin = b.inp("xin", [TOK, D])
    qt_i = b.inp("qt", [D, TOK], BF16)
    ckvT_i = b.inp("ckvT", [DC_KV, S], BF16)
    cmask = b.inp("cmask", [128, 512])
    W_uk = b.inp("W_uk", [DC_KV, D])
    W_uv = b.inp("W_uv", [DC_KV, D])
    W_o = b.inp("W_o", [D, D])
    xout = b.outp("xout", [TOK, D])

    QT = b.sb("QT", [128, NDC * TOK], BF16)
    ckvT = b.sb("ckvT_sb", [128, 2 * S], BF16)
    wuk = b.sb("wuk", [128, 2 * D], BF16)
    wuv = b.sb("wuv", [128, 2 * D], BF16)
    wo = b.sb("wo", [128, NDC * D], BF16)
    KT = b.sb("KT", [128, S], BF16)
    Vg = b.sb("Vg", [128, S], BF16)
    P = b.sb("P", [128, S], BF16)
    PT = b.sb("PT", [128, 4 * 128], BF16)
    outfm = b.sb("outfm", [128, NDC * TOK], BF16)
    lsums = b.sb("lsums", [128, 20])
    mask_sb = b.sb("mask", [128, 512])
    ind2 = b.const("ind2", np.concatenate(
        [np.repeat(np.eye(2, dtype=bf16)[0:1], 64, 0),
         np.repeat(np.eye(2, dtype=bf16)[1:2], 64, 0)], 0))
    ind2_sb = b.sb("ind2_sb", [128, 2], BF16)
    norm_sb = b.sb("norm", [128, 4])
    qkbf = b.sb("qkbf", [128, 1], BF16)
    msc = b.sb("msc", [1, 2])
    negm = b.sb("negm", [128, 2])
    negm_dr = b.nc.dram_tensor("negm_dr", [1, 2], F32, kind="Internal").ap()
    xs_sb = b.sb("xs", [128, 2 * D])
    out_sb = b.sb("outs", [128, 512])

    qld = p.dma("sync", QT[:].rearrange("p (c n) -> p c n", n=TOK),
                qt_i.rearrange("(c p) n -> p c n", p=128), sem="x")
    ckld = p.dma("sync", ckvT[:].rearrange("p (c n) -> p c n", n=S),
                 ckvT_i.rearrange("(c p) n -> p c n", p=128), sem="x")
    w1t = p.dma("gpsimd", wuk[:].rearrange("p (c n) -> p c n", n=D),
                W_uk.rearrange("(c p) n -> p c n", p=128), sem="w")
    w2t = p.dma("gpsimd", wuv[:].rearrange("p (c n) -> p c n", n=D),
                W_uv.rearrange("(c p) n -> p c n", p=128), sem="w")
    w3t = p.dma("gpsimd", wo[:].rearrange("p (c n) -> p c n", n=D),
                W_o.rearrange("(c p) n -> p c n", p=128), sem="w")
    mld = p.dma("sync", mask_sb[:], cmask, sem="m")
    ild = p.dma("sync", ind2_sb[:], ind2, sem="m")
    mld = ild
    ident_sb, id_tok = b.ident()
    wtok = w3t
    inp_tok = ckld

    final = []
    for hg in range(NDC):
        kv_free = p.engine_tok("tensor")
        for sc in range(S // 512):
            bank = sc % 2
            after0 = [wtok, inp_tok, b.psum_free[bank], kv_free]
            for cc in range(2):
                p.op("tensor", lambda t, sc=sc, cc=cc, bank=bank, hg=hg,
                     st=(cc == 0), sp=(cc == 1): t.matmul(
                         b.psum[bank][:, 0:512],
                         wuk[:, cc * D + hg * 128: cc * D + (hg + 1) * 128],
                         ckvT[:, cc * S + sc * 512: cc * S + (sc + 1) * 512],
                         start=st, stop=sp), after=after0 if cc == 0 else ())
            mm = p.engine_tok("tensor")
            ev = p.op("vector", lambda v, sc=sc, bank=bank: v.tensor_copy(
                KT[:, sc * 512:(sc + 1) * 512], b.psum[bank][:, 0:512]),
                after=mm)
            b.psum_free[bank] = ev
        for kt in range(S // 128):
            bank = 2 + kt % 2
            after0 = [wtok, inp_tok, b.psum_free[bank], kv_free]
            for cc in range(2):
                p.op("tensor", lambda t, kt=kt, cc=cc, bank=bank, hg=hg,
                     st=(cc == 0), sp=(cc == 1): t.matmul(
                         b.psum[bank][:, 0:128],
                         ckvT[:, cc * S + kt * 128: cc * S + (kt + 1) * 128],
                         wuv[:, cc * D + hg * 128: cc * D + (hg + 1) * 128],
                         start=st, stop=sp), after=after0 if cc == 0 else ())
            mm = p.engine_tok("tensor")
            ev = p.op("vector", lambda v, kt=kt, bank=bank: v.tensor_copy(
                Vg[:, kt * 128:(kt + 1) * 128], b.psum[bank][:, 0:128]),
                after=mm)
            b.psum_free[bank] = ev
        kv_done = p.engine_tok("vector")
        t1 = p.op("vector", lambda v: v.reduce_max(
            norm_sb[:, 1:2], KT[:], axis=AX.X, apply_absolute_value=True),
            after=kv_done)
        t2 = p.op("vector", lambda v, hg=hg: v.reduce_max(
            norm_sb[:, 0:1], QT[:, hg * TOK:(hg + 1) * TOK], axis=AX.X,
            apply_absolute_value=True), after=inp_tok)
        t3 = p.op("vector", lambda v: v.tensor_mul(norm_sb[:, 2:3],
                                                   norm_sb[:, 0:1],
                                                   norm_sb[:, 1:2]))
        t4 = p.op("vector", lambda v: v.tensor_copy(qkbf[:], norm_sb[:, 2:3]))
        mmn = p.op("tensor", lambda t: t.matmul(
            b.psum[4][0:1, 0:2], qkbf[:], ind2_sb[:],
            start=True, stop=True), after=[t4, ild, b.psum_free[4]])
        t5 = p.op("vector", lambda v: v.tensor_copy(msc[:], b.psum[4][0:1, 0:2]),
                  after=mmn)
        b.psum_free[4] = t5
        t6 = p.op("scalar", lambda s: s.mul(msc[:], msc[:], -0.125),
                  after=t5)
        nd1 = p.dma("sync", negm_dr, msc[:], sem="negm", after=t6)
        nd2 = p.dma("sync", negm[:], negm_dr.broadcast_to([128, 2]), sem="negm",
                    after=nd1)
        p_free = None
        for h in range(2):
            hb = hg * 128 + h * 64
            for i in range(NT):
                nkb = i + 1
                for kb in range(nkb):
                    bank = kb % 2
                    after0 = [kv_done, nd2, mld, b.psum_free[bank],
                              (p_free if kb == 0 else None)]
                    p.op("tensor", lambda t, i=i, kb=kb, bank=bank, h=h, hg=hg:
                         t.matmul(b.psum[bank][:, 0:512],
                                  QT[h * 64:(h + 1) * 64,
                                     hg * TOK + i * 128: hg * TOK + (i + 1) * 128],
                                  KT[h * 64:(h + 1) * 64, kb * 512:(kb + 1) * 512],
                                  start=True, stop=True), after=after0)
                    mm = p.engine_tok("tensor")
                    if kb == nkb - 1:
                        mk = p.op("vector", lambda v, bank=bank: v.tensor_add(
                            b.psum[bank][:, 0:512], b.psum[bank][:, 0:512],
                            mask_sb[:]), after=mm, chain=False)
                        mm = mk
                    ex = p.op("scalar", lambda s, kb=kb, h=h, bank=bank:
                              s.activation(P[:, kb * 512:(kb + 1) * 512],
                                           b.psum[bank][:, 0:512], AF.Exp,
                                           bias=negm[:, h:h + 1], scale=0.125,
                                           accum_out=lsums[:, kb:kb + 1]),
                              after=[mm] + ([p_free] if kb == 0 and
                                            p_free is not None else []))
                    b.psum_free[bank] = ex
                lt = p.op("vector", lambda v, nkb=nkb: v.reduce_sum(
                    lsums[:, 16:17], lsums[:, 0:nkb], axis=AX.X),
                    after=p.engine_tok("scalar"))
                lr = p.op("vector", lambda v: v.reciprocal(lsums[:, 17:18],
                                                           lsums[:, 16:17]))
                pssc = []
                for kb in range(nkb):
                    pssc.append(p.op("vector", lambda v, kb=kb: v.tensor_scalar(
                        P[:, kb * 512:(kb + 1) * 512],
                        P[:, kb * 512:(kb + 1) * 512],
                        lsums[:, 17:18], None, ALU.mult),
                        after=lr, chain=False))
                psc_done = pssc[-1]
                ob = 6 + h
                pv_after = [b.psum_free[ob]]
                pt_free = [None, None]
                for kt in range(4 * nkb):
                    tb = 4 + kt % 2
                    trafter = [psc_done, b.psum_free[tb], id_tok]
                    tr = p.op("tensor", lambda t, kt=kt, tb=tb: t.transpose(
                        b.psum[tb].bitcast(BF16)[0:128, 0:128],
                        P[:, kt * 128:(kt + 1) * 128], ident_sb[:]),
                        after=trafter)
                    tev = p.op("vector", lambda v, kt=kt, tb=tb: v.tensor_copy(
                        PT[:, (kt % 2) * 128:(kt % 2 + 1) * 128],
                        b.psum[tb].bitcast(BF16)[0:128, 0:128]),
                        after=[tr, pt_free[kt % 2]], chain=False)
                    b.psum_free[tb] = tev
                    p.op("tensor", lambda t, kt=kt, h=h, ob=ob, st=(kt == 0),
                         sp=(kt == 4 * nkb - 1): t.matmul(
                             b.psum[ob][0:64, 0:128],
                             Vg[:, kt * 128 + h * 64: kt * 128 + h * 64 + 64],
                             PT[:, (kt % 2) * 128:(kt % 2 + 1) * 128],
                             start=st, stop=sp),
                         after=[tev] + (pv_after if kt == 0 else []))
                    pt_free[kt % 2] = p.engine_tok("tensor")
                mmo = p.engine_tok("tensor")
                oev = p.op("vector", lambda v, hg=hg, h=h, i=i: v.tensor_copy(
                    outfm[h * 64:(h + 1) * 64,
                          hg * TOK + i * 128: hg * TOK + (i + 1) * 128],
                    b.psum[6 + h][0:64, 0:128]), after=mmo)
                b.psum_free[ob] = oev
                p_free = p.engine_tok("tensor")
    att_done = p.engine_tok("vector")
    xout_r = xout.rearrange("(t p) d -> p t d", p=128)
    xld_toks = [None, None]
    for t in range(NT):
        sl = t % 2
        xt = p.dma("sync", xs_sb[:, sl * D:(sl + 1) * D],
                   xin.rearrange("(t p) d -> p t d", p=128)[:, t, :],
                   sem=f"xs{sl}", after=xld_toks[sl])
        for dc2 in range(2):
            bank = dc2
            after0 = [att_done, wtok, b.psum_free[bank]]
            for hc in range(NDC):
                p.op("tensor", lambda tt, t=t, dc2=dc2, hc=hc, bank=bank,
                     st=(hc == 0), sp=(hc == NDC - 1): tt.matmul(
                         b.psum[bank][:, 0:512],
                         outfm[:, hc * TOK + t * 128: hc * TOK + (t + 1) * 128],
                         wo[:, hc * D + dc2 * 512: hc * D + (dc2 + 1) * 512],
                         start=st, stop=sp), after=after0 if hc == 0 else ())
            mm = p.engine_tok("tensor")
            ev = p.op("vector", lambda v, t=t, dc2=dc2, sl=sl, bank=bank:
                      v.tensor_add(out_sb[:], b.psum[bank][:, 0:512],
                                   xs_sb[:, sl * D + dc2 * 512:
                                         sl * D + (dc2 + 1) * 512]),
                      after=[mm, xt])
            b.psum_free[bank] = ev
            st = p.dma("sync", xout_r[:, t, dc2 * 512:(dc2 + 1) * 512],
                       out_sb[:], sem="st", after=ev)
            final.append(st)
            p.last_tok["vector"] = st
        xld_toks[sl] = p.engine_tok("vector")
    p.finalize(final[-1:])
    return b




def rope_tables(c):
    r, r4 = c // 4, c % 4
    tiles = np.arange(NT) * 4 + r4
    t = (tiles[:, None] * 128 + np.arange(128)[None, :]).reshape(-1)
    hd = 64
    inv = 1.0 / (10000.0 ** (np.arange(0, hd, 2, dtype=np.float64) / hd))
    fr = t[:, None].astype(np.float64) * inv[None, :]
    rope = np.concatenate([np.sin(fr), np.cos(fr)], -1)
    return (np.ascontiguousarray(rope[:, 0::2]).astype(np.float32),
            np.ascontiguousarray(rope[:, 1::2]).astype(np.float32))


def build_l1b_qkv():
    b = Builder("l1b_qkv")
    p = b.p
    xin = b.inp("xin", [TOK, D])
    wqkv = b.inp("wqkv", [D, 3 * D])
    bqkv = b.inp("bqkv", [3 * D])
    sin_t = b.inp("sin_t", [TOK, 32])
    cos_t = b.inp("cos_t", [TOK, 32])
    q_o = b.outp("q_o", [TOK, D], BF16)
    k_o = b.outp("k_o", [TOK, D], BF16)
    v_o = b.outp("v_o", [TOK, D], BF16)

    x_sb = b.sb("x", [128, NT * D])
    xh_sb = b.sb("xh", [128, 2 * D], BF16)
    xhT = b.sb("xhT", [128, NDC * TOK], BF16)
    scratch = b.sb("scr", [128, D])
    stats = b.sb("stats", [128, 5 * NT])
    wq_sb = b.sb("wq", [128, NDC * 3 * D], BF16)
    brow = b.sb("brow", [1, 3 * D], BF16)
    ones_bf = b.sb("ones_bf", [1, 128], BF16)
    sin_sb = b.sb("sin", [128, NT * 32])
    cos_sb = b.sb("cos", [128, NT * 32])
    qk_sb = b.sb("qks", [128, 2 * D])
    rot_sb = b.sb("rot", [128, 2 * D], BF16)
    v_sb = b.sb("vs", [128, 512], BF16)

    xld = p.dma("sync", x_sb[:].rearrange("p (t d) -> p t d", d=D),
                xin.rearrange("(t p) d -> p t d", p=128), sem="x")
    wld = p.dma("gpsimd", wq_sb[:].rearrange("p (c n) -> p c n", n=3 * D),
                wqkv.rearrange("(c p) n -> p c n", p=128), sem="w")
    bld = p.dma("gpsimd", brow[:], bqkv.rearrange("(o n) -> o n", o=1), sem="w")
    sld = p.dma("sync", sin_sb[:].rearrange("p (t n) -> p t n", n=32),
                sin_t.rearrange("(t p) n -> p t n", p=128), sem="m")
    cld = p.dma("sync", cos_sb[:].rearrange("p (t n) -> p t n", n=32),
                cos_t.rearrange("(t p) n -> p t n", p=128), sem="m")
    ot = p.op("vector", lambda v: v.memset(ones_bf[:], 1.0))
    ln_to_fm(b, x_sb, xh_sb, xhT, scratch, stats, x_tok=xld)
    fm_done = p.engine_tok("vector")
    wtok = bld

    q_r = q_o.rearrange("(t p) d -> p t d", p=128)
    k_r = k_o.rearrange("(t p) d -> p t d", p=128)
    v_r = v_o.rearrange("(t p) d -> p t d", p=128)
    final = []
    st_qk = None
    st_v = None
    for t in range(NT):
        for col in range(6):
            bank = col
            after0 = [fm_done, cld, ot, wtok, b.psum_free[bank]]
            for dc in range(NDC):
                p.op("tensor", lambda tt, t=t, col=col, dc=dc, bank=bank,
                     st=(dc == 0): tt.matmul(
                         b.psum[bank][:, 0:512],
                         xhT[:, dc * TOK + t * 128: dc * TOK + (t + 1) * 128],
                         wq_sb[:, dc * 3 * D + col * 512:
                               dc * 3 * D + (col + 1) * 512],
                         start=st, stop=False), after=after0 if dc == 0 else ())
            p.op("tensor", lambda tt, col=col, bank=bank: tt.matmul(
                b.psum[bank][:, 0:512], ones_bf[:],
                brow[:, col * 512:(col + 1) * 512], start=False, stop=True))
        mm = p.engine_tok("tensor")
        eq = []
        for col in range(4):
            eq.append(p.op("scalar", lambda s, col=col: s.copy(
                qk_sb[:, col * 512:(col + 1) * 512], b.psum[col][:, 0:512]),
                after=[mm, st_qk]))
        for col in range(4, 6):
            ev = p.op("vector", lambda v, col=col: v.tensor_copy(
                v_sb[:], b.psum[col][:, 0:512]),
                after=[mm, st_v])
            st_v = p.dma("sync", v_r[:, t, (col - 4) * 512:(col - 3) * 512],
                         v_sb[:], sem="st", after=ev)
            final.append(st_v)
            p.last_tok["vector"] = st_v
        for col in range(6):
            b.psum_free[col] = (eq[col] if col < 4 else st_v)
        qk_done = p.engine_tok("scalar")
        sc = sin_sb[:, t * 32:(t + 1) * 32]
        cc = cos_sb[:, t * 32:(t + 1) * 32]
        for half in range(2):
            base = half * D
            qkv_v = qk_sb[:, base:base + D].rearrange("p (h j) -> p h j", j=64)
            x1 = qkv_v[:, :, 0:64:2]
            x2 = qkv_v[:, :, 1:64:2]
            scb = sc.rearrange("p (o j) -> p o j", o=1).broadcast_to([128, 16, 32])
            ccb = cc.rearrange("p (o j) -> p o j", o=1).broadcast_to([128, 16, 32])
            od = rot_sb[:, base:base + D].rearrange("p (h j) -> p h j", j=64)
            s1 = scratch[:, 0:512].rearrange("p (h j) -> p h j", j=32)
            s2 = scratch[:, 512:1024].rearrange("p (h j) -> p h j", j=32)
            p.op("vector", lambda v, x1=x1, ccb=ccb, s1=s1: v.tensor_tensor(
                s1, x1, ccb, ALU.mult),
                after=[qk_done] + ([st_qk] if st_qk else []))
            p.op("vector", lambda v, x2=x2, scb=scb, s2=s2: v.tensor_tensor(
                s2, x2, scb, ALU.mult))
            p.op("vector", lambda v, od=od, s1=s1, s2=s2: v.tensor_tensor(
                od[:, :, 0:32], s1, s2, ALU.subtract))
            p.op("vector", lambda v, x1=x1, scb=scb, s1=s1: v.tensor_tensor(
                s1, x1, scb, ALU.mult))
            p.op("vector", lambda v, x2=x2, ccb=ccb, s2=s2: v.tensor_tensor(
                s2, x2, ccb, ALU.mult))
            p.op("vector", lambda v, od=od, s1=s1, s2=s2: v.tensor_tensor(
                od[:, :, 32:64], s1, s2, ALU.add))
        rot_done = p.engine_tok("vector")
        stq = p.dma("sync", q_r[:, t, :], rot_sb[:, 0:D], sem="st",
                    after=rot_done)
        st_qk = p.dma("sync", k_r[:, t, :], rot_sb[:, D:2 * D], sem="st")
        final.append(st_qk)
        p.last_tok["vector"] = st_qk
    p.finalize(final[-1:])
    return b


def build_l1b_attn():
    b = Builder("l1b_attn")
    p = b.p
    xin = b.inp("xin", [TOK, D])
    qt_i = b.inp("qt", [D, TOK], BF16)
    ktT_i = b.inp("ktT", [D, S], BF16)
    v_i = b.inp("v", [S, D], BF16)
    cmask = b.inp("cmask", [128, 512])
    out_w = b.inp("out_w", [D, D])
    out_b = b.inp("out_b", [D])
    xout = b.outp("xout", [TOK, D])

    QT = b.sb("QT", [128, NDC * TOK], BF16)
    KT = b.sb("KT", [128, S], BF16)
    Vg = b.sb("Vg", [128, S], BF16)
    P = b.sb("P", [128, S], BF16)
    PT = b.sb("PT", [128, 4 * 128], BF16)
    outfm = b.sb("outfm", [128, NDC * TOK], BF16)
    wo = b.sb("wo", [128, NDC * D], BF16)
    ob_row = b.sb("ob_row", [1, D], BF16)
    ones_bf = b.sb("ones_bf", [1, 128], BF16)
    lsums = b.sb("lsums", [128, 20])
    mask_sb = b.sb("mask", [128, 512])
    ind2 = b.const("ind2c", np.concatenate(
        [np.repeat(np.eye(2, dtype=bf16)[0:1], 64, 0),
         np.repeat(np.eye(2, dtype=bf16)[1:2], 64, 0)], 0))
    ind2_sb = b.sb("ind2_sbuf", [128, 2], BF16)
    norm_sb = b.sb("norm", [128, 4])
    qkbf = b.sb("qkbf", [128, 1], BF16)
    msc = b.sb("msc", [1, 2])
    negm = b.sb("negm", [128, 2])
    negm_dr = b.nc.dram_tensor("negm_dr", [1, 2], F32, kind="Internal").ap()
    xs_sb = b.sb("xs", [128, 2 * D])
    out_sb = b.sb("outs", [128, 512])
    ident_sb, id_tok = b.ident()

    qld = p.dma("sync", QT[:].rearrange("p (c n) -> p c n", n=TOK),
                qt_i.rearrange("(c p) n -> p c n", p=128), sem="x")
    wld = p.dma("gpsimd", wo[:].rearrange("p (c n) -> p c n", n=D),
                out_w.rearrange("(c p) n -> p c n", p=128), sem="w")
    bld = p.dma("gpsimd", ob_row[:], out_b.rearrange("(o n) -> o n", o=1),
                sem="w")
    mld0 = p.dma("sync", mask_sb[:], cmask, sem="m")
    ild = p.dma("sync", ind2_sb[:], ind2, sem="m")
    mld = ild
    ot = p.op("vector", lambda v: v.memset(ones_bf[:], 1.0))
    wtok = bld
    inp_tok = qld

    final = []
    p_free_hg = None
    for hg in range(NDC):
        kv_free = p.engine_tok("tensor")
        ktld = p.dma("sync", KT[:],
                     ktT_i[hg * 128:(hg + 1) * 128, :]
                     .rearrange("(o p) n -> p o n", p=128)[:, 0, :],
                     sem="kv", after=kv_free)
        vld_parts = []
        for part in range(8):
            vld_parts.append(p.dma(
                "sync", Vg[:, part * 1024:(part + 1) * 1024]
                .rearrange("p (kt n) -> p kt n", n=128),
                v_i.rearrange("(kt p) d -> p kt d", p=128)
                [:, part * 8:(part + 1) * 8, hg * 128:(hg + 1) * 128],
                sem="kv"))
        kv_done = vld_parts[-1]
        nsq = []
        for which, src, nch in ((0, QT[:, hg * TOK:(hg + 1) * TOK], TOK // 512),
                                (1, KT[:], S // 512)):
            sq = p.op("scalar", lambda s, src=src: s.activation(
                P[:, 0:src.shape[-1]], src, AF.Square),
                after=[kv_done, inp_tok, (p_free_hg if hg > 0 else None),
                       p.engine_tok("tensor")])
            mx = None
            for chv in range(nch):
                mmq = p.op("tensor", lambda t, chv=chv, src=src: t.matmul(
                    b.psum[4][0:2, 0:512], ind2_sb[:],
                    P[:, chv * 512:(chv + 1) * 512], start=True, stop=True),
                    after=[sq, ild, b.psum_free[4]])
                rm = p.op("vector", lambda v, which=which, chv=chv: (
                    v.reduce_max(norm_sb[0:2, which:which + 1],
                                 b.psum[4][0:2, 0:512], axis=AX.X)
                    if chv == 0 else
                    v.tensor_reduce(norm_sb[0:2, 2:3], b.psum[4][0:2, 0:512],
                                    axis=AX.X, op=ALU.max)), after=mmq)
                b.psum_free[4] = rm
                if chv > 0:
                    rm = p.op("vector", lambda v, which=which: v.tensor_max(
                        norm_sb[0:2, which:which + 1],
                        norm_sb[0:2, which:which + 1], norm_sb[0:2, 2:3]))
                mx = rm
            nsq.append(mx)
        t3 = p.op("vector", lambda v: v.tensor_mul(
            norm_sb[0:2, 3:4], norm_sb[0:2, 0:1], norm_sb[0:2, 1:2]),
            after=nsq)
        t4 = p.op("scalar", lambda s: s.activation(norm_sb[0:2, 3:4],
                                                   norm_sb[0:2, 3:4], AF.Sqrt),
                  after=t3)
        t6 = p.op("scalar", lambda s: s.mul(norm_sb[0:2, 3:4],
                                            norm_sb[0:2, 3:4], -0.125))
        nd1 = p.dma("sync", negm_dr.rearrange("o n -> (o n)")
                    .rearrange("(n o) -> n o", o=1), norm_sb[0:2, 3:4],
                    sem="negm", after=t6)
        nd2 = p.dma("sync", negm[:], negm_dr.broadcast_to([128, 2]), sem="negm",
                    after=nd1)
        p_free_sq = p.engine_tok("tensor")
        p_free = p_free_sq
        for h in range(2):
            for i in range(NT):
                nkb = i + 1
                for kb in range(nkb):
                    bank = kb % 2
                    after0 = [kv_done, nd2, mld0, mld, b.psum_free[bank],
                              (p_free if kb == 0 else None)]
                    p.op("tensor", lambda t, i=i, kb=kb, bank=bank, h=h, hg=hg:
                         t.matmul(b.psum[bank][:, 0:512],
                                  QT[h * 64:(h + 1) * 64,
                                     hg * TOK + i * 128: hg * TOK + (i + 1) * 128],
                                  KT[h * 64:(h + 1) * 64, kb * 512:(kb + 1) * 512],
                                  start=True, stop=True), after=after0)
                    mm = p.engine_tok("tensor")
                    if kb == nkb - 1:
                        mk = p.op("vector", lambda v, bank=bank: v.tensor_add(
                            b.psum[bank][:, 0:512], b.psum[bank][:, 0:512],
                            mask_sb[:]), after=mm, chain=False)
                        mm = mk
                    ex = p.op("scalar", lambda s, kb=kb, h=h, bank=bank:
                              s.activation(P[:, kb * 512:(kb + 1) * 512],
                                           b.psum[bank][:, 0:512], AF.Exp,
                                           bias=negm[:, h:h + 1], scale=0.125,
                                           accum_out=lsums[:, kb:kb + 1]),
                              after=[mm] + ([p_free] if kb == 0 and
                                            p_free is not None else []))
                    b.psum_free[bank] = ex
                lt = p.op("vector", lambda v, nkb=nkb: v.reduce_sum(
                    lsums[:, 16:17], lsums[:, 0:nkb], axis=AX.X),
                    after=p.engine_tok("scalar"))
                lr = p.op("vector", lambda v: v.reciprocal(lsums[:, 17:18],
                                                           lsums[:, 16:17]))
                pssc = []
                for kb in range(nkb):
                    pssc.append(p.op("vector", lambda v, kb=kb: v.tensor_scalar(
                        P[:, kb * 512:(kb + 1) * 512],
                        P[:, kb * 512:(kb + 1) * 512],
                        lsums[:, 17:18], None, ALU.mult),
                        after=lr, chain=False))
                psc_done = pssc[-1]
                ob = 6 + h
                pv_after = [b.psum_free[ob]]
                pt_free = [None, None]
                for kt in range(4 * nkb):
                    tb = 4 + kt % 2
                    trafter = [psc_done, b.psum_free[tb], id_tok]
                    tr = p.op("tensor", lambda t, kt=kt, tb=tb: t.transpose(
                        b.psum[tb].bitcast(BF16)[0:128, 0:128],
                        P[:, kt * 128:(kt + 1) * 128], ident_sb[:]),
                        after=trafter)
                    tev = p.op("vector", lambda v, kt=kt, tb=tb: v.tensor_copy(
                        PT[:, (kt % 2) * 128:(kt % 2 + 1) * 128],
                        b.psum[tb].bitcast(BF16)[0:128, 0:128]),
                        after=[tr, pt_free[kt % 2]], chain=False)
                    b.psum_free[tb] = tev
                    p.op("tensor", lambda t, kt=kt, h=h, ob=ob, st=(kt == 0),
                         sp=(kt == 4 * nkb - 1): t.matmul(
                             b.psum[ob][0:64, 0:128],
                             Vg[:, kt * 128 + h * 64: kt * 128 + h * 64 + 64],
                             PT[:, (kt % 2) * 128:(kt % 2 + 1) * 128],
                             start=st, stop=sp),
                         after=[tev] + (pv_after if kt == 0 else []))
                    pt_free[kt % 2] = p.engine_tok("tensor")
                mmo = p.engine_tok("tensor")
                oev = p.op("vector", lambda v, hg=hg, h=h, i=i: v.tensor_copy(
                    outfm[h * 64:(h + 1) * 64,
                          hg * TOK + i * 128: hg * TOK + (i + 1) * 128],
                    b.psum[6 + h][0:64, 0:128]), after=mmo)
                b.psum_free[ob] = oev
                p_free = p.engine_tok("tensor")
        p_free_hg = p.engine_tok("tensor")
    att_done = p.engine_tok("vector")
    xout_r = xout.rearrange("(t p) d -> p t d", p=128)
    xld_toks = [None, None]
    for t in range(NT):
        sl = t % 2
        xt = p.dma("sync", xs_sb[:, sl * D:(sl + 1) * D],
                   xin.rearrange("(t p) d -> p t d", p=128)[:, t, :],
                   sem=f"xs{sl}", after=xld_toks[sl])
        for dc2 in range(2):
            bank = dc2
            after0 = [att_done, wtok, b.psum_free[bank]]
            for hc in range(NDC):
                p.op("tensor", lambda tt, t=t, dc2=dc2, hc=hc, bank=bank,
                     st=(hc == 0), sp=(hc == NDC - 1): tt.matmul(
                         b.psum[bank][:, 0:512],
                         outfm[:, hc * TOK + t * 128: hc * TOK + (t + 1) * 128],
                         wo[:, hc * D + dc2 * 512: hc * D + (dc2 + 1) * 512],
                         start=st, stop=sp), after=after0 if hc == 0 else ())
            mm = p.engine_tok("tensor")
            ev = p.op("vector", lambda v, t=t, dc2=dc2, sl=sl, bank=bank:
                      v.tensor_add(out_sb[:], b.psum[bank][:, 0:512],
                                   xs_sb[:, sl * D + dc2 * 512:
                                         sl * D + (dc2 + 1) * 512]),
                      after=[mm, xt])
            b.psum_free[bank] = ev
            st = p.dma("sync", xout_r[:, t, dc2 * 512:(dc2 + 1) * 512],
                       out_sb[:], sem="st", after=ev)
            final.append(st)
            p.last_tok["vector"] = st
        xld_toks[sl] = p.engine_tok("vector")
    p.finalize(final[-1:])
    return b




def rope_tables(c):
    r, r4 = c // 4, c % 4
    tiles = np.arange(NT) * 4 + r4
    t = (tiles[:, None] * 128 + np.arange(128)[None, :]).reshape(-1)
    hd = 64
    inv = 1.0 / (10000.0 ** (np.arange(0, hd, 2, dtype=np.float64) / hd))
    fr = t[:, None].astype(np.float64) * inv[None, :]
    rope = np.concatenate([np.sin(fr), np.cos(fr)], -1)
    return (np.ascontiguousarray(rope[:, 0::2]).astype(np.float32),
            np.ascontiguousarray(rope[:, 1::2]).astype(np.float32))


def build_l1b_qkv():
    b = Builder("l1b_qkv")
    p = b.p
    xin = b.inp("xin", [TOK, D])
    wqkv = b.inp("wqkv", [D, 3 * D])
    bqkv = b.inp("bqkv", [3 * D])
    sin_t = b.inp("sin_t", [TOK, 32])
    cos_t = b.inp("cos_t", [TOK, 32])
    q_o = b.outp("q_o", [TOK, D], BF16)
    k_o = b.outp("k_o", [TOK, D], BF16)
    v_o = b.outp("v_o", [TOK, D], BF16)

    x_sb = b.sb("x", [128, NT * D])
    xh_sb = b.sb("xh", [128, 2 * D], BF16)
    xhT = b.sb("xhT", [128, NDC * TOK], BF16)
    scratch = b.sb("scr", [128, D])
    stats = b.sb("stats", [128, 5 * NT])
    wq_sb = b.sb("wq", [128, NDC * 3 * D], BF16)
    brow = b.sb("brow", [1, 3 * D], BF16)
    ones_bf = b.sb("ones_bf", [1, 128], BF16)
    sin_sb = b.sb("sin", [128, NT * 32])
    cos_sb = b.sb("cos", [128, NT * 32])
    qk_sb = b.sb("qks", [128, 2 * D])
    rot_sb = b.sb("rot", [128, 2 * D], BF16)
    v_sb = b.sb("vs", [128, 512], BF16)

    xld = p.dma("sync", x_sb[:].rearrange("p (t d) -> p t d", d=D),
                xin.rearrange("(t p) d -> p t d", p=128), sem="x")
    wld = p.dma("gpsimd", wq_sb[:].rearrange("p (c n) -> p c n", n=3 * D),
                wqkv.rearrange("(c p) n -> p c n", p=128), sem="w")
    bld = p.dma("gpsimd", brow[:], bqkv.rearrange("(o n) -> o n", o=1), sem="w")
    sld = p.dma("sync", sin_sb[:].rearrange("p (t n) -> p t n", n=32),
                sin_t.rearrange("(t p) n -> p t n", p=128), sem="m")
    cld = p.dma("sync", cos_sb[:].rearrange("p (t n) -> p t n", n=32),
                cos_t.rearrange("(t p) n -> p t n", p=128), sem="m")
    ot = p.op("vector", lambda v: v.memset(ones_bf[:], 1.0))
    ln_to_fm(b, x_sb, xh_sb, xhT, scratch, stats, x_tok=xld)
    fm_done = p.engine_tok("vector")
    wtok = bld

    q_r = q_o.rearrange("(t p) d -> p t d", p=128)
    k_r = k_o.rearrange("(t p) d -> p t d", p=128)
    v_r = v_o.rearrange("(t p) d -> p t d", p=128)
    final = []
    st_qk = None
    st_v = None
    for t in range(NT):
        for col in range(6):
            bank = col
            after0 = [fm_done, cld, ot, wtok, b.psum_free[bank]]
            for dc in range(NDC):
                p.op("tensor", lambda tt, t=t, col=col, dc=dc, bank=bank,
                     st=(dc == 0): tt.matmul(
                         b.psum[bank][:, 0:512],
                         xhT[:, dc * TOK + t * 128: dc * TOK + (t + 1) * 128],
                         wq_sb[:, dc * 3 * D + col * 512:
                               dc * 3 * D + (col + 1) * 512],
                         start=st, stop=False), after=after0 if dc == 0 else ())
            p.op("tensor", lambda tt, col=col, bank=bank: tt.matmul(
                b.psum[bank][:, 0:512], ones_bf[:],
                brow[:, col * 512:(col + 1) * 512], start=False, stop=True))
        mm = p.engine_tok("tensor")
        eq = []
        for col in range(4):
            eq.append(p.op("scalar", lambda s, col=col: s.copy(
                qk_sb[:, col * 512:(col + 1) * 512], b.psum[col][:, 0:512]),
                after=[mm, st_qk]))
        for col in range(4, 6):
            ev = p.op("vector", lambda v, col=col: v.tensor_copy(
                v_sb[:], b.psum[col][:, 0:512]),
                after=[mm, st_v])
            st_v = p.dma("sync", v_r[:, t, (col - 4) * 512:(col - 3) * 512],
                         v_sb[:], sem="st", after=ev)
            final.append(st_v)
            p.last_tok["vector"] = st_v
        for col in range(6):
            b.psum_free[col] = (eq[col] if col < 4 else st_v)
        qk_done = p.engine_tok("scalar")
        sc = sin_sb[:, t * 32:(t + 1) * 32]
        cc = cos_sb[:, t * 32:(t + 1) * 32]
        for half in range(2):
            base = half * D
            qkv_v = qk_sb[:, base:base + D].rearrange("p (h j) -> p h j", j=64)
            x1 = qkv_v[:, :, 0:64:2]
            x2 = qkv_v[:, :, 1:64:2]
            scb = sc.rearrange("p (o j) -> p o j", o=1).broadcast_to([128, 16, 32])
            ccb = cc.rearrange("p (o j) -> p o j", o=1).broadcast_to([128, 16, 32])
            od = rot_sb[:, base:base + D].rearrange("p (h j) -> p h j", j=64)
            s1 = scratch[:, 0:512].rearrange("p (h j) -> p h j", j=32)
            s2 = scratch[:, 512:1024].rearrange("p (h j) -> p h j", j=32)
            p.op("vector", lambda v, x1=x1, ccb=ccb, s1=s1: v.tensor_tensor(
                s1, x1, ccb, ALU.mult),
                after=[qk_done] + ([st_qk] if st_qk else []))
            p.op("vector", lambda v, x2=x2, scb=scb, s2=s2: v.tensor_tensor(
                s2, x2, scb, ALU.mult))
            p.op("vector", lambda v, od=od, s1=s1, s2=s2: v.tensor_tensor(
                od[:, :, 0:32], s1, s2, ALU.subtract))
            p.op("vector", lambda v, x1=x1, scb=scb, s1=s1: v.tensor_tensor(
                s1, x1, scb, ALU.mult))
            p.op("vector", lambda v, x2=x2, ccb=ccb, s2=s2: v.tensor_tensor(
                s2, x2, ccb, ALU.mult))
            p.op("vector", lambda v, od=od, s1=s1, s2=s2: v.tensor_tensor(
                od[:, :, 32:64], s1, s2, ALU.add))
        rot_done = p.engine_tok("vector")
        stq = p.dma("sync", q_r[:, t, :], rot_sb[:, 0:D], sem="st",
                    after=rot_done)
        st_qk = p.dma("sync", k_r[:, t, :], rot_sb[:, D:2 * D], sem="st")
        final.append(st_qk)
        p.last_tok["vector"] = st_qk
    p.finalize(final[-1:])
    return b


def build_l1b_attn():
    b = Builder("l1b_attn")
    p = b.p
    xin = b.inp("xin", [TOK, D])
    qt_i = b.inp("qt", [D, TOK], BF16)
    ktT_i = b.inp("ktT", [D, S], BF16)
    v_i = b.inp("v", [S, D], BF16)
    cmask = b.inp("cmask", [128, 512])
    out_w = b.inp("out_w", [D, D])
    out_b = b.inp("out_b", [D])
    xout = b.outp("xout", [TOK, D])

    QT = b.sb("QT", [128, NDC * TOK], BF16)
    KT = b.sb("KT", [128, S], BF16)
    Vg = b.sb("Vg", [128, S], BF16)
    P = b.sb("P", [128, S], BF16)
    PT = b.sb("PT", [128, 4 * 128], BF16)
    outfm = b.sb("outfm", [128, NDC * TOK], BF16)
    wo = b.sb("wo", [128, NDC * D], BF16)
    ob_row = b.sb("ob_row", [1, D], BF16)
    ones_bf = b.sb("ones_bf", [1, 128], BF16)
    lsums = b.sb("lsums", [128, 20])
    mask_sb = b.sb("mask", [128, 512])
    ind2 = b.const("ind2c", np.concatenate(
        [np.repeat(np.eye(2, dtype=bf16)[0:1], 64, 0),
         np.repeat(np.eye(2, dtype=bf16)[1:2], 64, 0)], 0))
    ind2_sb = b.sb("ind2_sbuf", [128, 2], BF16)
    norm_sb = b.sb("norm", [128, 4])
    qkbf = b.sb("qkbf", [128, 1], BF16)
    msc = b.sb("msc", [1, 2])
    negm = b.sb("negm", [128, 2])
    negm_dr = b.nc.dram_tensor("negm_dr", [1, 2], F32, kind="Internal").ap()
    xs_sb = b.sb("xs", [128, 2 * D])
    out_sb = b.sb("outs", [128, 512])
    ident_sb, id_tok = b.ident()

    qld = p.dma("sync", QT[:].rearrange("p (c n) -> p c n", n=TOK),
                qt_i.rearrange("(c p) n -> p c n", p=128), sem="x")
    wld = p.dma("gpsimd", wo[:].rearrange("p (c n) -> p c n", n=D),
                out_w.rearrange("(c p) n -> p c n", p=128), sem="w")
    bld = p.dma("gpsimd", ob_row[:], out_b.rearrange("(o n) -> o n", o=1),
                sem="w")
    mld0 = p.dma("sync", mask_sb[:], cmask, sem="m")
    ild = p.dma("sync", ind2_sb[:], ind2, sem="m")
    mld = ild
    ot = p.op("vector", lambda v: v.memset(ones_bf[:], 1.0))
    wtok = bld
    inp_tok = qld

    final = []
    p_free_hg = None
    for hg in range(NDC):
        kv_free = p.engine_tok("tensor")
        ktld = p.dma("sync", KT[:],
                     ktT_i[hg * 128:(hg + 1) * 128, :]
                     .rearrange("(o p) n -> p o n", p=128)[:, 0, :],
                     sem="kv", after=kv_free)
        vld_parts = []
        for part in range(8):
            vld_parts.append(p.dma(
                "sync", Vg[:, part * 1024:(part + 1) * 1024]
                .rearrange("p (kt n) -> p kt n", n=128),
                v_i.rearrange("(kt p) d -> p kt d", p=128)
                [:, part * 8:(part + 1) * 8, hg * 128:(hg + 1) * 128],
                sem="kv"))
        kv_done = vld_parts[-1]
        nsq = []
        for which, src, nch in ((0, QT[:, hg * TOK:(hg + 1) * TOK], TOK // 512),
                                (1, KT[:], S // 512)):
            sq = p.op("scalar", lambda s, src=src: s.activation(
                P[:, 0:src.shape[-1]], src, AF.Square),
                after=[kv_done, inp_tok, (p_free_hg if hg > 0 else None),
                       p.engine_tok("tensor")])
            mx = None
            for chv in range(nch):
                mmq = p.op("tensor", lambda t, chv=chv, src=src: t.matmul(
                    b.psum[4][0:2, 0:512], ind2_sb[:],
                    P[:, chv * 512:(chv + 1) * 512], start=True, stop=True),
                    after=[sq, ild, b.psum_free[4]])
                rm = p.op("vector", lambda v, which=which, chv=chv: (
                    v.reduce_max(norm_sb[0:2, which:which + 1],
                                 b.psum[4][0:2, 0:512], axis=AX.X)
                    if chv == 0 else
                    v.tensor_reduce(norm_sb[0:2, 2:3], b.psum[4][0:2, 0:512],
                                    axis=AX.X, op=ALU.max)), after=mmq)
                b.psum_free[4] = rm
                if chv > 0:
                    rm = p.op("vector", lambda v, which=which: v.tensor_max(
                        norm_sb[0:2, which:which + 1],
                        norm_sb[0:2, which:which + 1], norm_sb[0:2, 2:3]))
                mx = rm
            nsq.append(mx)
        t3 = p.op("vector", lambda v: v.tensor_mul(
            norm_sb[0:2, 3:4], norm_sb[0:2, 0:1], norm_sb[0:2, 1:2]),
            after=nsq)
        t4 = p.op("scalar", lambda s: s.activation(norm_sb[0:2, 3:4],
                                                   norm_sb[0:2, 3:4], AF.Sqrt),
                  after=t3)
        t6 = p.op("scalar", lambda s: s.mul(norm_sb[0:2, 3:4],
                                            norm_sb[0:2, 3:4], -0.125))
        nd1 = p.dma("sync", negm_dr.rearrange("o n -> (o n)")
                    .rearrange("(n o) -> n o", o=1), norm_sb[0:2, 3:4],
                    sem="negm", after=t6)
        nd2 = p.dma("sync", negm[:], negm_dr.broadcast_to([128, 2]), sem="negm",
                    after=nd1)
        p_free_sq = p.engine_tok("tensor")
        p_free = p_free_sq
        for h in range(2):
            for i in range(NT):
                nkb = i + 1
                for kb in range(nkb):
                    bank = kb % 2
                    after0 = [kv_done, nd2, mld0, mld, b.psum_free[bank],
                              (p_free if kb == 0 else None)]
                    p.op("tensor", lambda t, i=i, kb=kb, bank=bank, h=h, hg=hg:
                         t.matmul(b.psum[bank][:, 0:512],
                                  QT[h * 64:(h + 1) * 64,
                                     hg * TOK + i * 128: hg * TOK + (i + 1) * 128],
                                  KT[h * 64:(h + 1) * 64, kb * 512:(kb + 1) * 512],
                                  start=True, stop=True), after=after0)
                    mm = p.engine_tok("tensor")
                    if kb == nkb - 1:
                        mk = p.op("vector", lambda v, bank=bank: v.tensor_add(
                            b.psum[bank][:, 0:512], b.psum[bank][:, 0:512],
                            mask_sb[:]), after=mm, chain=False)
                        mm = mk
                    ex = p.op("scalar", lambda s, kb=kb, h=h, bank=bank:
                              s.activation(P[:, kb * 512:(kb + 1) * 512],
                                           b.psum[bank][:, 0:512], AF.Exp,
                                           bias=negm[:, h:h + 1], scale=0.125,
                                           accum_out=lsums[:, kb:kb + 1]),
                              after=[mm] + ([p_free] if kb == 0 and
                                            p_free is not None else []))
                    b.psum_free[bank] = ex
                lt = p.op("vector", lambda v, nkb=nkb: v.reduce_sum(
                    lsums[:, 16:17], lsums[:, 0:nkb], axis=AX.X),
                    after=p.engine_tok("scalar"))
                lr = p.op("vector", lambda v: v.reciprocal(lsums[:, 17:18],
                                                           lsums[:, 16:17]))
                pssc = []
                for kb in range(nkb):
                    pssc.append(p.op("vector", lambda v, kb=kb: v.tensor_scalar(
                        P[:, kb * 512:(kb + 1) * 512],
                        P[:, kb * 512:(kb + 1) * 512],
                        lsums[:, 17:18], None, ALU.mult),
                        after=lr, chain=False))
                psc_done = pssc[-1]
                ob = 6 + h
                pv_after = [b.psum_free[ob]]
                pt_free = [None, None]
                for kt in range(4 * nkb):
                    tb = 4 + kt % 2
                    trafter = [psc_done, b.psum_free[tb], id_tok]
                    tr = p.op("tensor", lambda t, kt=kt, tb=tb: t.transpose(
                        b.psum[tb].bitcast(BF16)[0:128, 0:128],
                        P[:, kt * 128:(kt + 1) * 128], ident_sb[:]),
                        after=trafter)
                    tev = p.op("vector", lambda v, kt=kt, tb=tb: v.tensor_copy(
                        PT[:, (kt % 2) * 128:(kt % 2 + 1) * 128],
                        b.psum[tb].bitcast(BF16)[0:128, 0:128]),
                        after=[tr, pt_free[kt % 2]], chain=False)
                    b.psum_free[tb] = tev
                    p.op("tensor", lambda t, kt=kt, h=h, ob=ob, st=(kt == 0),
                         sp=(kt == 4 * nkb - 1): t.matmul(
                             b.psum[ob][0:64, 0:128],
                             Vg[:, kt * 128 + h * 64: kt * 128 + h * 64 + 64],
                             PT[:, (kt % 2) * 128:(kt % 2 + 1) * 128],
                             start=st, stop=sp),
                         after=[tev] + (pv_after if kt == 0 else []))
                    pt_free[kt % 2] = p.engine_tok("tensor")
                mmo = p.engine_tok("tensor")
                oev = p.op("vector", lambda v, hg=hg, h=h, i=i: v.tensor_copy(
                    outfm[h * 64:(h + 1) * 64,
                          hg * TOK + i * 128: hg * TOK + (i + 1) * 128],
                    b.psum[6 + h][0:64, 0:128]), after=mmo)
                b.psum_free[ob] = oev
                p_free = p.engine_tok("tensor")
        p_free_hg = p.engine_tok("tensor")
    att_done = p.engine_tok("vector")
    xout_r = xout.rearrange("(t p) d -> p t d", p=128)
    xld_toks = [None, None]
    for t in range(NT):
        sl = t % 2
        xt = p.dma("sync", xs_sb[:, sl * D:(sl + 1) * D],
                   xin.rearrange("(t p) d -> p t d", p=128)[:, t, :],
                   sem="xs" + str(sl), after=xld_toks[sl])
        for dc2 in range(2):
            bank = dc2
            after0 = [att_done, wtok, ot, b.psum_free[bank]]
            for hc in range(NDC):
                p.op("tensor", lambda tt, t=t, dc2=dc2, hc=hc, bank=bank,
                     st=(hc == 0): tt.matmul(
                         b.psum[bank][:, 0:512],
                         outfm[:, hc * TOK + t * 128: hc * TOK + (t + 1) * 128],
                         wo[:, hc * D + dc2 * 512: hc * D + (dc2 + 1) * 512],
                         start=st, stop=False), after=after0 if hc == 0 else ())
            p.op("tensor", lambda tt, dc2=dc2, bank=bank: tt.matmul(
                b.psum[bank][:, 0:512], ones_bf[:],
                ob_row[:, dc2 * 512:(dc2 + 1) * 512], start=False, stop=True))
            mm = p.engine_tok("tensor")
            ev = p.op("vector", lambda v, t=t, dc2=dc2, sl=sl, bank=bank:
                      v.tensor_add(out_sb[:], b.psum[bank][:, 0:512],
                                   xs_sb[:, sl * D + dc2 * 512:
                                         sl * D + (dc2 + 1) * 512]),
                      after=[mm, xt])
            b.psum_free[bank] = ev
            st = p.dma("sync", xout_r[:, t, dc2 * 512:(dc2 + 1) * 512],
                       out_sb[:], sem="st", after=ev)
            final.append(st)
            p.last_tok["vector"] = st
        xld_toks[sl] = p.engine_tok("vector")
    p.finalize(final[-1:])
    return b




def build_l1a_qk():
    b = Builder("l1a_qk")
    p = b.p
    xin = b.inp("xin", [TOK, D])
    wqkv = b.inp("wqkv", [D, 3 * D])
    qt_o = b.outp("qt", [D, TOK], BF16)
    kt_o = b.outp("kt", [D, TOK], BF16)
    ktok_o = b.outp("ktok", [TOK, D], BF16)

    x_sb = b.sb("x", [128, NT * D])
    xh_sb = b.sb("xh", [128, 2 * D], BF16)
    xhT = b.sb("xhT", [128, NDC * TOK], BF16)
    scratch = b.sb("scr", [128, D])
    stats = b.sb("stats", [128, 5 * NT])
    wqk = b.sb("wqk", [128, NDC * 2 * D], BF16)
    ev_sb = b.sb("ev", [128, 2 * 512], BF16)

    xld = p.dma("sync", x_sb[:].rearrange("p (t d) -> p t d", d=D),
                xin.rearrange("(t p) d -> p t d", p=128), sem="x")
    wld = p.dma("gpsimd", wqk[:].rearrange("p (c n) -> p c n", n=2 * D),
                wqkv.rearrange("(c p) n -> p c n", p=128)[:, :, 0:2 * D],
                sem="w")
    ln_to_fm(b, x_sb, xh_sb, xhT, scratch, stats, x_tok=xld)
    fm_done = p.engine_tok("vector")

    qt_r = qt_o.rearrange("(c p) n -> p c n", p=128)
    kt_r = kt_o.rearrange("(c p) n -> p c n", p=128)
    ktok_r = ktok_o.rearrange("(t p) d -> p t d", p=128)
    final = []
    for which, outr in ((0, qt_r), (1, kt_r)):
        for hc in range(NDC):
            for s4 in range(4):
                bank = s4 % 2 + which * 2
                after0 = [fm_done, wld, b.psum_free[bank]]
                for dc in range(NDC):
                    p.op("tensor", lambda t, which=which, hc=hc, s4=s4, dc=dc,
                         bank=bank, st=(dc == 0), sp=(dc == NDC - 1): t.matmul(
                             b.psum[bank][:, 0:512],
                             wqk[:, dc * 2 * D + which * D + hc * 128:
                                 dc * 2 * D + which * D + (hc + 1) * 128],
                             xhT[:, dc * TOK + s4 * 512: dc * TOK + (s4 + 1) * 512],
                             start=st, stop=sp), after=after0 if dc == 0 else ())
                mm = p.engine_tok("tensor")
                sl = ev_sb[:, (s4 % 2) * 512:(s4 % 2 + 1) * 512]
                ev = p.op("vector", lambda v, sl=sl, bank=bank: v.tensor_copy(
                    sl, b.psum[bank][:, 0:512]), after=mm)
                b.psum_free[bank] = ev
                st = p.dma("sync", outr[:, hc, s4 * 512:(s4 + 1) * 512], sl,
                           sem="st", after=ev)
                final.append(st)
                p.last_tok["vector"] = st
    ev2 = b.sb("ev2", [128, 2 * 512], BF16)
    for t in range(NT):
        for dc2 in range(2):
            bank = 4 + (t * 2 + dc2) % 2
            after0 = [fm_done, wld, b.psum_free[bank]]
            for dc in range(NDC):
                p.op("tensor", lambda tt, t=t, dc2=dc2, dc=dc, bank=bank,
                     st=(dc == 0), sp=(dc == NDC - 1): tt.matmul(
                         b.psum[bank][:, 0:512],
                         xhT[:, dc * TOK + t * 128: dc * TOK + (t + 1) * 128],
                         wqk[:, dc * 2 * D + D + dc2 * 512:
                             dc * 2 * D + D + (dc2 + 1) * 512],
                         start=st, stop=sp), after=after0 if dc == 0 else ())
            mm = p.engine_tok("tensor")
            sl = ev2[:, dc2 * 512:(dc2 + 1) * 512]
            ev = p.op("vector", lambda v, sl=sl, bank=bank: v.tensor_copy(
                sl, b.psum[bank][:, 0:512]), after=mm)
            b.psum_free[bank] = ev
            st = p.dma("sync", ktok_r[:, t, dc2 * 512:(dc2 + 1) * 512], sl,
                       sem="st", after=ev)
            final.append(st)
            p.last_tok["vector"] = st
    p.finalize(final[-1:])
    return b


def build_l1a_attn():
    b = Builder("l1a_attn")
    p = b.p
    xin = b.inp("xin", [TOK, D])
    qt_i = b.inp("qt", [D, TOK], BF16)
    kt_i = b.inp("kt", [D, TOK], BF16)
    ktok_i = b.inp("ktok", [TOK, D], BF16)
    khT_i = b.inp("khT", [D, 256], BF16)
    khtok_i = b.inp("khtok", [256, D], BF16)
    w_out = b.inp("w_out", [D, D])
    b_out = b.inp("b_out", [D])
    xout = b.outp("xout", [TOK, D])

    QT = b.sb("QT", [128, NDC * TOK], BF16)
    KTf = b.sb("KTf", [128, NDC * TOK], BF16)
    Ktok = b.sb("Ktok", [128, NT * D], BF16)
    KhT = b.sb("KhT", [128, NDC * 256], BF16)
    Khtok = b.sb("Khtok", [128, 2 * D], BF16)
    wo = b.sb("wo", [128, NDC * D], BF16)
    ob_row = b.sb("ob_row", [1, D], BF16)
    ones_bf = b.sb("ones_bf", [1, 128], BF16)
    Pw = b.sb("Pw", [128, 256], BF16)
    PTw = b.sb("PTw", [128, 256], BF16)
    ao = b.sb("ao", [128, D], BF16)
    aoT = b.sb("aoT", [128, NDC * 128], BF16)
    lcol = b.sb("lcol", [128, 6])
    maskw = b.const("maskw", np.where(
        np.arange(128)[None, :] > np.arange(128)[:, None],
        np.float32(BIGNEG), np.float32(0.0)).astype(np.float32))
    mask_sb = b.sb("maskw_sb", [128, 128])
    xs_sb = b.sb("xs", [128, 2 * D])
    out_sb = b.sb("outs", [128, 512])
    ident_sb, id_tok = b.ident()

    qld = p.dma("sync", QT[:].rearrange("p (c n) -> p c n", n=TOK),
                qt_i.rearrange("(c p) n -> p c n", p=128), sem="x")
    kld = p.dma("sync", KTf[:].rearrange("p (c n) -> p c n", n=TOK),
                kt_i.rearrange("(c p) n -> p c n", p=128), sem="x")
    k2ld = p.dma("sync", Ktok[:].rearrange("p (t d) -> p t d", d=D),
                 ktok_i.rearrange("(t p) d -> p t d", p=128), sem="x")
    h1 = p.dma("sync", KhT[:].rearrange("p (c n) -> p c n", n=256),
               khT_i.rearrange("(c p) n -> p c n", p=128), sem="x")
    h2 = p.dma("sync", Khtok[:].rearrange("p (t d) -> p t d", d=D),
               khtok_i.rearrange("(t p) d -> p t d", p=128), sem="x")
    wld = p.dma("gpsimd", wo[:].rearrange("p (c n) -> p c n", n=D),
                w_out.rearrange("(c p) n -> p c n", p=128), sem="w")
    bld = p.dma("gpsimd", ob_row[:], b_out.rearrange("(o n) -> o n", o=1),
                sem="w")
    mld = p.dma("sync", mask_sb[:], maskw, sem="m")
    ot = p.op("vector", lambda v: v.memset(ones_bf[:], 1.0))
    inp_tok = h2
    wtok = bld

    xout_r = xout.rearrange("(t p) d -> p t d", p=128)
    final = []
    pw_free = None
    xld_toks = [None, None]
    for t in range(NT):
        for wb in range(2):
            bank = wb
            after0 = [inp_tok, qld, kld, mld, b.psum_free[bank]]
            for dc in range(NDC):
                if wb == 1:
                    rhs = KTf[:, dc * TOK + t * 128: dc * TOK + (t + 1) * 128]
                elif t % 8 == 0:
                    hb = t // 8
                    rhs = KhT[:, dc * 256 + hb * 128: dc * 256 + (hb + 1) * 128]
                else:
                    rhs = KTf[:, dc * TOK + (t - 1) * 128: dc * TOK + t * 128]
                lhsT = QT[:, dc * TOK + t * 128: dc * TOK + (t + 1) * 128]
                p.op("tensor", lambda tt, lhsT=lhsT, rhs=rhs, bank=bank,
                     st=(dc == 0), sp=(dc == NDC - 1): tt.matmul(
                         b.psum[bank][:, 0:128], lhsT, rhs, start=st, stop=sp),
                     after=after0 if dc == 0 else ())
        mm = p.engine_tok("tensor")
        mk = p.op("vector", lambda v: v.tensor_add(
            b.psum[1][:, 0:128], b.psum[1][:, 0:128], mask_sb[:]),
            after=mm, chain=False)
        r1 = p.op("vector", lambda v: v.reduce_max(
            lcol[:, 0:1], b.psum[0][:, 0:128], axis=AX.X), after=mm)
        r2 = p.op("vector", lambda v: v.reduce_max(
            lcol[:, 1:2], b.psum[1][:, 0:128], axis=AX.X), after=mk)
        r3 = p.op("vector", lambda v: v.tensor_max(lcol[:, 0:1], lcol[:, 0:1],
                                                   lcol[:, 1:2]))
        r4 = p.op("vector", lambda v: v.tensor_scalar(
            lcol[:, 1:2], lcol[:, 0:1], -0.03125, None, ALU.mult))
        e1 = p.op("scalar", lambda s: s.activation(
            Pw[:, 0:128], b.psum[0][:, 0:128], AF.Exp, bias=lcol[:, 1:2],
            scale=0.03125, accum_out=lcol[:, 2:3]), after=[r4, pw_free])
        e2 = p.op("scalar", lambda s: s.activation(
            Pw[:, 128:256], b.psum[1][:, 0:128], AF.Exp, bias=lcol[:, 1:2],
            scale=0.03125, accum_out=lcol[:, 3:4]), after=r4)
        b.psum_free[0] = e1
        b.psum_free[1] = e2
        l1 = p.op("vector", lambda v: v.tensor_add(lcol[:, 4:5], lcol[:, 2:3],
                                                   lcol[:, 3:4]), after=e2)
        l2 = p.op("vector", lambda v: v.reciprocal(lcol[:, 5:6], lcol[:, 4:5]))
        ps = p.op("vector", lambda v: v.tensor_scalar(
            Pw[:], Pw[:], lcol[:, 5:6], None, ALU.mult))
        evs = emit_transpose(b, [(Pw[:, 0:128], PTw[:, 0:128], ps),
                                 (Pw[:, 128:256], PTw[:, 128:256], ps)],
                             banks=(4, 5))
        pw_free = p.engine_tok("tensor")
        for dc2 in range(2):
            bank = 2 + dc2
            after0 = [evs[1], b.psum_free[bank]]
            for ks in range(2):
                if ks == 0:
                    if t % 8 == 0:
                        vsrc = Khtok[:, (t // 8) * D + dc2 * 512:
                                     (t // 8) * D + (dc2 + 1) * 512]
                    else:
                        vsrc = Ktok[:, (t - 1) * D + dc2 * 512:
                                    (t - 1) * D + (dc2 + 1) * 512]
                else:
                    vsrc = Ktok[:, t * D + dc2 * 512: t * D + (dc2 + 1) * 512]
                p.op("tensor", lambda tt, ks=ks, vsrc=vsrc, bank=bank,
                     st=(ks == 0), sp=(ks == 1): tt.matmul(
                         b.psum[bank][:, 0:512],
                         PTw[:, ks * 128:(ks + 1) * 128], vsrc,
                         start=st, stop=sp), after=after0 if ks == 0 else ())
            mm2 = p.engine_tok("tensor")
            aev = p.op("scalar", lambda sc, dc2=dc2, bank=bank: sc.copy(
                ao[:, dc2 * 512:(dc2 + 1) * 512], b.psum[bank][:, 0:512]),
                after=mm2)
            b.psum_free[bank] = aev
        ao_done = p.engine_tok("scalar")
        prs = []
        for dc in range(NDC):
            prs.append((ao[:, dc * 128:(dc + 1) * 128],
                        aoT[:, dc * 128:(dc + 1) * 128], ao_done))
        evs2 = emit_transpose(b, prs, banks=(4, 5))
        aoT_done = evs2[-1]
        sl = t % 2
        xt = p.dma("sync", xs_sb[:, sl * D:(sl + 1) * D],
                   xin.rearrange("(tt p) d -> p tt d", p=128)[:, t, :],
                   sem="xs" + str(sl), after=xld_toks[sl])
        for dc2 in range(2):
            bank = 6 + dc2
            after0 = [aoT_done, wtok, ot, b.psum_free[bank]]
            for hc in range(NDC):
                p.op("tensor", lambda tt, hc=hc, dc2=dc2, bank=bank,
                     st=(hc == 0): tt.matmul(
                         b.psum[bank][:, 0:512],
                         aoT[:, hc * 128:(hc + 1) * 128],
                         wo[:, hc * D + dc2 * 512: hc * D + (dc2 + 1) * 512],
                         start=st, stop=False), after=after0 if hc == 0 else ())
            p.op("tensor", lambda tt, dc2=dc2, bank=bank: tt.matmul(
                b.psum[bank][:, 0:512], ones_bf[:],
                ob_row[:, dc2 * 512:(dc2 + 1) * 512], start=False, stop=True))
            mm3 = p.engine_tok("tensor")
            ev = p.op("vector", lambda v, dc2=dc2, sl=sl, bank=bank:
                      v.tensor_add(out_sb[:], b.psum[bank][:, 0:512],
                                   xs_sb[:, sl * D + dc2 * 512:
                                         sl * D + (dc2 + 1) * 512]),
                      after=[mm3, xt])
            b.psum_free[bank] = ev
            st = p.dma("sync", xout_r[:, t, dc2 * 512:(dc2 + 1) * 512],
                       out_sb[:], sem="st", after=ev)
            final.append(st)
            p.last_tok["vector"] = st
        xld_toks[sl] = p.engine_tok("vector")
        pw_free = p.engine_tok("tensor")
    p.finalize(final[-1:])
    return b



LCH = 64
NCH = TOK // LCH
NST = 16


def build_ssm_tables():
    b = Builder("ssm_tables")
    p = b.p
    A_i = b.inp("A_g", [128, NST, NST])
    B_i = b.inp("B_g", [128, NST])
    C_i = b.inp("C_g", [128, NST])
    D_i = b.inp("D_g", [128])
    k_o = b.outp("k_o", [128, LCH])
    ur_o = b.outp("ur_o", [128, LCH * NST])
    vt_o = b.outp("vt_o", [128, LCH * NST])

    AL = b.sb("AL", [128, 6 * 256])
    U = b.sb("U", [128, LCH * NST])
    W = b.sb("W", [128, LCH * NST])
    Ur = b.sb("Ur", [128, LCH * NST])
    Bm = b.sb("Bm", [128, NST])
    Cm = b.sb("Cm", [128, NST])
    Dm = b.sb("Dm", [128, 1])
    kt = b.sb("kt", [128, LCH])
    tmp = b.sb("tmp", [128, 1024])

    ld1 = p.dma("sync", AL[:, 0:256], A_i.rearrange("p n m -> p (n m)"), sem="x")
    ld2 = p.dma("sync", Bm[:], B_i, sem="x")
    ld3 = p.dma("sync", Cm[:], C_i, sem="x")
    ld4 = p.dma("sync", Dm[:], D_i.rearrange("(p o) -> p o", o=1), sem="x")
    lds = ld4

    def matgrid(buf, off):
        return buf[:, off:off + 256].rearrange("p (n m) -> p n m", m=16)

    for j in range(5):
        src = AL[:, j * 256:(j + 1) * 256]
        dst = AL[:, (j + 1) * 256:(j + 2) * 256]
        for m in range(NST):
            in0 = src.rearrange("p (n m) -> p n m", m=16)[:, :, m:m + 1] \
                .broadcast_to([128, 16, 16])
            in1 = src[:, m * 16:(m + 1) * 16] \
                .rearrange("p (o q) -> p o q", o=1).broadcast_to([128, 16, 16])
            dv = dst.rearrange("p (n q) -> p n q", q=16)
            if m == 0:
                p.op("vector", lambda v, in0=in0, in1=in1, dv=dv:
                     v.tensor_tensor(dv, in0, in1, ALU.mult), after=lds)
            else:
                p.op("vector", lambda v, in0=in0, in1=in1:
                     v.tensor_tensor(tmp[:, 0:256]
                                     .rearrange("p (n q) -> p n q", q=16),
                                     in0, in1, ALU.mult))
                p.op("vector", lambda v, dv=dv: v.tensor_add(
                    dv, dv, tmp[:, 0:256].rearrange("p (n q) -> p n q", q=16)))

    p.op("vector", lambda v: v.tensor_copy(U[:, 0:16], Bm[:]), after=lds)
    for m in range(NST):
        in0 = AL[:, m * 16:(m + 1) * 16]
        cmc = Cm[:, m:m + 1].broadcast_to([128, 16])
        if m == 0:
            p.op("vector", lambda v, in0=in0, cmc=cmc: v.tensor_tensor(
                W[:, 0:16], in0, cmc, ALU.mult))
        else:
            p.op("vector", lambda v, in0=in0, cmc=cmc: v.tensor_tensor(
                tmp[:, 0:16], in0, cmc, ALU.mult))
            p.op("vector", lambda v: v.tensor_add(W[:, 0:16], W[:, 0:16],
                                                  tmp[:, 0:16]))

    for buf, trans in ((U, False), (W, True)):
        for j, L in enumerate((1, 2, 4, 8, 16, 32)):
            alo = j * 256
            for m in range(NST):
                if not trans:
                    in0 = AL[:, alo:alo + 256] \
                        .rearrange("p (n m) -> p n m", m=16)[:, :, m:m + 1] \
                        .rearrange("p n o -> p o n").broadcast_to([128, L, 16])
                    in1 = buf[:, 0:L * 16] \
                        .rearrange("p (t n) -> p t n", n=16)[:, :, m:m + 1] \
                        .broadcast_to([128, L, 16])
                else:
                    in0 = AL[:, alo + m * 16:alo + (m + 1) * 16] \
                        .rearrange("p (o n) -> p o n", o=1) \
                        .broadcast_to([128, L, 16])
                    in1 = buf[:, 0:L * 16] \
                        .rearrange("p (t n) -> p t n", n=16)[:, :, m:m + 1] \
                        .broadcast_to([128, L, 16])
                dv = buf[:, L * 16:2 * L * 16].rearrange("p (t n) -> p t n",
                                                         n=16)
                if m == 0:
                    p.op("vector", lambda v, in0=in0, in1=in1, dv=dv:
                         v.tensor_tensor(dv, in0, in1, ALU.mult))
                else:
                    p.op("vector", lambda v, in0=in0, in1=in1, L=L:
                         v.tensor_tensor(tmp[:, 0:L * 16]
                                         .rearrange("p (t n) -> p t n", n=16),
                                         in0, in1, ALU.mult))
                    p.op("vector", lambda v, dv=dv, L=L: v.tensor_add(
                        dv, dv, tmp[:, 0:L * 16]
                        .rearrange("p (t n) -> p t n", n=16)))

    cb = Cm[:].rearrange("p (o n) -> p o n", o=1).broadcast_to([128, LCH, 16])
    p.op("vector", lambda v: v.tensor_tensor(
        tmp[:, 0:1024].rearrange("p (t n) -> p t n", n=16),
        U[:].rearrange("p (t n) -> p t n", n=16), cb, ALU.mult))
    p.op("vector", lambda v: v.tensor_reduce(
        kt[:], tmp[:, 0:1024].rearrange("p (t n) -> p t n", n=16),
        axis=AX.X, op=ALU.add))
    p.op("vector", lambda v: v.tensor_add(kt[:, 0:1], kt[:, 0:1], Dm[:]))
    for sidx in range(LCH):
        p.op("vector", lambda v, sidx=sidx: v.tensor_copy(
            Ur[:, sidx * 16:(sidx + 1) * 16],
            U[:, (LCH - 1 - sidx) * 16:(LCH - sidx) * 16]), chain=(sidx == 0))
    done = p.engine_tok("vector")
    s1 = p.dma("sync", k_o, kt[:], sem="st", after=done)
    s2 = p.dma("sync", ur_o, Ur[:], sem="st")
    s3 = p.dma("sync", vt_o, W[:], sem="st")
    p.finalize([s3])
    return b


def build_l0b_a():
    b = Builder("l0b_a")
    p = b.p
    xin = b.inp("xin", [TOK, D])
    kt_i = b.inp("ktab", [D, LCH])
    ur_i = b.inp("urtab", [D, LCH * NST])
    yc_o = b.outp("yconv", [D, TOK])
    inj_o = b.outp("injs", [D, NCH * NST])

    x_sb = b.sb("x", [128, NT * D])
    xh_sb = b.sb("xh", [128, 2 * D], BF16)
    xhT = b.sb("xhT", [128, NDC * TOK], BF16)
    scratch = b.sb("scr", [128, D])
    stats = b.sb("stats", [128, 5 * NT])
    ktab = b.sb("ktab_sb", [128, NDC * LCH])
    urt = b.sb("urt", [128, NDC * LCH * NST])
    inj = b.sb("inj", [128, NDC * NCH * NST])
    tmp = b.sb("tmpb", [128, NCH * LCH])

    xld = p.dma("sync", x_sb[:].rearrange("p (t d) -> p t d", d=D),
                xin.rearrange("(t p) d -> p t d", p=128), sem="x")
    kld = p.dma("sync", ktab[:].rearrange("p (g n) -> p g n", n=LCH),
                kt_i.rearrange("(g p) n -> p g n", p=128), sem="w")
    uld = p.dma("sync", urt[:].rearrange("p (g n) -> p g n", n=LCH * NST),
                ur_i.rearrange("(g p) n -> p g n", p=128), sem="w")
    ln_to_fm(b, x_sb, xh_sb, xhT, scratch, stats, x_tok=xld)
    fm_done = p.engine_tok("vector")

    y = x_sb
    final = []
    for g in range(NDC):
        for tau in range(LCH):
            kcol = ktab[:, g * LCH + tau: g * LCH + tau + 1]
            xv = xhT[:, g * TOK: g * TOK + TOK] \
                .rearrange("p (c s) -> p c s", s=LCH)[:, :, 0:LCH - tau]
            yv = y[:, g * TOK: g * TOK + TOK] \
                .rearrange("p (c s) -> p c s", s=LCH)[:, :, tau:LCH]
            if tau == 0:
                p.op("vector", lambda v, xv=xv, yv=yv, kcol=kcol:
                     v.tensor_scalar(yv, xv, kcol, None, ALU.mult),
                     after=[fm_done, uld])
            else:
                p.op("vector", lambda v, xv=xv, yv=yv, kcol=kcol:
                     v.scalar_tensor_tensor(yv, xv, kcol, yv, ALU.mult,
                                            ALU.add))
    for g in range(NDC):
        for n in range(NST):
            xv = xhT[:, g * TOK: g * TOK + TOK] \
                .rearrange("p (c s) -> p c s", s=LCH)
            uv = urt[:, g * LCH * NST + n: g * LCH * NST + n + 1] \
                .rearrange("p (o q) -> p o q", o=1)
            uvb = urt[:, g * LCH * NST:(g + 1) * LCH * NST] \
                .rearrange("p (s n) -> p s n", n=NST)[:, :, n:n + 1] \
                .rearrange("p s o -> p o s").broadcast_to([128, NCH, LCH])
            tv = tmp[:].rearrange("p (c s) -> p c s", s=LCH)
            p.op("vector", lambda v, xv=xv, uvb=uvb, tv=tv:
                 v.tensor_tensor(tv, xv, uvb, ALU.mult))
            iv = inj[:, g * NCH * NST:(g + 1) * NCH * NST] \
                .rearrange("p (c q) -> p c q", q=NST)[:, :, n:n + 1]
            p.op("vector", lambda v, iv=iv, tv=tv: v.tensor_reduce(
                iv, tv, axis=AX.X, op=ALU.add))
    done = p.engine_tok("vector")
    s1 = p.dma("sync", yc_o.rearrange("(g p) n -> p g n", p=128),
               y[:].rearrange("p (g n) -> p g n", n=TOK), sem="st", after=done)
    s2 = p.dma("sync", inj_o.rearrange("(g p) n -> p g n", p=128),
               inj[:].rearrange("p (g n) -> p g n", n=NCH * NST), sem="st")
    p.finalize([s2])
    return b


def build_l0b_b():
    b = Builder("l0b_b")
    p = b.p
    xin = b.inp("xin", [TOK, D])
    yc_i = b.inp("yconv", [D, TOK])
    ip_i = b.inp("injprev", [D, NCH * NST])
    vt_i = b.inp("vtab", [D, LCH * NST])
    lnw_i = b.inp("ln_w", [D])
    lnb_i = b.inp("ln_b", [D])
    wo_i = b.inp("W_out", [D, D])
    bo_i = b.inp("b_out", [D])
    xout = b.outp("xout", [TOK, D])

    x_sb = b.sb("x", [128, NT * D])
    y_sb = b.sb("y", [128, NDC * TOK])
    injp = b.sb("injp", [128, NDC * NCH * NST])
    vg = b.sb("vg", [128, 2 * LCH * NST])
    tmp = b.sb("tmpb", [128, NCH * LCH])
    stats = b.sb("stats", [128, 96])
    scratch = b.sb("scr", [128, D])
    woT = b.sb("woT", [128, NDC * D], BF16)
    borow = b.sb("borow", [1, D], BF16)
    lnwb = b.sb("lnwb", [128, 2 * D])
    ytok = b.sb("ytok", [128, 2 * D])
    gel = b.sb("gel", [128, 2 * D], BF16)
    gfm = b.sb("gfm", [128, NDC * 128], BF16)
    ones_bf = b.sb("ones_bf", [1, 128], BF16)
    idf = b.const("identf", np.eye(128, dtype=np.float32))
    idf_sb = b.sb("identf_sb", [128, 128])
    out_sb = b.sb("outs", [128, 512])
    wo_bf = b.nc.dram_tensor("wo_bf", [D, D], BF16, kind="Internal").ap()

    xld = p.dma("sync", x_sb[:].rearrange("p (t d) -> p t d", d=D),
                xin.rearrange("(t p) d -> p t d", p=128), sem="x")
    yld = p.dma("sync", y_sb[:].rearrange("p (g n) -> p g n", n=TOK),
                yc_i.rearrange("(g p) n -> p g n", p=128), sem="x")
    ild = p.dma("sync", injp[:].rearrange("p (g n) -> p g n", n=NCH * NST),
                ip_i.rearrange("(g p) n -> p g n", p=128), sem="x")
    lwld = p.dma("sync", lnwb[:, 0:D], lnw_i.rearrange("(o n) -> o n", o=1)
                 .broadcast_to([128, D]), sem="w")
    lbld = p.dma("sync", lnwb[:, D:2 * D], lnb_i.rearrange("(o n) -> o n", o=1)
                 .broadcast_to([128, D]), sem="w")
    bold = p.dma("gpsimd", borow[:], bo_i.rearrange("(o n) -> o n", o=1),
                 sem="w2")
    wcast = p.dma("gpsimd", wo_bf, wo_i, sem="w2")
    wTs = []
    for dc in range(NDC):
        wTs.append(p.dma("sync", woT[:, dc * D:(dc + 1) * D],
                         wo_bf[:, dc * 128:(dc + 1) * 128], transpose=True,
                         sem="wT", after=wcast))
    wT_tok = wTs[-1]
    idld = p.dma("sync", idf_sb[:], idf, sem="w")
    ident_sb, id_tok = b.ident()
    ot = p.op("vector", lambda v: v.memset(ones_bf[:], 1.0))

    n = NT
    sums = stats[:, 0:n]; sumsq = stats[:, n:2 * n]; mu = stats[:, 2 * n:3 * n]
    rstd = stats[:, 3 * n:4 * n]; tmps = stats[:, 4 * n:5 * n]
    last_sq = None
    for t in range(NT):
        xt = x_sb[:, t * D:(t + 1) * D]
        last_sq = p.op("scalar", lambda s, xt=xt, t=t: s.activation(
            scratch[:], xt, AF.Square, accum_out=sumsq[:, t:t + 1]), after=xld)
        p.op("vector", lambda v, xt=xt, t=t: v.reduce_sum(
            sums[:, t:t + 1], xt, axis=AX.X), after=xld, chain=False)
    ln_tok = emit_ln_stats(b, sums, sumsq, mu, rstd, tmps, D,
                           after=(last_sq, p.engine_tok("vector")))

    vld = [None, None]
    for g in range(NDC):
        sl = g % 2
        vld[sl] = p.dma("sync", vg[:, sl * LCH * NST:(sl + 1) * LCH * NST],
                        vt_i.rearrange("(gg p) n -> p gg n", p=128)[:, g, :],
                        sem="vt" + str(sl), after=vld[sl])
        for nn in range(NST):
            vv = vg[:, sl * LCH * NST:(sl + 1) * LCH * NST] \
                .rearrange("p (t q) -> p t q", q=NST)[:, :, nn:nn + 1] \
                .rearrange("p t o -> p o t").broadcast_to([128, NCH, LCH])
            iv = injp[:, g * NCH * NST:(g + 1) * NCH * NST] \
                .rearrange("p (c q) -> p c q", q=NST)[:, :, nn:nn + 1] \
                .broadcast_to([128, NCH, LCH])
            tv = tmp[:].rearrange("p (c s) -> p c s", s=LCH)
            p.op("vector", lambda v, vv=vv, iv=iv, tv=tv: v.tensor_tensor(
                tv, vv, iv, ALU.mult), after=[vld[sl], ild, yld])
            yv = y_sb[:, g * TOK:(g + 1) * TOK] \
                .rearrange("p (c s) -> p c s", s=LCH)
            p.op("vector", lambda v, yv=yv, tv=tv: v.tensor_add(yv, yv, tv))
        vld[sl] = Tok("vector", p.cnt["vector"])
    corr_done = p.engine_tok("vector")

    xout_r = xout.rearrange("(t p) d -> p t d", p=128)
    final = []
    slot_free = [None, None]
    gfm_free = None
    ist = stats[:, 80:86]
    for t in range(NT):
        sl = t % 2
        yt = ytok[:, sl * D:(sl + 1) * D]
        for dc in range(NDC):
            src = y_sb[:, dc * TOK + t * 128: dc * TOK + (t + 1) * 128]
            dst = yt[:, dc * 128:(dc + 1) * 128]
            bk = (4, 5)[dc % 2]
            ps = b.psum[bk]
            tr = p.op("tensor", lambda tt, src=src, ps=ps: tt.transpose(
                ps[0:128, 0:128], src, idf_sb[:]),
                after=[corr_done, idld, b.psum_free[bk], slot_free[sl]])
            ev = p.op("vector", lambda v, dst=dst, ps=ps: v.tensor_copy(
                dst, ps[0:128, 0:128]), after=tr, chain=False)
            b.psum_free[bk] = ev
        ytr = p.engine_tok("vector")
        i0 = p.op("scalar", lambda s, yt=yt: s.activation(
            scratch[:], yt, AF.Square, accum_out=ist[:, 1:2]),
            after=[ytr, ln_tok])
        i1 = p.op("vector", lambda v, yt=yt: v.reduce_sum(
            ist[:, 0:1], yt, axis=AX.X), after=ytr)
        p.op("vector", lambda v: v.tensor_scalar(ist[:, 2:3], ist[:, 0:1],
                                                 1.0 / D, None, ALU.mult))
        p.op("vector", lambda v: v.tensor_mul(ist[:, 4:5], ist[:, 2:3],
                                              ist[:, 2:3]))
        p.op("vector", lambda v: v.tensor_scalar(ist[:, 5:6], ist[:, 1:2],
                                                 1.0 / D, 1e-5, ALU.mult,
                                                 ALU.add), after=i0)
        p.op("vector", lambda v: v.tensor_sub(ist[:, 4:5], ist[:, 5:6],
                                              ist[:, 4:5]))
        iq = p.op("scalar", lambda s: s.activation(ist[:, 5:6], ist[:, 4:5],
                                                   AF.Sqrt),
                  after=p.engine_tok("vector"))
        p.op("vector", lambda v: v.reciprocal(ist[:, 3:4], ist[:, 5:6]),
             after=iq)
        gl = gel[:, sl * D:(sl + 1) * D]
        p.op("vector", lambda v, yt=yt: v.tensor_scalar(
            scratch[:], yt, ist[:, 2:3], ist[:, 3:4], ALU.subtract, ALU.mult))
        p.op("vector", lambda v: v.tensor_mul(scratch[:], scratch[:],
                                              lnwb[:, 0:D]),
             after=[lwld])
        gv = p.op("vector", lambda v: v.tensor_add(scratch[:], scratch[:],
                                                   lnwb[:, D:2 * D]))
        ge = p.op("scalar", lambda s, gl=gl: s.activation(gl, scratch[:],
                                                          AF.Gelu),
                  after=[gv, gfm_free])
        prs2 = []
        for dc in range(NDC):
            prs2.append((gl[:, dc * 128:(dc + 1) * 128],
                         gfm[:, dc * 128:(dc + 1) * 128], ge))
        evs2 = emit_transpose(b, prs2, banks=(6, 7))
        g_done = evs2[-1]
        for dc2 in range(2):
            bank = dc2
            after0 = [g_done, wT_tok, bold, ot, b.psum_free[bank]]
            for hc in range(NDC):
                p.op("tensor", lambda tt, hc=hc, dc2=dc2, bank=bank,
                     st=(hc == 0): tt.matmul(
                         b.psum[bank][:, 0:512],
                         gfm[:, hc * 128:(hc + 1) * 128],
                         woT[:, hc * D + dc2 * 512: hc * D + (dc2 + 1) * 512],
                         start=st, stop=False), after=after0 if hc == 0 else ())
            p.op("tensor", lambda tt, dc2=dc2, bank=bank: tt.matmul(
                b.psum[bank][:, 0:512], ones_bf[:],
                borow[:, dc2 * 512:(dc2 + 1) * 512], start=False, stop=True))
            mm = p.engine_tok("tensor")
            xs = x_sb[:, t * D + dc2 * 512: t * D + (dc2 + 1) * 512]
            xh1 = p.op("vector", lambda v, xs=xs, t=t: v.tensor_scalar(
                scratch[:, dc2 * 512:(dc2 + 1) * 512] if False else
                scratch[:, 0:512], xs, mu[:, t:t + 1], rstd[:, t:t + 1],
                ALU.subtract, ALU.mult), after=[ln_tok, mm])
            ev1 = p.op("vector", lambda v, xs=xs, bank=bank: v.tensor_add(
                out_sb[:], b.psum[bank][:, 0:512], xs))
            ev2 = p.op("vector", lambda v: v.tensor_add(
                out_sb[:], out_sb[:], scratch[:, 0:512]))
            b.psum_free[bank] = ev2
            st = p.dma("sync", xout_r[:, t, dc2 * 512:(dc2 + 1) * 512],
                       out_sb[:], sem="st", after=ev2)
            final.append(st)
            p.last_tok["vector"] = st
        slot_free[sl] = p.engine_tok("tensor")
        gfm_free = p.engine_tok("tensor")
    p.finalize(final[-1:])
    return b


def _run(b, in_maps):
    for m in in_maps:
        m.update(b.consts)
    res = run_bass_kernel_spmd(b.nc, in_maps, core_ids=list(range(N_CORES)))
    return res.results



from concourse.bass import IndirectOffsetOnAxis

I32 = mybir.dt.int32


def fused_gather_idx_ffn0(c):
    r, r4 = c // 4, c % 4
    j = r4
    idx = np.empty(2048, np.int32)
    kpos = 0
    for e in (2 * j, 2 * j + 1):
        for t in range(e * 1024, (e + 1) * 1024):
            g = t // 128
            i = g % 4
            idx[kpos] = i * 2048 + (g // 4) * 128 + t % 128
            kpos += 1
    return idx.reshape(16, 128)


def fused_gather_idx_delta(c):
    r, r4 = c // 4, c % 4
    idx = np.empty(2048, np.int32)
    for tl in range(16):
        g = 4 * tl + r4
        for p in range(128):
            t = g * 128 + p
            e = t // 1024
            jsrc = e // 2
            pos = (e % 2) * 1024 + (t % 1024)
            idx[tl * 128 + p] = jsrc * 2048 + pos
    return idx.reshape(16, 128)


def fused_gather_idx_prevtile(c, oob=100000):
    r, r4 = c // 4, c % 4
    idx = np.empty((16, 128), np.int32)
    for tl in range(16):
        g = 4 * tl + r4 - 1
        if g < 0:
            idx[tl] = oob
        else:
            i = g % 4
            idx[tl] = i * 2048 + (g // 4) * 128 + np.arange(128)
    return idx


def fused_mask4(c):
    r4 = c % 4
    kp = np.arange(128)[:, None]
    qp = np.arange(128)[None, :]
    out = np.zeros((128, 4 * 128), np.float32)
    for cc in range(4):
        if cc > r4:
            out[:, cc * 128:(cc + 1) * 128] = BIGNEG
        elif cc == r4:
            out[:, cc * 128:(cc + 1) * 128] = np.where(kp > qp, BIGNEG, 0.0)
    return out.astype(bf16)


def emit_mha_st(b, KT, Vge, QT2, negm, outfm, mask4_sb, scale, kv_tok,
                q_tok, negm_tok, mask_tok, f32ev, pt_slot, hg, sbanks=(0, 1),
                obank=2, t2bank=3):
    p = b.p
    ident_sb, id_tok = b.ident()
    idf_sb, idf_tok = b.identf()
    nslots = len(sbanks)
    if not hasattr(b, "pt_free3") or len(b.pt_free3) != nslots:
        b.pt_free3 = [None] * nslots
    for h in range(2):
        for gI in range(4):
            ob = b.psum[obank]
            ob_free = b.psum_free[obank]
            nkt = 16 * (gI + 1)
            pv_q = []
            for kt in range(nkt):
                jmin = max(0, (kt - 16 * gI - 4) // 4 + 1) if kt >= 16 * gI + 4 else 0
                live = 4 - jmin
                sb_ = sbanks[kt % nslots]
                ps = b.psum[sb_]
                after0 = [kv_tok, q_tok, negm_tok, b.psum_free[sb_]]
                p.op("tensor", lambda t, h=h, gI=gI, kt=kt, jmin=jmin,
                     live=live, ps=ps: t.matmul(
                         ps[:, 0:live * 128],
                         KT[h * 64:(h + 1) * 64, kt * 128:(kt + 1) * 128],
                         QT2[h * 64:(h + 1) * 64,
                             (4 * gI + jmin) * 128:(4 * gI + 4) * 128],
                         start=True, stop=True), after=after0)
                mm = p.engine_tok("tensor")
                if kt >= 16 * gI:
                    jstar = (kt - 16 * gI) // 4
                    cc = kt - 16 * gI - 4 * jstar
                    mk = p.op("vector", lambda v, ps=ps, jstar=jstar, cc=cc,
                              jmin=jmin: v.tensor_add(
                                  ps[:, (jstar - jmin) * 128:
                                     (jstar - jmin + 1) * 128],
                                  ps[:, (jstar - jmin) * 128:
                                     (jstar - jmin + 1) * 128],
                                  mask4_sb[:, cc * 128:(cc + 1) * 128]),
                              after=[mm, mask_tok], chain=False)
                    mm = mk
                pslot = pt_slot[:, (kt % nslots) * 512:(kt % nslots) * 512 + live * 128]
                ex = p.op("scalar", lambda s, ps=ps, pslot=pslot, h=h:
                          s.activation(pslot, ps[:, 0:pslot.shape[-1]], AF.Exp,
                                       bias=negm[:, h:h + 1], scale=scale),
                          after=[mm, b.pt_free3[kt % nslots]])
                b.psum_free[sb_] = ex
                pv_q.append((kt, ex, jmin, pslot))
                if len(pv_q) >= nslots:
                    kq, exq, jm, pl = pv_q.pop(0)
                    p.op("tensor", lambda t, kq=kq, h=h, jm=jm, pl=pl, ob=ob,
                         st=(kq == 0), sp=(kq == nkt - 1):
                         t.matmul(ob[0:65, jm * 128:512],
                                  Vge[:, kq * 130 + h * 65:
                                      kq * 130 + h * 65 + 65],
                                  pl, start=st, stop=sp),
                         after=[exq] + ([ob_free] if kq == 0 else []))
                    b.pt_free3[kq % nslots] = p.engine_tok("tensor")
            while pv_q:
                kq, exq, jm, pl = pv_q.pop(0)
                p.op("tensor", lambda t, kq=kq, h=h, jm=jm, pl=pl, ob=ob,
                     st=(kq == 0), sp=(kq == nkt - 1):
                     t.matmul(ob[0:65, jm * 128:512],
                              Vge[:, kq * 130 + h * 65: kq * 130 + h * 65 + 65],
                              pl, start=st, stop=sp),
                     after=[exq] + ([ob_free] if kq == 0 else []))
                b.pt_free3[kq % nslots] = p.engine_tok("tensor")
            mmo = p.engine_tok("tensor")
            evf = p.op("vector", lambda v, ob=ob: v.tensor_copy(
                f32ev[0:65, 0:512], ob[0:65, 0:512]), after=mmo)
            b.psum_free[obank] = evf
            for j in range(4):
                tb = b.psum[t2bank]
                tr1 = p.op("tensor", lambda t, j=j, tb=tb: t.transpose(
                    tb[0:128, 0:65], f32ev[0:65, j * 128:(j + 1) * 128],
                    idf_sb[0:65, 0:65]),
                    after=[evf, idf_tok, b.psum_free[t2bank]])
                rc = p.op("vector", lambda v, tb=tb: v.reciprocal(
                    b.sc_col[:, 0:1], tb[0:128, 64:65]), after=tr1)
                ts = p.op("vector", lambda v, tb=tb: v.tensor_scalar(
                    b.bf_col[:, 0:64], tb[0:128, 0:64], b.sc_col[:, 0:1],
                    None, ALU.mult))
                b.psum_free[t2bank] = ts
                tr2 = p.op("tensor", lambda t, j=j, tb=tb: t.transpose(
                    b.psum[t2bank].bitcast(BF16)[0:64, 256:384],
                    b.bf_col[:, 0:64], ident_sb[:]),
                    after=[ts, id_tok])
                ev2 = p.op("vector", lambda v, h=h, gI=gI, j=j, hg=hg:
                           v.tensor_copy(
                               outfm[h * 64:(h + 1) * 64,
                                     hg * 2048 + (4 * gI + j) * 128:
                                     hg * 2048 + (4 * gI + j + 1) * 128],
                               b.psum[t2bank].bitcast(BF16)[0:64, 256:384]),
                           after=tr2, chain=False)
                b.psum_free[t2bank] = ev2


def fused_barrier(b):
    p = b.p
    toks = [Tok(e, p.cnt[e]) for e in ENGINES]
    toks += [Tok(k, p.cnt[k]) for k in p.sems if k.startswith("dma_")]
    for e in ENGINES:
        for t in toks:
            if t.val > 0:
                p._wait(e, t)


class NS:
    pass


def build_fused(upto=99, dbg=()):
    b = Builder("fused")
    p = b.p
    c = NS()
    c.dbg = set(dbg)

    def dump(name, ap, dt=None):
        if name in c.dbg:
            fused_barrier(b)
            o = b.outp("dbg_" + name, [128, ap.shape[-1]],
                       dt if dt is not None else BF16)
            c.final.append(p.dma("sync", o, ap, sem="dbg"))
            fused_barrier(b)
    c.dump = dump
    c.xin = b.inp("xin", [TOK, D])
    c.cmask4 = b.inp("cmask4", [128, 512], BF16)
    c.sin_t = b.inp("sin_t", [TOK, 32])
    c.cos_t = b.inp("cos_t", [TOK, 32])
    c.A_g = b.inp("A_g", [128, NST, NST])
    c.B_g = b.inp("B_g", [128, NST])
    c.C_g = b.inp("C_g", [128, NST])
    c.D_g = b.inp("D_g", [128])
    c.idx_prev = b.inp("idx_prev", [16, 128], I32)
    c.W_dq = b.inp("W_dq", [D, DC_KV])
    c.W_uq = b.inp("W_uq", [DC_KV, D])
    c.W_dkv = b.inp("W_dkv", [D, DC_KV])
    c.W_uk = b.inp("W_uk", [DC_KV, D])
    c.W_uv = b.inp("W_uv", [DC_KV, D])
    c.W_o = b.inp("W_o", [D, D])
    c.f0w1 = b.inp("f0w1", [D, DFF])
    c.f0w3 = b.inp("f0w3", [D, DFF])
    c.f0w2 = b.inp("f0w2", [DFF, D])
    c.ln_w = b.inp("ln_w", [D])
    c.ln_b = b.inp("ln_b", [D])
    c.Wb_out = b.inp("Wb_out", [D, D])
    c.bb_out = b.inp("bb_out", [D])
    c.wqkv_a = b.inp("wqkv_a", [D, 3 * D])
    c.wout_a = b.inp("wout_a", [D, D])
    c.bout_a = b.inp("bout_a", [D])
    c.qkv_w = b.inp("qkv_w", [D, 3 * D])
    c.qkv_b = b.inp("qkv_b", [3 * D])
    c.cmaskb = b.inp("cmaskb", [128, 512])
    c.out_w = b.inp("out_w", [D, D])
    c.out_b = b.inp("out_b", [D])
    c.Wv = b.inp("Wv", [D, D])
    c.bv = b.inp("bv", [D])
    c.Wg = b.inp("Wg", [NE, D])
    c.WgT = b.inp("WgT", [D, NE])
    c.bg = b.inp("bg", [NE])
    c.Wp = b.inp("Wp", [NE, NE, D])
    c.alpha = b.inp("alpha", [1])
    c.g_r = b.inp("g_r", [NE])
    c.b_r = b.inp("b_r", [NE])
    c.g_c = b.inp("g_c", [NE])
    c.b_c = b.inp("b_c", [NE])
    c.W1 = b.inp("W1", [NE, NI, D])
    c.b1 = b.inp("b1", [NE, NI])
    c.W2 = b.inp("W2", [NE, D, NI])
    c.b2 = b.inp("b2", [NE, D])
    c.xout = b.outp("xout", [TOK, D])

    c.x = b.sb("x_res", [128, NT * D])
    c.xhT = b.sb("fxhT", [128, NDC * TOK], BF16)
    c.A = b.sb("arA", [128, 16384], BF16)
    c.Bb = b.sb("arB", [128, 16384], BF16)
    c.Cb = b.sb("arC", [128, 16512], BF16)
    c.S4 = b.sb("arS", [128, 3072])
    c.scratch = c.S4[:, 1024:2048]
    c.stats = b.sb("fstats", [128, 96])
    c.ones_bf = b.sb("fones", [1, 128], BF16)
    c.negm = b.sb("fnegm", [128, 2])
    c.norm = b.sb("fnorm", [128, 4])
    c.msc = b.sb("fmsc", [1, 2])
    c.gains = b.sb("fgains", [128, 32])
    c.sig = b.sb("fsig", [128, 1])
    c.galph = b.sb("fgalph", [1, 2])
    c.mask4 = b.sb("fmask4", [128, 512], BF16)
    c.maskw = b.sb("fmaskw", [128, 128], BF16)
    c.idxs = b.sb("fidxs", [128, 48], I32)
    c.qkbf = b.sb("fqkbf", [128, 1], BF16)

    dr = b.nc.dram_tensor
    c.tab_in = dr("tab_in", [128, 2112], F32, kind="Internal").ap()
    c.tab_out = dr("tab_out", [1024, 2112], F32, kind="Internal").ap()
    c.ckv_in = dr("ckv_in", [256, TOK], BF16, kind="Internal").ap()
    c.ckv_out = dr("ckv_out", [1024, TOK], BF16, kind="Internal").ap()
    c.tok_in = dr("tok_in", [TOK, D], BF16, kind="Internal").ap()
    c.tok_out = dr("tok_out", [TOK, D], BF16, kind="Internal").ap()
    c.tok2_in = dr("tok2_in", [TOK, D], BF16, kind="Internal").ap()
    c.tok2_out = dr("tok2_out", [TOK, D], BF16, kind="Internal").ap()
    c.fm_in = dr("fm_in", [D, TOK], BF16, kind="Internal").ap()
    c.fm_out = dr("fm_out", [4 * D, TOK], BF16, kind="Internal").ap()
    c.halo_out = dr("halo_out", [4 * TOK, D], BF16, kind="Internal").ap()
    c.inj_in = dr("inj_in", [TOK, 128], F32, kind="Internal").ap()
    c.inj_out = dr("inj_out", [4 * TOK, 128], F32, kind="Internal").ap()
    c.negm_dr = dr("fnegm_dr", [1, 2], F32, kind="Internal").ap()
    c.sig_dr = dr("fsig_dr", [1, 1], F32, kind="Internal").ap()
    c.w_bf = dr("fw_bf", [D, D], BF16, kind="Internal").ap()
    c.w_bf2 = dr("fw_bf2", [NE * NE, D], BF16, kind="Internal").ap()
    c.w_bf3 = dr("fw_bf3", [NE * NI, D], BF16, kind="Internal").ap()
    c.f0w1_bf = dr("f0w1_bf", [D, DFF], BF16, kind="Internal").ap()
    c.f0w3_bf = dr("f0w3_bf", [D, DFF], BF16, kind="Internal").ap()
    c.f0w2_bf = dr("f0w2_bf", [DFF, D], BF16, kind="Internal").ap()

    c.final = []
    c.sect = 0

    xld = p.dma("sync", c.x[:].rearrange("p (t d) -> p t d", d=D),
                c.xin.rearrange("(t p) d -> p t d", p=128), sem="x")
    p.dma("sync", c.mask4[:], c.cmask4, sem="cst")
    mwc = b.const("maskwc", np.where(
        np.arange(128)[None, :] > np.arange(128)[:, None],
        np.float32(BIGNEG), np.float32(0.0)).astype(bf16))
    p.dma("sync", c.maskw[:], mwc, sem="cst")
    cst_tok = p.dma("sync", c.idxs[:, 32:48],
                    c.idx_prev.rearrange("t p -> p t"), sem="cst",
                    allow_slow_non_contiguous=True)
    ot = p.op("vector", lambda v: v.memset(c.ones_bf[:], 1.0))
    p.dma("gpsimd", c.f0w1_bf, c.f0w1, sem="f0c")
    p.dma("gpsimd", c.f0w3_bf, c.f0w3, sem="f0c")
    c.f0_cast = p.dma("gpsimd", c.f0w2_bf, c.f0w2, sem="f0c")
    b.ident()
    b.identf()
    c.xld = xld
    c.cst_tok = cst_tok
    c.ot = ot
    emit_fused_tables(b, c)
    if upto >= 1:
        emit_fused_l0a(b, c)
    if upto >= 2:
        emit_fused_ffn0(b, c)
    if upto >= 3:
        emit_fused_ssm(b, c)
    if upto >= 4:
        emit_fused_ffn0(b, c)
    if upto >= 5:
        emit_fused_l1a(b, c)
    if upto >= 6:
        emit_fused_ffn1(b, c)
    if upto >= 7:
        emit_fused_l1b(b, c)
    if upto >= 8:
        emit_fused_ffn1(b, c)

    fused_barrier(b)
    xout_r = c.xout.rearrange("(t p) d -> p t d", p=128)
    for t in range(NT):
        st = p.dma("sync", xout_r[:, t, :], c.x[:, t * D:(t + 1) * D], sem="st")
        c.final.append(st)
    p.finalize(c.final[-1:])
    return b


def emit_fused_tables(b, c):
    p = b.p
    Af = c.A[:].bitcast(F32)
    AL = Af[:, 0:1536]
    U = Af[:, 1536:2560]
    W = Af[:, 2560:3584]
    Ur = Af[:, 3584:4608]
    tmp = Af[:, 4608:5632]
    Bm = Af[:, 5632:5648]
    Cm = Af[:, 5648:5664]
    Dm = Af[:, 5664:5665]
    kt = Af[:, 5680:5744]
    ld1 = p.dma("sync", AL[:, 0:256], c.A_g.rearrange("p n m -> p (n m)"),
                sem="tb")
    p.dma("sync", Bm, c.B_g, sem="tb")
    p.dma("sync", Cm, c.C_g, sem="tb")
    lds = p.dma("sync", Dm, c.D_g.rearrange("(p o) -> p o", o=1), sem="tb")
    for j in range(5):
        src = AL[:, j * 256:(j + 1) * 256]
        dst = AL[:, (j + 1) * 256:(j + 2) * 256]
        for m in range(NST):
            in0 = src.rearrange("p (n mm) -> p n mm", mm=16)[:, :, m:m + 1] \
                .broadcast_to([128, 16, 16])
            in1 = src[:, m * 16:(m + 1) * 16] \
                .rearrange("p (o q) -> p o q", o=1).broadcast_to([128, 16, 16])
            dv = dst.rearrange("p (n q) -> p n q", q=16)
            if m == 0:
                p.op("vector", lambda v, in0=in0, in1=in1, dv=dv:
                     v.tensor_tensor(dv, in0, in1, ALU.mult), after=lds)
            else:
                p.op("vector", lambda v, in0=in0, in1=in1:
                     v.tensor_tensor(tmp[:, 0:256]
                                     .rearrange("p (n q) -> p n q", q=16),
                                     in0, in1, ALU.mult))
                p.op("vector", lambda v, dv=dv: v.tensor_add(
                    dv, dv, tmp[:, 0:256].rearrange("p (n q) -> p n q", q=16)))
    p.op("vector", lambda v: v.tensor_copy(U[:, 0:16], Bm), after=lds)
    for m in range(NST):
        in0 = AL[:, m * 16:(m + 1) * 16]
        cmc = Cm[:, m:m + 1].broadcast_to([128, 16])
        if m == 0:
            p.op("vector", lambda v, in0=in0, cmc=cmc: v.tensor_tensor(
                W[:, 0:16], in0, cmc, ALU.mult))
        else:
            p.op("vector", lambda v, in0=in0, cmc=cmc: v.tensor_tensor(
                tmp[:, 0:16], in0, cmc, ALU.mult))
            p.op("vector", lambda v: v.tensor_add(W[:, 0:16], W[:, 0:16],
                                                  tmp[:, 0:16]))
    for buf, trans in ((U, False), (W, True)):
        for j, L in enumerate((1, 2, 4, 8, 16, 32)):
            alo = j * 256
            for m in range(NST):
                if not trans:
                    in0 = AL[:, alo:alo + 256] \
                        .rearrange("p (n mm) -> p n mm", mm=16)[:, :, m:m + 1] \
                        .rearrange("p n o -> p o n").broadcast_to([128, L, 16])
                else:
                    in0 = AL[:, alo + m * 16:alo + (m + 1) * 16] \
                        .rearrange("p (o n) -> p o n", o=1) \
                        .broadcast_to([128, L, 16])
                in1 = buf[:, 0:L * 16] \
                    .rearrange("p (t n) -> p t n", n=16)[:, :, m:m + 1] \
                    .broadcast_to([128, L, 16])
                dv = buf[:, L * 16:2 * L * 16].rearrange("p (t n) -> p t n",
                                                         n=16)
                if m == 0:
                    p.op("vector", lambda v, in0=in0, in1=in1, dv=dv:
                         v.tensor_tensor(dv, in0, in1, ALU.mult))
                else:
                    p.op("vector", lambda v, in0=in0, in1=in1, L=L:
                         v.tensor_tensor(tmp[:, 0:L * 16]
                                         .rearrange("p (t n) -> p t n", n=16),
                                         in0, in1, ALU.mult))
                    p.op("vector", lambda v, dv=dv, L=L: v.tensor_add(
                        dv, dv, tmp[:, 0:L * 16]
                        .rearrange("p (t n) -> p t n", n=16)))
    cb = Cm.rearrange("p (o n) -> p o n", o=1).broadcast_to([128, LCH, 16])
    p.op("vector", lambda v: v.tensor_tensor(
        tmp[:, 0:1024].rearrange("p (t n) -> p t n", n=16),
        U[:].rearrange("p (t n) -> p t n", n=16), cb, ALU.mult))
    p.op("vector", lambda v: v.tensor_reduce(
        kt[:], tmp[:, 0:1024].rearrange("p (t n) -> p t n", n=16),
        axis=AX.X, op=ALU.add))
    p.op("vector", lambda v: v.tensor_add(kt[:, 0:1], kt[:, 0:1], Dm))
    for sidx in range(LCH):
        p.op("vector", lambda v, sidx=sidx: v.tensor_copy(
            Ur[:, sidx * 16:(sidx + 1) * 16],
            U[:, (LCH - 1 - sidx) * 16:(LCH - sidx) * 16]), chain=(sidx == 0))
    done = p.engine_tok("vector")
    d1 = p.dma("sync", c.tab_in[:, 0:64], kt, sem="tb2", after=done)
    d2 = p.dma("sync", c.tab_in[:, 64:1088], Ur, sem="tb2")
    d3 = p.dma("sync", c.tab_in[:, 1088:2112], W, sem="tb2")
    agt = p.op("gpsimd", lambda g: g.collective_compute(
        "AllGather", ALU.bypass, replica_groups=[list(range(8))],
        ins=[c.tab_in.opt()], outs=[c.tab_out.opt()]), after=d3)
    c.tab_tok = agt


def emit_fused_ln(b, c, x_tok=None):
    xh2 = c.S4[:].bitcast(BF16)[:, 0:2048]
    return ln_to_fm(b, c.x, xh2, c.xhT, c.scratch, c.stats, x_tok=x_tok,
                    banks=(6, 7))


def emit_bound_negm(b, c, KT, QT2, kv_tok, q_tok, scale):
    p = b.p
    if not hasattr(c, "ind2_sb"):
        ind2 = b.const("find2", np.concatenate(
            [np.repeat(np.eye(2, dtype=bf16)[0:1], 64, 0),
             np.repeat(np.eye(2, dtype=bf16)[1:2], 64, 0)], 0))
        c.ind2_sb = b.sb("find2_sb", [128, 2], BF16)
        c.ind2_tok = p.dma("sync", c.ind2_sb[:], ind2, sem="cst")
    t1 = p.op("vector", lambda v: v.reduce_max(
        c.norm[:, 1:2], KT, axis=AX.X, apply_absolute_value=True),
        after=kv_tok)
    t2 = p.op("vector", lambda v: v.reduce_max(
        c.norm[:, 0:1], QT2, axis=AX.X, apply_absolute_value=True),
        after=q_tok)
    p.op("vector", lambda v: v.tensor_mul(c.norm[:, 2:3], c.norm[:, 0:1],
                                          c.norm[:, 1:2]))
    t4 = p.op("vector", lambda v: v.tensor_copy(c.qkbf[:], c.norm[:, 2:3]))
    mmn = p.op("tensor", lambda t: t.matmul(
        b.psum[5][0:1, 0:2], c.qkbf[:], c.ind2_sb[:], start=True, stop=True),
        after=[t4, c.ind2_tok, b.psum_free[5]])
    t5 = p.op("vector", lambda v: v.tensor_copy(c.msc[:], b.psum[5][0:1, 0:2]),
              after=mmn)
    b.psum_free[5] = t5
    t6 = p.op("scalar", lambda s2, scale=scale: s2.mul(c.msc[:], c.msc[:],
                                                       -scale), after=t5)
    nd1 = p.dma("sync", c.negm_dr, c.msc[:], sem="negm", after=t6)
    nd2 = p.dma("sync", c.negm[:], c.negm_dr.broadcast_to([128, 2]),
                sem="negm", after=nd1)
    return nd2


def emit_fused_l0a(b, c):
    p = b.p
    fused_barrier(b)
    fm_tok = emit_fused_ln(b, c, x_tok=c.xld)
    S4b = c.S4[:].bitcast(BF16)
    cqT = S4b[:, 2048:6144]
    QT2 = c.xhT[:, 6144:8192]
    PTs = c.S4[:].bitcast(BF16)[:, 0:1536]
    f32ev = c.xhT[:].bitcast(F32)[:, 4608:5120]
    wdq = c.Cb[:, 0:2048]
    wdkv = c.Cb[:, 2048:4096]
    ckev = c.Cb[:, 4096:5120]
    w1t = p.dma("gpsimd", wdq.rearrange("p (cc n) -> p cc n", n=DC_KV),
                c.W_dq.rearrange("(cc p) n -> p cc n", p=128), sem="w0")
    wtok = p.dma("gpsimd", wdkv.rearrange("p (cc n) -> p cc n", n=DC_KV),
                 c.W_dkv.rearrange("(cc p) n -> p cc n", p=128), sem="w0")
    ckv_st = []
    wdqv = wdq.rearrange("p (dcc n) -> p dcc n", n=128)
    wdkvv = wdkv.rearrange("p (dcc n) -> p dcc n", n=128)
    for cc in range(2):
        for s4 in range(4):
            bank = s4 % 2
            after0 = [wtok, fm_tok, b.psum_free[bank]]
            for dc in range(NDC):
                p.op("tensor", lambda t, cc=cc, s4=s4, dc=dc, bank=bank,
                     st=(dc == 0), sp=(dc == NDC - 1): t.matmul(
                         b.psum[bank][:, 0:512],
                         wdqv[:, dc * 2 + cc, :],
                         c.xhT[:, dc * TOK + s4 * 512: dc * TOK + (s4 + 1) * 512],
                         start=st, stop=sp), after=after0 if dc == 0 else ())
            mm = p.engine_tok("tensor")
            ev = p.op("vector", lambda v, cc=cc, s4=s4, bank=bank: v.tensor_copy(
                cqT[:, cc * TOK + s4 * 512: cc * TOK + (s4 + 1) * 512],
                b.psum[bank][:, 0:512]), after=mm)
            b.psum_free[bank] = ev
            bank2 = 2 + s4 % 2
            after1 = [wtok, fm_tok, b.psum_free[bank2]]
            for dc in range(NDC):
                p.op("tensor", lambda t, cc=cc, s4=s4, dc=dc, bank2=bank2,
                     st=(dc == 0), sp=(dc == NDC - 1): t.matmul(
                         b.psum[bank2][:, 0:512],
                         wdkvv[:, dc * 2 + cc, :],
                         c.xhT[:, dc * TOK + s4 * 512: dc * TOK + (s4 + 1) * 512],
                         start=st, stop=sp), after=after1 if dc == 0 else ())
            mm2 = p.engine_tok("tensor")
            slot = ckev[:, (s4 % 2) * 512:(s4 % 2 + 1) * 512]
            ev2 = p.op("vector", lambda v, slot=slot, bank2=bank2: v.tensor_copy(
                slot, b.psum[bank2][:, 0:512]), after=mm2)
            b.psum_free[bank2] = ev2
            st = p.dma("sync", c.ckv_in[cc * 128:(cc + 1) * 128,
                                        s4 * 512:(s4 + 1) * 512],
                       slot, sem="ckv", after=ev2)
            p.last_tok["vector"] = st
            ckv_st.append(st)
    c.dump("xhT", c.xhT[:])
    c.dump("cqT", cqT)
    ag = p.op("gpsimd", lambda g: g.collective_compute(
        "AllGather", ALU.bypass, replica_groups=[[0, 1, 2, 3], [4, 5, 6, 7]],
        ins=[c.ckv_in.opt()], outs=[c.ckv_out.opt()]), after=ckv_st[-1])
    stage = c.Cb[:, 0:16384]
    ckvT = c.Bb[:]
    sld = p.dma("sync", stage.rearrange("p (x n) -> p x n", n=TOK),
                c.ckv_out.rearrange("(x p) n -> p x n", p=128), sem="ckvl",
                after=[ag, Tok("tensor", p.cnt["tensor"])])
    cp_toks = []
    for i in range(4):
        for cc in range(2):
            src = stage[:, (i * 2 + cc) * TOK:(i * 2 + cc + 1) * TOK] \
                .rearrange("p (tl q) -> p tl q", q=128)
            dst = ckvT[:, cc * S:(cc + 1) * S] \
                .rearrange("p (tt q) -> p tt q", q=128)[:, i::4, :]
            cp_toks.append(p.op("vector", lambda v, src=src, dst=dst:
                                v.tensor_copy(dst, src),
                                after=[sld], chain=False))
    ckv_tok = cp_toks[-1]
    p.last_tok["vector"] = ckv_tok
    c.dump("ckvT", ckvT)
    wuk = c.xhT[:, 0:2048].rearrange("p (cc n) -> p cc n", n=D)
    wuv = c.xhT[:, 2048:4096].rearrange("p (cc n) -> p cc n", n=D)
    wuq = c.xhT[:, 4096:6144].rearrange("p (cc n) -> p cc n", n=D)
    mm_all = Tok("tensor", p.cnt["tensor"])
    p.dma("gpsimd", wuk, c.W_uk.rearrange("(cc p) n -> p cc n", p=128),
          sem="w0b", after=[mm_all, fm_tok])
    p.dma("gpsimd", wuv, c.W_uv.rearrange("(cc p) n -> p cc n", p=128),
          sem="w0b")
    wkv_tok = p.dma("gpsimd", wuq,
                    c.W_uq.rearrange("(cc p) n -> p cc n", p=128), sem="w0b")
    KT = c.Cb[:, 0:8192]
    Vge = c.Cb[:, 8192:16512]
    outfm = c.A[:]
    qt2_tok = None
    for hg in range(NDC):
        kv_free = p.engine_tok("tensor")
        q_evs = []
        for s4 in range(4):
            bank = 4
            after0 = [wkv_tok, b.psum_free[bank], kv_free, qt2_tok]
            for cc in range(2):
                p.op("tensor", lambda t, s4=s4, cc=cc, hg=hg, bank=bank,
                     st=(cc == 0), sp=(cc == 1): t.matmul(
                         b.psum[bank][:, 0:512],
                         wuq[:, cc, hg * 128:(hg + 1) * 128],
                         cqT[:, cc * TOK + s4 * 512: cc * TOK + (s4 + 1) * 512],
                         start=st, stop=sp), after=after0 if cc == 0 else ())
            mm = p.engine_tok("tensor")
            ev = p.op("vector", lambda v, s4=s4, bank=bank: v.tensor_copy(
                QT2[:, s4 * 512:(s4 + 1) * 512], b.psum[bank][:, 0:512]),
                after=mm)
            b.psum_free[bank] = ev
            q_evs.append(ev)
        q_tok = q_evs[-1]
        for sc in range(16):
            bank = 4 + sc % 2
            after0 = [wkv_tok, ckv_tok, b.psum_free[bank], kv_free]
            for cc in range(2):
                p.op("tensor", lambda t, sc=sc, cc=cc, hg=hg, bank=bank,
                     st=(cc == 0), sp=(cc == 1): t.matmul(
                         b.psum[bank][:, 0:512],
                         wuk[:, cc, hg * 128:(hg + 1) * 128],
                         ckvT[:, cc * S + sc * 512: cc * S + (sc + 1) * 512],
                         start=st, stop=sp), after=after0 if cc == 0 else ())
            mm = p.engine_tok("tensor")
            ev = p.op("vector", lambda v, sc=sc, bank=bank: v.tensor_copy(
                KT[:, sc * 512:(sc + 1) * 512], b.psum[bank][:, 0:512]),
                after=mm)
            b.psum_free[bank] = ev
    
        one1 = p.op("vector", lambda v, hg=hg: v.memset(
            Vge.rearrange("p (kt q) -> p kt q", q=130)[:, :, 64:65], 1.0),
            after=[Tok("tensor", p.cnt["tensor"])], chain=True)
        one2 = p.op("vector", lambda v, hg=hg: v.memset(
            Vge.rearrange("p (kt q) -> p kt q", q=130)[:, :, 129:130], 1.0))
        for kb4 in range(16):
            bank = 4 + kb4 % 2
            after0 = [wkv_tok, ckv_tok, b.psum_free[bank]]
            for sub in range(4):
                kt = kb4 * 4 + sub
                for cc in range(2):
                    p.op("tensor", lambda t, kt=kt, sub=sub, cc=cc, hg=hg,
                         bank=bank, st=(sub == 0 and cc == 0),
                         sp=(sub == 3 and cc == 1): t.matmul(
                             b.psum[bank][:, sub * 128:(sub + 1) * 128],
                             ckvT[:, cc * S + kt * 128: cc * S + (kt + 1) * 128],
                             wuv[:, cc, hg * 128:(hg + 1) * 128],
                             start=st, stop=sp),
                         after=after0 if (sub == 0 and cc == 0) else ())
            mm = p.engine_tok("tensor")
            dstv = Vge.rearrange("p (kt q) -> p kt q", q=130)[
                :, kb4 * 4:(kb4 + 1) * 4, :] \
                .rearrange("p a (h n) -> p a h n", n=65)[:, :, :, 0:64]
            ev = p.op("vector", lambda v, kb4=kb4, bank=bank, dstv=dstv:
                      v.tensor_copy(
                          dstv,
                          b.psum[bank][:, 0:512]
                          .rearrange("p (a h n) -> p a h n", h=2, n=64)),
                      after=[mm, one2], chain=False)
            b.psum_free[bank] = ev
        kvdone = p.engine_tok("vector")
        if hg == 0:
            c.dump("KT0", KT)
            c.dump("Vge0", Vge)
            c.dump("QT20", QT2)
        negm_tok = emit_bound_negm(b, c, KT, QT2, kvdone, q_tok, 0.125)
        if hg == 0:
            c.dump("negm0", c.negm[:], F32)
        emit_mha_st(b, c.Cb[:, 0:8192], Vge, QT2, c.negm, outfm, c.mask4,
                    0.125, kvdone, q_tok, negm_tok, c.cst_tok, f32ev,
                    PTs, hg, sbanks=(0, 1, 4), obank=2, t2bank=3)
        qt2_tok = p.engine_tok("tensor")
    c.dump("outfm", outfm)
    att_done = p.engine_tok("vector")
    wo_v = c.xhT[:, 6144:14336].rearrange("p (hc n) -> p hc n", n=D)
    wkv_tok = p.dma("gpsimd", wo_v,
                    c.W_o.rearrange("(hc p) n -> p hc n", p=128), sem="w0d",
                    after=[att_done, Tok("tensor", p.cnt["tensor"])])
    for t in range(NT):
        for dc2 in range(2):
            bank = 4 + dc2
            after0 = [att_done, wkv_tok, b.psum_free[bank]]
            for hc in range(NDC):
                p.op("tensor", lambda tt, t=t, dc2=dc2, hc=hc, bank=bank,
                     st=(hc == 0), sp=(hc == NDC - 1): tt.matmul(
                         b.psum[bank][:, 0:512],
                         outfm[:, hc * TOK + t * 128: hc * TOK + (t + 1) * 128],
                         wo_v[:, hc, dc2 * 512:(dc2 + 1) * 512],
                         start=st, stop=sp), after=after0 if hc == 0 else ())
            mm = p.engine_tok("tensor")
            xs = c.x[:, t * D + dc2 * 512: t * D + (dc2 + 1) * 512]
            ev = p.op("vector", lambda v, xs=xs, bank=bank: v.tensor_add(
                xs, xs, b.psum[bank][:, 0:512]), after=mm)
            b.psum_free[bank] = ev


def emit_fused_ffn0(b, c):
    p = b.p
    fused_barrier(b)
    xh_tok = c.A[:]
    ap_toks, _, _ = emit_ln(b, c.x, xh_tok, c.scratch, c.stats)
    ln_done = p.engine_tok("vector")
    ctb = p.dma("sync", c.tok_in.rearrange("(t p) d -> p t d", p=128),
                xh_tok.rearrange("p (t d) -> p t d", d=D), sem="fx",
                after=ln_done)
    a2a = p.op("gpsimd", lambda g: g.collective_compute(
        "AllToAll", ALU.bypass, replica_groups=[list(range(N_CORES))],
        ins=[c.tok_in.opt()], outs=[c.tok_out.opt()]), after=ctb)
    lds = []
    for qt in range(16):
        rho, dlt = qt // 8, qt % 8
        ot = 8 * rho + 2 * (dlt % 4) + dlt // 4
        lds.append(p.dma("sync", xh_tok[:, qt * D:(qt + 1) * D],
                         c.tok_out[ot * 128:(ot + 1) * 128, :],
                         sem="fg", after=(a2a if qt == 0 else None)))
    c.dump("xhex", xh_tok)
    xeT = c.Bb[:]
    prs = []
    for t in range(NT):
        for dc in range(NDC):
            prs.append((xh_tok[:, t * D + dc * 128: t * D + (dc + 1) * 128],
                        xeT[:, dc * TOK + t * 128: dc * TOK + (t + 1) * 128],
                        lds[-1]))
    emit_transpose(b, prs, banks=(6, 7))
    fm_done = p.engine_tok("vector")
    c.dump("xeT", xeT)
    h_sb = c.Cb[:, 0:11264]
    wslot = [c.Cb[:, 11264:13312], c.Cb[:, 13312:15360]]
    silu_sb = c.S4[:].bitcast(BF16)[:, 0:512]
    dev_sb = c.S4[:].bitcast(BF16)[:, 512:1024]
    wfree = [None, None]
    hm_prev = None
    h_free = None
    dst_toks = []
    for blk4 in range(4):
        if True:
            tok0 = blk4 * 512
            for fc in range(NFC):
                buf = fc % 2
                wsb = wslot[buf]
                wtk1 = p.dma("gpsimd", wsb[:, 0:1024]
                             .rearrange("p (cc f) -> p cc f", f=128),
                             c.f0w1_bf[:, fc * 128:(fc + 1) * 128]
                             .rearrange("(cc p) f -> p cc f", p=128),
                             sem="fw" + str(buf),
                             after=[wfree[buf], c.f0_cast])
                wtk3 = p.dma("gpsimd", wsb[:, 1024:2048]
                             .rearrange("p (cc f) -> p cc f", f=128),
                             c.f0w3_bf[:, fc * 128:(fc + 1) * 128]
                             .rearrange("(cc p) f -> p cc f", p=128),
                             sem="fw" + str(buf))
                for which, bank in ((0, 0), (1, 1)):
                    wv = wsb[:, which * 1024:(which + 1) * 1024]
                    after0 = [wtk3, fm_done, b.psum_free[bank]]
                    for dc in range(NDC):
                        p.op("tensor", lambda t, wv=wv, dc=dc, bank=bank,
                             tok0=tok0, st=(dc == 0), sp=(dc == NDC - 1):
                             t.matmul(b.psum[bank][:, 0:512],
                                      wv[:, dc * 128:(dc + 1) * 128],
                                      xeT[:, dc * TOK + tok0:
                                          dc * TOK + tok0 + 512],
                                      start=st, stop=sp),
                             after=after0 if dc == 0 else ())
                mm = p.engine_tok("tensor")
                wfree[buf] = mm
                sg = p.op("scalar", lambda s2: s2.activation(
                    silu_sb, b.psum[0][:, 0:512], AF.Silu),
                    after=[mm, hm_prev])
                hv = h_sb[:, fc * 512:(fc + 1) * 512]
                hm = p.op("vector", lambda v, hv=hv: v.tensor_mul(
                    hv, silu_sb, b.psum[1][:, 0:512]),
                    after=[sg, mm] + ([h_free] if fc == 0 else []))
                b.psum_free[0] = hm
                b.psum_free[1] = hm
                hm_prev = hm
            h_done = hm_prev
            if blk4 == 0:
                c.dump("hsb0", h_sb)
            for fc in range(NFC):
                buf = fc % 2
                wsb = wslot[buf]
                w2t = p.dma("gpsimd",
                            wsb[:, 0:1024].rearrange("p (o d) -> p o d", o=1),
                            c.f0w2_bf[fc * 128:(fc + 1) * 128, :]
                            .rearrange("(o p) d -> p o d", p=128),
                            sem="fw" + str(buf),
                            after=[wfree[buf], h_done])
                after1 = [w2t] + ([pk for pk in b.psum_free] if fc == 0 else [])
                first = True
                for tt in range(4):
                    for dc2 in range(2):
                        bank = tt * 2 + dc2
                        p.op("tensor", lambda t, fc=fc, tt=tt, dc2=dc2,
                             bank=bank, wsb=wsb, st=(fc == 0),
                             sp=(fc == NFC - 1): t.matmul(
                                 b.psum[bank][:, 0:512],
                                 h_sb[:, fc * 512 + tt * 128:
                                      fc * 512 + (tt + 1) * 128],
                                 wsb[:, dc2 * 512:(dc2 + 1) * 512],
                                 start=st, stop=sp),
                             after=after1 if first else ())
                        first = False
                wfree[buf] = p.engine_tok("tensor")
            dmm = p.engine_tok("tensor")
            for tt in range(4):
                for dc2 in range(2):
                    bank = tt * 2 + dc2
                    qt = (tok0 + tt * 128) // 128
                    dtile = (4 * (qt // 8) + qt % 4) * 2 + (qt % 8) // 4
                    ev = p.op("vector", lambda v, bank=bank: v.tensor_copy(
                        dev_sb[:, 0:512], b.psum[bank][:, 0:512]),
                        after=dmm)
                    b.psum_free[bank] = ev
                    st = p.dma("sync",
                               c.tok2_in.rearrange("(t p) d -> p t d", p=128)
                               [:, dtile, dc2 * 512:(dc2 + 1) * 512],
                               dev_sb[:, 0:512], sem="fd", after=ev)
                    dst_toks.append(st)
                    p.last_tok["vector"] = st
            h_free = p.engine_tok("tensor")
    c.dump("hsb", h_sb)
    a2a2 = p.op("gpsimd", lambda g: g.collective_compute(
        "AllToAll", ALU.bypass, replica_groups=[list(range(N_CORES))],
        ins=[c.tok2_in.opt()], outs=[c.tok2_out.opt()]),
        after=dst_toks[-1])
    tr_done = Tok("tensor", p.cnt["tensor"])
    gt = p.dma("sync", xh_tok[:].rearrange("p (t d) -> p t d", d=D),
               c.tok2_out.rearrange("(t p) d -> p t d", p=128),
               sem="fg2", after=[a2a2, tr_done])
    for t in range(NT):
        p.op("vector", lambda v, t=t: v.tensor_add(
            c.x[:, t * D:(t + 1) * D], c.x[:, t * D:(t + 1) * D],
            xh_tok[:, t * D:(t + 1) * D]), after=gt)
    c.dump("delta", xh_tok)


def emit_fused_ssm(b, c):
    p = b.p
    fused_barrier(b)
    fm_tok = emit_fused_ln(b, c)
    urev = c.Cb[:, 0:8192]
    ktabf = c.Cb[:, 8192:9216].bitcast(F32)
    tmp = c.Cb[:].bitcast(F32)[:, 6144:8192]
    kld = p.dma("sync", ktabf.rearrange("p (g n) -> p g n", n=LCH),
                c.tab_out.rearrange("(g p) m -> p g m", p=128)[:, :, 0:64],
                sem="sw", after=c.tab_tok)
    uld = p.dma("gpsimd", urev.rearrange("p (g n) -> p g n", n=1024),
                c.tab_out.rearrange("(g p) m -> p g m", p=128)[:, :, 64:1088],
                sem="sw", after=c.tab_tok)
    yA = c.A[:].bitcast(F32)
    yB = c.Bb[:].bitcast(F32)

    def yv(g):
        return (yA if g < 4 else yB)[:, (g % 4) * TOK:(g % 4 + 1) * TOK]

    vslot = c.S4[:, 0:1024]
    injs_ = [c.S4[:, 1024:1536], c.S4[:, 1536:2048]]
    vld = None
    ild_ = None
    inj_st = []
    for g in range(NDC):
        yg = yv(g)
        xg = c.xhT[:, g * TOK:(g + 1) * TOK]
        for tau in range(LCH):
            kcol = ktabf[:, g * LCH + tau: g * LCH + tau + 1]
            xv = xg.rearrange("p (cc s) -> p cc s", s=LCH)[:, :, 0:LCH - tau]
            yvv = yg.rearrange("p (cc s) -> p cc s", s=LCH)[:, :, tau:LCH]
            if tau == 0:
                p.op("vector", lambda v, xv=xv, yvv=yvv, kcol=kcol:
                     v.tensor_scalar(yvv, xv, kcol, None, ALU.mult),
                     after=[fm_tok, kld, uld])
            else:
                p.op("vector", lambda v, xv=xv, yvv=yvv, kcol=kcol:
                     v.scalar_tensor_tensor(yvv, xv, kcol, yvv, ALU.mult,
                                            ALU.add))
        conv_done = p.engine_tok("vector")
        sl = g % 2
        injg = injs_[sl]
        rprev = None
        for n in range(NST):
            uvb = urev.rearrange("p (gg x) -> p gg x", x=1024)[
                :, g, :].rearrange("p (ss n) -> p ss n", n=NST)[:, :, n:n + 1] \
                .rearrange("p ss o -> p o ss").broadcast_to([128, NCH, LCH])
            xv = xg.rearrange("p (cc s) -> p cc s", s=LCH)
            tvr = tmp.rearrange("p (cc s) -> p cc s", s=LCH)
            mt = p.op("vector", lambda v, xv=xv, uvb=uvb, tvr=tvr:
                      v.tensor_tensor(tvr, xv, uvb, ALU.mult),
                      after=[(ild_ if n == 0 else None),
                             (uld if n == 0 else None),
                             (fm_tok if n == 0 else None), rprev])
            iv = injg.rearrange("p (cc n) -> p cc n", n=NST)[:, :, n:n + 1]
            rprev = p.op("vector", lambda v, iv=iv, tvr=tvr: v.tensor_reduce(
                iv, tvr, axis=AX.X, op=ALU.add), after=mt)
        inj_done = [p.engine_tok("vector"), p.engine_tok("gpsimd")]
        ist = p.dma("sync",
                    c.inj_in.rearrange("(tl p) gn -> p tl gn", p=128)
                    [:, :, g * NST:(g + 1) * NST],
                    injg.rearrange("p (cc n) -> p cc n", n=NST)[:, 1::2, :],
                    sem="si", after=inj_done)
        inj_st.append(ist)
        vld = p.dma("sync", vslot.rearrange("p (o n) -> p o n", o=1),
                    c.tab_out.rearrange("(gg p) m -> p gg m", p=128)
                    [:, g:g + 1, 1088:2112], sem="sv" + str(sl),
                    after=[vld, ild_])
        for n in range(NST):
            vvb = vslot.rearrange("p (tt n) -> p tt n", n=NST)[:, :, n:n + 1] \
                .rearrange("p tt o -> p o tt").broadcast_to([128, NT, LCH])
            ivb = injg.rearrange("p (cc n) -> p cc n", n=NST)[
                :, 0::2, n:n + 1].broadcast_to([128, NT, LCH])
            tvr = tmp.rearrange("p (cc s) -> p cc s", s=LCH)[:, 0:NT, :]
            p.op("vector", lambda v, vvb=vvb, ivb=ivb, tvr=tvr:
                 v.tensor_tensor(tvr, vvb, ivb, ALU.mult),
                 after=[(vld if n == 0 else None),
                        (conv_done if n == 0 else None)])
            yo = yg.rearrange("p (tl u s) -> p tl u s", u=2, s=LCH)[:, :, 1, :]
            p.op("vector", lambda v, yo=yo, tvr=tvr: v.tensor_add(yo, yo, tvr))
        ild_ = p.engine_tok("vector")
    agi = p.op("gpsimd", lambda g2: g2.collective_compute(
        "AllGather", ALU.bypass, replica_groups=[[0, 1, 2, 3], [4, 5, 6, 7]],
        ins=[c.inj_in.opt()], outs=[c.inj_out.opt()]),
        after=[inj_st[-1]])
    injga = c.S4[:, 1024:3072]
    mz = p.op("vector", lambda v: v.memset(injga, 0.0),
              after=[ild_, inj_st[-1]])
    g_toks = []
    for tl in range(NT):
        g_toks.append(p.idma(
            "gpsimd", injga[:, tl * 128:(tl + 1) * 128],
            c.inj_out, c.idxs[:, 32 + tl:33 + tl], 4 * TOK, sem="sg",
            after=[agi if tl == 0 else None, mz, c.cst_tok]))
    gat_done = g_toks[-1]
    vslot2 = [vslot, tmp[:, 1024:2048]]
    vld2_ = [None, None]
    ce_tok = [None, None]
    for g in range(NDC):
        sl2 = g % 2
        vs = vslot2[sl2]
        vld2 = p.dma("sync", vs.rearrange("p (o n) -> p o n", o=1),
                     c.tab_out.rearrange("(gg p) m -> p gg m", p=128)
                     [:, g:g + 1, 1088:2112], sem="sv2" + str(sl2),
                     after=[vld2_[sl2], ce_tok[sl2],
                            ild_ if g < 2 else None])
        vld2_[sl2] = vld2
        yg = yv(g)
        for n in range(NST):
            vvb = vs.rearrange("p (tt n) -> p tt n", n=NST)[:, :, n:n + 1] \
                .rearrange("p tt o -> p o tt").broadcast_to([128, NT, LCH])
            ivb = injga.rearrange("p (tl gn) -> p tl gn", gn=128)[
                :, :, g * NST + n: g * NST + n + 1].broadcast_to([128, NT, LCH])
            tvr = tmp.rearrange("p (cc s) -> p cc s", s=LCH)[:, 0:NT, :]
            p.op("vector", lambda v, vvb=vvb, ivb=ivb, tvr=tvr:
                 v.tensor_tensor(tvr, vvb, ivb, ALU.mult),
                 after=[vld2 if n == 0 else None,
                        gat_done if n == 0 else None])
            yo = yg.rearrange("p (tl u s) -> p tl u s", u=2, s=LCH)[:, :, 0, :]
            p.op("vector", lambda v, yo=yo, tvr=tvr: v.tensor_add(yo, yo, tvr))
        ce_tok[sl2] = p.engine_tok("vector")
    corr_done_g = p.engine_tok("gpsimd")

    corr_done = p.engine_tok("vector")
    c.dump("yg0", yv(0), F32)
    c.dump("injga", injga, F32)
    c.dump("injg7", injs_[1], F32)
    xhf = c.xhT[:].bitcast(F32)
    ytok = xhf[:, 0:2048]
    gel = c.xhT[:, 4096:6144]
    gfm = c.xhT[:, 6144:7168]
    woutT = c.xhT[:, 7168:15360]
    borow = c.xhT[0:1, 15360:16384]
    lnw_b = c.S4[:, 0:1024]
    lnb_b = c.S4[:, 2048:3072]
    xh_dead = Tok("vector", p.cnt["vector"])
    lw1 = p.dma("sync", lnw_b, c.ln_w.rearrange("(o n) -> o n", o=1)
                .broadcast_to([128, D]), sem="sl",
                after=[corr_done, corr_done_g])
    lw2 = p.dma("sync", lnb_b, c.ln_b.rearrange("(o n) -> o n", o=1)
                .broadcast_to([128, D]), sem="sl")
    bol = p.dma("gpsimd", borow.rearrange("o (x n) -> o x n", x=1),
                c.bb_out.rearrange("(o n) -> o n", o=1)
                .rearrange("o (x n) -> o x n", x=1), sem="sl2",
                after=[corr_done, corr_done_g])
    wc = p.dma("gpsimd", c.w_bf, c.Wb_out, sem="sl2")
    wT_toks = []
    for dc in range(NDC):
        wT_toks.append(p.dma("sync", woutT[:, dc * D:(dc + 1) * D],
                             c.w_bf[:, dc * 128:(dc + 1) * 128],
                             transpose=True, sem="swT", after=wc))
    wT_tok = wT_toks[-1]
    idf_sb, idf_tok = b.identf()
    ident_sb, id_tok = b.ident()
    n_ = NT
    mu = c.stats[:, 2 * n_:3 * n_]
    rstd = c.stats[:, 3 * n_:4 * n_]
    ist = c.stats[:, 80:86]
    slot_free = [None, None]
    gfm_free = None
    for t in range(NT):
        sl = t % 2
        yt = ytok[:, sl * D:(sl + 1) * D]
        for dc in range(NDC):
            src = yv(dc)[:, t * 128:(t + 1) * 128].bitcast(BF16) \
                if False else yv(dc)[:, t * 128:(t + 1) * 128]
            bk = (4, 5)[dc % 2]
            ps = b.psum[bk]
            tr = p.op("tensor", lambda tt, src=src, ps=ps: tt.transpose(
                ps[0:128, 0:128], src, idf_sb[:]),
                after=[corr_done, corr_done_g, idf_tok, b.psum_free[bk],
                       slot_free[sl], wT_tok])
            ev = p.op("vector", lambda v, yt=yt, dc=dc, ps=ps: v.tensor_copy(
                yt[:, dc * 128:(dc + 1) * 128], ps[0:128, 0:128]),
                after=tr, chain=False)
            b.psum_free[bk] = ev
        ytr = p.engine_tok("vector")
        i0 = p.op("scalar", lambda s2, yt=yt: s2.activation(
            c.scratch, yt, AF.Square, accum_out=ist[:, 1:2]), after=ytr)
        p.op("vector", lambda v, yt=yt: v.reduce_sum(
            ist[:, 0:1], yt, axis=AX.X), after=ytr)
        p.op("vector", lambda v: v.tensor_scalar(ist[:, 2:3], ist[:, 0:1],
                                                 1.0 / D, None, ALU.mult))
        p.op("vector", lambda v: v.tensor_mul(ist[:, 4:5], ist[:, 2:3],
                                              ist[:, 2:3]))
        p.op("vector", lambda v: v.tensor_scalar(ist[:, 5:6], ist[:, 1:2],
                                                 1.0 / D, 1e-5, ALU.mult,
                                                 ALU.add), after=i0)
        p.op("vector", lambda v: v.tensor_sub(ist[:, 4:5], ist[:, 5:6],
                                              ist[:, 4:5]))
        iq = p.op("scalar", lambda s2: s2.activation(ist[:, 5:6], ist[:, 4:5],
                                                     AF.Sqrt),
                  after=p.engine_tok("vector"))
        p.op("vector", lambda v: v.reciprocal(ist[:, 3:4], ist[:, 5:6]),
             after=iq)
        gl = gel[:, sl * D:(sl + 1) * D] if False else gel[:, 0:1024]
        p.op("vector", lambda v, yt=yt: v.tensor_scalar(
            tmp[:, 0:1024], yt, ist[:, 2:3], ist[:, 3:4], ALU.subtract,
            ALU.mult))
        p.op("vector", lambda v: v.tensor_mul(tmp[:, 0:1024], tmp[:, 0:1024],
                                              lnw_b), after=[lw2])
        gv = p.op("vector", lambda v: v.tensor_add(tmp[:, 0:1024],
                                                   tmp[:, 0:1024], lnb_b))
        ge = p.op("scalar", lambda s2, gl=gl: s2.activation(gl, tmp[:, 0:1024],
                                                            AF.Gelu),
                  after=[gv, gfm_free])
        prs2 = []
        for dc in range(NDC):
            prs2.append((gl[:, dc * 128:(dc + 1) * 128],
                         gfm[:, dc * 128:(dc + 1) * 128], ge))
        evs2 = emit_transpose(b, prs2, banks=(6, 7))
        g_done = evs2[-1]
        for dc2 in range(2):
            bank = dc2
            after0 = [g_done, wT_tok, bol, c.ot, b.psum_free[bank]]
            for hc in range(NDC):
                p.op("tensor", lambda tt, hc=hc, dc2=dc2, bank=bank,
                     st=(hc == 0): tt.matmul(
                         b.psum[bank][:, 0:512],
                         gfm[:, hc * 128:(hc + 1) * 128],
                         woutT[:, hc * D + dc2 * 512: hc * D + (dc2 + 1) * 512],
                         start=st, stop=False), after=after0 if hc == 0 else ())
            p.op("tensor", lambda tt, dc2=dc2, bank=bank: tt.matmul(
                b.psum[bank][:, 0:512], c.ones_bf[:],
                borow[:, dc2 * 512:(dc2 + 1) * 512], start=False, stop=True))
            mm = p.engine_tok("tensor")
            xs = c.x[:, t * D + dc2 * 512: t * D + (dc2 + 1) * 512]
            xh1 = p.op("vector", lambda v, xs=xs, t=t: v.tensor_scalar(
                tmp[:, 1024:1536], xs, mu[:, t:t + 1], rstd[:, t:t + 1],
                ALU.subtract, ALU.mult), after=[mm])
            ev1 = p.op("vector", lambda v, xs=xs, bank=bank: v.tensor_add(
                xs, b.psum[bank][:, 0:512], xs))
            ev2 = p.op("vector", lambda v, xs=xs: v.tensor_add(
                xs, xs, tmp[:, 1024:1536]))
            b.psum_free[bank] = ev2
        slot_free[sl] = p.engine_tok("tensor")
        gfm_free = p.engine_tok("tensor")


def emit_fused_l1a(b, c):
    p = b.p
    fused_barrier(b)
    fm_tok = emit_fused_ln(b, c)
    S4b = c.S4[:].bitcast(BF16)
    wsl = [S4b[:, 0:1024], S4b[:, 1024:2048]]
    QT = c.A[:]
    KTf = c.Bb[:]
    Ktok = c.Cb[:, 0:16384]
    wo = c.xhT[:, 0:8192]
    halo = c.xhT[:, 8192:12288]
    khfm = c.xhT[:, 12288:16384]
    Pw = S4b[:, 2048:2304]
    PTw = S4b[:, 2304:2560]
    ao = S4b[:, 2560:3584]
    aoT = S4b[:, 3584:4608]
    ob_row = S4b[0:1, 4608:5632]
    lcol = c.stats[:, 80:86]
    wfree = [None, None]

    def produce(which, dstf, first_after=None):
        for hc in range(NDC):
            sl = hc % 2
            wld = p.dma("gpsimd",
                        wsl[sl].rearrange("p (dcc n) -> p dcc n", n=128),
                        c.wqkv_a.rearrange("(dcc p) n -> p dcc n", p=128)
                        [:, :, which * D + hc * 128: which * D + (hc + 1) * 128],
                        sem="lw" + str(sl),
                        after=[wfree[sl], first_after if hc < 2 else None])
            for s4 in range(4):
                bank = s4 % 2
                after0 = [fm_tok, wld, b.psum_free[bank]]
                for dc in range(NDC):
                    p.op("tensor", lambda t, sl=sl, dc=dc, s4=s4, bank=bank,
                         st=(dc == 0), sp=(dc == NDC - 1): t.matmul(
                             b.psum[bank][:, 0:512],
                             wsl[sl][:, dc * 128:(dc + 1) * 128],
                             c.xhT[:, dc * TOK + s4 * 512:
                                   dc * TOK + (s4 + 1) * 512],
                             start=st, stop=sp), after=after0 if dc == 0 else ())
                mm = p.engine_tok("tensor")
                ev = p.op("vector", lambda v, hc=hc, s4=s4, bank=bank,
                          dstf=dstf: v.tensor_copy(
                              dstf[:, hc * TOK + s4 * 512:
                                   hc * TOK + (s4 + 1) * 512],
                              b.psum[bank][:, 0:512]), after=mm)
                b.psum_free[bank] = ev
            wfree[sl] = p.engine_tok("tensor")

    produce(1, KTf, first_after=fm_tok)
    kdone_v = p.engine_tok("vector")
    prs = []
    for t in range(NT):
        for hc in range(NDC):
            prs.append((KTf[:, hc * TOK + t * 128: hc * TOK + (t + 1) * 128],
                        Ktok[:, t * D + hc * 128: t * D + (hc + 1) * 128],
                        kdone_v))
    evs = emit_transpose(b, prs, banks=(4, 5))
    stg = p.dma("sync", c.tok_in.rearrange("(t p) d -> p t d", p=128),
                Ktok.rearrange("p (t d) -> p t d", d=D), sem="lkst",
                after=evs[-1])
    ag = p.op("gpsimd", lambda g: g.collective_compute(
        "AllGather", ALU.bypass, replica_groups=[[0, 1, 2, 3], [4, 5, 6, 7]],
        ins=[c.tok_in.opt()], outs=[c.halo_out.opt()]), after=stg)
    produce(0, QT)
    q_done_t = p.engine_tok("tensor")
    q_done_v = p.engine_tok("vector")
    wol = p.dma("gpsimd", wo.rearrange("p (cc n) -> p cc n", n=D),
                c.wout_a.rearrange("(cc p) n -> p cc n", p=128), sem="lwo",
                after=q_done_t)
    bol = p.dma("gpsimd", ob_row.rearrange("o (x n) -> o x n", x=1),
                c.bout_a.rearrange("(o n) -> o n", o=1)
                .rearrange("o (x n) -> o x n", x=1), sem="lwo")
    mz = p.op("vector", lambda v: v.memset(halo, 0.0), after=q_done_t)
    c.dump("l1a_qt", QT)
    pw_free = None
    slot_free4 = [None, None, None, None]
    for t in range(NT):
        sl4 = t % 4
        gt = p.idma("gpsimd", halo[:, sl4 * 1024:(sl4 + 1) * 1024],
                    c.halo_out, c.idxs[:, 32 + t:33 + t], 4 * TOK,
                    sem="hg" + str(sl4),
                    after=[ag if t == 0 else None, mz if t < 4 else None,
                           c.cst_tok if t == 0 else None, slot_free4[sl4]])
        fm_prs = []
        for hc in range(NDC):
            fm_prs.append((halo[:, sl4 * 1024 + hc * 128:
                                sl4 * 1024 + (hc + 1) * 128],
                           khfm[:, sl4 * 1024 + hc * 128:
                                sl4 * 1024 + (hc + 1) * 128], gt))
        evh = emit_transpose(b, fm_prs, banks=(4, 5))
        for wb in range(2):
            bank = wb
            after0 = [q_done_v if t == 0 else None, b.psum_free[bank],
                      evh[-1] if wb == 0 else None]
            for dc in range(NDC):
                if wb == 0:
                    rhs = khfm[:, sl4 * 1024 + dc * 128:
                               sl4 * 1024 + (dc + 1) * 128]
                else:
                    rhs = KTf[:, dc * TOK + t * 128: dc * TOK + (t + 1) * 128]
                lhsT = QT[:, dc * TOK + t * 128: dc * TOK + (t + 1) * 128]
                p.op("tensor", lambda tt, lhsT=lhsT, rhs=rhs, bank=bank,
                     st=(dc == 0), sp=(dc == NDC - 1): tt.matmul(
                         b.psum[bank][:, 0:128], lhsT, rhs, start=st, stop=sp),
                     after=after0 if dc == 0 else ())
        mm = p.engine_tok("tensor")
        mk = p.op("vector", lambda v: v.tensor_add(
            b.psum[1][:, 0:128], b.psum[1][:, 0:128], c.maskw[:]),
            after=mm, chain=False)
        r1 = p.op("vector", lambda v: v.reduce_max(
            lcol[:, 0:1], b.psum[0][:, 0:128], axis=AX.X), after=mm)
        r2 = p.op("vector", lambda v: v.reduce_max(
            lcol[:, 1:2], b.psum[1][:, 0:128], axis=AX.X), after=mk)
        r3 = p.op("vector", lambda v: v.tensor_max(lcol[:, 0:1], lcol[:, 0:1],
                                                   lcol[:, 1:2]))
        r4 = p.op("vector", lambda v: v.tensor_scalar(
            lcol[:, 1:2], lcol[:, 0:1], -0.03125, None, ALU.mult))
        e1 = p.op("scalar", lambda s: s.activation(
            Pw[:, 0:128], b.psum[0][:, 0:128], AF.Exp, bias=lcol[:, 1:2],
            scale=0.03125, accum_out=lcol[:, 2:3]), after=[r4, pw_free])
        e2 = p.op("scalar", lambda s: s.activation(
            Pw[:, 128:256], b.psum[1][:, 0:128], AF.Exp, bias=lcol[:, 1:2],
            scale=0.03125, accum_out=lcol[:, 3:4]), after=r4)
        b.psum_free[0] = e1
        b.psum_free[1] = e2
        l1 = p.op("vector", lambda v: v.tensor_add(lcol[:, 4:5], lcol[:, 2:3],
                                                   lcol[:, 3:4]), after=e2)
        l2 = p.op("vector", lambda v: v.reciprocal(lcol[:, 5:6], lcol[:, 4:5]))
        ps = p.op("vector", lambda v: v.tensor_scalar(
            Pw[:], Pw[:], lcol[:, 5:6], None, ALU.mult))
        evp = emit_transpose(b, [(Pw[:, 0:128], PTw[:, 0:128], ps),
                                 (Pw[:, 128:256], PTw[:, 128:256], ps)],
                             banks=(4, 5))
        pw_free = p.engine_tok("tensor")
        for dc2 in range(2):
            bank = 2 + dc2
            after0 = [evp[1], b.psum_free[bank]]
            for ks in range(2):
                if ks == 0:
                    vsrc = halo[:, sl4 * 1024 + dc2 * 512:
                                sl4 * 1024 + (dc2 + 1) * 512]
                else:
                    vsrc = Ktok[:, t * D + dc2 * 512: t * D + (dc2 + 1) * 512]
                p.op("tensor", lambda tt, ks=ks, vsrc=vsrc, bank=bank,
                     st=(ks == 0), sp=(ks == 1): tt.matmul(
                         b.psum[bank][:, 0:512],
                         PTw[:, ks * 128:(ks + 1) * 128], vsrc,
                         start=st, stop=sp), after=after0 if ks == 0 else ())
            mm2 = p.engine_tok("tensor")
            aev = p.op("scalar", lambda sc, dc2=dc2, bank=bank: sc.copy(
                ao[:, dc2 * 512:(dc2 + 1) * 512], b.psum[bank][:, 0:512]),
                after=mm2)
            b.psum_free[bank] = aev
        ao_done = p.engine_tok("scalar")
        prs2 = []
        for dc in range(NDC):
            prs2.append((ao[:, dc * 128:(dc + 1) * 128],
                         aoT[:, dc * 128:(dc + 1) * 128], ao_done))
        evs2 = emit_transpose(b, prs2, banks=(4, 5))
        aoT_done = evs2[-1]
        for dc2 in range(2):
            bank = 6 + dc2
            after0 = [aoT_done, wol, bol, c.ot, b.psum_free[bank]]
            for hc in range(NDC):
                p.op("tensor", lambda tt, hc=hc, dc2=dc2, bank=bank,
                     st=(hc == 0): tt.matmul(
                         b.psum[bank][:, 0:512],
                         aoT[:, hc * 128:(hc + 1) * 128],
                         wo[:, hc * D + dc2 * 512: hc * D + (dc2 + 1) * 512],
                         start=st, stop=False), after=after0 if hc == 0 else ())
            p.op("tensor", lambda tt, dc2=dc2, bank=bank: tt.matmul(
                b.psum[bank][:, 0:512], c.ones_bf[:],
                ob_row[:, dc2 * 512:(dc2 + 1) * 512], start=False, stop=True))
            mm3 = p.engine_tok("tensor")
            xs = c.x[:, t * D + dc2 * 512: t * D + (dc2 + 1) * 512]
            ev = p.op("vector", lambda v, xs=xs, bank=bank: v.tensor_add(
                xs, b.psum[bank][:, 0:512], xs), after=mm3)
            b.psum_free[bank] = ev
        slot_free4[sl4] = p.engine_tok("tensor")
        pw_free = p.engine_tok("tensor")


def emit_fused_ffn1(b, c):
    p = b.p
    fused_barrier(b)
    S4b = c.S4[:].bitcast(BF16)
    VT = c.A[:]
    WvT = c.Bb[:, 0:8192]
    WgpT = c.Bb[:, 8192:8832]
    W1T = c.Bb[:, 8832:10880]
    W2nat = c.Bb[:, 10880:12928]
    W2b = c.Cb[0:64, 0:8192]
    rows_bf = c.Cb[0:1, 8192:9296]
    b1rows = c.Cb[0:1, 9296:9552]
    v2 = S4b[:, 4096:6144]
    hslot = [S4b[:, 0:64], S4b[:, 64:128]]
    hT = [S4b[0:64, 128:256], S4b[0:64, 256:384]]
    gate = c.stats[:, 0:16]
    we_sb = c.stats[:, 16:24]
    lncr = c.stats[:, 24:40]
    scratch = c.scratch

    w2z = p.op("vector", lambda v: v.memset(W2b[32:64, :], 0.0))
    c1 = p.dma("gpsimd", c.w_bf, c.Wv, sem="wcast")
    for dc in range(NDC):
        p.dma("gpsimd", WgpT[:, dc * 80:dc * 80 + 8],
              c.WgT[dc * 128:(dc + 1) * 128, :], sem="wgt")
    c3 = p.dma("gpsimd", c.w_bf2, c.Wp.rearrange("e q d -> (e q) d"),
               sem="wcast")
    c4 = p.dma("gpsimd", c.w_bf3, c.W1.rearrange("e i d -> (e i) d"),
               sem="wcast")
    tts = []
    for dc in range(NDC):
        tts.append(p.dma("sync", WvT[:, dc * D:(dc + 1) * D],
                         c.w_bf[:, dc * 128:(dc + 1) * 128], transpose=True,
                         sem="wT", after=c4))
        tts.append(p.dma("sync", WgpT[:, dc * 80 + 16:dc * 80 + 80],
                         c.w_bf2[:, dc * 128:(dc + 1) * 128], transpose=True,
                         sem="wT"))
        tts.append(p.dma("sync", W1T[:, dc * 256:(dc + 1) * 256],
                         c.w_bf3[:, dc * 128:(dc + 1) * 128], transpose=True,
                         sem="wT"))
    wT_tok = tts[-1]
    wgt_tok = Tok("dma_wgt_gpsimd", p.cnt["dma_wgt_gpsimd"])
    gap0 = p.op("vector", lambda v: v.memset(
        WgpT[:].rearrange("p (cc e) -> p cc e", e=80)[:, :, 8:16], 0.0))
    w2n_toks = []
    for e in range(NE):
        for dc in range(NDC):
            w2n_toks.append(p.dma(
                "gpsimd", W2nat[:, (e * NDC + dc) * NI:(e * NDC + dc + 1) * NI],
                c.W2[e, dc * 128:(dc + 1) * 128, :], sem="w2n"))
        p.dma("gpsimd", W2b[32:33, e * D:(e + 1) * D], c.b2[e:e + 1, :],
              sem="rows", after=(w2z if e == 0 else None))
    p.dma("gpsimd", b1rows[:], c.b1.rearrange("e i -> (e i)")
          .rearrange("(o n) -> o n", o=1), sem="rows")
    p.dma("gpsimd", rows_bf[:, 0:D], c.bv.rearrange("(o n) -> o n", o=1),
          sem="rows")
    p.dma("gpsimd", rows_bf[:, D:D + 8], c.bg.rearrange("(o n) -> o n", o=1),
          sem="rows")
    p.dma("sync", c.gains[:, 0:8], c.g_r.rearrange("(o n) -> o n", o=1)
          .broadcast_to([128, 8]), sem="rows_s")
    p.dma("sync", c.gains[:, 8:16], c.b_r.rearrange("(o n) -> o n", o=1)
          .broadcast_to([128, 8]), sem="rows_s")
    p.dma("sync", c.gains[:, 16:24], c.g_c.rearrange("(o n) -> o n", o=1)
          .broadcast_to([128, 8]), sem="rows_s")
    p.dma("sync", c.gains[:, 24:32], c.b_c.rearrange("(o n) -> o n", o=1)
          .broadcast_to([128, 8]), sem="rows_s")
    al_tok = p.dma("sync", c.msc[:, 0:1],
                   c.alpha.rearrange("(o n) -> o n", o=1), sem="rows_s")
    rows_tok = Tok("dma_rows_s_sync", p.cnt["dma_rows_s_sync"])
    rowsbf_tok = Tok("dma_rows_gpsimd", p.cnt["dma_rows_gpsimd"])
    zpad = p.op("vector", lambda v: v.memset(rows_bf[:, D + 8:D + 80], 0.0))
    sg_a = p.op("scalar", lambda s: s.activation(c.galph[:, 0:1],
                                                 c.msc[:, 0:1], AF.Sigmoid),
                after=al_tok)
    sd1 = p.dma("sync", c.sig_dr, c.galph[:, 0:1], sem="sig", after=sg_a)
    ev_s = p.dma("sync", c.sig[:], c.sig_dr.broadcast_to([128, 1]), sem="sig",
                 after=sd1)
    fm_done = emit_fused_ln(b, c)
    p.op("vector", lambda v: v.memset(hslot[0][:, 33:64], 0.0), after=fm_done)
    p.op("vector", lambda v: v.memset(hslot[1][:, 33:64], 0.0))
    for e in range(NE):
        pairs = []
        for dc in range(NDC):
            src = W2nat[:, (e * NDC + dc) * NI:(e * NDC + dc + 1) * NI]
            dst = W2b[0:32, e * D + dc * 128: e * D + (dc + 1) * 128]
            pairs.append((src, dst, w2n_toks[-1]))
        emit_transpose(b, pairs, banks=(4, 5))

    vfree = {}
    for t in range(NT):
        for half in range(2):
            bank = 2 + half
            after0 = [fm_done, zpad, c.ot, rowsbf_tok, wT_tok,
                      b.psum_free[bank]]
            for hc in range(NDC):
                lhsT = c.xhT[:, hc * TOK + t * 128: hc * TOK + (t + 1) * 128]
                rhs = WvT[:, hc * D + half * 512: hc * D + (half + 1) * 512]
                p.op("tensor", lambda tt, o=b.psum[bank][:, 0:512], l=lhsT,
                     r=rhs, st=(hc == 0): tt.matmul(o, l, r, start=st,
                                                    stop=False),
                     after=after0 if hc == 0 else ())
            p.op("tensor", lambda tt, o=b.psum[bank][:, 0:512],
                 r=rows_bf[:, half * 512:(half + 1) * 512]:
                 tt.matmul(o, c.ones_bf[:], r, start=False, stop=True))
            mm = p.engine_tok("tensor")
            xs = c.x[:, t * D + half * 512: t * D + (half + 1) * 512]
            ad = p.op("vector", lambda v, xs=xs, bank=bank: v.tensor_add(
                xs, xs, b.psum[bank][:, 0:512]), after=mm)
            vslot = v2[:, (t % 2) * D + half * 512:
                       (t % 2) * D + (half + 1) * 512]
            cv = p.op("scalar", lambda s, vslot=vslot, bank=bank: s.copy(
                vslot, b.psum[bank][:, 0:512]),
                after=[mm, ad, vfree.get((t % 2, half))])
            b.psum_free[bank] = cv
            prs = []
            for q in range(4):
                dc = half * 4 + q
                src = vslot[:, q * 128:(q + 1) * 128]
                dst = VT[:, dc * TOK + t * 128: dc * TOK + (t + 1) * 128]
                prs.append((src, dst, cv))
            emit_transpose(b, prs, banks=(6, 7))
            vfree[(t % 2, half)] = p.engine_tok("tensor")
    vt_done = p.engine_tok("vector")
    if "f1_bar" in c.dbg:
        fused_barrier(b)
    c.dump("f1_vt", VT)

    xw = [wT_tok, rows_tok, ev_s, rowsbf_tok] if "f1_xw" in c.dbg else []
    wet = None
    for t in range(NT):
        after0 = [vt_done, b.psum_free[0], wgt_tok if t == 0 else None] + xw
        for dc in range(NDC):
            lhsT = VT[:, dc * TOK + t * 128: dc * TOK + (t + 1) * 128]
            rhs = WgpT[:, dc * 80:(dc + 1) * 80]
            p.op("tensor", lambda tt, l=lhsT, r=rhs, st=(dc == 0):
                 tt.matmul(b.psum[0][:, 0:80], l, r, start=st, stop=False),
                 after=after0 if dc == 0 else ())
        p.op("tensor", lambda tt: tt.matmul(
            b.psum[0][:, 0:80], c.ones_bf[:], rows_bf[:, D:D + 80],
            start=False, stop=True))
        mm = p.engine_tok("tensor")
        ser = "f1_ser" in c.dbg
        sq = p.op("scalar", lambda s: s.activation(
            scratch[:, 0:64], b.psum[0][:, 16:80], AF.Square),
            after=[mm, wet])
        rc = p.op("vector", lambda v: v.tensor_reduce(
            gate[:, 8:16], scratch[:, 0:64].rearrange("p (e q) -> p e q", q=8),
            axis=AX.X, op=ALU.add),
            after=[sq] + ([rows_tok, ev_s] if t == 0 else []))
        rcp = p.op("vector", lambda v: v.tensor_copy(gate[:, 0:8],
                                                     b.psum[0][:, 0:8]),
                   after=[mm, sq if ser else None])
        if t == 5 and "f1_psv" in c.dbg:
            p.op("vector", lambda v: v.tensor_copy(c.S4[:, 2048:2112],
                                                   scratch[:, 0:64]))
            p.op("vector", lambda v: v.tensor_copy(c.S4[:, 2112:2120],
                                                   gate[:, 8:16]))
        psvall = "f1_psvall" in c.dbg
        b.psum_free[0] = rcp
        p.op("vector", lambda v: v.tensor_scalar(gate[:, 8:16], gate[:, 8:16],
                                                 0.125, None, ALU.mult))
        cs = p.op("scalar", lambda s: s.activation(gate[:, 8:16],
                                                   gate[:, 8:16], AF.Sqrt),
                  after=p.engine_tok("vector"))
        if psvall:
            p.op("vector", lambda v, t=t: v.tensor_copy(
                c.S4[:, 2048 + 40 * t:2048 + 40 * t + 16], gate[:, 0:16]),
                after=cs)
        for k, (src, dst) in enumerate(((gate[:, 0:8], lncr[:, 0:8]),
                                        (gate[:, 8:16], lncr[:, 8:16]))):
            p.op("vector", lambda v, src=src: v.reduce_sum(
                c.stats[:, 80:81], src, axis=AX.X), after=cs)
            p.op("vector", lambda v, src=src: v.tensor_scalar(
                c.stats[:, 80:81], c.stats[:, 80:81], 0.125, None, ALU.mult))
            p.op("vector", lambda v, src=src, dst=dst: v.tensor_scalar(
                dst, src, c.stats[:, 80:81], None, ALU.subtract))
            p.op("vector", lambda v, dst=dst: v.tensor_mul(
                scratch[:, 64:72], dst, dst))
            p.op("vector", lambda v: v.reduce_sum(
                c.stats[:, 81:82], scratch[:, 64:72], axis=AX.X))
            p.op("vector", lambda v: v.tensor_scalar(
                c.stats[:, 81:82], c.stats[:, 81:82], 0.125, 1e-5, ALU.mult,
                ALU.add))
            sqv = p.op("scalar", lambda s: s.activation(
                c.stats[:, 82:83], c.stats[:, 81:82], AF.Sqrt),
                after=p.engine_tok("vector"))
            p.op("vector", lambda v: v.reciprocal(c.stats[:, 83:84],
                                                  c.stats[:, 82:83]),
                 after=sqv)
            p.op("vector", lambda v, dst=dst: v.tensor_scalar(
                dst, dst, c.stats[:, 83:84], None, ALU.mult))
        p.op("vector", lambda v: v.tensor_mul(lncr[:, 8:16], lncr[:, 8:16],
                                              c.gains[:, 16:24]))
        p.op("vector", lambda v: v.tensor_add(lncr[:, 8:16], lncr[:, 8:16],
                                              c.gains[:, 24:32]))
        p.op("vector", lambda v: v.tensor_scalar(lncr[:, 8:16], lncr[:, 8:16],
                                                 c.sig[:, 0:1], None,
                                                 ALU.mult))
        p.op("vector", lambda v: v.tensor_mul(lncr[:, 0:8], lncr[:, 0:8],
                                              c.gains[:, 0:8]))
        p.op("vector", lambda v: v.tensor_add(lncr[:, 0:8], lncr[:, 0:8],
                                              c.gains[:, 8:16]))
        p.op("vector", lambda v: v.tensor_add(gate[:, 0:8], lncr[:, 0:8],
                                              lncr[:, 8:16]))
        if psvall:
            p.op("vector", lambda v, t=t: v.tensor_copy(
                c.S4[:, 2048 + 40 * t + 16:2048 + 40 * t + 32], lncr[:, 0:16]))
        p.op("vector", lambda v: v.reduce_max(gate[:, 8:9], gate[:, 0:8],
                                              axis=AX.X))
        p.op("vector", lambda v: v.tensor_scalar(we_sb[:], gate[:, 0:8],
                                                 gate[:, 8:9], None,
                                                 ALU.is_equal))
        p.op("vector", lambda v: v.scalar_tensor_tensor(
            gate[:, 0:8], we_sb[:], BIGNEG, gate[:, 0:8], ALU.mult, ALU.add))
        p.op("vector", lambda v: v.reduce_max(gate[:, 9:10], gate[:, 0:8],
                                              axis=AX.X))
        p.op("vector", lambda v: v.tensor_scalar(gate[:, 10:11], gate[:, 8:9],
                                                 gate[:, 9:10], None,
                                                 ALU.subtract))
        sgw = p.op("scalar", lambda s: s.activation(
            gate[:, 11:12], gate[:, 10:11], AF.Sigmoid),
            after=p.engine_tok("vector"))
        p.op("vector", lambda v: v.tensor_scalar(gate[:, 12:13],
                                                 gate[:, 11:12],
                                                 -1.0, 1.0, ALU.mult, ALU.add),
             after=sgw)
        p.op("vector", lambda v: v.tensor_scalar(scratch[:, 0:8], gate[:, 0:8],
                                                 gate[:, 9:10], None,
                                                 ALU.is_equal))
        p.op("vector", lambda v: v.tensor_scalar(we_sb[:], we_sb[:],
                                                 gate[:, 11:12], None,
                                                 ALU.mult))
        wet = p.op("vector", lambda v: v.scalar_tensor_tensor(
            we_sb[:], scratch[:, 0:8], gate[:, 12:13], we_sb[:],
            ALU.mult, ALU.add))
        if psvall:
            wet = p.op("vector", lambda v, t=t: v.tensor_copy(
                c.S4[:, 2048 + 40 * t + 32:2048 + 40 * t + 40], we_sb[:]))
        if t == 5:
            c.dump("f1_we5", we_sb, F32)
            c.dump("f1_g5", gate, F32)
        dmm_prev = None
        tr_prev2 = [None, None]
        pend = None
        trevs = None
        for e in range(NE + 1):
            if e < NE:
                sl_h = e % 2
                hafter = [wet, b.psum_free[1]] + xw
                for dc in range(NDC):
                    lhsT = VT[:, dc * TOK + t * 128: dc * TOK + (t + 1) * 128]
                    rhs = W1T[:, dc * 256 + e * NI: dc * 256 + (e + 1) * NI]
                    p.op("tensor", lambda tt, l=lhsT, r=rhs, st=(dc == 0):
                         tt.matmul(b.psum[1][:, 0:NI], l, r, start=st,
                                   stop=False), after=hafter if dc == 0 else ())
                p.op("tensor", lambda tt, e=e: tt.matmul(
                    b.psum[1][:, 0:NI], c.ones_bf[:],
                    b1rows[:, e * NI:(e + 1) * NI], start=False, stop=True))
                hmm = p.engine_tok("tensor")
                hsl = p.op("scalar", lambda s, sl_h=sl_h: s.activation(
                    hslot[sl_h][:, 0:NI], b.psum[1][:, 0:NI], AF.Silu),
                    after=[hmm, tr_prev2[sl_h]])
                b.psum_free[1] = hsl
                hsc = p.op("vector", lambda v, e=e, sl_h=sl_h: v.tensor_scalar(
                    hslot[sl_h][:, 0:NI], hslot[sl_h][:, 0:NI],
                    we_sb[:, e:e + 1], None, ALU.mult), after=hsl)
                wcp = p.op("vector", lambda v, e=e, sl_h=sl_h: v.tensor_copy(
                    hslot[sl_h][:, 32:33], we_sb[:, e:e + 1]))
            if pend is not None:
                ep, slp, wcpp = pend
                tr = emit_transpose(b, [(hslot[slp][:, 0:64], hT[slp][0:64, :],
                                         wcpp, dmm_prev)], banks=(6,))[0]
                tr_prev2[slp] = p.engine_tok("tensor")
                for dc2 in range(2):
                    bank = 2 + dc2
                    dafter = [tr, b.psum_free[bank]] if ep == 0 else [tr]
                    p.op("tensor", lambda tt, ep=ep, dc2=dc2, bank=bank,
                         slp=slp, st=(ep == 0), sp=(ep == NE - 1): tt.matmul(
                             b.psum[bank][:, 0:512], hT[slp][0:64, :],
                             W2b[0:64, ep * D + dc2 * 512:
                                 ep * D + (dc2 + 1) * 512],
                             start=st, stop=sp),
                         after=dafter)
                dmm_prev = p.engine_tok("tensor")
            if e < NE:
                pend = (e, sl_h, wcp)
        dmm = p.engine_tok("tensor")
        for dc2 in range(2):
            bank = 2 + dc2
            xs = c.x[:, t * D + dc2 * 512: t * D + (dc2 + 1) * 512]
            ev = p.op("vector", lambda v, xs=xs, bank=bank: v.tensor_add(
                xs, xs, b.psum[bank][:, 0:512]), after=dmm)
            b.psum_free[bank] = ev
    c.dump("f1_xhT_end", c.xhT[:])
    c.dump("f1_vt_end", VT)
    c.dump("f1_psv", c.S4[:, 2048:2120], F32)
    c.dump("f1_psvall", c.S4[:, 2048:2688], F32)



DC_KV = 256
NH = 16
DK = 64



def emit_fused_l1b(b, c):
    p = b.p
    fused_barrier(b)
    fm_tok = emit_fused_ln(b, c)
    S4b = c.S4[:].bitcast(BF16)
    QT = c.A[:]
    wqq = c.Bb[:, 0:8192]
    wqk = c.Bb[:, 8192:16384]
    wqv = c.Cb[:, 0:8192]
    brow = c.Cb[0:1, 8192:11264]
    kfm = [c.Cb[:, 12288:13312], c.Cb[:, 13312:14336]]
    vsl = [c.Cb[:, 14336:15360], c.Cb[:, 15360:16384]]
    rot = S4b[:, 0:2048]
    sin_sb = c.S4[:, 2048:2560]
    cos_sb = c.S4[:, 2560:3072]
    scratch = c.scratch
    if not hasattr(c, "ind2_sb"):
        ind2 = b.const("find2", np.concatenate(
            [np.repeat(np.eye(2, dtype=bf16)[0:1], 64, 0),
             np.repeat(np.eye(2, dtype=bf16)[1:2], 64, 0)], 0))
        c.ind2_sb = b.sb("find2_sb", [128, 2], BF16)
        c.ind2_tok = p.dma("sync", c.ind2_sb[:], ind2, sem="cst2")
    ident_sb, id_tok = b.ident()

    wl1 = p.dma("gpsimd", wqq.rearrange("p (dcc n) -> p dcc n", n=1024),
                c.qkv_w.rearrange("(dcc p) n -> p dcc n", p=128)[:, :, 0:D],
                sem="bw")
    wl2 = p.dma("gpsimd", wqk.rearrange("p (dcc n) -> p dcc n", n=1024),
                c.qkv_w.rearrange("(dcc p) n -> p dcc n", p=128)
                [:, :, D:2 * D], sem="bw")
    wl3 = p.dma("gpsimd", wqv.rearrange("p (dcc n) -> p dcc n", n=1024),
                c.qkv_w.rearrange("(dcc p) n -> p dcc n", p=128)
                [:, :, 2 * D:3 * D], sem="bw")
    bl = p.dma("gpsimd", brow.rearrange("o (x n) -> o x n", x=1),
               c.qkv_b.rearrange("(o n) -> o n", o=1)
               .rearrange("o (x n) -> o x n", x=1), sem="bw")
    sld = p.dma("sync", sin_sb.rearrange("p (t n) -> p t n", n=32),
                c.sin_t.rearrange("(t p) n -> p t n", p=128), sem="bm")
    cldd = p.dma("sync", cos_sb.rearrange("p (t n) -> p t n", n=32),
                 c.cos_t.rearrange("(t p) n -> p t n", p=128), sem="bm")
    wtok = Tok("dma_bw_gpsimd", p.cnt["dma_bw_gpsimd"])

    stg_k = []
    stg_v = []
    kslot_free = [None, None]
    vslot_free = [None, None]
    rope_done = None
    for t in range(NT):
        for col in range(6):
            bank = col
            wbuf = (wqq, wqq, wqk, wqk, wqv, wqv)[col]
            after0 = [fm_tok, cldd, c.ot, wtok, b.psum_free[bank]]
            for dc in range(NDC):
                p.op("tensor", lambda tt, t=t, col=col, dc=dc, bank=bank,
                     wbuf=wbuf, st=(dc == 0): tt.matmul(
                         b.psum[bank][:, 0:512],
                         c.xhT[:, dc * TOK + t * 128: dc * TOK + (t + 1) * 128],
                         wbuf[:, dc * 1024 + (col % 2) * 512:
                              dc * 1024 + (col % 2 + 1) * 512],
                         start=st, stop=False), after=after0 if dc == 0 else ())
            p.op("tensor", lambda tt, col=col, bank=bank: tt.matmul(
                b.psum[bank][:, 0:512], c.ones_bf[:],
                brow[:, col * 512:(col + 1) * 512], start=False, stop=True))
        mm = p.engine_tok("tensor")
        sv = t % 2
        for col in (4, 5):
            ev = p.op("vector", lambda v, col=col, sv=sv: v.tensor_copy(
                vsl[sv][:, (col - 4) * 512:(col - 3) * 512],
                b.psum[col][:, 0:512]),
                after=[mm, vslot_free[sv] if col == 4 else None])
            b.psum_free[col] = ev
        sv_t = p.dma("sync", c.tok_in.rearrange("(t p) d -> p t d", p=128)
                     [:, t, :], vsl[sv], sem="bv", after=p.engine_tok("vector"))
        stg_v.append(sv_t)
        vslot_free[sv] = sv_t
        sc = sin_sb[:, t * 32:(t + 1) * 32]
        cc2 = cos_sb[:, t * 32:(t + 1) * 32]
        scb = sc.rearrange("p (o j) -> p o j", o=1).broadcast_to([128, 8, 32])
        ccb = cc2.rearrange("p (o j) -> p o j", o=1).broadcast_to([128, 8, 32])
        for col in range(4):
            pvv = b.psum[col][:, 0:512].rearrange("p (h j) -> p h j", j=64)
            x1 = pvv[:, :, 0:64:2]
            x2 = pvv[:, :, 1:64:2]
            od = rot[:, col * 512:(col + 1) * 512] \
                .rearrange("p (h j) -> p h j", j=64)
            s1 = scratch[:, 0:256].rearrange("p (h j) -> p h j", j=32)
            s2 = scratch[:, 256:512].rearrange("p (h j) -> p h j", j=32)
            r1 = p.op("vector", lambda v, x1=x1, ccb=ccb, s1=s1:
                      v.tensor_tensor(s1, x1, ccb, ALU.mult),
                      after=[mm, rope_done if col == 0 else None])
            p.op("vector", lambda v, x2=x2, scb=scb, s2=s2:
                 v.tensor_tensor(s2, x2, scb, ALU.mult))
            p.op("vector", lambda v, od=od, s1=s1, s2=s2:
                 v.tensor_tensor(od[:, :, 0:32], s1, s2, ALU.subtract))
            p.op("vector", lambda v, x1=x1, scb=scb, s1=s1:
                 v.tensor_tensor(s1, x1, scb, ALU.mult))
            p.op("vector", lambda v, x2=x2, ccb=ccb, s2=s2:
                 v.tensor_tensor(s2, x2, ccb, ALU.mult))
            rlast = p.op("vector", lambda v, od=od, s1=s1, s2=s2:
                         v.tensor_tensor(od[:, :, 32:64], s1, s2, ALU.add))
            b.psum_free[col] = rlast
        rope_done = p.engine_tok("vector")
        sk = t % 2
        prsq = []
        prsk = []
        for dc in range(NDC):
            prsq.append((rot[:, dc * 128:(dc + 1) * 128],
                         QT[:, dc * TOK + t * 128: dc * TOK + (t + 1) * 128],
                         rope_done))
            prsk.append((rot[:, 1024 + dc * 128:1024 + (dc + 1) * 128],
                         kfm[sk][:, dc * 128:(dc + 1) * 128],
                         rope_done, kslot_free[sk] if dc == 0 else None))
        emit_transpose(b, prsq, banks=(6, 7))
        emit_transpose(b, prsk, banks=(6, 7))
        ktr = p.engine_tok("vector")
        sk_t = p.dma("sync", c.fm_in.rearrange("(dcc p) n -> p dcc n", p=128)
                     [:, :, t * 128:(t + 1) * 128],
                     kfm[sk].rearrange("p (dcc n) -> p dcc n", n=128),
                     sem="bk", after=ktr)
        stg_k.append(sk_t)
        kslot_free[sk] = sk_t
        rope_done = p.engine_tok("tensor")
    prod_done_t = p.engine_tok("tensor")
    agk = p.op("gpsimd", lambda g: g.collective_compute(
        "AllGather", ALU.bypass, replica_groups=[[0, 1, 2, 3], [4, 5, 6, 7]],
        ins=[c.fm_in.opt()], outs=[c.fm_out.opt()]), after=stg_k[-1])
    agv = p.op("gpsimd", lambda g: g.collective_compute(
        "AllGather", ALU.bypass, replica_groups=[[0, 1, 2, 3], [4, 5, 6, 7]],
        ins=[c.tok_in.opt()], outs=[c.halo_out.opt()]), after=stg_v[-1])
    wo = c.Bb[:, 8192:16384]
    P = c.Bb[:, 0:8192]
    KT = c.Cb[:, 0:8192]
    Vg = c.Cb[:, 8192:16384]
    outfm = c.xhT[:]
    lsums = c.stats[:, 0:20]
    wcst = p.dma("gpsimd", c.w_bf, c.out_w, sem="bw2", after=prod_done_t)
    wld = p.dma("gpsimd", wo.rearrange("p (cc n) -> p cc n", n=D),
                c.w_bf.rearrange("(cc p) n -> p cc n", p=128), sem="bw2",
                after=wcst)
    ob_row = c.S4[:].bitcast(BF16)[0:1, 5120:6144]
    bld = p.dma("gpsimd", ob_row.rearrange("o (x n) -> o x n", x=1),
                c.out_b.rearrange("(o n) -> o n", o=1)
                .rearrange("o (x n) -> o x n", x=1), sem="bw2",
                after=prod_done_t)
    wtok2 = Tok("dma_bw2_gpsimd", p.cnt["dma_bw2_gpsimd"])

    q_ready = p.engine_tok("vector")
    Vge = c.Cb[:, 8192:16512]
    KTb = c.Cb[:, 0:8192]
    vgv = Vge.rearrange("p (ktt n2) -> p ktt n2", n2=130)
    p.op("vector", lambda v: v.memset(vgv[:, :, 64:65], 1.0),
         after=prod_done_t)
    p.op("vector", lambda v: v.memset(vgv[:, :, 129:130], 1.0))
    ones_vge = p.engine_tok("vector")
    f32ev = c.S4[:, 0:512]
    pt_slot = c.S4[:].bitcast(BF16)[:, 2048:3584]
    nscr = c.Bb[:, 0:8192]
    for hg in range(NDC):
        kv_free = p.engine_tok("tensor")
        kt_l = []
        vg_l = []
        for l in range(4):
            kt_l.append(p.dma(
                "sync", KTb.rearrange("p (u l4 n) -> p u l4 n", l4=4, n=128)
                [:, :, l, :],
                c.fm_out.rearrange("(l dcp pp) n -> pp l dcp n", pp=128, dcp=8)
                [:, l, hg, :].rearrange("p (u n) -> p u n", n=128),
                sem="bkv", after=[agk if hg == 0 and l == 0 else None,
                                  kv_free if l == 0 else None]))
            for hh in range(2):
                vg_l.append(p.dma(
                    "sync",
                    Vge.rearrange("p (u l4 hh n) -> p u l4 hh n",
                                  l4=4, hh=2, n=65)[:, :, l, hh, 0:64],
                    c.halo_out.rearrange("(l u pp) d -> pp l u d",
                                         pp=128, u=16)
                    [:, l, :, hg * 128 + hh * 64: hg * 128 + hh * 64 + 64],
                    sem="bkv",
                    after=([agv, stg_k[-1], ones_vge]
                           if hg == 0 and l == 0 and hh == 0 else None)))
        kv_done = vg_l[-1]
        nsq = []
        for which, src, nch in ((0, c.A[:, hg * TOK:(hg + 1) * TOK],
                                 TOK // 512), (1, KTb, S // 512)):
            sq = p.op("scalar", lambda s, src=src: s.activation(
                nscr[:, 0:src.shape[-1]], src, AF.Square),
                after=[kv_done, p.engine_tok("tensor")])
            mx = None
            for chv in range(nch):
                mmq = p.op("tensor", lambda t, chv=chv: t.matmul(
                    b.psum[4][0:2, 0:512], c.ind2_sb[:],
                    nscr[:, chv * 512:(chv + 1) * 512], start=True, stop=True),
                    after=[sq, c.ind2_tok, b.psum_free[4]])
                rm = p.op("vector", lambda v, which=which, chv=chv: (
                    v.reduce_max(c.norm[0:2, which:which + 1],
                                 b.psum[4][0:2, 0:512], axis=AX.X)
                    if chv == 0 else
                    v.tensor_reduce(c.norm[0:2, 2:3], b.psum[4][0:2, 0:512],
                                    axis=AX.X, op=ALU.max)), after=mmq)
                b.psum_free[4] = rm
                if chv > 0:
                    rm = p.op("vector", lambda v, which=which: v.tensor_max(
                        c.norm[0:2, which:which + 1],
                        c.norm[0:2, which:which + 1], c.norm[0:2, 2:3]))
                mx = rm
            nsq.append(mx)
        t3 = p.op("vector", lambda v: v.tensor_mul(
            c.norm[0:2, 3:4], c.norm[0:2, 0:1], c.norm[0:2, 1:2]),
            after=nsq)
        t4 = p.op("scalar", lambda s: s.activation(c.norm[0:2, 3:4],
                                                   c.norm[0:2, 3:4], AF.Sqrt),
                  after=t3)
        t6 = p.op("scalar", lambda s: s.mul(c.norm[0:2, 3:4],
                                            c.norm[0:2, 3:4], -0.125))
        nd1 = p.dma("sync", c.negm_dr.rearrange("o n -> (o n)")
                    .rearrange("(n o) -> n o", o=1), c.norm[0:2, 3:4],
                    sem="bnegm", after=t6)
        nd2 = p.dma("sync", c.negm[:], c.negm_dr.broadcast_to([128, 2]),
                    sem="bnegm", after=nd1)
        emit_mha_st(b, KTb, Vge, c.A[:, hg * TOK:(hg + 1) * TOK], c.negm,
                    outfm, c.mask4[:], 0.125, kv_done, q_ready, nd2,
                    c.cst_tok, f32ev, pt_slot, hg,
                    sbanks=(0, 1, 4), obank=2, t2bank=3)
    att_done = p.engine_tok("vector")
    for t in range(NT):
        for dc2 in range(2):
            bank = dc2
            after0 = [att_done, wld, wtok2, c.ot, b.psum_free[bank]]
            for hc in range(NDC):
                p.op("tensor", lambda tt, t=t, dc2=dc2, hc=hc, bank=bank,
                     st=(hc == 0): tt.matmul(
                         b.psum[bank][:, 0:512],
                         outfm[:, hc * TOK + t * 128: hc * TOK + (t + 1) * 128],
                         wo[:, hc * D + dc2 * 512: hc * D + (dc2 + 1) * 512],
                         start=st, stop=False), after=after0 if hc == 0 else ())
            p.op("tensor", lambda tt, dc2=dc2, bank=bank: tt.matmul(
                b.psum[bank][:, 0:512], c.ones_bf[:],
                ob_row[:, dc2 * 512:(dc2 + 1) * 512], start=False, stop=True))
            mm = p.engine_tok("tensor")
            xs = c.x[:, t * D + dc2 * 512: t * D + (dc2 + 1) * 512]
            ev = p.op("vector", lambda v, xs=xs, bank=bank: v.tensor_add(
                xs, b.psum[bank][:, 0:512], xs), after=mm)
            b.psum_free[bank] = ev



_BUILDERS = {}


def _get(name, fn):
    if name not in _BUILDERS:
        _BUILDERS[name] = fn()
    return _BUILDERS[name]


def kernel(**inputs):
    g = lambda n: np.asarray(inputs[n])
    x = np.ascontiguousarray(g("L0_attn_a_x").reshape(B * S, D))
    b = _get("fused", lambda: build_fused(upto=8))
    maps = []
    for c in range(N_CORES):
        s_t, c_t = rope_tables(c)
        maps.append({
            "xin": shard_att(x, c),
            "cmask4": fused_mask4(c),
            "cmaskb": att_mask(c),
            "sin_t": s_t, "cos_t": c_t,
            "A_g": np.ascontiguousarray(g("L0_attn_b_A")[c * 128:(c + 1) * 128]),
            "B_g": np.ascontiguousarray(g("L0_attn_b_Bm")[c * 128:(c + 1) * 128]),
            "C_g": np.ascontiguousarray(g("L0_attn_b_Cm")[c * 128:(c + 1) * 128]),
            "D_g": np.ascontiguousarray(g("L0_attn_b_Dm")[c * 128:(c + 1) * 128]),
            "idx_prev": fused_gather_idx_prevtile(c),
            "W_dq": g("L0_attn_a_W_dq"), "W_uq": g("L0_attn_a_W_uq"),
            "W_dkv": g("L0_attn_a_W_dkv"), "W_uk": g("L0_attn_a_W_uk"),
            "W_uv": g("L0_attn_a_W_uv"), "W_o": g("L0_attn_a_W_o"),
            "f0w1": np.ascontiguousarray(g("L0_ffn_w1")[c]),
            "f0w3": np.ascontiguousarray(g("L0_ffn_w3")[c]),
            "f0w2": np.ascontiguousarray(g("L0_ffn_w2")[c]),
            "ln_w": g("L0_attn_b_ln_w"), "ln_b": g("L0_attn_b_ln_b"),
            "Wb_out": g("L0_attn_b_W_out"), "bb_out": g("L0_attn_b_b_out"),
            "wqkv_a": g("L1_attn_a_w_qkv"), "wout_a": g("L1_attn_a_w_out"),
            "bout_a": g("L1_attn_a_b_out"),
            "qkv_w": g("L1_attn_b_qkv_w"), "qkv_b": g("L1_attn_b_qkv_b"),
            "out_w": g("L1_attn_b_out_w"), "out_b": g("L1_attn_b_out_b"),
            "Wv": g("L1_ffn_Wv"), "bv": g("L1_ffn_bv"),
            "Wg": g("L1_ffn_Wg"),
            "WgT": np.ascontiguousarray(g("L1_ffn_Wg").T),
            "bg": g("L1_ffn_bg"), "Wp": g("L1_ffn_Wp"),
            "alpha": np.asarray(g("L1_ffn_alpha"), np.float32).reshape(1),
            "g_r": g("L1_ffn_g_r"), "b_r": g("L1_ffn_b_r"),
            "g_c": g("L1_ffn_g_c"), "b_c": g("L1_ffn_b_c"),
            "W1": g("L1_ffn_W1"), "b1": g("L1_ffn_b1"),
            "W2": g("L1_ffn_W2"), "b2": g("L1_ffn_b2"),
        })
    res = _run(b, maps)
    x = unshard_att([res[c]["xout"] for c in range(N_CORES)])
    return np.ascontiguousarray(x.reshape(B, S, D)).astype(np.float32, copy=False)
